# version 02
# baseline (speedup 1.0000x reference)
import numpy as np
import ml_dtypes
import concourse.bass as bass
import concourse.mybir as mybir
from concourse.bass_utils import run_bass_kernel_spmd
from contextlib import ExitStack

F32 = mybir.dt.float32
BF16 = mybir.dt.bfloat16
AF = mybir.ActivationFunctionType
ALU = mybir.AluOpType
AX = mybir.AxisListType

D = 1024
NCK = 8
TPC = 2048
TCH = 512
NT = 4
N16 = 16
E8 = 8
EPS = 1e-5

_UPTO = 99
_SUB = 99
_DBG_PAIRS = 2
_DBG_MC = 8
_DBG_DOPS = 9
_DBG_W1OFF = True
_DBG_NODEL = False

O_X = 1536
O_XNR = O_X + 65536
O_WBUF = O_XNR + 32768
O_BIGA = O_WBUF + 49152
O_BIGB = O_BIGA + 32768
O_SCR = O_BIGB + 16384
O_WST = O_SCR + 16384
O_SMALL = O_WST + 8192

ENG2SEM = {"tensor": "sT", "vector": "sV", "scalar": "sS", "gpsimd": "sG", "sync": None}
ALLSEMS = ("qW", "qP", "sT", "sV", "sS", "sG")


class Prog:
    ENGINES = ("sync", "tensor", "vector", "scalar", "gpsimd")

    def __init__(self):
        self.stream = {e: [] for e in self.ENGINES}
        self.count = {s: 0 for s in ALLSEMS}
        self.buf = {}

    def _b(self, name):
        return self.buf.setdefault(name, {"w": None, "r": []})

    def op(self, eng, fn, r=(), w=(), dma=None, extra=()):
        deps = []
        myeng0 = None if dma else eng
        for name in r:
            b = self._b(name)
            if b["w"]:
                deps.append(b["w"])
            if name.startswith("ps"):
                deps.extend(m for m in b["r"] if m[2] != myeng0)
        for name in w:
            b = self._b(name)
            if b["w"]:
                deps.append(b["w"])
            deps.extend(b["r"])
        waits = {}
        for (sem, val, weng) in deps:
            if val > waits.get(sem, 0):
                waits[sem] = val
        if dma:
            prev = self.count[dma]
            if prev > waits.get(dma, 0):
                waits[dma] = prev
        for (sem, val) in extra:
            if val > waits.get(sem, 0):
                waits[sem] = val
        if dma:
            sem = dma
            self.count[sem] += 16
            mi = (sem, self.count[sem], None)
        else:
            sem = ENG2SEM[eng]
            if sem is None:
                sem = "sG"
            self.count[sem] += 1
            mi = (sem, self.count[sem], eng)
        self.stream[eng].append((tuple(sorted(waits.items())), fn, mi[0], 16 if dma else 1))
        for name in r:
            self._b(name)["r"].append(mi)
        for name in w:
            b = self._b(name)
            b["w"] = mi
            b["r"] = []
        return mi

    def barrier(self):
        snap = [(s, self.count[s]) for s in ALLSEMS if self.count[s] > 0]
        for e in self.ENGINES:
            self.stream[e].append((tuple(snap), None, None, 0))
        self.buf = {}


def build_program(emit_upto=99, sub=None):
    if sub is None:
        sub = _SUB
    _DBG_DOPS = globals()["_DBG_DOPS"]
    _DBG_W1OFF = globals()["_DBG_W1OFF"]
    _DBG_NODEL = globals()["_DBG_NODEL"]
    nc = bass.Bass("TRN2", target_bir_lowering=False, dynamic_dma_scratch_size=512)
    dr = {}

    def din(name, shape, dtype=F32):
        dr[name] = nc.dram_tensor(name, list(shape), dtype, kind="ExternalInput")
        return dr[name]

    din("x_fm", (D, TPC))
    din("aWl1", (D, 128)); din("aWl2", (128, D)); din("aWf1", (2 * D, 128))
    din("aWf2", (128, D)); din("aW1", (D, D)); din("aW2", (D, N16)); din("aW3", (D, N16))
    din("aBpp", (128, 26)); din("aB23", (16, 2)); din("aA", (128, 128)); din("aST", (128, 128))
    din("bIpjT", (D, 3 * D)); din("bOpjT", (D, D)); din("bGpp", (128, 64)); din("bOpb", (128, 8))
    din("fGwT", (D, E8))
    din("fW1T", (E8, D, D)); din("fW2T", (E8, D, D)); din("fW3T", (E8, D, D))
    din("fSw1T", (D, 2 * D)); din("fSw2T", (D, 2 * D)); din("fSw3T", (2 * D, D))
    din("fBpp", (128, 192))
    din("c_ident", (128, 128)); din("c_E16", (128, 128), BF16); din("c_F16", (16, 1024), BF16)
    din("c_const", (128, 8))
    din("c_E16f", (128, 128))
    din("c_sel16", (16, 2048), BF16)
    din("c_sel8", (8, 1024))
    out_fm = nc.dram_tensor("out_fm", [D, TPC], F32, kind="ExternalOutput")

    p = Prog()

    with ExitStack() as ctx:
        at = nc.alloc_sbuf_tensor_at
        X = at("X", [128, 16384], F32, offset=O_X)
        XN = at("XN", [128, 16384], BF16, offset=O_XNR)
        XT32 = at("XT32", [128, 8192], F32, offset=O_XNR)
        ATTO = at("ATTO", [128, 16384], BF16, offset=O_XNR)
        AW = at("AW", [128, 16384], BF16, offset=O_WBUF)
        FW = [at(f"FW{i}", [128, 8192], BF16, offset=O_WBUF + 16384 * i) for i in range(3)]
        KBUF = at("KBUF", [128, 16384], BF16, offset=O_WBUF)
        V1B = at("V1B", [128, 8192], BF16, offset=O_WBUF + 32768)
        A_XN = at("A_XN", [128, 8192], BF16, offset=O_BIGA)
        A_LIF = at("A_LIF", [128, 8192], BF16, offset=O_BIGA + 16384)
        A_DEL = at("A_DEL", [128, 8192], F32, offset=O_BIGA)
        QBUF = at("QBUF", [128, 16384], BF16, offset=O_BIGA)
        FFT = at("FFT", [128, 4096], F32, offset=O_BIGA)
        FFT2 = at("FFT2", [128, 4096], F32, offset=O_BIGA + 16384)
        GTM = at("GTM", [128, 768], F32, offset=O_BIGA + 16384)
        GSML = at("GSML", [128, 64], F32, offset=O_BIGA + 16384 + 3072)
        A_XTB = at("A_XTB", [128, 8192], BF16, offset=O_BIGB)
        HB = at("HB", [128, 4096], BF16, offset=O_BIGB)
        PRB = at("PRB", [128, 8192], BF16, offset=O_BIGB)
        V2B = at("V2B", [128, 8192], BF16, offset=O_BIGB)
        SEL16B = at("SEL16B", [16, 2048], BF16, offset=O_WBUF + 28672)
        FSEL8F = at("FSEL8F", [8, 1024], F32, offset=O_BIGB + 8192)
        FSEL16 = at("FSEL16", [8, 1024], BF16, offset=O_BIGB + 12288)
        BWJ4 = at("BWJ4", [16, 2048], BF16, offset=O_WST + 4096)
        BF16C = at("BF16C", [16, 1024], BF16, offset=O_WST)
        BZB1 = at("BZB1", [128, 512], F32, offset=O_SCR + 12288)
        A_SCR = at("A_SCR", [128, 1536], F32, offset=O_SCR)
        A_WMX = at("A_WMX", [128, 512], F32, offset=O_SCR + 6144)
        A_CM = at("A_CM", [16, 1024], F32, offset=O_SCR + 8192)
        A_CMBF = at("A_CMBF", [16, 1024], BF16, offset=O_SCR + 12288)
        A_CMROW = at("A_CMROW", [1, 512], F32, offset=O_SCR + 14336)
        SFM = at("SFM", [8, 2048], F32, offset=O_SCR)
        MFMB = at("MFMB", [8, 2048], BF16, offset=O_SCR + 8192)
        BSCB = at("BSCB", [16, 2048], F32, offset=O_SCR)
        BMX = at("BMX", [16, 512], F32, offset=O_SCR + 8192)
        BSUM = at("BSUM", [16, 512], F32, offset=O_SCR + 10240)
        BRC = at("BRC", [16, 512], F32, offset=O_SCR + 12288)
        BOAC = at("BOAC", [128, 512], F32, offset=O_SCR + 14336)
        WST = [at(f"WST{i}", [128, 1024], F32, offset=O_WST + 4096 * i) for i in range(2)]
        A_LNROW = at("A_LNROW", [1, 1024], F32, offset=O_WST)
        A_BM = at("A_BM", [16, 512], F32, offset=O_WST + 4096)
        HLB = at("HLB", [128, 1024], BF16, offset=O_WST + 6144)
        BZB = at("BZB", [128, 1024], BF16, offset=O_WST)
        BWJ = at("BWJ", [16, 512], BF16, offset=O_WST + 2048)
        SMB = at("SMB", [128, 424], F32, offset=O_SMALL)
        SM_A = at("SM_A", [128, 128], F32, offset=O_SMALL + 1696)
        SM_AM = at("SM_AM", [128, 128], F32, offset=O_SMALL + 2208)
        SM_NAP = at("SM_NAP", [128, 128], F32, offset=O_SMALL + 2720)
        SM_ST = at("SM_ST", [128, 128], F32, offset=O_SMALL + 3232)
        SM_B23 = at("SM_B23", [16, 2], F32, offset=O_SMALL + 3744)
        IDENT = at("IDENT", [128, 128], F32, offset=O_SMALL + 3776)
        ONES = at("ONES", [1, 128], F32, offset=O_SMALL + 4288)
        BE16F = at("BE16F", [128, 128], F32, offset=O_SMALL + 4800)
        ONESC = at("ONESC", [128, 1], F32, offset=O_SMALL + 6336)
        CC = at("CC", [128, 8], F32, offset=O_SMALL + 6368)
        PS = [ctx.enter_context(nc.psum_tensor(f"ps{i}", [128, 512], F32)) for i in range(8)]
        sems = {s: ctx.enter_context(nc.semaphore(s)) for s in ALLSEMS}
        block = ctx.enter_context(nc.Block())

        def Xs(dc, tc, lo=0, ln=TCH):
            return X[:, dc * 2048 + tc * TCH + lo: dc * 2048 + tc * TCH + lo + ln]

        def xb(dc, tc):
            return f"x{dc}_{tc}"

        def XNs(dc, tc, n=TCH):
            return XN[:, dc * 2048 + tc * TCH: dc * 2048 + tc * TCH + n]

        def dma(q, out, in_, w, r=(), queue_sem="qW"):
            return p.op(q, (lambda o, i: (lambda e: e.dma_start(out=o, in_=i)))(out, in_),
                        r=r, w=w, dma=queue_sem)

        dma("sync", CC[:, :], dr["c_const"][:, :], w=["cc"])
        dma("sync", SEL16B[:, :], dr["c_sel16"][:, :], w=["sel16"])
        dma("sync", BE16F[:, :], dr["c_E16f"][:, :], w=["be16f"])
        for dc in range(NCK):
            dma("sync", X[:, dc * 2048:(dc + 1) * 2048], dr["x_fm"][dc * 128:(dc + 1) * 128, :],
                w=[xb(dc, t) for t in range(NT)])
        dma("sync", SMB[:, 0:26], dr["aBpp"][:, :], w=["smb"])
        dma("sync", SMB[:, 26:90], dr["bGpp"][:, :], w=["smb"])
        dma("sync", SMB[:, 90:282], dr["fBpp"][:, :], w=["smb"])
        dma("sync", SMB[:, 394:402], dr["bOpb"][:, :], w=["smb"])
        for k in range(NCK):
            dma("sync", SMB[:, 282 + k * 8:282 + (k + 1) * 8], dr["fGwT"][k * 128:(k + 1) * 128, :],
                w=["smb"])
        dma("sync", SM_A[:, :], dr["aA"][:, :], w=["sma"])
        dma("sync", SM_ST[:, :], dr["aST"][:, :], w=["smst"])
        dma("sync", SM_B23[:, :], dr["aB23"][:, :], w=["smb23"])
        dma("sync", IDENT[:, :], dr["c_ident"][:, :], w=["ident"])
        p.op("vector", lambda e: e.tensor_scalar_min(SM_AM[:, :], SM_A[:, :], 0.0),
             r=["sma"], w=["smam"])
        p.op("vector", lambda e: e.tensor_scalar(SM_NAP[:, :], SM_A[:, :], 0.0, -1.0, ALU.max, ALU.mult),
             r=["sma"], w=["smnap"])
        p.op("vector", lambda e: e.tensor_tensor(SMB[:, 346:370], SMB[:, 26:50], SMB[:, 50:74], ALU.mult),
             r=["smb"], w=["smb"])
        p.op("vector", lambda e: e.tensor_copy(SMB[:, 370:394], SMB[:, 26:50]), r=["smb"], w=["smb"])
        p.op("vector", lambda e: e.tensor_scalar_mul(SMB[:, 370:378], SMB[:, 370:378], 0.125),
             r=["smb"], w=["smb"])
        p.op("vector", lambda e: e.tensor_scalar_mul(SMB[:, 346:354], SMB[:, 346:354], 0.125),
             r=["smb"], w=["smb"])
        p.op("vector", lambda e: e.tensor_tensor(SMB[:, 402:410], SMB[:, 82:90], SMB[:, 394:402], ALU.mult),
             r=["smb"], w=["smb"])
        p.op("vector", lambda e: e.memset(ONES[:, :], 1.0), w=["ones"])
        p.op("vector", lambda e: e.memset(ONESC[:, :], 1.0), w=["onesc"])

        AWO = {"wl1": 0, "wl2": 1024, "wf1": 2048, "wf2": 4096, "w1": 5120, "w2": 13312, "w3": 13440}
        aw_specs = []
        aw_specs += [(dr["aWl1"], k * 128, 128, AWO["wl1"] + k * 128) for k in range(8)]
        aw_specs += [(dr["aWl2"], 0, 1024, AWO["wl2"])]
        aw_specs += [(dr["aWf1"], k * 128, 128, AWO["wf1"] + k * 128) for k in range(16)]
        aw_specs += [(dr["aWf2"], 0, 1024, AWO["wf2"])]
        aw_specs += [(dr["aW1"], k * 128, 1024, AWO["w1"] + k * 1024) for k in range(8)]
        aw_specs += [(dr["aW2"], k * 128, 16, AWO["w2"] + k * 16) for k in range(8)]
        aw_specs += [(dr["aW3"], k * 128, 16, AWO["w3"] + k * 16) for k in range(8)]
        for i, (dt_, r0, ncols, ao) in enumerate(aw_specs):
            st = WST[i % 2]
            stn = f"wst{i % 2}"
            dma("sync", st[:, 0:ncols], dt_[r0:r0 + 128, :], w=[stn])
            p.op("scalar",
                 (lambda s_, n_, a_: (lambda e: e.copy(AW[:, a_:a_ + n_], s_[:, 0:n_])))(st, ncols, ao),
                 r=[stn, "cc"], w=["aw"])

        p.barrier()

        def emit_ln(tc, dst_fn, rowbuf, sqbuf, rown, sqn, xn32=None):
            dsts = [dst_fn(d) for d in range(NCK)]
            b_s1, b_s2, b_mb, b_rb = PS[4], PS[5], PS[6], PS[7]
            for dc in range(NCK):
                p.op("tensor",
                     (lambda d: (lambda e: e.matmul(b_s1[0:1, :], ONESC[:, :], Xs(d, tc),
                                                    start=(d == 0), stop=(d == NCK - 1))))(dc),
                     r=["onesc", xb(dc, tc)], w=(["ps4"] if dc == 0 else ["ps4"]))
            for dc in range(NCK):
                p.op("scalar",
                     (lambda d: (lambda e: e.activation(sqbuf[:, :], Xs(d, tc), AF.Square)))(dc),
                     r=[xb(dc, tc)], w=[sqn])
                p.op("tensor",
                     (lambda d: (lambda e: e.matmul(b_s2[0:1, :], ONESC[:, :], sqbuf[:, :],
                                                    start=(d == 0), stop=(d == NCK - 1))))(dc),
                     r=["onesc", sqn], w=["ps5"])
            mrow = rowbuf[0:1, 0:512]
            rrow = rowbuf[0:1, 512:1024]
            p.op("vector", lambda e, m=mrow: e.tensor_scalar_mul(m, b_s1[0:1, :], 1.0 / D),
                 r=["ps4"], w=[rown])
            p.op("vector", lambda e, rr=rrow: e.tensor_scalar_mul(rr, b_s2[0:1, :], 1.0 / D),
                 r=["ps5"], w=[rown])
            p.op("vector", lambda e, m=mrow: e.tensor_tensor(sqbuf[0:1, 0:512], m, m, ALU.mult),
                 r=[rown], w=[sqn])
            p.op("vector", lambda e, rr=rrow: e.tensor_tensor(rr, rr, sqbuf[0:1, 0:512], ALU.subtract),
                 r=[rown, sqn], w=[rown])
            p.op("scalar", lambda e, rr=rrow: e.activation(rr, rr, AF.Ln, bias=CC[0:1, 0:1]),
                 r=[rown, "cc"], w=[rown])
            p.op("scalar", lambda e, rr=rrow: e.activation(rr, rr, AF.Exp, scale=CC[0:1, 3:4]),
                 r=[rown, "cc"], w=[rown])
            p.op("tensor", lambda e, m=mrow: e.matmul(b_mb[:, :], ONES[0:1, 0:128], m, start=True, stop=True),
                 r=["ones", rown], w=["ps6"])
            p.op("tensor", lambda e, rr=rrow: e.matmul(b_rb[:, :], ONES[0:1, 0:128], rr, start=True, stop=True),
                 r=["ones", rown], w=["ps7"])
            for dc in range(NCK):
                p.op("vector",
                     (lambda d: (lambda e: e.tensor_tensor(sqbuf[:, :], Xs(d, tc), b_mb[:, :], ALU.subtract)))(dc),
                     r=[xb(dc, tc), "ps6"], w=[sqn])
                if xn32 is not None:
                    p.op("vector",
                         (lambda d: (lambda e: e.tensor_tensor(xn32[:, d * TCH:(d + 1) * TCH], sqbuf[:, :],
                                                               b_rb[:, :], ALU.mult)))(dc),
                         r=[sqn, "ps7"], w=["xn32"])
                    p.op("scalar",
                         (lambda d, dst=dsts[dc]: (lambda e: e.copy(dst, xn32[:, d * TCH:(d + 1) * TCH])))(dc),
                         r=["xn32"], w=["xn" + str(dc)])
                else:
                    p.op("vector",
                         (lambda d, dst=dsts[dc]: (lambda e: e.tensor_tensor(dst, sqbuf[:, :], b_rb[:, :], ALU.mult)))(dc),
                         r=[sqn, "ps7"], w=["xn" + str(dc)])

        if emit_upto >= 1:
            for pr in range(2):
                tcs = (2 * pr, 2 * pr + 1)
                for j, tc in enumerate(tcs):
                    emit_ln(tc, (lambda jj: (lambda d: A_XN[:, jj * 4096 + d * TCH:jj * 4096 + (d + 1) * TCH]))(j),
                            A_LNROW, A_WMX, "lnrow", "wmx")
                    for dc in range(NCK):
                        p.buf[f"axn{j}_{dc}"] = p.buf.pop("xn" + str(dc))
                for j, tc in enumerate(tcs if sub >= 2 else ()):
                    hl = HLB[:, j * 512:(j + 1) * 512]
                    hln = f"hlb{j}"
                    b_h = PS[0]
                    for k in range(NCK):
                        p.op("tensor",
                             (lambda kk, jj: (lambda e: e.matmul(
                                 b_h[:, :], AW[:, AWO["wl1"] + kk * 128:AWO["wl1"] + (kk + 1) * 128],
                                 A_XN[:, jj * 4096 + kk * TCH:jj * 4096 + (kk + 1) * TCH],
                                 start=(kk == 0), stop=(kk == NCK - 1))))(k, j),
                             r=["aw", f"axn{j}_{k}"], w=["ps0"])
                    p.op("scalar", lambda e, h=hl: e.activation(h, b_h[:, :], AF.Relu, bias=SMB[:, 0:1]),
                         r=["ps0", "smb"], w=[hln])
                    for mc in range(NCK):
                        bi = mc % 2
                        bnk = PS[bi]
                        p.op("tensor",
                             (lambda m, h=hl, b=bnk: (lambda e: e.matmul(
                                 b[:, :], AW[:, AWO["wl2"] + m * 128:AWO["wl2"] + (m + 1) * 128], h,
                                 start=True, stop=True)))(mc),
                             r=["aw", hln], w=[f"ps{bi}"])
                        p.op("scalar",
                             (lambda m, jj, b=bnk: (lambda e: e.activation(
                                 A_LIF[:, jj * 4096 + m * TCH:jj * 4096 + (m + 1) * TCH], b[:, :],
                                 AF.Tanh, bias=SMB[:, 1 + m:2 + m])))(mc, j),
                             r=[f"ps{bi}", "smb"], w=[f"alif{j}_{mc}"])
                    b_h1 = PS[2]
                    for k in range(16):
                        src = A_XN if k < 8 else A_LIF
                        srcn = (f"axn{j}_{k}" if k < 8 else f"alif{j}_{k - 8}")
                        kk = k % 8
                        p.op("tensor",
                             (lambda sr, k2, kk2, jj: (lambda e: e.matmul(
                                 b_h1[:, :], AW[:, AWO["wf1"] + k2 * 128:AWO["wf1"] + (k2 + 1) * 128],
                                 sr[:, jj * 4096 + kk2 * TCH:jj * 4096 + (kk2 + 1) * TCH],
                                 start=(k2 == 0), stop=(k2 == 15))))(src, k, kk, j),
                             r=["aw", srcn], w=["ps2"])
                    p.op("scalar", lambda e, h=hl: e.activation(h, b_h1[:, :], AF.Relu, bias=SMB[:, 9:10]),
                         r=["ps2", "smb"], w=[hln])
                    for mc in range(NCK):
                        bi = mc % 2
                        bnk = PS[bi]
                        p.op("tensor",
                             (lambda m, h=hl, b=bnk: (lambda e: e.matmul(
                                 b[:, :], AW[:, AWO["wf2"] + m * 128:AWO["wf2"] + (m + 1) * 128], h,
                                 start=True, stop=True)))(mc),
                             r=["aw", hln], w=[f"ps{bi}"])
                        p.op("vector",
                             (lambda m, jj, b=bnk: (lambda e: e.tensor_scalar_add(
                                 XT32[:, jj * 4096 + m * TCH:jj * 4096 + (m + 1) * TCH], b[:, :],
                                 SMB[:, 10 + m:11 + m])))(mc, j),
                             r=[f"ps{bi}", "smb"], w=[f"xt32_{j}_{mc}"])
                        p.op("vector",
                             (lambda m, jj: (lambda e: e.tensor_copy(
                                 A_XTB[:, jj * 4096 + m * TCH:jj * 4096 + (m + 1) * TCH],
                                 XT32[:, jj * 4096 + m * TCH:jj * 4096 + (m + 1) * TCH])))(mc, j),
                             r=[f"xt32_{j}_{mc}"], w=[f"xtb{j}_{mc}"])
                for j, tc in enumerate(tcs if sub >= 3 else ()):
                    if pr > _DBG_PAIRS - 1:
                        continue
                    for mc in range(min(NCK, _DBG_MC)):
                        bi = mc % 2
                        bnk = PS[bi]
                        w1base = AWO["w1"] if _DBG_W1OFF else AWO["wl1"]
                        for k in range(NCK):
                            off = (w1base + kk2 * 1024 + mc * 128 if _DBG_W1OFF else AWO["wl1"]) if False else None
                            p.op("tensor",
                                 (lambda kk, m, jj, b=bnk: (lambda e: e.matmul(
                                     b[:, :],
                                     (AW[:, AWO["w1"] + kk * 1024 + m * 128:AWO["w1"] + kk * 1024 + (m + 1) * 128]
                                      if _DBG_W1OFF else
                                      AW[:, AWO["wl1"] + kk * 128:AWO["wl1"] + (kk + 1) * 128]),
                                     A_XTB[:, jj * 4096 + kk * TCH:jj * 4096 + (kk + 1) * TCH],
                                     start=(kk == 0), stop=(kk == NCK - 1))))(k, mc, j),
                                 r=["aw", f"xtb{j}_{k}"], w=[f"ps{bi}"])
                        bcol = SMB[:, 18 + mc:19 + mc]
                        if _DBG_DOPS >= 1:
                            p.op("vector",
                                 (lambda b=bnk, bc=bcol: (lambda e: e.tensor_scalar(
                                     A_WMX[:, :], b[:, :], bc, 0.0, ALU.add, ALU.max)))(),
                                 r=[f"ps{bi}", "smb"], w=["wmx"])
                        else:
                            p.op("vector", (lambda b=bnk: (lambda e: e.tensor_copy(A_WMX[:, :], b[:, :])))(),
                                 r=[f"ps{bi}"], w=["wmx"])
                        if _DBG_DOPS >= 2:
                            p.op("scalar",
                                 (lambda b=bnk, bc=bcol: (lambda e: e.activation(
                                     A_SCR[:, 0:512], b[:, :], AF.Abs, bias=bc)))(),
                                 r=[f"ps{bi}", "smb"], w=["scrP"])
                        else:
                            p.op("scalar", (lambda b=bnk: (lambda e: e.copy(A_SCR[:, 0:512], b[:, :])))(),
                                 r=[f"ps{bi}"], w=["scrP"])
                        if _DBG_DOPS >= 3:
                            p.op("scalar", lambda e: e.activation(A_SCR[:, 512:1024], A_SCR[:, 0:512],
                                                                  AF.Exp, scale=CC[:, 2:3]),
                                 r=["scrP", "cc"], w=["scrT"])
                        else:
                            p.op("scalar", lambda e: e.copy(A_SCR[:, 512:1024], A_SCR[:, 0:512]),
                                 r=["scrP"], w=["scrT"])
                        if _DBG_DOPS >= 4:
                            p.op("scalar", lambda e: e.activation(A_SCR[:, 1024:1536], A_SCR[:, 512:1024],
                                                                  AF.Ln, bias=CC[:, 1:2]),
                                 r=["scrT", "cc"], w=["scrF"])
                        else:
                            p.op("scalar", lambda e: e.copy(A_SCR[:, 1024:1536], A_SCR[:, 512:1024]),
                                 r=["scrT"], w=["scrF"])
                        if _DBG_NODEL:
                            p.op("vector",
                                 lambda e: e.tensor_tensor(A_WMX[:, :], A_WMX[:, :], A_SCR[:, 1024:1536],
                                                           ALU.add),
                                 r=["wmx", "scrF"], w=["wmx"])
                            p.buf[f"del{j}_{mc}"] = {"w": None, "r": []}
                        else:
                            p.op("vector",
                                 (lambda m, jj: (lambda e: e.tensor_tensor(
                                     A_DEL[:, jj * 4096 + m * TCH:jj * 4096 + (m + 1) * TCH],
                                     A_WMX[:, :], A_SCR[:, 1024:1536], ALU.add)))(mc, j),
                                 r=["wmx", "scrF"] + [f"axn{j}_{mc}", f"alif{j}_{mc}"],
                                 w=[f"del{j}_{mc}"])
                for j, tc in enumerate(tcs if sub >= 4 else ()):
                    b_bc = PS[2]
                    for wname, dst, dstn, bc in (("w2", A_BM[:, :], "abm", SM_B23[:, 0:1]),
                                                 ("w3", A_CM[:, j * 512:(j + 1) * 512], f"acm{j}", SM_B23[:, 1:2])):
                        for k in range(NCK):
                            p.op("tensor",
                                 (lambda kk, wn, jj: (lambda e: e.matmul(
                                     b_bc[0:16, :], AW[:, AWO[wn] + kk * 16:AWO[wn] + (kk + 1) * 16],
                                     A_XTB[:, jj * 4096 + kk * TCH:jj * 4096 + (kk + 1) * TCH],
                                     start=(kk == 0), stop=(kk == NCK - 1))))(k, wname, j),
                                 r=["aw", f"xtb{j}_{k}"], w=["ps2"])
                        p.op("vector",
                             (lambda d2=dst, b2=bc: (lambda e: e.tensor_scalar_add(d2, b_bc[0:16, :], b2)))(),
                             r=["ps2", "smb23"], w=[dstn])
                    p.op("vector",
                         (lambda jj: (lambda e: e.tensor_copy(A_CMBF[:, jj * 512:(jj + 1) * 512],
                                                              A_CM[:, jj * 512:(jj + 1) * 512])))(j),
                         r=[f"acm{j}"], w=[f"acmbf{j}"])
                    p.op("vector",
                         (lambda jj: (lambda e: e.tensor_tensor(A_BM[:, :], A_BM[:, :],
                                                                A_CM[:, jj * 512:(jj + 1) * 512], ALU.mult)))(j),
                         r=["abm", f"acm{j}"], w=["abm"])
                    p.op("tensor", lambda e: e.matmul(PS[3][0:1, :], ONESC[0:16, :], A_BM[:, :],
                                                      start=True, stop=True),
                         r=["onesc", "abm"], w=["ps3"])
                    p.op("scalar", lambda e: e.copy(A_CMROW[0:1, 0:512], PS[3][0:1, :]),
                         r=["ps3"], w=["cmrow0"])
                    p.op("tensor", lambda e: e.matmul(PS[3][:, :], ONES[0:1, 0:128], A_CMROW[0:1, 0:512],
                                                      start=True, stop=True),
                         r=["ones", "cmrow0"], w=["ps3"])
                    for dc in range(NCK):
                        p.op("vector",
                             (lambda d, jj: (lambda e: e.tensor_tensor(
                                 A_WMX[:, :], XT32[:, jj * 4096 + d * TCH:jj * 4096 + (d + 1) * TCH],
                                 A_DEL[:, jj * 4096 + d * TCH:jj * 4096 + (d + 1) * TCH], ALU.mult)))(dc, j),
                             r=[f"xt32_{j}_{dc}", f"del{j}_{dc}"], w=["wmx"])
                        p.op("vector", lambda e: e.tensor_tensor(A_WMX[:, :], A_WMX[:, :], PS[3][:, :], ALU.mult),
                             r=["wmx", "ps3"], w=["wmx"])
                        p.op("vector",
                             (lambda d, t: (lambda e: e.tensor_tensor(Xs(d, t), Xs(d, t), A_WMX[:, :],
                                                                      ALU.add)))(dc, tc),
                             r=["wmx", xb(dc, tc)], w=[xb(dc, tc)])
                for ni in range(N16 if sub >= 5 else 0):
                    for j, tc in enumerate(tcs):
                        ridx = (ni * 2 + j) % 2
                        bi = 2 + ridx
                        bnk = PS[bi]
                        p.op("tensor",
                             (lambda nn, jj, b=bnk: (lambda e: e.matmul(
                                 b[:, :], SEL16B[:, nn * 128:(nn + 1) * 128],
                                 A_CMBF[:, jj * 512:(jj + 1) * 512], start=True, stop=True)))(ni, j),
                             r=["sel16", f"acmbf{j}"], w=[f"ps{bi}"])
                        for dc in range(NCK):
                            ca = dc * 16 + ni
                            p.op("scalar",
                                 (lambda d, jj, c: (lambda e: e.activation(
                                     A_SCR[:, 0:512], A_DEL[:, jj * 4096 + d * TCH:jj * 4096 + (d + 1) * TCH],
                                     AF.Exp, scale=SM_AM[:, c:c + 1])))(dc, j, ca),
                                 r=[f"del{j}_{dc}", "smam"], w=["scrP"])
                            p.op("vector",
                                 (lambda d, jj, c: (lambda e: e.scalar_tensor_tensor(
                                     A_SCR[:, 512:1024], A_DEL[:, jj * 4096 + d * TCH:jj * 4096 + (d + 1) * TCH],
                                     SM_NAP[:, c:c + 1], A_SCR[:, 0:512], ALU.mult, ALU.subtract)))(dc, j, ca),
                                 r=[f"del{j}_{dc}", "smnap", "scrP"], w=["scrT"])
                            p.op("scalar", lambda e: e.activation(A_SCR[:, 1024:1536], A_SCR[:, 512:1024],
                                                                  AF.Exp, bias=CC[:, 1:2]),
                                 r=["scrT", "cc"], w=["scrF"])
                            p.op("vector",
                                 (lambda c, b=bnk: (lambda e: e.scalar_tensor_tensor(
                                     A_WMX[:, :], A_SCR[:, 1024:1536], SM_ST[:, c:c + 1], b[:, :],
                                     ALU.mult, ALU.mult)))(ca),
                                 r=["scrF", "smst", f"ps{bi}"], w=["wmx"])
                            p.op("vector",
                                 (lambda d, t: (lambda e: e.tensor_tensor(Xs(d, t), Xs(d, t), A_WMX[:, :],
                                                                          ALU.add)))(dc, tc),
                                 r=["wmx", xb(dc, tc)], w=[xb(dc, tc)])
                p.barrier()


        def emit_ffn(call_idx):
            ci = str(call_idx)
            FLNROW = A_CM
            flnrow = nc.alloc_sbuf_tensor_at(f"FLNROW{ci}", [1, 1024], F32, offset=O_SCR + 12288)
            sq = FFT2[:, 3584:4096]
            p.op("vector", lambda e: e.memset(XN[:, :], 0.0), w=["xnall"])
            for tc in range(NT):
                emit_ln(tc, (lambda d: XNs(d, tc)), flnrow, sq, "flnrow", "fsq", xn32=FFT)
                for dc in range(NCK):
                    p.buf[f"fxn{tc}_{dc}"] = p.buf.pop("xn" + str(dc))
                for k in range(NCK):
                    p.op("tensor",
                         (lambda kk, t: (lambda e: e.matmul(
                             PS[2][0:8, :], SMB[:, 282 + kk * 8:282 + (kk + 1) * 8],
                             FFT[:, kk * TCH:(kk + 1) * TCH], start=(kk == 0), stop=(kk == NCK - 1))))(k, tc),
                         r=["smb", "xn32"], w=["ps2"])
                p.op("scalar",
                     (lambda t: (lambda e: e.copy(SFM[:, t * TCH:(t + 1) * TCH], PS[2][0:8, :])))(tc),
                     r=["ps2"], w=["glog"])
            for i in range(16):
                p.op("tensor",
                     (lambda ii: (lambda e: e.transpose(PS[3][:, ii * 8:(ii + 1) * 8],
                                                        SFM[0:8, ii * 128:(ii + 1) * 128],
                                                        IDENT[0:8, 0:8])))(i),
                     r=["glog", "ident"], w=["ps3"])
            p.op("scalar", lambda e: e.copy(GTM[:, 0:128], PS[3][:, 0:128]), r=["ps3"], w=["gt_l"])
            p.op("scalar", lambda e: e.activation(GTM[:, 128:256], GTM[:, 0:128], AF.Exp),
                 r=["gt_l"], w=["gt_e"])
            p.op("vector", lambda e: e.tensor_reduce(GSML[:, 0:16],
                                                     GTM[:, 128:256].rearrange("p (a b) -> p a b", b=8),
                                                     AX.X, ALU.add),
                 r=["gt_e"], w=["g_sum"])
            p.op("vector", lambda e: e.reciprocal(GSML[:, 16:32], GSML[:, 0:16]), r=["g_sum"], w=["g_rcp"])
            for i in range(16):
                p.op("vector",
                     (lambda ii: (lambda e: e.tensor_scalar_mul(GTM[:, 256 + ii * 8:256 + (ii + 1) * 8],
                                                                GTM[:, 128 + ii * 8:128 + (ii + 1) * 8],
                                                                GSML[:, 16 + ii:17 + ii])))(i),
                     r=["gt_e", "g_rcp"], w=["gt_s"])
            p.op("vector", lambda e: e.tensor_reduce(GSML[:, 32:48],
                                                     GTM[:, 256:384].rearrange("p (a b) -> p a b", b=8),
                                                     AX.X, ALU.max),
                 r=["gt_s"], w=["g_m1"])
            for i in range(16):
                p.op("vector",
                     (lambda ii: (lambda e: e.tensor_scalar(GTM[:, 384 + ii * 8:384 + (ii + 1) * 8],
                                                            GTM[:, 256 + ii * 8:256 + (ii + 1) * 8],
                                                            GSML[:, 32 + ii:33 + ii], None, ALU.is_ge)))(i),
                     r=["gt_s", "g_m1"], w=["gt_ge"])
            p.op("vector", lambda e: e.scalar_tensor_tensor(GTM[:, 384:512], GTM[:, 384:512], -1e30,
                                                            GTM[:, 256:384], ALU.mult, ALU.add),
                 r=["gt_ge", "gt_s"], w=["gt_mk"])
            p.op("vector", lambda e: e.tensor_reduce(GSML[:, 48:64],
                                                     GTM[:, 384:512].rearrange("p (a b) -> p a b", b=8),
                                                     AX.X, ALU.max),
                 r=["gt_mk"], w=["g_m2"])
            for i in range(16):
                p.op("vector",
                     (lambda ii: (lambda e: e.tensor_scalar(GTM[:, 640 + ii * 8:640 + (ii + 1) * 8],
                                                            GTM[:, 256 + ii * 8:256 + (ii + 1) * 8],
                                                            GSML[:, 48 + ii:49 + ii], None, ALU.is_ge)))(i),
                     r=["gt_s", "g_m2"], w=["gt_mask"])
            for tc in range(NT):
                for w4 in range(4):
                    i = tc * 4 + w4
                    p.op("tensor",
                         (lambda ii, ww: (lambda e: e.transpose(PS[2][0:8, ww * 128:(ww + 1) * 128],
                                                                GTM[:, 256 + ii * 8:256 + (ii + 1) * 8],
                                                                IDENT[:, 0:128])))(i, w4),
                         r=["gt_s", "ident"], w=["ps2"])
                p.op("scalar",
                     (lambda t: (lambda e: e.copy(SFM[:, t * TCH:(t + 1) * TCH], PS[2][0:8, :])))(tc),
                     r=["ps2"], w=["sfm"])
                for w4 in range(4):
                    i = tc * 4 + w4
                    p.op("tensor",
                         (lambda ii, ww: (lambda e: e.transpose(PS[3][0:8, ww * 128:(ww + 1) * 128],
                                                                GTM[:, 640 + ii * 8:640 + (ii + 1) * 8],
                                                                IDENT[:, 0:128])))(i, w4),
                         r=["gt_mask", "ident"], w=["ps3"])
                p.op("scalar",
                     (lambda t: (lambda e: e.copy(MFMB[:, t * TCH:(t + 1) * TCH], PS[3][0:8, :])))(tc),
                     r=["ps3"], w=["mfm"])
            p.barrier()

            stcnt = [0]

            def stage(drt, row0, colspec, slot, sloto, slotname):
                i = stcnt[0]
                stcnt[0] += 1
                st = WST[i % 2]
                stn = f"fwst{i % 2}"
                dma("sync", st[:, 0:1024], drt[row0:row0 + 128, colspec[0]:colspec[1]]
                    if colspec else drt[row0:row0 + 128, :], w=[stn])
                p.op("scalar",
                     (lambda s_, so: (lambda e: e.copy(FW[slot][:, so:so + 1024], s_[:, 0:1024])))(st, sloto),
                     r=[stn, "cc"], w=[slotname])

            for h in range(2):
                for kc in range(NCK):
                    stage(dr["fSw1T"], kc * 128, (h * 1024, (h + 1) * 1024), 0, kc * 1024, "fw0")
                for kc in range(NCK):
                    stage(dr["fSw2T"], kc * 128, (h * 1024, (h + 1) * 1024), 1, kc * 1024, "fw1")
                for icl in range(8):
                    stage(dr["fSw3T"], (h * 8 + icl) * 128, None, 2, icl * 1024, "fw2")
                for tc in range(NT):
                    for icl in range(8):
                        bi = icl % 2
                        for kc in range(NCK):
                            p.op("tensor",
                                 (lambda k2, il, t, b=PS[icl % 2]: (lambda e: e.matmul(
                                     b[:, :], FW[0][:, k2 * 1024 + il * 128:k2 * 1024 + (il + 1) * 128],
                                     XNs(k2, t), start=(k2 == 0), stop=(k2 == NCK - 1))))(kc, icl, tc),
                                 r=["fw0", f"fxn{tc}_{kc}"], w=[f"ps{icl % 2}"])
                        for kc in range(NCK):
                            p.op("tensor",
                                 (lambda k2, il, t, b=PS[2 + icl % 2]: (lambda e: e.matmul(
                                     b[:, :], FW[1][:, k2 * 1024 + il * 128:k2 * 1024 + (il + 1) * 128],
                                     XNs(k2, t), start=(k2 == 0), stop=(k2 == NCK - 1))))(kc, icl, tc),
                                 r=["fw1", f"fxn{tc}_{kc}"], w=[f"ps{2 + icl % 2}"])
                        p.op("scalar",
                             (lambda b=PS[icl % 2]: (lambda e: e.activation(FFT[:, 0:512], b[:, :],
                                                                            AF.Silu)))(),
                             r=[f"ps{icl % 2}"], w=["fab"])
                        p.op("vector",
                             (lambda il, b=PS[2 + icl % 2]: (lambda e: e.tensor_tensor(
                                 PRB[:, il * 512:(il + 1) * 512], FFT[:, 0:512], b[:, :], ALU.mult)))(icl),
                             r=["fab", f"ps{2 + icl % 2}"], w=[f"prb{icl}"])
                    for dc in range(NCK):
                        bi = 4 + dc % 2
                        for icl in range(8):
                            p.op("tensor",
                                 (lambda il, d2, b=PS[4 + dc % 2]: (lambda e: e.matmul(
                                     b[:, :], FW[2][:, il * 1024 + d2 * 128:il * 1024 + (d2 + 1) * 128],
                                     PRB[:, il * 512:(il + 1) * 512],
                                     start=(il == 0), stop=(il == 7))))(icl, dc),
                                 r=["fw2"] + [f"prb{il2}" for il2 in range(8)], w=[f"ps{4 + dc % 2}"])
                        p.op("vector",
                             (lambda d2, t=tc, b=PS[4 + dc % 2]: (lambda e: e.tensor_tensor(
                                 Xs(d2, t), Xs(d2, t), b[:, :], ALU.add)))(dc),
                             r=[f"ps{4 + dc % 2}", xb(dc, tc)], w=[xb(dc, tc)])

            def stage_expert_matrix(e_i, mat, slot, slotname):
                drt = {0: dr["fW1T"], 1: dr["fW3T"], 2: dr["fW2T"]}[mat]
                for kc in range(NCK):
                    i = stcnt[0]
                    stcnt[0] += 1
                    st = WST[i % 2]
                    stn = f"fwst{i % 2}"
                    dma("sync", st[:, 0:1024], drt[e_i, kc * 128:(kc + 1) * 128, :], w=[stn])
                    p.op("scalar",
                         (lambda s_, so: (lambda e: e.copy(FW[slot][:, so:so + 1024], s_[:, 0:1024])))(st, kc * 1024),
                         r=[stn, "cc"], w=[slotname])

            dma("sync", FSEL8F[:, :], dr["c_sel8"][:, :], w=["fsel8"])
            dma("sync", FSEL16[:, :], dr["c_sel16"][0:8, 0:1024], w=["fsel16m"])
            stage_expert_matrix(0, 0, 0, "fw0")
            stage_expert_matrix(0, 1, 1, "fw1")
            stage_expert_matrix(0, 2, 2, "fw2")
            for e_i in range(E8):
                for tc in range(NT):
                    p.op("tensor",
                         (lambda ee, t: (lambda e: e.matmul(PS[2][:, :], FSEL8F[:, ee * 128:(ee + 1) * 128],
                                                            SFM[0:8, t * TCH:(t + 1) * TCH],
                                                            start=True, stop=True)))(e_i, tc),
                         r=["fsel8", "sfm"], w=["ps2"])
                    p.op("scalar", (lambda: (lambda e: e.copy(FFT[:, 1536:2048], PS[2][:, :])))(),
                         r=["ps2"], w=["fsbb"])
                    for hc in range(NCK):
                        for kc in range(NCK):
                            p.op("tensor",
                                 (lambda k2, h2, t, b=PS[hc % 2]: (lambda e: e.matmul(
                                     b[:, :], FW[0][:, k2 * 1024 + h2 * 128:k2 * 1024 + (h2 + 1) * 128],
                                     XNs(k2, t), start=(k2 == 0), stop=(k2 == NCK - 1))))(kc, hc, tc),
                                 r=["fw0", f"fxn{tc}_{kc}"], w=[f"ps{hc % 2}"])
                        p.op("vector",
                             (lambda b=PS[hc % 2]: (lambda e: e.tensor_tensor(FFT[:, 0:512], b[:, :],
                                                                              FFT[:, 1536:2048], ALU.mult)))(),
                             r=[f"ps{hc % 2}", "fsbb"], w=["fhp"])
                        p.op("scalar",
                             (lambda ee, h2: (lambda e: e.activation(HB[:, h2 * 512:(h2 + 1) * 512],
                                                                     FFT[:, 0:512], AF.Silu,
                                                                     bias=SMB[:, 90 + ee * 8 + h2:91 + ee * 8 + h2])))(e_i, hc),
                             r=["fhp", "smb"], w=[f"fh{hc}"])
                    p.op("tensor",
                         (lambda ee, t: (lambda e: e.matmul(PS[3][:, :], FSEL16[:, ee * 128:(ee + 1) * 128],
                                                            MFMB[0:8, t * TCH:(t + 1) * TCH],
                                                            start=True, stop=True)))(e_i, tc),
                         r=["fsel16m", "mfm"], w=["ps3"])
                    for dc in range(NCK):
                        for kc in range(NCK):
                            p.op("tensor",
                                 (lambda k2, d2, t, b=PS[4 + dc % 2]: (lambda e: e.matmul(
                                     b[:, :], FW[1][:, k2 * 1024 + d2 * 128:k2 * 1024 + (d2 + 1) * 128],
                                     XNs(k2, t), start=(k2 == 0), stop=(k2 == NCK - 1))))(kc, dc, tc),
                                 r=["fw1", f"fxn{tc}_{kc}"], w=[f"ps{4 + dc % 2}"])
                        p.op("vector",
                             (lambda b=PS[4 + dc % 2]: (lambda e: e.tensor_tensor(
                                 FFT[:, 512:1024], b[:, :], FFT[:, 1536:2048], ALU.mult)))(),
                             r=[f"ps{4 + dc % 2}", "fsbb"], w=["ft3"])
                        p.op("vector",
                             (lambda ee, d2: (lambda e: e.tensor_scalar_add(
                                 FFT2[:, d2 * 512:(d2 + 1) * 512], FFT[:, 512:1024],
                                 SMB[:, 218 + ee * 8 + d2:219 + ee * 8 + d2])))(e_i, dc),
                             r=["ft3", "smb"], w=[f"fx3_{dc}"])
                    for dc in range(NCK):
                        for hc in range(NCK):
                            p.op("tensor",
                                 (lambda h2, d2, b=PS[6 + dc % 2]: (lambda e: e.matmul(
                                     b[:, :], FW[2][:, h2 * 1024 + d2 * 128:h2 * 1024 + (d2 + 1) * 128],
                                     HB[:, h2 * 512:(h2 + 1) * 512],
                                     start=(h2 == 0), stop=(h2 == NCK - 1))))(hc, dc),
                                 r=["fw2"] + [f"fh{h3}" for h3 in range(NCK)], w=[f"ps{6 + dc % 2}"])
                        p.op("vector",
                             (lambda ee, d2, b=PS[6 + dc % 2]: (lambda e: e.scalar_tensor_tensor(
                                 FFT[:, 2048:2560], b[:, :],
                                 SMB[:, 154 + ee * 8 + d2:155 + ee * 8 + d2],
                                 FFT2[:, d2 * 512:(d2 + 1) * 512], ALU.add, ALU.mult)))(e_i, dc),
                             r=[f"ps{6 + dc % 2}", "smb", f"fx3_{dc}"], w=["fc0"])
                        p.op("vector",
                             (lambda: (lambda e: e.tensor_tensor(FFT[:, 2560:3072], FFT[:, 2048:2560],
                                                                 PS[3][:, :], ALU.mult)))(),
                             r=["fc0", "ps3"], w=["fcm"])
                        p.op("vector",
                             (lambda d2, t=tc: (lambda e: e.tensor_tensor(Xs(d2, t), Xs(d2, t),
                                                                          FFT[:, 2560:3072], ALU.add)))(dc),
                             r=["fcm", xb(dc, tc)], w=[xb(dc, tc)])
                    if tc == 0 and e_i + 1 < E8:
                        pass
                if e_i + 1 < E8:
                    stage_expert_matrix(e_i + 1, 0, 0, "fw0")
                    stage_expert_matrix(e_i + 1, 1, 1, "fw1")
                    stage_expert_matrix(e_i + 1, 2, 2, "fw2")
            p.barrier()

        if emit_upto >= 2:
            emit_ffn(1)

        def emit_attnb():
            blnrow = nc.alloc_sbuf_tensor_at("BLNROW", [1, 1024], F32, offset=O_SCR)
            bsq = nc.alloc_sbuf_tensor_at("BSQ", [128, 512], F32, offset=O_SCR + 4096)
            for tc in range(NT):
                emit_ln(tc, (lambda d: XNs(d, tc)), blnrow, bsq, "blnrow", "bsq")
                for dc in range(NCK):
                    p.buf[f"bxn{tc}_{dc}"] = p.buf.pop("xn" + str(dc))
            IPJW = [nc.alloc_sbuf_tensor_at(f"IPJW{i}", [128, 1024], BF16, offset=O_WST + 4096 + 2048 * i)
                    for i in range(2)]
            def qkv_dst(oc, tc):
                if oc < 8:
                    return QBUF[:, oc * 2048 + tc * TCH: oc * 2048 + (tc + 1) * TCH], f"q{oc}"
                if oc < 16:
                    cc = oc - 8
                    return KBUF[:, cc * 2048 + tc * TCH: cc * 2048 + (tc + 1) * TCH], f"k{cc}"
                cc = oc - 16
                if cc < 4:
                    return V1B[:, cc * 2048 + tc * TCH: cc * 2048 + (tc + 1) * TCH], f"v{cc}"
                return V2B[:, (cc - 4) * 2048 + tc * TCH: (cc - 4) * 2048 + (tc + 1) * TCH], f"v{cc}"

            for oc in range(24):
                ipw = IPJW[oc % 2]
                ipwn = f"ipjw{oc % 2}"
                src_ap = bass.AP(dr["bIpjT"], oc * 128, [[3072, 128], [128 * 3072, 8], [1, 128]])
                dma("sync", WST[0][:, :].rearrange("p (a b) -> p a b", b=128), src_ap, w=["bwst0"])
                p.op("scalar", (lambda ip=ipw: (lambda e: e.copy(ip[:, :], WST[0][:, :])))(),
                     r=["bwst0", "cc"], w=[ipwn])
                for tc in range(NT):
                    bi = 6 + oc % 2
                    for kc in range(NCK):
                        p.op("tensor",
                             (lambda k2, t, ip=ipw, b=PS[6 + oc % 2]: (lambda e: e.matmul(
                                 b[:, :], ip[:, k2 * 128:(k2 + 1) * 128], XNs(k2, t),
                                 start=(k2 == 0), stop=(k2 == NCK - 1))))(kc, tc),
                             r=[ipwn, f"bxn{tc}_{kc}"], w=[f"ps{6 + oc % 2}"])
                    dst, dstn = qkv_dst(oc, tc)
                    p.op("vector",
                         (lambda o2=oc, d2=dst, b=PS[6 + oc % 2]: (lambda e: e.tensor_scalar(
                             d2, b[:, :], SMB[:, 370 + o2:371 + o2], SMB[:, 346 + o2:347 + o2],
                             ALU.mult, ALU.add)))(),
                         r=[f"ps{6 + oc % 2}", "smb"], w=[dstn])
            p.barrier()
            dma("sync", BF16C[:, :], dr["c_F16"][:, :], w=["bf16c"])

            def vslice(cc, tc):
                if cc < 4:
                    return V1B[:, cc * 2048 + tc * TCH: cc * 2048 + (tc + 1) * TCH]
                return V2B[:, (cc - 4) * 2048 + tc * TCH: (cc - 4) * 2048 + (tc + 1) * TCH]

            for s in range(4):
                if s == 0:
                    for cc in range(NCK):
                        p.op("vector",
                             (lambda c2: (lambda e: e.tensor_scalar_mul(
                                 ATTO[:, c2 * 2048:c2 * 2048 + TCH], vslice(c2, 0),
                                 SMB[:, 74 + c2:75 + c2])))(cc),
                             r=[f"v{cc}", "smb"], w=[f"atto{cc}_0"])
                    continue
                for sp in range(s + 1):
                    for cc in range(NCK):
                        p.op("vector",
                             (lambda c2, s2=s, s3=sp: (lambda e: e.tensor_tensor(
                                 BZB1[:, :], QBUF[:, c2 * 2048 + s2 * TCH:c2 * 2048 + (s2 + 1) * TCH],
                                 KBUF[:, c2 * 2048 + s3 * TCH:c2 * 2048 + (s3 + 1) * TCH], ALU.mult)))(cc),
                             r=[f"q{cc}", f"k{cc}"], w=["bzb"])
                        p.op("tensor",
                             (lambda c2, b=PS[sp]: (lambda e: e.matmul(
                                 b[0:16, :], BE16F[:, c2 * 16:(c2 + 1) * 16], BZB1[:, :],
                                 start=(c2 == 0), stop=(c2 == NCK - 1))))(cc),
                             r=["be16f", "bzb"], w=[f"ps{sp}"])
                    p.op("scalar",
                         (lambda s3=sp, b=PS[sp]: (lambda e: e.copy(
                             BSCB[:, s3 * TCH:(s3 + 1) * TCH], b[0:16, :])))(),
                         r=[f"ps{sp}"], w=[f"scb{sp}"])
                p.op("vector", (lambda s2=s: (lambda e: e.tensor_tensor(
                    BMX[:, :], BSCB[:, 0:TCH], BSCB[:, TCH:2 * TCH], ALU.max)))(),
                    r=["scb0", "scb1"], w=["bmx"])
                for sp in range(2, s + 1):
                    p.op("vector", (lambda s3=sp: (lambda e: e.tensor_tensor(
                        BMX[:, :], BMX[:, :], BSCB[:, s3 * TCH:(s3 + 1) * TCH], ALU.max)))(),
                        r=["bmx", f"scb{sp}"], w=["bmx"])
                for sp in range(s + 1):
                    p.op("vector", (lambda s3=sp: (lambda e: e.tensor_tensor(
                        BSCB[:, s3 * TCH:(s3 + 1) * TCH], BSCB[:, s3 * TCH:(s3 + 1) * TCH],
                        BMX[:, :], ALU.subtract)))(),
                        r=[f"scb{sp}", "bmx"], w=[f"scb{sp}"])
                    p.op("scalar", (lambda s3=sp: (lambda e: e.activation(
                        BSCB[:, s3 * TCH:(s3 + 1) * TCH], BSCB[:, s3 * TCH:(s3 + 1) * TCH], AF.Exp)))(),
                        r=[f"scb{sp}"], w=[f"scb{sp}"])
                p.op("vector", lambda e: e.tensor_tensor(BSUM[:, :], BSCB[:, 0:TCH],
                                                         BSCB[:, TCH:2 * TCH], ALU.add),
                     r=["scb0", "scb1"], w=["bsum"])
                for sp in range(2, s + 1):
                    p.op("vector", (lambda s3=sp: (lambda e: e.tensor_tensor(
                        BSUM[:, :], BSUM[:, :], BSCB[:, s3 * TCH:(s3 + 1) * TCH], ALU.add)))(),
                        r=["bsum", f"scb{sp}"], w=["bsum"])
                p.op("vector", lambda e: e.reciprocal(BMX[:, :], BSUM[:, :]), r=["bsum"], w=["brc"])
                for sp in range(s + 1):
                    p.op("vector", (lambda s3=sp: (lambda e: e.tensor_tensor(
                        BSCB[:, s3 * TCH:(s3 + 1) * TCH], BSCB[:, s3 * TCH:(s3 + 1) * TCH],
                        BMX[:, :], ALU.mult)))(),
                        r=[f"scb{sp}", "brc"], w=[f"scb{sp}"])
                    p.op("scalar", (lambda s3=sp: (lambda e: e.copy(
                        BWJ4[0:16, s3 * TCH:(s3 + 1) * TCH], BSCB[:, s3 * TCH:(s3 + 1) * TCH])))(),
                        r=[f"scb{sp}"], w=[f"bwj{sp}"])
                for cc in range(NCK):
                    for sp in range(s + 1):
                        p.op("tensor",
                             (lambda c2, s3=sp, b=PS[4 + sp % 2]: (lambda e: e.matmul(
                                 b[:, :], BF16C[:, c2 * 128:(c2 + 1) * 128],
                                 BWJ4[0:16, s3 * TCH:(s3 + 1) * TCH], start=True, stop=True)))(cc),
                             r=["bf16c", f"bwj{sp}"], w=[f"ps{4 + sp % 2}"])
                        if sp == 0:
                            p.op("vector",
                                 (lambda c2, s3=sp, b=PS[4 + sp % 2]: (lambda e: e.tensor_tensor(
                                     BOAC[:, :], b[:, :], vslice(c2, s3), ALU.mult)))(cc),
                                 r=[f"ps{4 + sp % 2}", f"v{cc}"], w=["boac"])
                        else:
                            p.op("vector",
                                 (lambda c2, s3=sp, b=PS[4 + sp % 2]: (lambda e: e.tensor_tensor(
                                     WST[0][:, 512:1024], b[:, :], vslice(c2, s3), ALU.mult)))(cc),
                                 r=[f"ps{4 + sp % 2}", f"v{cc}"], w=["bwst0h"])
                            p.op("vector", lambda e: e.tensor_tensor(BOAC[:, :], BOAC[:, :],
                                                                     WST[0][:, 512:1024], ALU.add),
                                 r=["boac", "bwst0h"], w=["boac"])
                    p.op("vector",
                         (lambda c2, s2=s: (lambda e: e.tensor_scalar_mul(
                             ATTO[:, c2 * 2048 + s2 * TCH:c2 * 2048 + (s2 + 1) * TCH], BOAC[:, :],
                             SMB[:, 74 + c2:75 + c2])))(cc),
                         r=["boac", "smb"], w=[f"atto{cc}_{s}"])
            p.barrier()
            for oc in range(NCK):
                ipw = IPJW[oc % 2]
                ipwn = f"opjw{oc % 2}"
                src_ap = bass.AP(dr["bOpjT"], oc * 128, [[1024, 128], [128 * 1024, 8], [1, 128]])
                dma("sync", WST[0][:, :].rearrange("p (a b) -> p a b", b=128), src_ap, w=["bwst0"])
                p.op("scalar", (lambda ip=ipw: (lambda e: e.copy(ip[:, :], WST[0][:, :])))(),
                     r=["bwst0", "cc"], w=[ipwn])
                for tc in range(NT):
                    for kc in range(NCK):
                        p.op("tensor",
                             (lambda k2, t, ip=ipw, b=PS[6 + oc % 2]: (lambda e: e.matmul(
                                 b[:, :], ip[:, k2 * 128:(k2 + 1) * 128],
                                 ATTO[:, k2 * 2048 + t * TCH:k2 * 2048 + (t + 1) * TCH],
                                 start=(k2 == 0), stop=(k2 == NCK - 1))))(kc, tc),
                             r=[ipwn, f"atto{kc}_{tc}"], w=[f"ps{6 + oc % 2}"])
                    p.op("vector",
                         (lambda o2=oc, b=PS[6 + oc % 2]: (lambda e: e.tensor_scalar(
                             BOAC[:, :], b[:, :], SMB[:, 82 + o2:83 + o2], SMB[:, 402 + o2:403 + o2],
                             ALU.mult, ALU.add)))(),
                         r=[f"ps{6 + oc % 2}", "smb"], w=["boac"])
                    p.op("vector",
                         (lambda o2=oc, t=tc: (lambda e: e.tensor_tensor(Xs(o2, t), Xs(o2, t),
                                                                         BOAC[:, :], ALU.add)))(),
                         r=["boac", xb(oc, tc)], w=[xb(oc, tc)])
            p.barrier()

        if emit_upto >= 3:
            emit_attnb()

        if emit_upto >= 4:
            emit_ffn(2)

        for dc in range(NCK):
            dma("gpsimd", out_fm[dc * 128:(dc + 1) * 128, :], X[:, dc * 2048:(dc + 1) * 2048],
                w=[f"storedma{dc}"], r=[xb(dc, t) for t in range(NT)], queue_sem="qP")
        p.stream["gpsimd"].append(((("qP", p.count["qP"]),), None, None, 0))

        def make_body(name):
            def body(eng):
                for (waits, fn, incsem, incn) in p.stream[name]:
                    for (s, v) in waits:
                        if v > 0:
                            eng.wait_ge(sems[s], v)
                    inst = fn(eng) if fn is not None else None
                    if incsem is not None:
                        if inst is None:
                            inst = eng.nop()
                        inst.then_inc(sems[incsem], incn)
            return body

        block.sync(make_body("sync"))
        block.tensor(make_body("tensor"))
        block.vector(make_body("vector"))
        block.scalar(make_body("scalar"))
        block.gpsimd(make_body("gpsimd"))

    return nc


def _prep_inputs(inputs):
    f = lambda a: np.ascontiguousarray(a, dtype=np.float32)
    g = {}
    g["aWl1"] = f(inputs["L0_attn_a_Wl1"]); g["aWl2"] = f(inputs["L0_attn_a_Wl2"])
    g["aWf1"] = f(inputs["L0_attn_a_Wf1"]); g["aWf2"] = f(inputs["L0_attn_a_Wf2"])
    g["aW1"] = f(inputs["L0_attn_a_W1"]); g["aW2"] = f(inputs["L0_attn_a_W2"])
    g["aW3"] = f(inputs["L0_attn_a_W3"])
    aB = np.zeros((128, 26), np.float32)
    aB[:, 0] = inputs["L0_attn_a_bl1"]
    aB[:, 1:9] = np.asarray(inputs["L0_attn_a_bl2"]).reshape(8, 128).T
    aB[:, 9] = inputs["L0_attn_a_bf1"]
    aB[:, 10:18] = np.asarray(inputs["L0_attn_a_bf2"]).reshape(8, 128).T
    aB[:, 18:26] = np.asarray(inputs["L0_attn_a_b1"]).reshape(8, 128).T
    g["aBpp"] = aB
    g["aB23"] = f(np.stack([inputs["L0_attn_a_b2"], inputs["L0_attn_a_b3"]], axis=1))
    g["aA"] = f(np.asarray(inputs["L0_attn_a_A"]).reshape(8, 128, 16).transpose(1, 0, 2).reshape(128, 128))
    g["aST"] = f(np.asarray(inputs["L0_attn_a_state_init"]).reshape(8, 128, 16)
                 .transpose(1, 0, 2).reshape(128, 128))
    g["bIpjT"] = f(np.asarray(inputs["L0_attn_b_in_proj_weight"]).T)
    g["bOpjT"] = f(np.asarray(inputs["L0_attn_b_out_proj_weight"]).T)
    bG = np.zeros((128, 64), np.float32)
    bG[:, 0:24] = np.asarray(inputs["L0_attn_b_gamma"]).reshape(24, 128).T
    bG[:, 24:48] = np.asarray(inputs["L0_attn_b_in_proj_bias"]).reshape(24, 128).T
    bG[:, 48:56] = np.asarray(inputs["L0_attn_b_linear_alpha"]).reshape(8, 128).T
    bG[:, 56:64] = np.asarray(inputs["L0_attn_b_linear_gamma"]).reshape(8, 128).T
    g["bGpp"] = bG
    g["bOpb"] = f(np.asarray(inputs["L0_attn_b_out_proj_bias"]).reshape(8, 128).T)
    g["fGwT"] = f(np.asarray(inputs["L0_ffn_gate_w"]).T)
    g["fW1T"] = f(np.asarray(inputs["L0_ffn_w1"]).transpose(0, 2, 1))
    g["fW2T"] = f(np.asarray(inputs["L0_ffn_w2"]).transpose(0, 2, 1))
    g["fW3T"] = f(np.asarray(inputs["L0_ffn_w3"]).transpose(0, 2, 1))
    g["fSw1T"] = f(np.asarray(inputs["L0_ffn_sw1"]).T)
    g["fSw2T"] = f(np.asarray(inputs["L0_ffn_sw2"]).T)
    g["fSw3T"] = f(np.asarray(inputs["L0_ffn_sw3"]).T)
    fB = np.zeros((128, 192), np.float32)
    fB[:, 0:64] = np.asarray(inputs["L0_ffn_b1"]).reshape(8, 8, 128).transpose(2, 0, 1).reshape(128, 64)
    fB[:, 64:128] = np.asarray(inputs["L0_ffn_b2"]).reshape(8, 8, 128).transpose(2, 0, 1).reshape(128, 64)
    fB[:, 128:192] = np.asarray(inputs["L0_ffn_b3"]).reshape(8, 8, 128).transpose(2, 0, 1).reshape(128, 64)
    g["fBpp"] = fB
    g["c_ident"] = np.eye(128, dtype=np.float32)
    cc = np.zeros((128, 8), np.float32)
    cc[:, 0] = EPS; cc[:, 1] = 1.0; cc[:, 2] = -1.0; cc[:, 3] = -0.5; cc[:, 4] = 0.125
    g["c_const"] = cc
    sel16 = np.zeros((16, 2048), np.float32)
    for n in range(16):
        sel16[n, n * 128:(n + 1) * 128] = 1.0
    g["c_sel16"] = sel16.astype(ml_dtypes.bfloat16)
    sel8 = np.zeros((8, 1024), np.float32)
    for e in range(8):
        sel8[e, e * 128:(e + 1) * 128] = 1.0
    g["c_sel8"] = sel8
    e16 = np.zeros((128, 128), np.float32)
    for cc in range(8):
        for k in range(128):
            e16[k, cc * 16 + 2 * cc + (k // 64)] = 1.0
    g["c_E16"] = e16.astype(ml_dtypes.bfloat16)
    g["c_E16f"] = e16.astype(np.float32)
    f16 = np.zeros((16, 1024), np.float32)
    for cc in range(8):
        for m in range(128):
            f16[2 * cc + (m // 64), cc * 128 + m] = 1.0
    g["c_F16"] = f16.astype(ml_dtypes.bfloat16)
    return g


def kernel(**inputs):
    x = np.asarray(inputs["L0_attn_a_x"])
    B, L, _ = x.shape
    shared = _prep_inputs(inputs)
    nc = build_program(_UPTO, _SUB)
    in_maps = []
    for c in range(8):
        xc = np.ascontiguousarray(x[:, c * 512:(c + 1) * 512, :].reshape(B * 512, D).T)
        m = dict(shared)
        m["x_fm"] = xc
        in_maps.append(m)
    res = run_bass_kernel_spmd(nc, in_maps, core_ids=list(range(8)))
    out = np.empty((B, L, D), np.float32)
    for c in range(8):
        out[:, c * 512:(c + 1) * 512, :] = res.results[c]["out_fm"].T.reshape(B, 512, D)
    return out


# version 03
# speedup vs baseline: 1.7944x; 1.7944x over previous
import numpy as np
import ml_dtypes
import concourse.bass as bass
import concourse.mybir as mybir
from concourse.bass_utils import run_bass_kernel_spmd
from contextlib import ExitStack

F32 = mybir.dt.float32
BF16 = mybir.dt.bfloat16
AF = mybir.ActivationFunctionType
ALU = mybir.AluOpType
AX = mybir.AxisListType

D = 1024
NCK = 8
TPC = 2048
TCH = 512
NT = 4
N16 = 16
E8 = 8
EPS = 1e-5

_UPTO = 99
_SUB = 99
_DBG_PAIRS = 2
_DBG_MC = 8
_DBG_DOPS = 9
_DBG_W1OFF = True
_DBG_NODEL = False

O_X = 1536
O_XNR = O_X + 65536
O_WBUF = O_XNR + 32768
O_BIGA = O_WBUF + 49152
O_BIGB = O_BIGA + 32768
O_SCR = O_BIGB + 16384
O_WST = O_SCR + 16384
O_SMALL = O_WST + 8192

ENG2SEM = {"tensor": "sT", "vector": "sV", "scalar": "sS", "gpsimd": "sG", "sync": None}
ALLSEMS = ("qW", "qP", "sT", "sV", "sS", "sG")


class Prog:
    ENGINES = ("sync", "tensor", "vector", "scalar", "gpsimd")

    def __init__(self):
        self.stream = {e: [] for e in self.ENGINES}
        self.count = {s: 0 for s in ALLSEMS}
        self.buf = {}
        self.hiwater = {e: {} for e in self.ENGINES}

    def _b(self, name):
        return self.buf.setdefault(name, {"w": None, "r": []})

    def op(self, eng, fn, r=(), w=(), dma=None, extra=()):
        deps = []
        myeng0 = None if dma else eng
        for name in r:
            b = self._b(name)
            if b["w"]:
                deps.append(b["w"])
            if name.startswith("ps"):
                deps.extend(m for m in b["r"] if m[2] != myeng0)
        for name in w:
            b = self._b(name)
            if b["w"]:
                deps.append(b["w"])
            deps.extend(b["r"])
        waits = {}
        for (sem, val, weng) in deps:
            if weng == "tensor" and myeng0 == "tensor":
                continue
            if val > waits.get(sem, 0):
                waits[sem] = val
        if dma:
            prev = self.count[dma]
            if prev > waits.get(dma, 0):
                waits[dma] = prev
        for (sem, val) in extra:
            if val > waits.get(sem, 0):
                waits[sem] = val
        hw = self.hiwater[eng]
        waits = {s: v for s, v in waits.items() if v > hw.get(s, 0)}
        for s, v in waits.items():
            hw[s] = v
        if dma:
            sem = dma
            self.count[sem] += 16
            mi = (sem, self.count[sem], None)
        else:
            sem = ENG2SEM[eng]
            if sem is None:
                sem = "sG"
            self.count[sem] += 1
            mi = (sem, self.count[sem], eng)
        self.stream[eng].append((tuple(sorted(waits.items())), fn, mi[0], 16 if dma else 1))
        for name in r:
            self._b(name)["r"].append(mi)
        for name in w:
            b = self._b(name)
            b["w"] = mi
            b["r"] = []
        return mi

    def barrier(self):
        snap = [(s, self.count[s]) for s in ALLSEMS if self.count[s] > 0]
        for e in self.ENGINES:
            self.stream[e].append((tuple(snap), None, None, 0))
            for s, v in snap:
                if v > self.hiwater[e].get(s, 0):
                    self.hiwater[e][s] = v
        self.buf = {}


def build_program(emit_upto=99, sub=None):
    if sub is None:
        sub = _SUB
    _DBG_DOPS = globals()["_DBG_DOPS"]
    _DBG_W1OFF = globals()["_DBG_W1OFF"]
    _DBG_NODEL = globals()["_DBG_NODEL"]
    nc = bass.Bass("TRN2", target_bir_lowering=False, dynamic_dma_scratch_size=512)
    dr = {}

    def din(name, shape, dtype=F32):
        dr[name] = nc.dram_tensor(name, list(shape), dtype, kind="ExternalInput")
        return dr[name]

    din("x_fm", (D, TPC))
    din("aWl1", (D, 128)); din("aWl2", (128, D)); din("aWf1", (2 * D, 128))
    din("aWf2", (128, D)); din("aW1", (D, D)); din("aW2", (D, N16)); din("aW3", (D, N16))
    din("aBpp", (128, 26)); din("aB23", (16, 2)); din("aA", (128, 128)); din("aST", (128, 128))
    din("bIpjT", (D, 3 * D)); din("bOpjT", (D, D)); din("bGpp", (128, 64)); din("bOpb", (128, 8))
    din("fGwT", (D, E8))
    din("fW1T", (E8, D, D)); din("fW2T", (E8, D, D)); din("fW3T", (E8, D, D))
    din("fSw1T", (D, 2 * D)); din("fSw2T", (D, 2 * D)); din("fSw3T", (2 * D, D))
    din("fBpp", (128, 192))
    din("c_ident", (128, 128)); din("c_E16", (128, 128), BF16); din("c_F16", (16, 1024), BF16)
    din("c_const", (128, 8))
    din("c_E16f", (128, 128))
    din("c_sel16", (16, 2048), BF16)
    din("c_sel8", (8, 1024))
    out_fm = nc.dram_tensor("out_fm", [D, TPC], F32, kind="ExternalOutput")

    p = Prog()

    with ExitStack() as ctx:
        at = nc.alloc_sbuf_tensor_at
        X = at("X", [128, 16384], F32, offset=O_X)
        XN = at("XN", [128, 16384], BF16, offset=O_XNR)
        XT32 = at("XT32", [128, 8192], F32, offset=O_XNR)
        ATTO = at("ATTO", [128, 16384], BF16, offset=O_XNR)
        AW = at("AW", [128, 16384], BF16, offset=O_WBUF)
        FW = [at(f"FW{i}", [128, 8192], BF16, offset=O_WBUF + 16384 * i) for i in range(3)]
        KBUF = at("KBUF", [128, 16384], BF16, offset=O_WBUF)
        V1B = at("V1B", [128, 8192], BF16, offset=O_WBUF + 32768)
        A_XN = at("A_XN", [128, 8192], BF16, offset=O_BIGA)
        A_LIF = at("A_LIF", [128, 8192], BF16, offset=O_BIGA + 16384)
        A_DEL = at("A_DEL", [128, 8192], F32, offset=O_BIGA)
        QBUF = at("QBUF", [128, 16384], BF16, offset=O_BIGA)
        FFT = at("FFT", [128, 4096], F32, offset=O_BIGA)
        FFT2 = at("FFT2", [128, 4096], F32, offset=O_BIGA + 16384)
        GTM = at("GTM", [128, 768], F32, offset=O_BIGA + 16384)
        GSML = at("GSML", [128, 64], F32, offset=O_BIGA + 16384 + 3072)
        A_XTB = at("A_XTB", [128, 8192], BF16, offset=O_BIGB)
        HB = at("HB", [128, 4096], BF16, offset=O_BIGB)
        PRB = at("PRB", [128, 8192], BF16, offset=O_BIGB)
        V2B = at("V2B", [128, 8192], BF16, offset=O_BIGB)
        SEL16B = at("SEL16B", [16, 2048], BF16, offset=O_WBUF + 28672)
        FSEL8F = at("FSEL8F", [8, 1024], F32, offset=O_BIGB + 8192)
        FSEL16 = at("FSEL16", [8, 1024], BF16, offset=O_BIGB + 12288)
        BWJ4 = at("BWJ4", [16, 2048], BF16, offset=O_WST + 4096)
        BF16C = at("BF16C", [16, 1024], BF16, offset=O_WST)
        BZB1 = at("BZB1", [128, 512], F32, offset=O_SCR + 12288)
        A_SCR = at("A_SCR", [128, 1536], F32, offset=O_SCR)
        A_WMX = at("A_WMX", [128, 512], F32, offset=O_SCR + 6144)
        A_CM = at("A_CM", [16, 1024], F32, offset=O_SCR + 8192)
        A_CMBF = at("A_CMBF", [16, 1024], BF16, offset=O_SCR + 12288)
        A_CMROW = at("A_CMROW", [1, 512], F32, offset=O_SCR + 14336)
        SFM = at("SFM", [8, 2048], F32, offset=O_SCR)
        MFMB = at("MFMB", [8, 2048], BF16, offset=O_SCR + 8192)
        BSCB = at("BSCB", [16, 2048], F32, offset=O_SCR)
        BMX = at("BMX", [16, 512], F32, offset=O_SCR + 8192)
        BSUM = at("BSUM", [16, 512], F32, offset=O_SCR + 10240)
        BRC = at("BRC", [16, 512], F32, offset=O_SCR + 12288)
        BOAC = at("BOAC", [128, 512], F32, offset=O_SCR + 14336)
        WST = [at(f"WST{i}", [128, 1024], F32, offset=O_WST + 4096 * i) for i in range(2)]
        A_LNROW = at("A_LNROW", [1, 1024], F32, offset=O_WST)
        A_BM = at("A_BM", [16, 512], F32, offset=O_WST + 4096)
        HLB = at("HLB", [128, 1024], BF16, offset=O_WST + 6144)
        BZB = at("BZB", [128, 1024], BF16, offset=O_WST)
        BWJ = at("BWJ", [16, 512], BF16, offset=O_WST + 2048)
        SMB = at("SMB", [128, 424], F32, offset=O_SMALL)
        SM_A = at("SM_A", [128, 128], F32, offset=O_SMALL + 1696)
        SM_AM = at("SM_AM", [128, 128], F32, offset=O_SMALL + 2208)
        SM_NAP = at("SM_NAP", [128, 128], F32, offset=O_SMALL + 2720)
        SM_ST = at("SM_ST", [128, 128], F32, offset=O_SMALL + 3232)
        SM_B23 = at("SM_B23", [16, 2], F32, offset=O_SMALL + 3744)
        IDENT = at("IDENT", [128, 128], F32, offset=O_SMALL + 3776)
        ONES = at("ONES", [1, 128], F32, offset=O_SMALL + 4288)
        BE16F = at("BE16F", [128, 128], F32, offset=O_SMALL + 4800)
        ONESC = at("ONESC", [128, 1], F32, offset=O_SMALL + 6336)
        CC = at("CC", [128, 8], F32, offset=O_SMALL + 6368)
        PS = [ctx.enter_context(nc.psum_tensor(f"ps{i}", [128, 512], F32)) for i in range(8)]
        sems = {s: ctx.enter_context(nc.semaphore(s)) for s in ALLSEMS}
        block = ctx.enter_context(nc.Block())

        def Xs(dc, tc, lo=0, ln=TCH):
            return X[:, dc * 2048 + tc * TCH + lo: dc * 2048 + tc * TCH + lo + ln]

        def xb(dc, tc):
            return f"x{dc}_{tc}"

        def XNs(dc, tc, n=TCH):
            return XN[:, dc * 2048 + tc * TCH: dc * 2048 + tc * TCH + n]

        def dma(q, out, in_, w, r=(), queue_sem="qW"):
            return p.op(q, (lambda o, i: (lambda e: e.dma_start(out=o, in_=i)))(out, in_),
                        r=r, w=w, dma=queue_sem)

        dma("sync", CC[:, :], dr["c_const"][:, :], w=["cc"])
        dma("sync", SEL16B[:, :], dr["c_sel16"][:, :], w=["sel16"])
        dma("sync", BE16F[:, :], dr["c_E16f"][:, :], w=["be16f"])
        for dc in range(NCK):
            dma("sync", X[:, dc * 2048:(dc + 1) * 2048], dr["x_fm"][dc * 128:(dc + 1) * 128, :],
                w=[xb(dc, t) for t in range(NT)])
        dma("sync", SMB[:, 0:26], dr["aBpp"][:, :], w=["smb"])
        dma("sync", SMB[:, 26:90], dr["bGpp"][:, :], w=["smb"])
        dma("sync", SMB[:, 90:282], dr["fBpp"][:, :], w=["smb"])
        dma("sync", SMB[:, 394:402], dr["bOpb"][:, :], w=["smb"])
        for k in range(NCK):
            dma("sync", SMB[:, 282 + k * 8:282 + (k + 1) * 8], dr["fGwT"][k * 128:(k + 1) * 128, :],
                w=["smb"])
        dma("sync", SM_A[:, :], dr["aA"][:, :], w=["sma"])
        dma("sync", SM_ST[:, :], dr["aST"][:, :], w=["smst"])
        dma("sync", SM_B23[:, :], dr["aB23"][:, :], w=["smb23"])
        dma("sync", IDENT[:, :], dr["c_ident"][:, :], w=["ident"])
        p.op("vector", lambda e: e.tensor_scalar_min(SM_AM[:, :], SM_A[:, :], 0.0),
             r=["sma"], w=["smam"])
        p.op("vector", lambda e: e.tensor_scalar(SM_NAP[:, :], SM_A[:, :], 0.0, -1.0, ALU.max, ALU.mult),
             r=["sma"], w=["smnap"])
        p.op("vector", lambda e: e.tensor_tensor(SMB[:, 346:370], SMB[:, 26:50], SMB[:, 50:74], ALU.mult),
             r=["smb"], w=["smb"])
        p.op("vector", lambda e: e.tensor_copy(SMB[:, 370:394], SMB[:, 26:50]), r=["smb"], w=["smb"])
        p.op("vector", lambda e: e.tensor_scalar_mul(SMB[:, 370:378], SMB[:, 370:378], 0.125),
             r=["smb"], w=["smb"])
        p.op("vector", lambda e: e.tensor_scalar_mul(SMB[:, 346:354], SMB[:, 346:354], 0.125),
             r=["smb"], w=["smb"])
        p.op("vector", lambda e: e.tensor_tensor(SMB[:, 402:410], SMB[:, 82:90], SMB[:, 394:402], ALU.mult),
             r=["smb"], w=["smb"])
        p.op("vector", lambda e: e.memset(ONES[:, :], 1.0), w=["ones"])
        p.op("vector", lambda e: e.memset(ONESC[:, :], 1.0), w=["onesc"])

        AWO = {"wl1": 0, "wl2": 1024, "wf1": 2048, "wf2": 4096, "w1": 5120, "w2": 13312, "w3": 13440}
        aw_specs = []
        aw_specs += [(dr["aWl1"], k * 128, 128, AWO["wl1"] + k * 128) for k in range(8)]
        aw_specs += [(dr["aWl2"], 0, 1024, AWO["wl2"])]
        aw_specs += [(dr["aWf1"], k * 128, 128, AWO["wf1"] + k * 128) for k in range(16)]
        aw_specs += [(dr["aWf2"], 0, 1024, AWO["wf2"])]
        aw_specs += [(dr["aW1"], k * 128, 1024, AWO["w1"] + k * 1024) for k in range(8)]
        aw_specs += [(dr["aW2"], k * 128, 16, AWO["w2"] + k * 16) for k in range(8)]
        aw_specs += [(dr["aW3"], k * 128, 16, AWO["w3"] + k * 16) for k in range(8)]
        for i, (dt_, r0, ncols, ao) in enumerate(aw_specs):
            st = WST[i % 2]
            stn = f"wst{i % 2}"
            dma("sync", st[:, 0:ncols], dt_[r0:r0 + 128, :], w=[stn])
            p.op("scalar",
                 (lambda s_, n_, a_: (lambda e: e.copy(AW[:, a_:a_ + n_], s_[:, 0:n_])))(st, ncols, ao),
                 r=[stn, "cc"], w=["aw"])

        p.barrier()

        def emit_ln(tc, dst_fn, rowbuf, sqbuf, rown, sqn, xn32=None):
            dsts = [dst_fn(d) for d in range(NCK)]
            b_s1, b_s2, b_mb, b_rb = PS[4], PS[5], PS[6], PS[7]
            for dc in range(NCK):
                p.op("tensor",
                     (lambda d: (lambda e: e.matmul(b_s1[0:1, :], ONESC[:, :], Xs(d, tc),
                                                    start=(d == 0), stop=(d == NCK - 1))))(dc),
                     r=["onesc", xb(dc, tc)], w=(["ps4"] if dc == 0 else ["ps4"]))
            for dc in range(NCK):
                p.op("scalar",
                     (lambda d: (lambda e: e.activation(sqbuf[:, :], Xs(d, tc), AF.Square)))(dc),
                     r=[xb(dc, tc)], w=[sqn])
                p.op("tensor",
                     (lambda d: (lambda e: e.matmul(b_s2[0:1, :], ONESC[:, :], sqbuf[:, :],
                                                    start=(d == 0), stop=(d == NCK - 1))))(dc),
                     r=["onesc", sqn], w=["ps5"])
            mrow = rowbuf[0:1, 0:512]
            rrow = rowbuf[0:1, 512:1024]
            p.op("vector", lambda e, m=mrow: e.tensor_scalar_mul(m, b_s1[0:1, :], 1.0 / D),
                 r=["ps4"], w=[rown])
            p.op("vector", lambda e, rr=rrow: e.tensor_scalar_mul(rr, b_s2[0:1, :], 1.0 / D),
                 r=["ps5"], w=[rown])
            p.op("vector", lambda e, m=mrow: e.tensor_tensor(sqbuf[0:1, 0:512], m, m, ALU.mult),
                 r=[rown], w=[sqn])
            p.op("vector", lambda e, rr=rrow: e.tensor_tensor(rr, rr, sqbuf[0:1, 0:512], ALU.subtract),
                 r=[rown, sqn], w=[rown])
            p.op("scalar", lambda e, rr=rrow: e.activation(rr, rr, AF.Ln, bias=CC[0:1, 0:1]),
                 r=[rown, "cc"], w=[rown])
            p.op("scalar", lambda e, rr=rrow: e.activation(rr, rr, AF.Exp, scale=CC[0:1, 3:4]),
                 r=[rown, "cc"], w=[rown])
            p.op("tensor", lambda e, m=mrow: e.matmul(b_mb[:, :], ONES[0:1, 0:128], m, start=True, stop=True),
                 r=["ones", rown], w=["ps6"])
            p.op("tensor", lambda e, rr=rrow: e.matmul(b_rb[:, :], ONES[0:1, 0:128], rr, start=True, stop=True),
                 r=["ones", rown], w=["ps7"])
            for dc in range(NCK):
                p.op("vector",
                     (lambda d: (lambda e: e.tensor_tensor(sqbuf[:, :], Xs(d, tc), b_mb[:, :], ALU.subtract)))(dc),
                     r=[xb(dc, tc), "ps6"], w=[sqn])
                if xn32 is not None:
                    p.op("vector",
                         (lambda d: (lambda e: e.tensor_tensor(xn32[:, d * TCH:(d + 1) * TCH], sqbuf[:, :],
                                                               b_rb[:, :], ALU.mult)))(dc),
                         r=[sqn, "ps7"], w=["xn32"])
                    p.op("scalar",
                         (lambda d, dst=dsts[dc]: (lambda e: e.copy(dst, xn32[:, d * TCH:(d + 1) * TCH])))(dc),
                         r=["xn32"], w=["xn" + str(dc)])
                else:
                    p.op("vector",
                         (lambda d, dst=dsts[dc]: (lambda e: e.tensor_tensor(dst, sqbuf[:, :], b_rb[:, :], ALU.mult)))(dc),
                         r=[sqn, "ps7"], w=["xn" + str(dc)])

        if emit_upto >= 1:
            for pr in range(2):
                tcs = (2 * pr, 2 * pr + 1)
                for j, tc in enumerate(tcs):
                    emit_ln(tc, (lambda jj: (lambda d: A_XN[:, jj * 4096 + d * TCH:jj * 4096 + (d + 1) * TCH]))(j),
                            A_LNROW, A_WMX, "lnrow", "wmx")
                    for dc in range(NCK):
                        p.buf[f"axn{j}_{dc}"] = p.buf.pop("xn" + str(dc))
                for j, tc in enumerate(tcs if sub >= 2 else ()):
                    hl = HLB[:, j * 512:(j + 1) * 512]
                    hln = f"hlb{j}"
                    b_h = PS[0]
                    for k in range(NCK):
                        p.op("tensor",
                             (lambda kk, jj: (lambda e: e.matmul(
                                 b_h[:, :], AW[:, AWO["wl1"] + kk * 128:AWO["wl1"] + (kk + 1) * 128],
                                 A_XN[:, jj * 4096 + kk * TCH:jj * 4096 + (kk + 1) * TCH],
                                 start=(kk == 0), stop=(kk == NCK - 1))))(k, j),
                             r=["aw", f"axn{j}_{k}"], w=["ps0"])
                    p.op("scalar", lambda e, h=hl: e.activation(h, b_h[:, :], AF.Relu, bias=SMB[:, 0:1]),
                         r=["ps0", "smb"], w=[hln])
                    for mc in range(NCK):
                        bi = mc % 2
                        bnk = PS[bi]
                        p.op("tensor",
                             (lambda m, h=hl, b=bnk: (lambda e: e.matmul(
                                 b[:, :], AW[:, AWO["wl2"] + m * 128:AWO["wl2"] + (m + 1) * 128], h,
                                 start=True, stop=True)))(mc),
                             r=["aw", hln], w=[f"ps{bi}"])
                        p.op("scalar",
                             (lambda m, jj, b=bnk: (lambda e: e.activation(
                                 A_LIF[:, jj * 4096 + m * TCH:jj * 4096 + (m + 1) * TCH], b[:, :],
                                 AF.Tanh, bias=SMB[:, 1 + m:2 + m])))(mc, j),
                             r=[f"ps{bi}", "smb"], w=[f"alif{j}_{mc}"])
                    b_h1 = PS[2]
                    for k in range(16):
                        src = A_XN if k < 8 else A_LIF
                        srcn = (f"axn{j}_{k}" if k < 8 else f"alif{j}_{k - 8}")
                        kk = k % 8
                        p.op("tensor",
                             (lambda sr, k2, kk2, jj: (lambda e: e.matmul(
                                 b_h1[:, :], AW[:, AWO["wf1"] + k2 * 128:AWO["wf1"] + (k2 + 1) * 128],
                                 sr[:, jj * 4096 + kk2 * TCH:jj * 4096 + (kk2 + 1) * TCH],
                                 start=(k2 == 0), stop=(k2 == 15))))(src, k, kk, j),
                             r=["aw", srcn], w=["ps2"])
                    p.op("scalar", lambda e, h=hl: e.activation(h, b_h1[:, :], AF.Relu, bias=SMB[:, 9:10]),
                         r=["ps2", "smb"], w=[hln])
                    for mc in range(NCK):
                        bi = mc % 2
                        bnk = PS[bi]
                        p.op("tensor",
                             (lambda m, h=hl, b=bnk: (lambda e: e.matmul(
                                 b[:, :], AW[:, AWO["wf2"] + m * 128:AWO["wf2"] + (m + 1) * 128], h,
                                 start=True, stop=True)))(mc),
                             r=["aw", hln], w=[f"ps{bi}"])
                        p.op("vector",
                             (lambda m, jj, b=bnk: (lambda e: e.tensor_scalar_add(
                                 XT32[:, jj * 4096 + m * TCH:jj * 4096 + (m + 1) * TCH], b[:, :],
                                 SMB[:, 10 + m:11 + m])))(mc, j),
                             r=[f"ps{bi}", "smb"], w=[f"xt32_{j}_{mc}"])
                        p.op("vector",
                             (lambda m, jj: (lambda e: e.tensor_copy(
                                 A_XTB[:, jj * 4096 + m * TCH:jj * 4096 + (m + 1) * TCH],
                                 XT32[:, jj * 4096 + m * TCH:jj * 4096 + (m + 1) * TCH])))(mc, j),
                             r=[f"xt32_{j}_{mc}"], w=[f"xtb{j}_{mc}"])
                for j, tc in enumerate(tcs if sub >= 3 else ()):
                    if pr > _DBG_PAIRS - 1:
                        continue
                    for mc in range(min(NCK, _DBG_MC)):
                        bi = mc % 2
                        bnk = PS[bi]
                        w1base = AWO["w1"] if _DBG_W1OFF else AWO["wl1"]
                        for k in range(NCK):
                            off = (w1base + kk2 * 1024 + mc * 128 if _DBG_W1OFF else AWO["wl1"]) if False else None
                            p.op("tensor",
                                 (lambda kk, m, jj, b=bnk: (lambda e: e.matmul(
                                     b[:, :],
                                     (AW[:, AWO["w1"] + kk * 1024 + m * 128:AWO["w1"] + kk * 1024 + (m + 1) * 128]
                                      if _DBG_W1OFF else
                                      AW[:, AWO["wl1"] + kk * 128:AWO["wl1"] + (kk + 1) * 128]),
                                     A_XTB[:, jj * 4096 + kk * TCH:jj * 4096 + (kk + 1) * TCH],
                                     start=(kk == 0), stop=(kk == NCK - 1))))(k, mc, j),
                                 r=["aw", f"xtb{j}_{k}"], w=[f"ps{bi}"])
                        bcol = SMB[:, 18 + mc:19 + mc]
                        if _DBG_DOPS >= 1:
                            p.op("vector",
                                 (lambda b=bnk, bc=bcol: (lambda e: e.tensor_scalar(
                                     A_WMX[:, :], b[:, :], bc, 0.0, ALU.add, ALU.max)))(),
                                 r=[f"ps{bi}", "smb"], w=["wmx"])
                        else:
                            p.op("vector", (lambda b=bnk: (lambda e: e.tensor_copy(A_WMX[:, :], b[:, :])))(),
                                 r=[f"ps{bi}"], w=["wmx"])
                        if _DBG_DOPS >= 2:
                            p.op("scalar",
                                 (lambda b=bnk, bc=bcol: (lambda e: e.activation(
                                     A_SCR[:, 0:512], b[:, :], AF.Abs, bias=bc)))(),
                                 r=[f"ps{bi}", "smb"], w=["scrP"])
                        else:
                            p.op("scalar", (lambda b=bnk: (lambda e: e.copy(A_SCR[:, 0:512], b[:, :])))(),
                                 r=[f"ps{bi}"], w=["scrP"])
                        if _DBG_DOPS >= 3:
                            p.op("scalar", lambda e: e.activation(A_SCR[:, 512:1024], A_SCR[:, 0:512],
                                                                  AF.Exp, scale=CC[:, 2:3]),
                                 r=["scrP", "cc"], w=["scrT"])
                        else:
                            p.op("scalar", lambda e: e.copy(A_SCR[:, 512:1024], A_SCR[:, 0:512]),
                                 r=["scrP"], w=["scrT"])
                        if _DBG_DOPS >= 4:
                            p.op("scalar", lambda e: e.activation(A_SCR[:, 1024:1536], A_SCR[:, 512:1024],
                                                                  AF.Ln, bias=CC[:, 1:2]),
                                 r=["scrT", "cc"], w=["scrF"])
                        else:
                            p.op("scalar", lambda e: e.copy(A_SCR[:, 1024:1536], A_SCR[:, 512:1024]),
                                 r=["scrT"], w=["scrF"])
                        if _DBG_NODEL:
                            p.op("vector",
                                 lambda e: e.tensor_tensor(A_WMX[:, :], A_WMX[:, :], A_SCR[:, 1024:1536],
                                                           ALU.add),
                                 r=["wmx", "scrF"], w=["wmx"])
                            p.buf[f"del{j}_{mc}"] = {"w": None, "r": []}
                        else:
                            p.op("vector",
                                 (lambda m, jj: (lambda e: e.tensor_tensor(
                                     A_DEL[:, jj * 4096 + m * TCH:jj * 4096 + (m + 1) * TCH],
                                     A_WMX[:, :], A_SCR[:, 1024:1536], ALU.add)))(mc, j),
                                 r=["wmx", "scrF"] + [f"axn{j}_{mc}", f"alif{j}_{mc}"],
                                 w=[f"del{j}_{mc}"])
                for j, tc in enumerate(tcs if sub >= 4 else ()):
                    b_bc = PS[2]
                    for wname, dst, dstn, bc in (("w2", A_BM[:, :], "abm", SM_B23[:, 0:1]),
                                                 ("w3", A_CM[:, j * 512:(j + 1) * 512], f"acm{j}", SM_B23[:, 1:2])):
                        for k in range(NCK):
                            p.op("tensor",
                                 (lambda kk, wn, jj: (lambda e: e.matmul(
                                     b_bc[0:16, :], AW[:, AWO[wn] + kk * 16:AWO[wn] + (kk + 1) * 16],
                                     A_XTB[:, jj * 4096 + kk * TCH:jj * 4096 + (kk + 1) * TCH],
                                     start=(kk == 0), stop=(kk == NCK - 1))))(k, wname, j),
                                 r=["aw", f"xtb{j}_{k}"], w=["ps2"])
                        p.op("vector",
                             (lambda d2=dst, b2=bc: (lambda e: e.tensor_scalar_add(d2, b_bc[0:16, :], b2)))(),
                             r=["ps2", "smb23"], w=[dstn])
                    p.op("vector",
                         (lambda jj: (lambda e: e.tensor_copy(A_CMBF[:, jj * 512:(jj + 1) * 512],
                                                              A_CM[:, jj * 512:(jj + 1) * 512])))(j),
                         r=[f"acm{j}"], w=[f"acmbf{j}"])
                    p.op("vector",
                         (lambda jj: (lambda e: e.tensor_tensor(A_BM[:, :], A_BM[:, :],
                                                                A_CM[:, jj * 512:(jj + 1) * 512], ALU.mult)))(j),
                         r=["abm", f"acm{j}"], w=["abm"])
                    p.op("tensor", lambda e: e.matmul(PS[3][0:1, :], ONESC[0:16, :], A_BM[:, :],
                                                      start=True, stop=True),
                         r=["onesc", "abm"], w=["ps3"])
                    p.op("scalar", lambda e: e.copy(A_CMROW[0:1, 0:512], PS[3][0:1, :]),
                         r=["ps3"], w=["cmrow0"])
                    p.op("tensor", lambda e: e.matmul(PS[3][:, :], ONES[0:1, 0:128], A_CMROW[0:1, 0:512],
                                                      start=True, stop=True),
                         r=["ones", "cmrow0"], w=["ps3"])
                    for dc in range(NCK):
                        p.op("vector",
                             (lambda d, jj: (lambda e: e.tensor_tensor(
                                 A_WMX[:, :], XT32[:, jj * 4096 + d * TCH:jj * 4096 + (d + 1) * TCH],
                                 A_DEL[:, jj * 4096 + d * TCH:jj * 4096 + (d + 1) * TCH], ALU.mult)))(dc, j),
                             r=[f"xt32_{j}_{dc}", f"del{j}_{dc}"], w=["wmx"])
                        p.op("vector", lambda e: e.tensor_tensor(A_WMX[:, :], A_WMX[:, :], PS[3][:, :], ALU.mult),
                             r=["wmx", "ps3"], w=["wmx"])
                        p.op("vector",
                             (lambda d, t: (lambda e: e.tensor_tensor(Xs(d, t), Xs(d, t), A_WMX[:, :],
                                                                      ALU.add)))(dc, tc),
                             r=["wmx", xb(dc, tc)], w=[xb(dc, tc)])
                for ni in range(N16 if sub >= 5 else 0):
                    for j, tc in enumerate(tcs):
                        ridx = (ni * 2 + j) % 2
                        bi = 2 + ridx
                        bnk = PS[bi]
                        p.op("tensor",
                             (lambda nn, jj, b=bnk: (lambda e: e.matmul(
                                 b[:, :], SEL16B[:, nn * 128:(nn + 1) * 128],
                                 A_CMBF[:, jj * 512:(jj + 1) * 512], start=True, stop=True)))(ni, j),
                             r=["sel16", f"acmbf{j}"], w=[f"ps{bi}"])
                        for dc in range(NCK):
                            ca = dc * 16 + ni
                            p.op("scalar",
                                 (lambda d, jj, c: (lambda e: e.activation(
                                     A_SCR[:, 0:512], A_DEL[:, jj * 4096 + d * TCH:jj * 4096 + (d + 1) * TCH],
                                     AF.Exp, scale=SM_AM[:, c:c + 1])))(dc, j, ca),
                                 r=[f"del{j}_{dc}", "smam"], w=["scrP"])
                            p.op("vector",
                                 (lambda d, jj, c: (lambda e: e.scalar_tensor_tensor(
                                     A_SCR[:, 512:1024], A_DEL[:, jj * 4096 + d * TCH:jj * 4096 + (d + 1) * TCH],
                                     SM_NAP[:, c:c + 1], A_SCR[:, 0:512], ALU.mult, ALU.subtract)))(dc, j, ca),
                                 r=[f"del{j}_{dc}", "smnap", "scrP"], w=["scrT"])
                            p.op("scalar", lambda e: e.activation(A_SCR[:, 1024:1536], A_SCR[:, 512:1024],
                                                                  AF.Exp, bias=CC[:, 1:2]),
                                 r=["scrT", "cc"], w=["scrF"])
                            p.op("vector",
                                 (lambda c, b=bnk: (lambda e: e.scalar_tensor_tensor(
                                     A_WMX[:, :], A_SCR[:, 1024:1536], SM_ST[:, c:c + 1], b[:, :],
                                     ALU.mult, ALU.mult)))(ca),
                                 r=["scrF", "smst", f"ps{bi}"], w=["wmx"])
                            p.op("vector",
                                 (lambda d, t: (lambda e: e.tensor_tensor(Xs(d, t), Xs(d, t), A_WMX[:, :],
                                                                          ALU.add)))(dc, tc),
                                 r=["wmx", xb(dc, tc)], w=[xb(dc, tc)])
                p.barrier()


        def emit_ffn(call_idx):
            ci = str(call_idx)
            FLNROW = A_CM
            flnrow = nc.alloc_sbuf_tensor_at(f"FLNROW{ci}", [1, 1024], F32, offset=O_SCR + 12288)
            sq = FFT2[:, 3584:4096]
            p.op("vector", lambda e: e.memset(XN[:, :], 0.0), w=["xnall"])
            for tc in range(NT):
                emit_ln(tc, (lambda d: XNs(d, tc)), flnrow, sq, "flnrow", "fsq", xn32=FFT)
                for dc in range(NCK):
                    p.buf[f"fxn{tc}_{dc}"] = p.buf.pop("xn" + str(dc))
                for k in range(NCK):
                    p.op("tensor",
                         (lambda kk, t: (lambda e: e.matmul(
                             PS[2][0:8, :], SMB[:, 282 + kk * 8:282 + (kk + 1) * 8],
                             FFT[:, kk * TCH:(kk + 1) * TCH], start=(kk == 0), stop=(kk == NCK - 1))))(k, tc),
                         r=["smb", "xn32"], w=["ps2"])
                p.op("scalar",
                     (lambda t: (lambda e: e.copy(SFM[:, t * TCH:(t + 1) * TCH], PS[2][0:8, :])))(tc),
                     r=["ps2"], w=["glog"])
            for i in range(16):
                p.op("tensor",
                     (lambda ii: (lambda e: e.transpose(PS[3][:, ii * 8:(ii + 1) * 8],
                                                        SFM[0:8, ii * 128:(ii + 1) * 128],
                                                        IDENT[0:8, 0:8])))(i),
                     r=["glog", "ident"], w=["ps3"])
            p.op("scalar", lambda e: e.copy(GTM[:, 0:128], PS[3][:, 0:128]), r=["ps3"], w=["gt_l"])
            p.op("scalar", lambda e: e.activation(GTM[:, 128:256], GTM[:, 0:128], AF.Exp),
                 r=["gt_l"], w=["gt_e"])
            p.op("vector", lambda e: e.tensor_reduce(GSML[:, 0:16],
                                                     GTM[:, 128:256].rearrange("p (a b) -> p a b", b=8),
                                                     AX.X, ALU.add),
                 r=["gt_e"], w=["g_sum"])
            p.op("vector", lambda e: e.reciprocal(GSML[:, 16:32], GSML[:, 0:16]), r=["g_sum"], w=["g_rcp"])
            for i in range(16):
                p.op("vector",
                     (lambda ii: (lambda e: e.tensor_scalar_mul(GTM[:, 256 + ii * 8:256 + (ii + 1) * 8],
                                                                GTM[:, 128 + ii * 8:128 + (ii + 1) * 8],
                                                                GSML[:, 16 + ii:17 + ii])))(i),
                     r=["gt_e", "g_rcp"], w=["gt_s"])
            p.op("vector", lambda e: e.tensor_reduce(GSML[:, 32:48],
                                                     GTM[:, 256:384].rearrange("p (a b) -> p a b", b=8),
                                                     AX.X, ALU.max),
                 r=["gt_s"], w=["g_m1"])
            for i in range(16):
                p.op("vector",
                     (lambda ii: (lambda e: e.tensor_scalar(GTM[:, 384 + ii * 8:384 + (ii + 1) * 8],
                                                            GTM[:, 256 + ii * 8:256 + (ii + 1) * 8],
                                                            GSML[:, 32 + ii:33 + ii], None, ALU.is_ge)))(i),
                     r=["gt_s", "g_m1"], w=["gt_ge"])
            p.op("vector", lambda e: e.scalar_tensor_tensor(GTM[:, 384:512], GTM[:, 384:512], -1e30,
                                                            GTM[:, 256:384], ALU.mult, ALU.add),
                 r=["gt_ge", "gt_s"], w=["gt_mk"])
            p.op("vector", lambda e: e.tensor_reduce(GSML[:, 48:64],
                                                     GTM[:, 384:512].rearrange("p (a b) -> p a b", b=8),
                                                     AX.X, ALU.max),
                 r=["gt_mk"], w=["g_m2"])
            for i in range(16):
                p.op("vector",
                     (lambda ii: (lambda e: e.tensor_scalar(GTM[:, 640 + ii * 8:640 + (ii + 1) * 8],
                                                            GTM[:, 256 + ii * 8:256 + (ii + 1) * 8],
                                                            GSML[:, 48 + ii:49 + ii], None, ALU.is_ge)))(i),
                     r=["gt_s", "g_m2"], w=["gt_mask"])
            for tc in range(NT):
                for w4 in range(4):
                    i = tc * 4 + w4
                    p.op("tensor",
                         (lambda ii, ww: (lambda e: e.transpose(PS[2][0:8, ww * 128:(ww + 1) * 128],
                                                                GTM[:, 256 + ii * 8:256 + (ii + 1) * 8],
                                                                IDENT[:, 0:128])))(i, w4),
                         r=["gt_s", "ident"], w=["ps2"])
                p.op("scalar",
                     (lambda t: (lambda e: e.copy(SFM[:, t * TCH:(t + 1) * TCH], PS[2][0:8, :])))(tc),
                     r=["ps2"], w=["sfm"])
                for w4 in range(4):
                    i = tc * 4 + w4
                    p.op("tensor",
                         (lambda ii, ww: (lambda e: e.transpose(PS[3][0:8, ww * 128:(ww + 1) * 128],
                                                                GTM[:, 640 + ii * 8:640 + (ii + 1) * 8],
                                                                IDENT[:, 0:128])))(i, w4),
                         r=["gt_mask", "ident"], w=["ps3"])
                p.op("scalar",
                     (lambda t: (lambda e: e.copy(MFMB[:, t * TCH:(t + 1) * TCH], PS[3][0:8, :])))(tc),
                     r=["ps3"], w=["mfm"])
            p.barrier()

            stcnt = [0]

            def stage(drt, row0, colspec, slot, sloto, slotname):
                i = stcnt[0]
                stcnt[0] += 1
                st = WST[i % 2]
                stn = f"fwst{i % 2}"
                dma("sync", st[:, 0:1024], drt[row0:row0 + 128, colspec[0]:colspec[1]]
                    if colspec else drt[row0:row0 + 128, :], w=[stn])
                p.op("scalar",
                     (lambda s_, so: (lambda e: e.copy(FW[slot][:, so:so + 1024], s_[:, 0:1024])))(st, sloto),
                     r=[stn, "cc"], w=[slotname])

            for h in range(2):
                for kc in range(NCK):
                    stage(dr["fSw1T"], kc * 128, (h * 1024, (h + 1) * 1024), 0, kc * 1024, "fw0")
                for kc in range(NCK):
                    stage(dr["fSw2T"], kc * 128, (h * 1024, (h + 1) * 1024), 1, kc * 1024, "fw1")
                for icl in range(8):
                    stage(dr["fSw3T"], (h * 8 + icl) * 128, None, 2, icl * 1024, "fw2")
                for tc in range(NT):
                    for icl in range(8):
                        bi = icl % 2
                        for kc in range(NCK):
                            p.op("tensor",
                                 (lambda k2, il, t, b=PS[icl % 2]: (lambda e: e.matmul(
                                     b[:, :], FW[0][:, k2 * 1024 + il * 128:k2 * 1024 + (il + 1) * 128],
                                     XNs(k2, t), start=(k2 == 0), stop=(k2 == NCK - 1))))(kc, icl, tc),
                                 r=["fw0", f"fxn{tc}_{kc}"], w=[f"ps{icl % 2}"])
                        for kc in range(NCK):
                            p.op("tensor",
                                 (lambda k2, il, t, b=PS[2 + icl % 2]: (lambda e: e.matmul(
                                     b[:, :], FW[1][:, k2 * 1024 + il * 128:k2 * 1024 + (il + 1) * 128],
                                     XNs(k2, t), start=(k2 == 0), stop=(k2 == NCK - 1))))(kc, icl, tc),
                                 r=["fw1", f"fxn{tc}_{kc}"], w=[f"ps{2 + icl % 2}"])
                        p.op("scalar",
                             (lambda b=PS[icl % 2]: (lambda e: e.activation(FFT[:, 0:512], b[:, :],
                                                                            AF.Silu)))(),
                             r=[f"ps{icl % 2}"], w=["fab"])
                        p.op("vector",
                             (lambda il, b=PS[2 + icl % 2]: (lambda e: e.tensor_tensor(
                                 PRB[:, il * 512:(il + 1) * 512], FFT[:, 0:512], b[:, :], ALU.mult)))(icl),
                             r=["fab", f"ps{2 + icl % 2}"], w=[f"prb{icl}"])
                    for dc in range(NCK):
                        bi = 4 + dc % 2
                        for icl in range(8):
                            p.op("tensor",
                                 (lambda il, d2, b=PS[4 + dc % 2]: (lambda e: e.matmul(
                                     b[:, :], FW[2][:, il * 1024 + d2 * 128:il * 1024 + (d2 + 1) * 128],
                                     PRB[:, il * 512:(il + 1) * 512],
                                     start=(il == 0), stop=(il == 7))))(icl, dc),
                                 r=["fw2"] + [f"prb{il2}" for il2 in range(8)], w=[f"ps{4 + dc % 2}"])
                        p.op("vector",
                             (lambda d2, t=tc, b=PS[4 + dc % 2]: (lambda e: e.tensor_tensor(
                                 Xs(d2, t), Xs(d2, t), b[:, :], ALU.add)))(dc),
                             r=[f"ps{4 + dc % 2}", xb(dc, tc)], w=[xb(dc, tc)])

            def stage_expert_matrix(e_i, mat, slot, slotname):
                drt = {0: dr["fW1T"], 1: dr["fW3T"], 2: dr["fW2T"]}[mat]
                for kc in range(NCK):
                    i = stcnt[0]
                    stcnt[0] += 1
                    st = WST[i % 2]
                    stn = f"fwst{i % 2}"
                    dma("sync", st[:, 0:1024], drt[e_i, kc * 128:(kc + 1) * 128, :], w=[stn])
                    p.op("scalar",
                         (lambda s_, so: (lambda e: e.copy(FW[slot][:, so:so + 1024], s_[:, 0:1024])))(st, kc * 1024),
                         r=[stn, "cc"], w=[slotname])

            dma("sync", FSEL8F[:, :], dr["c_sel8"][:, :], w=["fsel8"])
            dma("sync", FSEL16[:, :], dr["c_sel16"][0:8, 0:1024], w=["fsel16m"])
            stage_expert_matrix(0, 0, 0, "fw0")
            stage_expert_matrix(0, 1, 1, "fw1")
            stage_expert_matrix(0, 2, 2, "fw2")
            for e_i in range(E8):
                for tc in range(NT):
                    p.op("tensor",
                         (lambda ee, t: (lambda e: e.matmul(PS[2][:, :], FSEL8F[:, ee * 128:(ee + 1) * 128],
                                                            SFM[0:8, t * TCH:(t + 1) * TCH],
                                                            start=True, stop=True)))(e_i, tc),
                         r=["fsel8", "sfm"], w=["ps2"])
                    p.op("scalar", (lambda: (lambda e: e.copy(FFT[:, 1536:2048], PS[2][:, :])))(),
                         r=["ps2"], w=["fsbb"])
                    for hc in range(NCK):
                        for kc in range(NCK):
                            p.op("tensor",
                                 (lambda k2, h2, t, b=PS[hc % 2]: (lambda e: e.matmul(
                                     b[:, :], FW[0][:, k2 * 1024 + h2 * 128:k2 * 1024 + (h2 + 1) * 128],
                                     XNs(k2, t), start=(k2 == 0), stop=(k2 == NCK - 1))))(kc, hc, tc),
                                 r=["fw0", f"fxn{tc}_{kc}"], w=[f"ps{hc % 2}"])
                        p.op("vector",
                             (lambda b=PS[hc % 2]: (lambda e: e.tensor_tensor(FFT[:, 0:512], b[:, :],
                                                                              FFT[:, 1536:2048], ALU.mult)))(),
                             r=[f"ps{hc % 2}", "fsbb"], w=["fhp"])
                        p.op("scalar",
                             (lambda ee, h2: (lambda e: e.activation(HB[:, h2 * 512:(h2 + 1) * 512],
                                                                     FFT[:, 0:512], AF.Silu,
                                                                     bias=SMB[:, 90 + ee * 8 + h2:91 + ee * 8 + h2])))(e_i, hc),
                             r=["fhp", "smb"], w=[f"fh{hc}"])
                    p.op("tensor",
                         (lambda ee, t: (lambda e: e.matmul(PS[3][:, :], FSEL16[:, ee * 128:(ee + 1) * 128],
                                                            MFMB[0:8, t * TCH:(t + 1) * TCH],
                                                            start=True, stop=True)))(e_i, tc),
                         r=["fsel16m", "mfm"], w=["ps3"])
                    for dc in range(NCK):
                        for kc in range(NCK):
                            p.op("tensor",
                                 (lambda k2, d2, t, b=PS[4 + dc % 2]: (lambda e: e.matmul(
                                     b[:, :], FW[1][:, k2 * 1024 + d2 * 128:k2 * 1024 + (d2 + 1) * 128],
                                     XNs(k2, t), start=(k2 == 0), stop=(k2 == NCK - 1))))(kc, dc, tc),
                                 r=["fw1", f"fxn{tc}_{kc}"], w=[f"ps{4 + dc % 2}"])
                        p.op("vector",
                             (lambda b=PS[4 + dc % 2]: (lambda e: e.tensor_tensor(
                                 FFT[:, 512:1024], b[:, :], FFT[:, 1536:2048], ALU.mult)))(),
                             r=[f"ps{4 + dc % 2}", "fsbb"], w=["ft3"])
                        p.op("vector",
                             (lambda ee, d2: (lambda e: e.tensor_scalar_add(
                                 FFT2[:, d2 * 512:(d2 + 1) * 512], FFT[:, 512:1024],
                                 SMB[:, 218 + ee * 8 + d2:219 + ee * 8 + d2])))(e_i, dc),
                             r=["ft3", "smb"], w=[f"fx3_{dc}"])
                    for dc in range(NCK):
                        for hc in range(NCK):
                            p.op("tensor",
                                 (lambda h2, d2, b=PS[6 + dc % 2]: (lambda e: e.matmul(
                                     b[:, :], FW[2][:, h2 * 1024 + d2 * 128:h2 * 1024 + (d2 + 1) * 128],
                                     HB[:, h2 * 512:(h2 + 1) * 512],
                                     start=(h2 == 0), stop=(h2 == NCK - 1))))(hc, dc),
                                 r=["fw2"] + [f"fh{h3}" for h3 in range(NCK)], w=[f"ps{6 + dc % 2}"])
                        p.op("vector",
                             (lambda ee, d2, b=PS[6 + dc % 2]: (lambda e: e.scalar_tensor_tensor(
                                 FFT[:, 2048:2560], b[:, :],
                                 SMB[:, 154 + ee * 8 + d2:155 + ee * 8 + d2],
                                 FFT2[:, d2 * 512:(d2 + 1) * 512], ALU.add, ALU.mult)))(e_i, dc),
                             r=[f"ps{6 + dc % 2}", "smb", f"fx3_{dc}"], w=["fc0"])
                        p.op("vector",
                             (lambda: (lambda e: e.tensor_tensor(FFT[:, 2560:3072], FFT[:, 2048:2560],
                                                                 PS[3][:, :], ALU.mult)))(),
                             r=["fc0", "ps3"], w=["fcm"])
                        p.op("vector",
                             (lambda d2, t=tc: (lambda e: e.tensor_tensor(Xs(d2, t), Xs(d2, t),
                                                                          FFT[:, 2560:3072], ALU.add)))(dc),
                             r=["fcm", xb(dc, tc)], w=[xb(dc, tc)])
                    if tc == 0 and e_i + 1 < E8:
                        pass
                if e_i + 1 < E8:
                    stage_expert_matrix(e_i + 1, 0, 0, "fw0")
                    stage_expert_matrix(e_i + 1, 1, 1, "fw1")
                    stage_expert_matrix(e_i + 1, 2, 2, "fw2")
            p.barrier()

        if emit_upto >= 2:
            emit_ffn(1)

        def emit_attnb():
            blnrow = nc.alloc_sbuf_tensor_at("BLNROW", [1, 1024], F32, offset=O_SCR)
            bsq = nc.alloc_sbuf_tensor_at("BSQ", [128, 512], F32, offset=O_SCR + 4096)
            for tc in range(NT):
                emit_ln(tc, (lambda d: XNs(d, tc)), blnrow, bsq, "blnrow", "bsq")
                for dc in range(NCK):
                    p.buf[f"bxn{tc}_{dc}"] = p.buf.pop("xn" + str(dc))
            IPJW = [nc.alloc_sbuf_tensor_at(f"IPJW{i}", [128, 1024], BF16, offset=O_WST + 4096 + 2048 * i)
                    for i in range(2)]
            def qkv_dst(oc, tc):
                if oc < 8:
                    return QBUF[:, oc * 2048 + tc * TCH: oc * 2048 + (tc + 1) * TCH], f"q{oc}"
                if oc < 16:
                    cc = oc - 8
                    return KBUF[:, cc * 2048 + tc * TCH: cc * 2048 + (tc + 1) * TCH], f"k{cc}"
                cc = oc - 16
                if cc < 4:
                    return V1B[:, cc * 2048 + tc * TCH: cc * 2048 + (tc + 1) * TCH], f"v{cc}"
                return V2B[:, (cc - 4) * 2048 + tc * TCH: (cc - 4) * 2048 + (tc + 1) * TCH], f"v{cc}"

            for oc in range(24):
                ipw = IPJW[oc % 2]
                ipwn = f"ipjw{oc % 2}"
                src_ap = bass.AP(dr["bIpjT"], oc * 128, [[3072, 128], [128 * 3072, 8], [1, 128]])
                dma("sync", WST[0][:, :].rearrange("p (a b) -> p a b", b=128), src_ap, w=["bwst0"])
                p.op("scalar", (lambda ip=ipw: (lambda e: e.copy(ip[:, :], WST[0][:, :])))(),
                     r=["bwst0", "cc"], w=[ipwn])
                for tc in range(NT):
                    bi = 6 + oc % 2
                    for kc in range(NCK):
                        p.op("tensor",
                             (lambda k2, t, ip=ipw, b=PS[6 + oc % 2]: (lambda e: e.matmul(
                                 b[:, :], ip[:, k2 * 128:(k2 + 1) * 128], XNs(k2, t),
                                 start=(k2 == 0), stop=(k2 == NCK - 1))))(kc, tc),
                             r=[ipwn, f"bxn{tc}_{kc}"], w=[f"ps{6 + oc % 2}"])
                    dst, dstn = qkv_dst(oc, tc)
                    p.op("vector",
                         (lambda o2=oc, d2=dst, b=PS[6 + oc % 2]: (lambda e: e.tensor_scalar(
                             d2, b[:, :], SMB[:, 370 + o2:371 + o2], SMB[:, 346 + o2:347 + o2],
                             ALU.mult, ALU.add)))(),
                         r=[f"ps{6 + oc % 2}", "smb"], w=[dstn])
            p.barrier()
            dma("sync", BF16C[:, :], dr["c_F16"][:, :], w=["bf16c"])

            def vslice(cc, tc):
                if cc < 4:
                    return V1B[:, cc * 2048 + tc * TCH: cc * 2048 + (tc + 1) * TCH]
                return V2B[:, (cc - 4) * 2048 + tc * TCH: (cc - 4) * 2048 + (tc + 1) * TCH]

            for s in range(4):
                if s == 0:
                    for cc in range(NCK):
                        p.op("vector",
                             (lambda c2: (lambda e: e.tensor_scalar_mul(
                                 ATTO[:, c2 * 2048:c2 * 2048 + TCH], vslice(c2, 0),
                                 SMB[:, 74 + c2:75 + c2])))(cc),
                             r=[f"v{cc}", "smb"], w=[f"atto{cc}_0"])
                    continue
                for sp in range(s + 1):
                    for cc in range(NCK):
                        p.op("vector",
                             (lambda c2, s2=s, s3=sp: (lambda e: e.tensor_tensor(
                                 BZB1[:, :], QBUF[:, c2 * 2048 + s2 * TCH:c2 * 2048 + (s2 + 1) * TCH],
                                 KBUF[:, c2 * 2048 + s3 * TCH:c2 * 2048 + (s3 + 1) * TCH], ALU.mult)))(cc),
                             r=[f"q{cc}", f"k{cc}"], w=["bzb"])
                        p.op("tensor",
                             (lambda c2, b=PS[sp]: (lambda e: e.matmul(
                                 b[0:16, :], BE16F[:, c2 * 16:(c2 + 1) * 16], BZB1[:, :],
                                 start=(c2 == 0), stop=(c2 == NCK - 1))))(cc),
                             r=["be16f", "bzb"], w=[f"ps{sp}"])
                    p.op("scalar",
                         (lambda s3=sp, b=PS[sp]: (lambda e: e.copy(
                             BSCB[:, s3 * TCH:(s3 + 1) * TCH], b[0:16, :])))(),
                         r=[f"ps{sp}"], w=[f"scb{sp}"])
                p.op("vector", (lambda s2=s: (lambda e: e.tensor_tensor(
                    BMX[:, :], BSCB[:, 0:TCH], BSCB[:, TCH:2 * TCH], ALU.max)))(),
                    r=["scb0", "scb1"], w=["bmx"])
                for sp in range(2, s + 1):
                    p.op("vector", (lambda s3=sp: (lambda e: e.tensor_tensor(
                        BMX[:, :], BMX[:, :], BSCB[:, s3 * TCH:(s3 + 1) * TCH], ALU.max)))(),
                        r=["bmx", f"scb{sp}"], w=["bmx"])
                for sp in range(s + 1):
                    p.op("vector", (lambda s3=sp: (lambda e: e.tensor_tensor(
                        BSCB[:, s3 * TCH:(s3 + 1) * TCH], BSCB[:, s3 * TCH:(s3 + 1) * TCH],
                        BMX[:, :], ALU.subtract)))(),
                        r=[f"scb{sp}", "bmx"], w=[f"scb{sp}"])
                    p.op("scalar", (lambda s3=sp: (lambda e: e.activation(
                        BSCB[:, s3 * TCH:(s3 + 1) * TCH], BSCB[:, s3 * TCH:(s3 + 1) * TCH], AF.Exp)))(),
                        r=[f"scb{sp}"], w=[f"scb{sp}"])
                p.op("vector", lambda e: e.tensor_tensor(BSUM[:, :], BSCB[:, 0:TCH],
                                                         BSCB[:, TCH:2 * TCH], ALU.add),
                     r=["scb0", "scb1"], w=["bsum"])
                for sp in range(2, s + 1):
                    p.op("vector", (lambda s3=sp: (lambda e: e.tensor_tensor(
                        BSUM[:, :], BSUM[:, :], BSCB[:, s3 * TCH:(s3 + 1) * TCH], ALU.add)))(),
                        r=["bsum", f"scb{sp}"], w=["bsum"])
                p.op("vector", lambda e: e.reciprocal(BMX[:, :], BSUM[:, :]), r=["bsum"], w=["brc"])
                for sp in range(s + 1):
                    p.op("vector", (lambda s3=sp: (lambda e: e.tensor_tensor(
                        BSCB[:, s3 * TCH:(s3 + 1) * TCH], BSCB[:, s3 * TCH:(s3 + 1) * TCH],
                        BMX[:, :], ALU.mult)))(),
                        r=[f"scb{sp}", "brc"], w=[f"scb{sp}"])
                    p.op("scalar", (lambda s3=sp: (lambda e: e.copy(
                        BWJ4[0:16, s3 * TCH:(s3 + 1) * TCH], BSCB[:, s3 * TCH:(s3 + 1) * TCH])))(),
                        r=[f"scb{sp}"], w=[f"bwj{sp}"])
                for cc in range(NCK):
                    for sp in range(s + 1):
                        p.op("tensor",
                             (lambda c2, s3=sp, b=PS[4 + sp % 2]: (lambda e: e.matmul(
                                 b[:, :], BF16C[:, c2 * 128:(c2 + 1) * 128],
                                 BWJ4[0:16, s3 * TCH:(s3 + 1) * TCH], start=True, stop=True)))(cc),
                             r=["bf16c", f"bwj{sp}"], w=[f"ps{4 + sp % 2}"])
                        if sp == 0:
                            p.op("vector",
                                 (lambda c2, s3=sp, b=PS[4 + sp % 2]: (lambda e: e.tensor_tensor(
                                     BOAC[:, :], b[:, :], vslice(c2, s3), ALU.mult)))(cc),
                                 r=[f"ps{4 + sp % 2}", f"v{cc}"], w=["boac"])
                        else:
                            p.op("vector",
                                 (lambda c2, s3=sp, b=PS[4 + sp % 2]: (lambda e: e.tensor_tensor(
                                     WST[0][:, 512:1024], b[:, :], vslice(c2, s3), ALU.mult)))(cc),
                                 r=[f"ps{4 + sp % 2}", f"v{cc}"], w=["bwst0h"])
                            p.op("vector", lambda e: e.tensor_tensor(BOAC[:, :], BOAC[:, :],
                                                                     WST[0][:, 512:1024], ALU.add),
                                 r=["boac", "bwst0h"], w=["boac"])
                    p.op("vector",
                         (lambda c2, s2=s: (lambda e: e.tensor_scalar_mul(
                             ATTO[:, c2 * 2048 + s2 * TCH:c2 * 2048 + (s2 + 1) * TCH], BOAC[:, :],
                             SMB[:, 74 + c2:75 + c2])))(cc),
                         r=["boac", "smb"], w=[f"atto{cc}_{s}"])
            p.barrier()
            for oc in range(NCK):
                ipw = IPJW[oc % 2]
                ipwn = f"opjw{oc % 2}"
                src_ap = bass.AP(dr["bOpjT"], oc * 128, [[1024, 128], [128 * 1024, 8], [1, 128]])
                dma("sync", WST[0][:, :].rearrange("p (a b) -> p a b", b=128), src_ap, w=["bwst0"])
                p.op("scalar", (lambda ip=ipw: (lambda e: e.copy(ip[:, :], WST[0][:, :])))(),
                     r=["bwst0", "cc"], w=[ipwn])
                for tc in range(NT):
                    for kc in range(NCK):
                        p.op("tensor",
                             (lambda k2, t, ip=ipw, b=PS[6 + oc % 2]: (lambda e: e.matmul(
                                 b[:, :], ip[:, k2 * 128:(k2 + 1) * 128],
                                 ATTO[:, k2 * 2048 + t * TCH:k2 * 2048 + (t + 1) * TCH],
                                 start=(k2 == 0), stop=(k2 == NCK - 1))))(kc, tc),
                             r=[ipwn, f"atto{kc}_{tc}"], w=[f"ps{6 + oc % 2}"])
                    p.op("vector",
                         (lambda o2=oc, b=PS[6 + oc % 2]: (lambda e: e.tensor_scalar(
                             BOAC[:, :], b[:, :], SMB[:, 82 + o2:83 + o2], SMB[:, 402 + o2:403 + o2],
                             ALU.mult, ALU.add)))(),
                         r=[f"ps{6 + oc % 2}", "smb"], w=["boac"])
                    p.op("vector",
                         (lambda o2=oc, t=tc: (lambda e: e.tensor_tensor(Xs(o2, t), Xs(o2, t),
                                                                         BOAC[:, :], ALU.add)))(),
                         r=["boac", xb(oc, tc)], w=[xb(oc, tc)])
            p.barrier()

        if emit_upto >= 3:
            emit_attnb()

        if emit_upto >= 4:
            emit_ffn(2)

        for dc in range(NCK):
            dma("gpsimd", out_fm[dc * 128:(dc + 1) * 128, :], X[:, dc * 2048:(dc + 1) * 2048],
                w=[f"storedma{dc}"], r=[xb(dc, t) for t in range(NT)], queue_sem="qP")
        p.stream["gpsimd"].append(((("qP", p.count["qP"]),), None, None, 0))

        def make_body(name):
            def body(eng):
                for (waits, fn, incsem, incn) in p.stream[name]:
                    for (s, v) in waits:
                        if v > 0:
                            eng.wait_ge(sems[s], v)
                    inst = fn(eng) if fn is not None else None
                    if incsem is not None:
                        if inst is None:
                            inst = eng.nop()
                        inst.then_inc(sems[incsem], incn)
            return body

        block.sync(make_body("sync"))
        block.tensor(make_body("tensor"))
        block.vector(make_body("vector"))
        block.scalar(make_body("scalar"))
        block.gpsimd(make_body("gpsimd"))

    return nc


def _prep_inputs(inputs):
    f = lambda a: np.ascontiguousarray(a, dtype=np.float32)
    g = {}
    g["aWl1"] = f(inputs["L0_attn_a_Wl1"]); g["aWl2"] = f(inputs["L0_attn_a_Wl2"])
    g["aWf1"] = f(inputs["L0_attn_a_Wf1"]); g["aWf2"] = f(inputs["L0_attn_a_Wf2"])
    g["aW1"] = f(inputs["L0_attn_a_W1"]); g["aW2"] = f(inputs["L0_attn_a_W2"])
    g["aW3"] = f(inputs["L0_attn_a_W3"])
    aB = np.zeros((128, 26), np.float32)
    aB[:, 0] = inputs["L0_attn_a_bl1"]
    aB[:, 1:9] = np.asarray(inputs["L0_attn_a_bl2"]).reshape(8, 128).T
    aB[:, 9] = inputs["L0_attn_a_bf1"]
    aB[:, 10:18] = np.asarray(inputs["L0_attn_a_bf2"]).reshape(8, 128).T
    aB[:, 18:26] = np.asarray(inputs["L0_attn_a_b1"]).reshape(8, 128).T
    g["aBpp"] = aB
    g["aB23"] = f(np.stack([inputs["L0_attn_a_b2"], inputs["L0_attn_a_b3"]], axis=1))
    g["aA"] = f(np.asarray(inputs["L0_attn_a_A"]).reshape(8, 128, 16).transpose(1, 0, 2).reshape(128, 128))
    g["aST"] = f(np.asarray(inputs["L0_attn_a_state_init"]).reshape(8, 128, 16)
                 .transpose(1, 0, 2).reshape(128, 128))
    g["bIpjT"] = f(np.asarray(inputs["L0_attn_b_in_proj_weight"]).T)
    g["bOpjT"] = f(np.asarray(inputs["L0_attn_b_out_proj_weight"]).T)
    bG = np.zeros((128, 64), np.float32)
    bG[:, 0:24] = np.asarray(inputs["L0_attn_b_gamma"]).reshape(24, 128).T
    bG[:, 24:48] = np.asarray(inputs["L0_attn_b_in_proj_bias"]).reshape(24, 128).T
    bG[:, 48:56] = np.asarray(inputs["L0_attn_b_linear_alpha"]).reshape(8, 128).T
    bG[:, 56:64] = np.asarray(inputs["L0_attn_b_linear_gamma"]).reshape(8, 128).T
    g["bGpp"] = bG
    g["bOpb"] = f(np.asarray(inputs["L0_attn_b_out_proj_bias"]).reshape(8, 128).T)
    g["fGwT"] = f(np.asarray(inputs["L0_ffn_gate_w"]).T)
    g["fW1T"] = f(np.asarray(inputs["L0_ffn_w1"]).transpose(0, 2, 1))
    g["fW2T"] = f(np.asarray(inputs["L0_ffn_w2"]).transpose(0, 2, 1))
    g["fW3T"] = f(np.asarray(inputs["L0_ffn_w3"]).transpose(0, 2, 1))
    g["fSw1T"] = f(np.asarray(inputs["L0_ffn_sw1"]).T)
    g["fSw2T"] = f(np.asarray(inputs["L0_ffn_sw2"]).T)
    g["fSw3T"] = f(np.asarray(inputs["L0_ffn_sw3"]).T)
    fB = np.zeros((128, 192), np.float32)
    fB[:, 0:64] = np.asarray(inputs["L0_ffn_b1"]).reshape(8, 8, 128).transpose(2, 0, 1).reshape(128, 64)
    fB[:, 64:128] = np.asarray(inputs["L0_ffn_b2"]).reshape(8, 8, 128).transpose(2, 0, 1).reshape(128, 64)
    fB[:, 128:192] = np.asarray(inputs["L0_ffn_b3"]).reshape(8, 8, 128).transpose(2, 0, 1).reshape(128, 64)
    g["fBpp"] = fB
    g["c_ident"] = np.eye(128, dtype=np.float32)
    cc = np.zeros((128, 8), np.float32)
    cc[:, 0] = EPS; cc[:, 1] = 1.0; cc[:, 2] = -1.0; cc[:, 3] = -0.5; cc[:, 4] = 0.125
    g["c_const"] = cc
    sel16 = np.zeros((16, 2048), np.float32)
    for n in range(16):
        sel16[n, n * 128:(n + 1) * 128] = 1.0
    g["c_sel16"] = sel16.astype(ml_dtypes.bfloat16)
    sel8 = np.zeros((8, 1024), np.float32)
    for e in range(8):
        sel8[e, e * 128:(e + 1) * 128] = 1.0
    g["c_sel8"] = sel8
    e16 = np.zeros((128, 128), np.float32)
    for cc in range(8):
        for k in range(128):
            e16[k, cc * 16 + 2 * cc + (k // 64)] = 1.0
    g["c_E16"] = e16.astype(ml_dtypes.bfloat16)
    g["c_E16f"] = e16.astype(np.float32)
    f16 = np.zeros((16, 1024), np.float32)
    for cc in range(8):
        for m in range(128):
            f16[2 * cc + (m // 64), cc * 128 + m] = 1.0
    g["c_F16"] = f16.astype(ml_dtypes.bfloat16)
    return g


def kernel(**inputs):
    x = np.asarray(inputs["L0_attn_a_x"])
    B, L, _ = x.shape
    shared = _prep_inputs(inputs)
    nc = build_program(_UPTO, _SUB)
    in_maps = []
    for c in range(8):
        xc = np.ascontiguousarray(x[:, c * 512:(c + 1) * 512, :].reshape(B * 512, D).T)
        m = dict(shared)
        m["x_fm"] = xc
        in_maps.append(m)
    res = run_bass_kernel_spmd(nc, in_maps, core_ids=list(range(8)))
    out = np.empty((B, L, D), np.float32)
    for c in range(8):
        out[:, c * 512:(c + 1) * 512, :] = res.results[c]["out_fm"].T.reshape(B, 512, D)
    return out


# version 04
# speedup vs baseline: 1.8309x; 1.0203x over previous
import numpy as np
import ml_dtypes
import concourse.bass as bass
import concourse.mybir as mybir
from concourse.bass_utils import run_bass_kernel_spmd
from contextlib import ExitStack

F32 = mybir.dt.float32
BF16 = mybir.dt.bfloat16
AF = mybir.ActivationFunctionType
ALU = mybir.AluOpType
AX = mybir.AxisListType

D = 1024
NCK = 8
TPC = 2048
TCH = 512
NT = 4
N16 = 16
E8 = 8
EPS = 1e-5

_UPTO = 99
_SUB = 99
_DBG_PAIRS = 2
_DBG_MC = 8
_DBG_DOPS = 9
_DBG_W1OFF = True
_DBG_NODEL = False

O_X = 1536
O_XNR = O_X + 65536
O_WBUF = O_XNR + 32768
O_BIGA = O_WBUF + 49152
O_BIGB = O_BIGA + 32768
O_SCR = O_BIGB + 16384
O_WST = O_SCR + 16384
O_SMALL = O_WST + 8192

ENG2SEM = {"tensor": "sT", "vector": "sV", "scalar": "sS", "gpsimd": "sG", "sync": None}
ALLSEMS = ("qW", "qP", "sT", "sV", "sS", "sG")


class Prog:
    ENGINES = ("sync", "tensor", "vector", "scalar", "gpsimd")

    def __init__(self):
        self.stream = {e: [] for e in self.ENGINES}
        self.count = {s: 0 for s in ALLSEMS}
        self.buf = {}
        self.hiwater = {e: {} for e in self.ENGINES}

    def _b(self, name):
        return self.buf.setdefault(name, {"w": None, "r": []})

    def op(self, eng, fn, r=(), w=(), dma=None, extra=()):
        deps = []
        myeng0 = None if dma else eng
        for name in r:
            b = self._b(name)
            if b["w"]:
                deps.append(b["w"])
            if name.startswith("ps"):
                deps.extend(m for m in b["r"] if m[2] != myeng0)
        for name in w:
            b = self._b(name)
            if b["w"]:
                deps.append(b["w"])
            deps.extend(b["r"])
        waits = {}
        for (sem, val, weng) in deps:
            if weng == "tensor" and myeng0 == "tensor":
                continue
            if val > waits.get(sem, 0):
                waits[sem] = val
        if dma:
            prev = self.count[dma]
            if prev > waits.get(dma, 0):
                waits[dma] = prev
        for (sem, val) in extra:
            if val > waits.get(sem, 0):
                waits[sem] = val
        hw = self.hiwater[eng]
        waits = {s: v for s, v in waits.items() if v > hw.get(s, 0)}
        for s, v in waits.items():
            hw[s] = v
        if dma:
            sem = dma
            self.count[sem] += 16
            mi = (sem, self.count[sem], None)
        else:
            sem = ENG2SEM[eng]
            if sem is None:
                sem = "sG"
            self.count[sem] += 1
            mi = (sem, self.count[sem], eng)
        self.stream[eng].append((tuple(sorted(waits.items())), fn, mi[0], 16 if dma else 1))
        for name in r:
            self._b(name)["r"].append(mi)
        for name in w:
            b = self._b(name)
            b["w"] = mi
            b["r"] = []
        return mi

    def barrier(self):
        snap = [(s, self.count[s]) for s in ALLSEMS if self.count[s] > 0]
        for e in self.ENGINES:
            self.stream[e].append((tuple(snap), None, None, 0))
            for s, v in snap:
                if v > self.hiwater[e].get(s, 0):
                    self.hiwater[e][s] = v
        self.buf = {}


def build_program(emit_upto=99, sub=None):
    if sub is None:
        sub = _SUB
    _DBG_DOPS = globals()["_DBG_DOPS"]
    _DBG_W1OFF = globals()["_DBG_W1OFF"]
    _DBG_NODEL = globals()["_DBG_NODEL"]
    nc = bass.Bass("TRN2", target_bir_lowering=False, dynamic_dma_scratch_size=512)
    dr = {}

    def din(name, shape, dtype=F32):
        dr[name] = nc.dram_tensor(name, list(shape), dtype, kind="ExternalInput")
        return dr[name]

    din("x_fm", (D, TPC))
    din("aWl1", (D, 128)); din("aWl2", (128, D)); din("aWf1", (2 * D, 128))
    din("aWf2", (128, D)); din("aW1", (D, D)); din("aW2", (D, N16)); din("aW3", (D, N16))
    din("aBpp", (128, 26)); din("aB23", (16, 2)); din("aA", (128, 128)); din("aST", (128, 128))
    din("bIpjT", (D, 3 * D)); din("bOpjT", (D, D)); din("bGpp", (128, 64)); din("bOpb", (128, 8))
    din("fGwT", (D, E8))
    din("fW1T", (E8, D, D)); din("fW2T", (E8, D, D)); din("fW3T", (E8, D, D))
    din("fSw1T", (D, 2 * D)); din("fSw2T", (D, 2 * D)); din("fSw3T", (2 * D, D))
    din("fBpp", (128, 192))
    din("c_ident", (128, 128)); din("c_E16", (128, 128), BF16); din("c_F16", (16, 1024), BF16)
    din("c_const", (128, 8))
    din("c_E16f", (128, 128))
    din("c_sel16", (16, 2048), BF16)
    din("c_sel8", (8, 1024))
    out_fm = nc.dram_tensor("out_fm", [D, TPC], F32, kind="ExternalOutput")

    p = Prog()

    with ExitStack() as ctx:
        at = nc.alloc_sbuf_tensor_at
        X = at("X", [128, 16384], F32, offset=O_X)
        XN = at("XN", [128, 16384], BF16, offset=O_XNR)
        XT32 = at("XT32", [128, 8192], F32, offset=O_XNR)
        ATTO = at("ATTO", [128, 16384], BF16, offset=O_XNR)
        AW = at("AW", [128, 16384], BF16, offset=O_WBUF)
        FW = [at(f"FW{i}", [128, 8192], BF16, offset=O_WBUF + 16384 * i) for i in range(3)]
        KBUF = at("KBUF", [128, 16384], BF16, offset=O_WBUF)
        V1B = at("V1B", [128, 8192], BF16, offset=O_WBUF + 32768)
        A_XN = at("A_XN", [128, 8192], BF16, offset=O_BIGA)
        A_LIF = at("A_LIF", [128, 8192], BF16, offset=O_BIGA + 16384)
        A_DEL = at("A_DEL", [128, 8192], F32, offset=O_BIGA)
        QBUF = at("QBUF", [128, 16384], BF16, offset=O_BIGA)
        FFT = at("FFT", [128, 4096], F32, offset=O_BIGA)
        FFT2 = at("FFT2", [128, 4096], F32, offset=O_BIGA + 16384)
        GTM = at("GTM", [128, 768], F32, offset=O_BIGA + 16384)
        GSML = at("GSML", [128, 64], F32, offset=O_BIGA + 16384 + 3072)
        A_XTB = at("A_XTB", [128, 8192], BF16, offset=O_BIGB)
        HB = at("HB", [128, 4096], BF16, offset=O_BIGB)
        PRB = at("PRB", [128, 8192], BF16, offset=O_BIGB)
        V2B = at("V2B", [128, 8192], BF16, offset=O_BIGB)
        SEL16B = at("SEL16B", [16, 2048], BF16, offset=O_WBUF + 28672)
        FSEL8F = at("FSEL8F", [8, 1024], F32, offset=O_BIGB + 8192)
        FSEL16 = at("FSEL16", [8, 1024], BF16, offset=O_BIGB + 12288)
        BWJ4 = at("BWJ4", [16, 2048], BF16, offset=O_WST + 4096)
        BF16C = at("BF16C", [16, 1024], BF16, offset=O_WST)
        BZB1 = at("BZB1", [128, 512], F32, offset=O_SCR + 12288)
        A_SCR = at("A_SCR", [128, 1536], F32, offset=O_SCR)
        A_WMX = at("A_WMX", [128, 512], F32, offset=O_SCR + 6144)
        A_CM = at("A_CM", [16, 1024], F32, offset=O_SCR + 8192)
        A_CMBF = at("A_CMBF", [16, 1024], BF16, offset=O_SCR + 12288)
        A_CMROW = at("A_CMROW", [1, 512], F32, offset=O_SCR + 14336)
        SFM = at("SFM", [8, 2048], F32, offset=O_SCR)
        MFMB = at("MFMB", [8, 2048], BF16, offset=O_SCR + 8192)
        BSCB = at("BSCB", [16, 2048], F32, offset=O_SCR)
        BMX = at("BMX", [16, 512], F32, offset=O_SCR + 8192)
        BSUM = at("BSUM", [16, 512], F32, offset=O_SCR + 10240)
        BRC = at("BRC", [16, 512], F32, offset=O_SCR + 12288)
        BOAC = at("BOAC", [128, 512], F32, offset=O_SCR + 14336)
        WST = [at(f"WST{i}", [128, 1024], F32, offset=O_WST + 4096 * i) for i in range(2)]
        A_LNROW = at("A_LNROW", [1, 1024], F32, offset=O_WST)
        A_BM = at("A_BM", [16, 512], F32, offset=O_WST + 4096)
        HLB = at("HLB", [128, 1024], BF16, offset=O_WST + 6144)
        BZB = at("BZB", [128, 1024], BF16, offset=O_WST)
        BWJ = at("BWJ", [16, 512], BF16, offset=O_WST + 2048)
        SMB = at("SMB", [128, 424], F32, offset=O_SMALL)
        SM_A = at("SM_A", [128, 128], F32, offset=O_SMALL + 1696)
        SM_AM = at("SM_AM", [128, 128], F32, offset=O_SMALL + 2208)
        SM_NAP = at("SM_NAP", [128, 128], F32, offset=O_SMALL + 2720)
        SM_ST = at("SM_ST", [128, 128], F32, offset=O_SMALL + 3232)
        SM_B23 = at("SM_B23", [16, 2], F32, offset=O_SMALL + 3744)
        IDENT = at("IDENT", [128, 128], F32, offset=O_SMALL + 3776)
        ONES = at("ONES", [1, 128], F32, offset=O_SMALL + 4288)
        BE16F = at("BE16F", [128, 128], F32, offset=O_SMALL + 4800)
        ONESC = at("ONESC", [128, 1], F32, offset=O_SMALL + 6336)
        CC = at("CC", [128, 8], F32, offset=O_SMALL + 6368)
        PS = [ctx.enter_context(nc.psum_tensor(f"ps{i}", [128, 512], F32)) for i in range(8)]
        sems = {s: ctx.enter_context(nc.semaphore(s)) for s in ALLSEMS}
        block = ctx.enter_context(nc.Block())

        def Xs(dc, tc, lo=0, ln=TCH):
            return X[:, dc * 2048 + tc * TCH + lo: dc * 2048 + tc * TCH + lo + ln]

        def xb(dc, tc):
            return f"x{dc}_{tc}"

        def XNs(dc, tc, n=TCH):
            return XN[:, dc * 2048 + tc * TCH: dc * 2048 + tc * TCH + n]

        def dma(q, out, in_, w, r=(), queue_sem="qW"):
            return p.op(q, (lambda o, i: (lambda e: e.dma_start(out=o, in_=i)))(out, in_),
                        r=r, w=w, dma=queue_sem)

        dma("sync", CC[:, :], dr["c_const"][:, :], w=["cc"])
        dma("sync", SEL16B[:, :], dr["c_sel16"][:, :], w=["sel16"])
        dma("sync", BE16F[:, :], dr["c_E16f"][:, :], w=["be16f"])
        for dc in range(NCK):
            dma("sync", X[:, dc * 2048:(dc + 1) * 2048], dr["x_fm"][dc * 128:(dc + 1) * 128, :],
                w=[xb(dc, t) for t in range(NT)])
        dma("sync", SMB[:, 0:26], dr["aBpp"][:, :], w=["smb"])
        dma("sync", SMB[:, 26:90], dr["bGpp"][:, :], w=["smb"])
        dma("sync", SMB[:, 90:282], dr["fBpp"][:, :], w=["smb"])
        dma("sync", SMB[:, 394:402], dr["bOpb"][:, :], w=["smb"])
        for k in range(NCK):
            dma("sync", SMB[:, 282 + k * 8:282 + (k + 1) * 8], dr["fGwT"][k * 128:(k + 1) * 128, :],
                w=["smb"])
        dma("sync", SM_A[:, :], dr["aA"][:, :], w=["sma"])
        dma("sync", SM_ST[:, :], dr["aST"][:, :], w=["smst"])
        dma("sync", SM_B23[:, :], dr["aB23"][:, :], w=["smb23"])
        dma("sync", IDENT[:, :], dr["c_ident"][:, :], w=["ident"])
        p.op("vector", lambda e: e.tensor_scalar_min(SM_AM[:, :], SM_A[:, :], 0.0),
             r=["sma"], w=["smam"])
        p.op("vector", lambda e: e.tensor_scalar(SM_NAP[:, :], SM_A[:, :], 0.0, -1.0, ALU.max, ALU.mult),
             r=["sma"], w=["smnap"])
        p.op("vector", lambda e: e.tensor_tensor(SMB[:, 346:370], SMB[:, 26:50], SMB[:, 50:74], ALU.mult),
             r=["smb"], w=["smb"])
        p.op("vector", lambda e: e.tensor_copy(SMB[:, 370:394], SMB[:, 26:50]), r=["smb"], w=["smb"])
        p.op("vector", lambda e: e.tensor_scalar_mul(SMB[:, 370:378], SMB[:, 370:378], 0.125),
             r=["smb"], w=["smb"])
        p.op("vector", lambda e: e.tensor_scalar_mul(SMB[:, 346:354], SMB[:, 346:354], 0.125),
             r=["smb"], w=["smb"])
        p.op("vector", lambda e: e.tensor_tensor(SMB[:, 402:410], SMB[:, 82:90], SMB[:, 394:402], ALU.mult),
             r=["smb"], w=["smb"])
        p.op("vector", lambda e: e.memset(ONES[:, :], 1.0), w=["ones"])
        p.op("vector", lambda e: e.memset(ONESC[:, :], 1.0), w=["onesc"])

        AWO = {"wl1": 0, "wl2": 1024, "wf1": 2048, "wf2": 4096, "w1": 5120, "w2": 13312, "w3": 13440}
        aw_specs = []
        aw_specs += [(dr["aWl1"], k * 128, 128, AWO["wl1"] + k * 128) for k in range(8)]
        aw_specs += [(dr["aWl2"], 0, 1024, AWO["wl2"])]
        aw_specs += [(dr["aWf1"], k * 128, 128, AWO["wf1"] + k * 128) for k in range(16)]
        aw_specs += [(dr["aWf2"], 0, 1024, AWO["wf2"])]
        aw_specs += [(dr["aW1"], k * 128, 1024, AWO["w1"] + k * 1024) for k in range(8)]
        aw_specs += [(dr["aW2"], k * 128, 16, AWO["w2"] + k * 16) for k in range(8)]
        aw_specs += [(dr["aW3"], k * 128, 16, AWO["w3"] + k * 16) for k in range(8)]
        for i, (dt_, r0, ncols, ao) in enumerate(aw_specs):
            st = WST[i % 2]
            stn = f"wst{i % 2}"
            dma("sync", st[:, 0:ncols], dt_[r0:r0 + 128, :], w=[stn])
            p.op("scalar",
                 (lambda s_, n_, a_: (lambda e: e.copy(AW[:, a_:a_ + n_], s_[:, 0:n_])))(st, ncols, ao),
                 r=[stn, "cc"], w=["aw"])

        p.barrier()

        def emit_ln(tc, dst_fn, rowbuf, sqbuf, rown, sqn, xn32=None):
            dsts = [dst_fn(d) for d in range(NCK)]
            b_s1, b_s2, b_mb, b_rb = PS[4], PS[5], PS[6], PS[7]
            for dc in range(NCK):
                p.op("tensor",
                     (lambda d: (lambda e: e.matmul(b_s1[0:1, :], ONESC[:, :], Xs(d, tc),
                                                    start=(d == 0), stop=(d == NCK - 1))))(dc),
                     r=["onesc", xb(dc, tc)], w=(["ps4"] if dc == 0 else ["ps4"]))
            for dc in range(NCK):
                p.op("scalar",
                     (lambda d: (lambda e: e.activation(sqbuf[:, :], Xs(d, tc), AF.Square)))(dc),
                     r=[xb(dc, tc)], w=[sqn])
                p.op("tensor",
                     (lambda d: (lambda e: e.matmul(b_s2[0:1, :], ONESC[:, :], sqbuf[:, :],
                                                    start=(d == 0), stop=(d == NCK - 1))))(dc),
                     r=["onesc", sqn], w=["ps5"])
            mrow = rowbuf[0:1, 0:512]
            rrow = rowbuf[0:1, 512:1024]
            p.op("vector", lambda e, m=mrow: e.tensor_scalar_mul(m, b_s1[0:1, :], 1.0 / D),
                 r=["ps4"], w=[rown])
            p.op("vector", lambda e, rr=rrow: e.tensor_scalar_mul(rr, b_s2[0:1, :], 1.0 / D),
                 r=["ps5"], w=[rown])
            p.op("vector", lambda e, m=mrow: e.tensor_tensor(sqbuf[0:1, 0:512], m, m, ALU.mult),
                 r=[rown], w=[sqn])
            p.op("vector", lambda e, rr=rrow: e.tensor_tensor(rr, rr, sqbuf[0:1, 0:512], ALU.subtract),
                 r=[rown, sqn], w=[rown])
            p.op("scalar", lambda e, rr=rrow: e.activation(rr, rr, AF.Ln, bias=CC[0:1, 0:1]),
                 r=[rown, "cc"], w=[rown])
            p.op("scalar", lambda e, rr=rrow: e.activation(rr, rr, AF.Exp, scale=CC[0:1, 3:4]),
                 r=[rown, "cc"], w=[rown])
            p.op("tensor", lambda e, m=mrow: e.matmul(b_mb[:, :], ONES[0:1, 0:128], m, start=True, stop=True),
                 r=["ones", rown], w=["ps6"])
            p.op("tensor", lambda e, rr=rrow: e.matmul(b_rb[:, :], ONES[0:1, 0:128], rr, start=True, stop=True),
                 r=["ones", rown], w=["ps7"])
            for dc in range(NCK):
                p.op("vector",
                     (lambda d: (lambda e: e.tensor_tensor(sqbuf[:, :], Xs(d, tc), b_mb[:, :], ALU.subtract)))(dc),
                     r=[xb(dc, tc), "ps6"], w=[sqn])
                if xn32 is not None:
                    p.op("vector",
                         (lambda d: (lambda e: e.tensor_tensor(xn32[:, d * TCH:(d + 1) * TCH], sqbuf[:, :],
                                                               b_rb[:, :], ALU.mult)))(dc),
                         r=[sqn, "ps7"], w=["xn32"])
                    p.op("scalar",
                         (lambda d, dst=dsts[dc]: (lambda e: e.copy(dst, xn32[:, d * TCH:(d + 1) * TCH])))(dc),
                         r=["xn32"], w=["xn" + str(dc)])
                else:
                    p.op("vector",
                         (lambda d, dst=dsts[dc]: (lambda e: e.tensor_tensor(dst, sqbuf[:, :], b_rb[:, :], ALU.mult)))(dc),
                         r=[sqn, "ps7"], w=["xn" + str(dc)])

        if emit_upto >= 1:
            for pr in range(2):
                tcs = (2 * pr, 2 * pr + 1)
                for j, tc in enumerate(tcs):
                    emit_ln(tc, (lambda jj: (lambda d: A_XN[:, jj * 4096 + d * TCH:jj * 4096 + (d + 1) * TCH]))(j),
                            A_LNROW, A_WMX, "lnrow", "wmx")
                    for dc in range(NCK):
                        p.buf[f"axn{j}_{dc}"] = p.buf.pop("xn" + str(dc))
                for j, tc in enumerate(tcs if sub >= 2 else ()):
                    hl = HLB[:, j * 512:(j + 1) * 512]
                    hln = f"hlb{j}"
                    b_h = PS[0]
                    for k in range(NCK):
                        p.op("tensor",
                             (lambda kk, jj: (lambda e: e.matmul(
                                 b_h[:, :], AW[:, AWO["wl1"] + kk * 128:AWO["wl1"] + (kk + 1) * 128],
                                 A_XN[:, jj * 4096 + kk * TCH:jj * 4096 + (kk + 1) * TCH],
                                 start=(kk == 0), stop=(kk == NCK - 1))))(k, j),
                             r=["aw", f"axn{j}_{k}"], w=["ps0"])
                    p.op("scalar", lambda e, h=hl: e.activation(h, b_h[:, :], AF.Relu, bias=SMB[:, 0:1]),
                         r=["ps0", "smb"], w=[hln])
                    for mc in range(NCK):
                        bi = mc % 2
                        bnk = PS[bi]
                        p.op("tensor",
                             (lambda m, h=hl, b=bnk: (lambda e: e.matmul(
                                 b[:, :], AW[:, AWO["wl2"] + m * 128:AWO["wl2"] + (m + 1) * 128], h,
                                 start=True, stop=True)))(mc),
                             r=["aw", hln], w=[f"ps{bi}"])
                        p.op("scalar",
                             (lambda m, jj, b=bnk: (lambda e: e.activation(
                                 A_LIF[:, jj * 4096 + m * TCH:jj * 4096 + (m + 1) * TCH], b[:, :],
                                 AF.Tanh, bias=SMB[:, 1 + m:2 + m])))(mc, j),
                             r=[f"ps{bi}", "smb"], w=[f"alif{j}_{mc}"])
                    b_h1 = PS[2]
                    for k in range(16):
                        src = A_XN if k < 8 else A_LIF
                        srcn = (f"axn{j}_{k}" if k < 8 else f"alif{j}_{k - 8}")
                        kk = k % 8
                        p.op("tensor",
                             (lambda sr, k2, kk2, jj: (lambda e: e.matmul(
                                 b_h1[:, :], AW[:, AWO["wf1"] + k2 * 128:AWO["wf1"] + (k2 + 1) * 128],
                                 sr[:, jj * 4096 + kk2 * TCH:jj * 4096 + (kk2 + 1) * TCH],
                                 start=(k2 == 0), stop=(k2 == 15))))(src, k, kk, j),
                             r=["aw", srcn], w=["ps2"])
                    p.op("scalar", lambda e, h=hl: e.activation(h, b_h1[:, :], AF.Relu, bias=SMB[:, 9:10]),
                         r=["ps2", "smb"], w=[hln])
                    for mc in range(NCK):
                        bi = mc % 2
                        bnk = PS[bi]
                        p.op("tensor",
                             (lambda m, h=hl, b=bnk: (lambda e: e.matmul(
                                 b[:, :], AW[:, AWO["wf2"] + m * 128:AWO["wf2"] + (m + 1) * 128], h,
                                 start=True, stop=True)))(mc),
                             r=["aw", hln], w=[f"ps{bi}"])
                        p.op("vector",
                             (lambda m, jj, b=bnk: (lambda e: e.tensor_scalar_add(
                                 XT32[:, jj * 4096 + m * TCH:jj * 4096 + (m + 1) * TCH], b[:, :],
                                 SMB[:, 10 + m:11 + m])))(mc, j),
                             r=[f"ps{bi}", "smb"], w=[f"xt32_{j}_{mc}"])
                        p.op("vector",
                             (lambda m, jj: (lambda e: e.tensor_copy(
                                 A_XTB[:, jj * 4096 + m * TCH:jj * 4096 + (m + 1) * TCH],
                                 XT32[:, jj * 4096 + m * TCH:jj * 4096 + (m + 1) * TCH])))(mc, j),
                             r=[f"xt32_{j}_{mc}"], w=[f"xtb{j}_{mc}"])
                for j, tc in enumerate(tcs if sub >= 3 else ()):
                    if pr > _DBG_PAIRS - 1:
                        continue
                    for mc in range(min(NCK, _DBG_MC)):
                        bi = mc % 2
                        bnk = PS[bi]
                        w1base = AWO["w1"] if _DBG_W1OFF else AWO["wl1"]
                        for k in range(NCK):
                            off = (w1base + kk2 * 1024 + mc * 128 if _DBG_W1OFF else AWO["wl1"]) if False else None
                            p.op("tensor",
                                 (lambda kk, m, jj, b=bnk: (lambda e: e.matmul(
                                     b[:, :],
                                     (AW[:, AWO["w1"] + kk * 1024 + m * 128:AWO["w1"] + kk * 1024 + (m + 1) * 128]
                                      if _DBG_W1OFF else
                                      AW[:, AWO["wl1"] + kk * 128:AWO["wl1"] + (kk + 1) * 128]),
                                     A_XTB[:, jj * 4096 + kk * TCH:jj * 4096 + (kk + 1) * TCH],
                                     start=(kk == 0), stop=(kk == NCK - 1))))(k, mc, j),
                                 r=["aw", f"xtb{j}_{k}"], w=[f"ps{bi}"])
                        bcol = SMB[:, 18 + mc:19 + mc]
                        if _DBG_DOPS >= 1:
                            p.op("vector",
                                 (lambda b=bnk, bc=bcol: (lambda e: e.tensor_scalar(
                                     A_WMX[:, :], b[:, :], bc, 0.0, ALU.add, ALU.max)))(),
                                 r=[f"ps{bi}", "smb"], w=["wmx"])
                        else:
                            p.op("vector", (lambda b=bnk: (lambda e: e.tensor_copy(A_WMX[:, :], b[:, :])))(),
                                 r=[f"ps{bi}"], w=["wmx"])
                        if _DBG_DOPS >= 2:
                            p.op("scalar",
                                 (lambda b=bnk, bc=bcol: (lambda e: e.activation(
                                     A_SCR[:, 0:512], b[:, :], AF.Abs, bias=bc)))(),
                                 r=[f"ps{bi}", "smb"], w=["scrP"])
                        else:
                            p.op("scalar", (lambda b=bnk: (lambda e: e.copy(A_SCR[:, 0:512], b[:, :])))(),
                                 r=[f"ps{bi}"], w=["scrP"])
                        if _DBG_DOPS >= 3:
                            p.op("scalar", lambda e: e.activation(A_SCR[:, 512:1024], A_SCR[:, 0:512],
                                                                  AF.Exp, scale=CC[:, 2:3]),
                                 r=["scrP", "cc"], w=["scrT"])
                        else:
                            p.op("scalar", lambda e: e.copy(A_SCR[:, 512:1024], A_SCR[:, 0:512]),
                                 r=["scrP"], w=["scrT"])
                        if _DBG_DOPS >= 4:
                            p.op("scalar", lambda e: e.activation(A_SCR[:, 1024:1536], A_SCR[:, 512:1024],
                                                                  AF.Ln, bias=CC[:, 1:2]),
                                 r=["scrT", "cc"], w=["scrF"])
                        else:
                            p.op("scalar", lambda e: e.copy(A_SCR[:, 1024:1536], A_SCR[:, 512:1024]),
                                 r=["scrT"], w=["scrF"])
                        if _DBG_NODEL:
                            p.op("vector",
                                 lambda e: e.tensor_tensor(A_WMX[:, :], A_WMX[:, :], A_SCR[:, 1024:1536],
                                                           ALU.add),
                                 r=["wmx", "scrF"], w=["wmx"])
                            p.buf[f"del{j}_{mc}"] = {"w": None, "r": []}
                        else:
                            p.op("vector",
                                 (lambda m, jj: (lambda e: e.tensor_tensor(
                                     A_DEL[:, jj * 4096 + m * TCH:jj * 4096 + (m + 1) * TCH],
                                     A_WMX[:, :], A_SCR[:, 1024:1536], ALU.add)))(mc, j),
                                 r=["wmx", "scrF"] + [f"axn{j}_{mc}", f"alif{j}_{mc}"],
                                 w=[f"del{j}_{mc}"])
                for j, tc in enumerate(tcs if sub >= 4 else ()):
                    b_bc = PS[2]
                    for wname, dst, dstn, bc in (("w2", A_BM[:, :], "abm", SM_B23[:, 0:1]),
                                                 ("w3", A_CM[:, j * 512:(j + 1) * 512], f"acm{j}", SM_B23[:, 1:2])):
                        for k in range(NCK):
                            p.op("tensor",
                                 (lambda kk, wn, jj: (lambda e: e.matmul(
                                     b_bc[0:16, :], AW[:, AWO[wn] + kk * 16:AWO[wn] + (kk + 1) * 16],
                                     A_XTB[:, jj * 4096 + kk * TCH:jj * 4096 + (kk + 1) * TCH],
                                     start=(kk == 0), stop=(kk == NCK - 1))))(k, wname, j),
                                 r=["aw", f"xtb{j}_{k}"], w=["ps2"])
                        p.op("vector",
                             (lambda d2=dst, b2=bc: (lambda e: e.tensor_scalar_add(d2, b_bc[0:16, :], b2)))(),
                             r=["ps2", "smb23"], w=[dstn])
                    p.op("vector",
                         (lambda jj: (lambda e: e.tensor_copy(A_CMBF[:, jj * 512:(jj + 1) * 512],
                                                              A_CM[:, jj * 512:(jj + 1) * 512])))(j),
                         r=[f"acm{j}"], w=[f"acmbf{j}"])
                    p.op("vector",
                         (lambda jj: (lambda e: e.tensor_tensor(A_BM[:, :], A_BM[:, :],
                                                                A_CM[:, jj * 512:(jj + 1) * 512], ALU.mult)))(j),
                         r=["abm", f"acm{j}"], w=["abm"])
                    p.op("tensor", lambda e: e.matmul(PS[3][0:1, :], ONESC[0:16, :], A_BM[:, :],
                                                      start=True, stop=True),
                         r=["onesc", "abm"], w=["ps3"])
                    p.op("scalar", lambda e: e.copy(A_CMROW[0:1, 0:512], PS[3][0:1, :]),
                         r=["ps3"], w=["cmrow0"])
                    p.op("tensor", lambda e: e.matmul(PS[3][:, :], ONES[0:1, 0:128], A_CMROW[0:1, 0:512],
                                                      start=True, stop=True),
                         r=["ones", "cmrow0"], w=["ps3"])
                    for dc in range(NCK):
                        p.op("vector",
                             (lambda d, jj: (lambda e: e.tensor_tensor(
                                 A_WMX[:, :], XT32[:, jj * 4096 + d * TCH:jj * 4096 + (d + 1) * TCH],
                                 A_DEL[:, jj * 4096 + d * TCH:jj * 4096 + (d + 1) * TCH], ALU.mult)))(dc, j),
                             r=[f"xt32_{j}_{dc}", f"del{j}_{dc}"], w=["wmx"])
                        p.op("vector", lambda e: e.tensor_tensor(A_WMX[:, :], A_WMX[:, :], PS[3][:, :], ALU.mult),
                             r=["wmx", "ps3"], w=["wmx"])
                        p.op("vector",
                             (lambda d, t: (lambda e: e.tensor_tensor(Xs(d, t), Xs(d, t), A_WMX[:, :],
                                                                      ALU.add)))(dc, tc),
                             r=["wmx", xb(dc, tc)], w=[xb(dc, tc)])
                for ni in range(N16 if sub >= 5 else 0):
                    for j, tc in enumerate(tcs):
                        ridx = (ni * 2 + j) % 2
                        bi = 2 + ridx
                        bnk = PS[bi]
                        p.op("tensor",
                             (lambda nn, jj, b=bnk: (lambda e: e.matmul(
                                 b[:, :], SEL16B[:, nn * 128:(nn + 1) * 128],
                                 A_CMBF[:, jj * 512:(jj + 1) * 512], start=True, stop=True)))(ni, j),
                             r=["sel16", f"acmbf{j}"], w=[f"ps{bi}"])
                        for dc in range(NCK):
                            ca = dc * 16 + ni
                            p.op("scalar",
                                 (lambda d, jj, c: (lambda e: e.activation(
                                     A_SCR[:, 0:512], A_DEL[:, jj * 4096 + d * TCH:jj * 4096 + (d + 1) * TCH],
                                     AF.Exp, scale=SM_AM[:, c:c + 1])))(dc, j, ca),
                                 r=[f"del{j}_{dc}", "smam"], w=["scrP"])
                            p.op("vector",
                                 (lambda d, jj, c: (lambda e: e.scalar_tensor_tensor(
                                     A_SCR[:, 512:1024], A_DEL[:, jj * 4096 + d * TCH:jj * 4096 + (d + 1) * TCH],
                                     SM_NAP[:, c:c + 1], A_SCR[:, 0:512], ALU.mult, ALU.subtract)))(dc, j, ca),
                                 r=[f"del{j}_{dc}", "smnap", "scrP"], w=["scrT"])
                            p.op("scalar", lambda e: e.activation(A_SCR[:, 1024:1536], A_SCR[:, 512:1024],
                                                                  AF.Exp, bias=CC[:, 1:2]),
                                 r=["scrT", "cc"], w=["scrF"])
                            p.op("vector",
                                 (lambda c, b=bnk: (lambda e: e.scalar_tensor_tensor(
                                     A_WMX[:, :], A_SCR[:, 1024:1536], SM_ST[:, c:c + 1], b[:, :],
                                     ALU.mult, ALU.mult)))(ca),
                                 r=["scrF", "smst", f"ps{bi}"], w=["wmx"])
                            p.op("vector",
                                 (lambda d, t: (lambda e: e.tensor_tensor(Xs(d, t), Xs(d, t), A_WMX[:, :],
                                                                          ALU.add)))(dc, tc),
                                 r=["wmx", xb(dc, tc)], w=[xb(dc, tc)])
                p.barrier()


        def emit_ffn(call_idx):
            ci = str(call_idx)
            FLNROW = A_CM
            flnrow = nc.alloc_sbuf_tensor_at(f"FLNROW{ci}", [1, 1024], F32, offset=O_SCR + 12288)
            sq = FFT2[:, 3584:4096]
            p.op("vector", lambda e: e.memset(XN[:, :], 0.0), w=["xnall"])
            for tc in range(NT):
                emit_ln(tc, (lambda d: XNs(d, tc)), flnrow, sq, "flnrow", "fsq", xn32=FFT)
                for dc in range(NCK):
                    p.buf[f"fxn{tc}_{dc}"] = p.buf.pop("xn" + str(dc))
                for k in range(NCK):
                    p.op("tensor",
                         (lambda kk, t: (lambda e: e.matmul(
                             PS[2][0:8, :], SMB[:, 282 + kk * 8:282 + (kk + 1) * 8],
                             FFT[:, kk * TCH:(kk + 1) * TCH], start=(kk == 0), stop=(kk == NCK - 1))))(k, tc),
                         r=["smb", "xn32"], w=["ps2"])
                p.op("scalar",
                     (lambda t: (lambda e: e.copy(SFM[:, t * TCH:(t + 1) * TCH], PS[2][0:8, :])))(tc),
                     r=["ps2"], w=["glog"])
            for i in range(16):
                p.op("tensor",
                     (lambda ii: (lambda e: e.transpose(PS[3][:, ii * 8:(ii + 1) * 8],
                                                        SFM[0:8, ii * 128:(ii + 1) * 128],
                                                        IDENT[0:8, 0:8])))(i),
                     r=["glog", "ident"], w=["ps3"])
            p.op("scalar", lambda e: e.copy(GTM[:, 0:128], PS[3][:, 0:128]), r=["ps3"], w=["gt_l"])
            p.op("scalar", lambda e: e.activation(GTM[:, 128:256], GTM[:, 0:128], AF.Exp),
                 r=["gt_l"], w=["gt_e"])
            p.op("vector", lambda e: e.tensor_reduce(GSML[:, 0:16],
                                                     GTM[:, 128:256].rearrange("p (a b) -> p a b", b=8),
                                                     AX.X, ALU.add),
                 r=["gt_e"], w=["g_sum"])
            p.op("vector", lambda e: e.reciprocal(GSML[:, 16:32], GSML[:, 0:16]), r=["g_sum"], w=["g_rcp"])
            for i in range(16):
                p.op("vector",
                     (lambda ii: (lambda e: e.tensor_scalar_mul(GTM[:, 256 + ii * 8:256 + (ii + 1) * 8],
                                                                GTM[:, 128 + ii * 8:128 + (ii + 1) * 8],
                                                                GSML[:, 16 + ii:17 + ii])))(i),
                     r=["gt_e", "g_rcp"], w=["gt_s"])
            p.op("vector", lambda e: e.tensor_reduce(GSML[:, 32:48],
                                                     GTM[:, 256:384].rearrange("p (a b) -> p a b", b=8),
                                                     AX.X, ALU.max),
                 r=["gt_s"], w=["g_m1"])
            for i in range(16):
                p.op("vector",
                     (lambda ii: (lambda e: e.tensor_scalar(GTM[:, 384 + ii * 8:384 + (ii + 1) * 8],
                                                            GTM[:, 256 + ii * 8:256 + (ii + 1) * 8],
                                                            GSML[:, 32 + ii:33 + ii], None, ALU.is_ge)))(i),
                     r=["gt_s", "g_m1"], w=["gt_ge"])
            p.op("vector", lambda e: e.scalar_tensor_tensor(GTM[:, 384:512], GTM[:, 384:512], -1e30,
                                                            GTM[:, 256:384], ALU.mult, ALU.add),
                 r=["gt_ge", "gt_s"], w=["gt_mk"])
            p.op("vector", lambda e: e.tensor_reduce(GSML[:, 48:64],
                                                     GTM[:, 384:512].rearrange("p (a b) -> p a b", b=8),
                                                     AX.X, ALU.max),
                 r=["gt_mk"], w=["g_m2"])
            for i in range(16):
                p.op("vector",
                     (lambda ii: (lambda e: e.tensor_scalar(GTM[:, 640 + ii * 8:640 + (ii + 1) * 8],
                                                            GTM[:, 256 + ii * 8:256 + (ii + 1) * 8],
                                                            GSML[:, 48 + ii:49 + ii], None, ALU.is_ge)))(i),
                     r=["gt_s", "g_m2"], w=["gt_mask"])
            for tc in range(NT):
                for w4 in range(4):
                    i = tc * 4 + w4
                    p.op("tensor",
                         (lambda ii, ww: (lambda e: e.transpose(PS[2][0:8, ww * 128:(ww + 1) * 128],
                                                                GTM[:, 256 + ii * 8:256 + (ii + 1) * 8],
                                                                IDENT[:, 0:128])))(i, w4),
                         r=["gt_s", "ident"], w=["ps2"])
                p.op("scalar",
                     (lambda t: (lambda e: e.copy(SFM[:, t * TCH:(t + 1) * TCH], PS[2][0:8, :])))(tc),
                     r=["ps2"], w=["sfm"])
                for w4 in range(4):
                    i = tc * 4 + w4
                    p.op("tensor",
                         (lambda ii, ww: (lambda e: e.transpose(PS[3][0:8, ww * 128:(ww + 1) * 128],
                                                                GTM[:, 640 + ii * 8:640 + (ii + 1) * 8],
                                                                IDENT[:, 0:128])))(i, w4),
                         r=["gt_mask", "ident"], w=["ps3"])
                p.op("scalar",
                     (lambda t: (lambda e: e.copy(MFMB[:, t * TCH:(t + 1) * TCH], PS[3][0:8, :])))(tc),
                     r=["ps3"], w=["mfm"])
            p.barrier()

            stcnt = [0]

            def stage(drt, row0, colspec, slot, sloto, slotname):
                i = stcnt[0]
                stcnt[0] += 1
                st = WST[i % 2]
                stn = f"fwst{i % 2}"
                dma("sync", st[:, 0:1024], drt[row0:row0 + 128, colspec[0]:colspec[1]]
                    if colspec else drt[row0:row0 + 128, :], w=[stn])
                p.op("scalar",
                     (lambda s_, so: (lambda e: e.copy(FW[slot][:, so:so + 1024], s_[:, 0:1024])))(st, sloto),
                     r=[stn, "cc"], w=[slotname])

            for h in range(2):
                for kc in range(NCK):
                    stage(dr["fSw1T"], kc * 128, (h * 1024, (h + 1) * 1024), 0, kc * 1024, "fw0")
                for kc in range(NCK):
                    stage(dr["fSw2T"], kc * 128, (h * 1024, (h + 1) * 1024), 1, kc * 1024, "fw1")
                for icl in range(8):
                    stage(dr["fSw3T"], (h * 8 + icl) * 128, None, 2, icl * 1024, "fw2")
                for tc in range(NT):
                    for icl in range(8):
                        bi = icl % 2
                        for kc in range(NCK):
                            p.op("tensor",
                                 (lambda k2, il, t, b=PS[icl % 2]: (lambda e: e.matmul(
                                     b[:, :], FW[0][:, k2 * 1024 + il * 128:k2 * 1024 + (il + 1) * 128],
                                     XNs(k2, t), start=(k2 == 0), stop=(k2 == NCK - 1))))(kc, icl, tc),
                                 r=["fw0", f"fxn{tc}_{kc}"], w=[f"ps{icl % 2}"])
                        for kc in range(NCK):
                            p.op("tensor",
                                 (lambda k2, il, t, b=PS[2 + icl % 2]: (lambda e: e.matmul(
                                     b[:, :], FW[1][:, k2 * 1024 + il * 128:k2 * 1024 + (il + 1) * 128],
                                     XNs(k2, t), start=(k2 == 0), stop=(k2 == NCK - 1))))(kc, icl, tc),
                                 r=["fw1", f"fxn{tc}_{kc}"], w=[f"ps{2 + icl % 2}"])
                        p.op("scalar",
                             (lambda b=PS[icl % 2]: (lambda e: e.activation(FFT[:, 0:512], b[:, :],
                                                                            AF.Silu)))(),
                             r=[f"ps{icl % 2}"], w=["fab"])
                        p.op("vector",
                             (lambda il, b=PS[2 + icl % 2]: (lambda e: e.tensor_tensor(
                                 PRB[:, il * 512:(il + 1) * 512], FFT[:, 0:512], b[:, :], ALU.mult)))(icl),
                             r=["fab", f"ps{2 + icl % 2}"], w=[f"prb{icl}"])
                    for dc in range(NCK):
                        bi = 4 + dc % 2
                        for icl in range(8):
                            p.op("tensor",
                                 (lambda il, d2, b=PS[4 + dc % 2]: (lambda e: e.matmul(
                                     b[:, :], FW[2][:, il * 1024 + d2 * 128:il * 1024 + (d2 + 1) * 128],
                                     PRB[:, il * 512:(il + 1) * 512],
                                     start=(il == 0), stop=(il == 7))))(icl, dc),
                                 r=["fw2"] + [f"prb{il2}" for il2 in range(8)], w=[f"ps{4 + dc % 2}"])
                        p.op("vector",
                             (lambda d2, t=tc, b=PS[4 + dc % 2]: (lambda e: e.tensor_tensor(
                                 Xs(d2, t), Xs(d2, t), b[:, :], ALU.add)))(dc),
                             r=[f"ps{4 + dc % 2}", xb(dc, tc)], w=[xb(dc, tc)])

            def stage_expert_matrix(e_i, mat, slot, slotname):
                drt = {0: dr["fW1T"], 1: dr["fW3T"], 2: dr["fW2T"]}[mat]
                for kc in range(NCK):
                    i = stcnt[0]
                    stcnt[0] += 1
                    st = WST[i % 2]
                    stn = f"fwst{i % 2}"
                    dma("sync", st[:, 0:1024], drt[e_i, kc * 128:(kc + 1) * 128, :], w=[stn])
                    p.op("scalar",
                         (lambda s_, so: (lambda e: e.copy(FW[slot][:, so:so + 1024], s_[:, 0:1024])))(st, kc * 1024),
                         r=[stn, "cc"], w=[slotname])

            dma("sync", FSEL8F[:, :], dr["c_sel8"][:, :], w=["fsel8"])
            dma("sync", FSEL16[:, :], dr["c_sel16"][0:8, 0:1024], w=["fsel16m"])
            stage_expert_matrix(0, 0, 0, "fw0")
            stage_expert_matrix(0, 1, 1, "fw1")
            stage_expert_matrix(0, 2, 2, "fw2")
            for e_i in range(E8):
                for tc in range(NT):
                    p.op("tensor",
                         (lambda ee, t: (lambda e: e.matmul(PS[2][:, :], FSEL8F[:, ee * 128:(ee + 1) * 128],
                                                            SFM[0:8, t * TCH:(t + 1) * TCH],
                                                            start=True, stop=True)))(e_i, tc),
                         r=["fsel8", "sfm"], w=["ps2"])
                    p.op("scalar", (lambda: (lambda e: e.copy(FFT[:, 1536:2048], PS[2][:, :])))(),
                         r=["ps2"], w=["fsbb"])
                    for hc in range(NCK):
                        for kc in range(NCK):
                            p.op("tensor",
                                 (lambda k2, h2, t, b=PS[hc % 2]: (lambda e: e.matmul(
                                     b[:, :], FW[0][:, k2 * 1024 + h2 * 128:k2 * 1024 + (h2 + 1) * 128],
                                     XNs(k2, t), start=(k2 == 0), stop=(k2 == NCK - 1))))(kc, hc, tc),
                                 r=["fw0", f"fxn{tc}_{kc}"], w=[f"ps{hc % 2}"])
                        p.op("vector",
                             (lambda b=PS[hc % 2]: (lambda e: e.tensor_tensor(FFT[:, 0:512], b[:, :],
                                                                              FFT[:, 1536:2048], ALU.mult)))(),
                             r=[f"ps{hc % 2}", "fsbb"], w=["fhp"])
                        p.op("scalar",
                             (lambda ee, h2: (lambda e: e.activation(HB[:, h2 * 512:(h2 + 1) * 512],
                                                                     FFT[:, 0:512], AF.Silu,
                                                                     bias=SMB[:, 90 + ee * 8 + h2:91 + ee * 8 + h2])))(e_i, hc),
                             r=["fhp", "smb"], w=[f"fh{hc}"])
                    p.op("tensor",
                         (lambda ee, t: (lambda e: e.matmul(PS[3][:, :], FSEL16[:, ee * 128:(ee + 1) * 128],
                                                            MFMB[0:8, t * TCH:(t + 1) * TCH],
                                                            start=True, stop=True)))(e_i, tc),
                         r=["fsel16m", "mfm"], w=["ps3"])
                    p.op("scalar", (lambda: (lambda e: e.copy(FFT[:, 3072:3584], PS[3][:, :])))(),
                         r=["ps3"], w=["fmbb"])
                    for dc in range(NCK):
                        for kc in range(NCK):
                            p.op("tensor",
                                 (lambda k2, d2, t, b=PS[4 + dc % 2]: (lambda e: e.matmul(
                                     b[:, :], FW[1][:, k2 * 1024 + d2 * 128:k2 * 1024 + (d2 + 1) * 128],
                                     XNs(k2, t), start=(k2 == 0), stop=(k2 == NCK - 1))))(kc, dc, tc),
                                 r=["fw1", f"fxn{tc}_{kc}"], w=[f"ps{4 + dc % 2}"])
                        p.op("vector",
                             (lambda b=PS[4 + dc % 2]: (lambda e: e.tensor_tensor(
                                 FFT[:, 512:1024], b[:, :], FFT[:, 1536:2048], ALU.mult)))(),
                             r=[f"ps{4 + dc % 2}", "fsbb"], w=["ft3"])
                        p.op("vector",
                             (lambda ee, d2: (lambda e: e.tensor_scalar_add(
                                 FFT2[:, d2 * 512:(d2 + 1) * 512], FFT[:, 512:1024],
                                 SMB[:, 218 + ee * 8 + d2:219 + ee * 8 + d2])))(e_i, dc),
                             r=["ft3", "smb"], w=[f"fx3_{dc}"])
                    for dc in range(NCK):
                        for hc in range(NCK):
                            p.op("tensor",
                                 (lambda h2, d2, b=PS[6 + dc % 2]: (lambda e: e.matmul(
                                     b[:, :], FW[2][:, h2 * 1024 + d2 * 128:h2 * 1024 + (d2 + 1) * 128],
                                     HB[:, h2 * 512:(h2 + 1) * 512],
                                     start=(h2 == 0), stop=(h2 == NCK - 1))))(hc, dc),
                                 r=["fw2"] + [f"fh{h3}" for h3 in range(NCK)], w=[f"ps{6 + dc % 2}"])
                        p.op("vector",
                             (lambda ee, d2, b=PS[6 + dc % 2]: (lambda e: e.scalar_tensor_tensor(
                                 FFT[:, 2048:2560], b[:, :],
                                 SMB[:, 154 + ee * 8 + d2:155 + ee * 8 + d2],
                                 FFT2[:, d2 * 512:(d2 + 1) * 512], ALU.add, ALU.mult)))(e_i, dc),
                             r=[f"ps{6 + dc % 2}", "smb", f"fx3_{dc}"], w=["fc0"])
                        cmslot = 2560 if dc % 2 == 0 else 3584
                        p.op("vector",
                             (lambda cs=cmslot: (lambda e: e.tensor_tensor(FFT[:, cs:cs + 512],
                                                                           FFT[:, 2048:2560],
                                                                           FFT[:, 3072:3584], ALU.mult)))(),
                             r=["fc0", "fmbb"], w=[f"fcm{dc % 2}"])
                        p.op("gpsimd",
                             (lambda d2, t=tc, cs=cmslot: (lambda e: e.tensor_tensor(
                                 Xs(d2, t), Xs(d2, t), FFT[:, cs:cs + 512], ALU.add)))(dc),
                             r=[f"fcm{dc % 2}", xb(dc, tc)], w=[xb(dc, tc)])
                    if tc == 0 and e_i + 1 < E8:
                        pass
                if e_i + 1 < E8:
                    stage_expert_matrix(e_i + 1, 0, 0, "fw0")
                    stage_expert_matrix(e_i + 1, 1, 1, "fw1")
                    stage_expert_matrix(e_i + 1, 2, 2, "fw2")
            p.barrier()

        if emit_upto >= 2:
            emit_ffn(1)

        def emit_attnb():
            blnrow = nc.alloc_sbuf_tensor_at("BLNROW", [1, 1024], F32, offset=O_SCR)
            bsq = nc.alloc_sbuf_tensor_at("BSQ", [128, 512], F32, offset=O_SCR + 4096)
            for tc in range(NT):
                emit_ln(tc, (lambda d: XNs(d, tc)), blnrow, bsq, "blnrow", "bsq")
                for dc in range(NCK):
                    p.buf[f"bxn{tc}_{dc}"] = p.buf.pop("xn" + str(dc))
            IPJW = [nc.alloc_sbuf_tensor_at(f"IPJW{i}", [128, 1024], BF16, offset=O_WST + 4096 + 2048 * i)
                    for i in range(2)]
            def qkv_dst(oc, tc):
                if oc < 8:
                    return QBUF[:, oc * 2048 + tc * TCH: oc * 2048 + (tc + 1) * TCH], f"q{oc}"
                if oc < 16:
                    cc = oc - 8
                    return KBUF[:, cc * 2048 + tc * TCH: cc * 2048 + (tc + 1) * TCH], f"k{cc}"
                cc = oc - 16
                if cc < 4:
                    return V1B[:, cc * 2048 + tc * TCH: cc * 2048 + (tc + 1) * TCH], f"v{cc}"
                return V2B[:, (cc - 4) * 2048 + tc * TCH: (cc - 4) * 2048 + (tc + 1) * TCH], f"v{cc}"

            for oc in range(24):
                ipw = IPJW[oc % 2]
                ipwn = f"ipjw{oc % 2}"
                src_ap = bass.AP(dr["bIpjT"], oc * 128, [[3072, 128], [128 * 3072, 8], [1, 128]])
                dma("sync", WST[0][:, :].rearrange("p (a b) -> p a b", b=128), src_ap, w=["bwst0"])
                p.op("scalar", (lambda ip=ipw: (lambda e: e.copy(ip[:, :], WST[0][:, :])))(),
                     r=["bwst0", "cc"], w=[ipwn])
                for tc in range(NT):
                    bi = 6 + oc % 2
                    for kc in range(NCK):
                        p.op("tensor",
                             (lambda k2, t, ip=ipw, b=PS[6 + oc % 2]: (lambda e: e.matmul(
                                 b[:, :], ip[:, k2 * 128:(k2 + 1) * 128], XNs(k2, t),
                                 start=(k2 == 0), stop=(k2 == NCK - 1))))(kc, tc),
                             r=[ipwn, f"bxn{tc}_{kc}"], w=[f"ps{6 + oc % 2}"])
                    dst, dstn = qkv_dst(oc, tc)
                    p.op("vector",
                         (lambda o2=oc, d2=dst, b=PS[6 + oc % 2]: (lambda e: e.tensor_scalar(
                             d2, b[:, :], SMB[:, 370 + o2:371 + o2], SMB[:, 346 + o2:347 + o2],
                             ALU.mult, ALU.add)))(),
                         r=[f"ps{6 + oc % 2}", "smb"], w=[dstn])
            p.barrier()
            dma("sync", BF16C[:, :], dr["c_F16"][:, :], w=["bf16c"])

            def vslice(cc, tc):
                if cc < 4:
                    return V1B[:, cc * 2048 + tc * TCH: cc * 2048 + (tc + 1) * TCH]
                return V2B[:, (cc - 4) * 2048 + tc * TCH: (cc - 4) * 2048 + (tc + 1) * TCH]

            for s in range(4):
                if s == 0:
                    for cc in range(NCK):
                        p.op("vector",
                             (lambda c2: (lambda e: e.tensor_scalar_mul(
                                 ATTO[:, c2 * 2048:c2 * 2048 + TCH], vslice(c2, 0),
                                 SMB[:, 74 + c2:75 + c2])))(cc),
                             r=[f"v{cc}", "smb"], w=[f"atto{cc}_0"])
                    continue
                for sp in range(s + 1):
                    for cc in range(NCK):
                        p.op("vector",
                             (lambda c2, s2=s, s3=sp: (lambda e: e.tensor_tensor(
                                 BZB1[:, :], QBUF[:, c2 * 2048 + s2 * TCH:c2 * 2048 + (s2 + 1) * TCH],
                                 KBUF[:, c2 * 2048 + s3 * TCH:c2 * 2048 + (s3 + 1) * TCH], ALU.mult)))(cc),
                             r=[f"q{cc}", f"k{cc}"], w=["bzb"])
                        p.op("tensor",
                             (lambda c2, b=PS[sp]: (lambda e: e.matmul(
                                 b[0:16, :], BE16F[:, c2 * 16:(c2 + 1) * 16], BZB1[:, :],
                                 start=(c2 == 0), stop=(c2 == NCK - 1))))(cc),
                             r=["be16f", "bzb"], w=[f"ps{sp}"])
                    p.op("scalar",
                         (lambda s3=sp, b=PS[sp]: (lambda e: e.copy(
                             BSCB[:, s3 * TCH:(s3 + 1) * TCH], b[0:16, :])))(),
                         r=[f"ps{sp}"], w=[f"scb{sp}"])
                p.op("vector", (lambda s2=s: (lambda e: e.tensor_tensor(
                    BMX[:, :], BSCB[:, 0:TCH], BSCB[:, TCH:2 * TCH], ALU.max)))(),
                    r=["scb0", "scb1"], w=["bmx"])
                for sp in range(2, s + 1):
                    p.op("vector", (lambda s3=sp: (lambda e: e.tensor_tensor(
                        BMX[:, :], BMX[:, :], BSCB[:, s3 * TCH:(s3 + 1) * TCH], ALU.max)))(),
                        r=["bmx", f"scb{sp}"], w=["bmx"])
                for sp in range(s + 1):
                    p.op("vector", (lambda s3=sp: (lambda e: e.tensor_tensor(
                        BSCB[:, s3 * TCH:(s3 + 1) * TCH], BSCB[:, s3 * TCH:(s3 + 1) * TCH],
                        BMX[:, :], ALU.subtract)))(),
                        r=[f"scb{sp}", "bmx"], w=[f"scb{sp}"])
                    p.op("scalar", (lambda s3=sp: (lambda e: e.activation(
                        BSCB[:, s3 * TCH:(s3 + 1) * TCH], BSCB[:, s3 * TCH:(s3 + 1) * TCH], AF.Exp)))(),
                        r=[f"scb{sp}"], w=[f"scb{sp}"])
                p.op("vector", lambda e: e.tensor_tensor(BSUM[:, :], BSCB[:, 0:TCH],
                                                         BSCB[:, TCH:2 * TCH], ALU.add),
                     r=["scb0", "scb1"], w=["bsum"])
                for sp in range(2, s + 1):
                    p.op("vector", (lambda s3=sp: (lambda e: e.tensor_tensor(
                        BSUM[:, :], BSUM[:, :], BSCB[:, s3 * TCH:(s3 + 1) * TCH], ALU.add)))(),
                        r=["bsum", f"scb{sp}"], w=["bsum"])
                p.op("vector", lambda e: e.reciprocal(BMX[:, :], BSUM[:, :]), r=["bsum"], w=["brc"])
                for sp in range(s + 1):
                    p.op("vector", (lambda s3=sp: (lambda e: e.tensor_tensor(
                        BSCB[:, s3 * TCH:(s3 + 1) * TCH], BSCB[:, s3 * TCH:(s3 + 1) * TCH],
                        BMX[:, :], ALU.mult)))(),
                        r=[f"scb{sp}", "brc"], w=[f"scb{sp}"])
                    p.op("scalar", (lambda s3=sp: (lambda e: e.copy(
                        BWJ4[0:16, s3 * TCH:(s3 + 1) * TCH], BSCB[:, s3 * TCH:(s3 + 1) * TCH])))(),
                        r=[f"scb{sp}"], w=[f"bwj{sp}"])
                for cc in range(NCK):
                    for sp in range(s + 1):
                        p.op("tensor",
                             (lambda c2, s3=sp, b=PS[4 + sp % 2]: (lambda e: e.matmul(
                                 b[:, :], BF16C[:, c2 * 128:(c2 + 1) * 128],
                                 BWJ4[0:16, s3 * TCH:(s3 + 1) * TCH], start=True, stop=True)))(cc),
                             r=["bf16c", f"bwj{sp}"], w=[f"ps{4 + sp % 2}"])
                        if sp == 0:
                            p.op("vector",
                                 (lambda c2, s3=sp, b=PS[4 + sp % 2]: (lambda e: e.tensor_tensor(
                                     BOAC[:, :], b[:, :], vslice(c2, s3), ALU.mult)))(cc),
                                 r=[f"ps{4 + sp % 2}", f"v{cc}"], w=["boac"])
                        else:
                            p.op("vector",
                                 (lambda c2, s3=sp, b=PS[4 + sp % 2]: (lambda e: e.tensor_tensor(
                                     WST[0][:, 512:1024], b[:, :], vslice(c2, s3), ALU.mult)))(cc),
                                 r=[f"ps{4 + sp % 2}", f"v{cc}"], w=["bwst0h"])
                            p.op("vector", lambda e: e.tensor_tensor(BOAC[:, :], BOAC[:, :],
                                                                     WST[0][:, 512:1024], ALU.add),
                                 r=["boac", "bwst0h"], w=["boac"])
                    p.op("vector",
                         (lambda c2, s2=s: (lambda e: e.tensor_scalar_mul(
                             ATTO[:, c2 * 2048 + s2 * TCH:c2 * 2048 + (s2 + 1) * TCH], BOAC[:, :],
                             SMB[:, 74 + c2:75 + c2])))(cc),
                         r=["boac", "smb"], w=[f"atto{cc}_{s}"])
            p.barrier()
            for oc in range(NCK):
                ipw = IPJW[oc % 2]
                ipwn = f"opjw{oc % 2}"
                src_ap = bass.AP(dr["bOpjT"], oc * 128, [[1024, 128], [128 * 1024, 8], [1, 128]])
                dma("sync", WST[0][:, :].rearrange("p (a b) -> p a b", b=128), src_ap, w=["bwst0"])
                p.op("scalar", (lambda ip=ipw: (lambda e: e.copy(ip[:, :], WST[0][:, :])))(),
                     r=["bwst0", "cc"], w=[ipwn])
                for tc in range(NT):
                    for kc in range(NCK):
                        p.op("tensor",
                             (lambda k2, t, ip=ipw, b=PS[6 + oc % 2]: (lambda e: e.matmul(
                                 b[:, :], ip[:, k2 * 128:(k2 + 1) * 128],
                                 ATTO[:, k2 * 2048 + t * TCH:k2 * 2048 + (t + 1) * TCH],
                                 start=(k2 == 0), stop=(k2 == NCK - 1))))(kc, tc),
                             r=[ipwn, f"atto{kc}_{tc}"], w=[f"ps{6 + oc % 2}"])
                    p.op("vector",
                         (lambda o2=oc, b=PS[6 + oc % 2]: (lambda e: e.tensor_scalar(
                             BOAC[:, :], b[:, :], SMB[:, 82 + o2:83 + o2], SMB[:, 402 + o2:403 + o2],
                             ALU.mult, ALU.add)))(),
                         r=[f"ps{6 + oc % 2}", "smb"], w=["boac"])
                    p.op("vector",
                         (lambda o2=oc, t=tc: (lambda e: e.tensor_tensor(Xs(o2, t), Xs(o2, t),
                                                                         BOAC[:, :], ALU.add)))(),
                         r=["boac", xb(oc, tc)], w=[xb(oc, tc)])
            p.barrier()

        if emit_upto >= 3:
            emit_attnb()

        if emit_upto >= 4:
            emit_ffn(2)

        for dc in range(NCK):
            dma("gpsimd", out_fm[dc * 128:(dc + 1) * 128, :], X[:, dc * 2048:(dc + 1) * 2048],
                w=[f"storedma{dc}"], r=[xb(dc, t) for t in range(NT)], queue_sem="qP")
        p.stream["gpsimd"].append(((("qP", p.count["qP"]),), None, None, 0))

        def make_body(name):
            def body(eng):
                for (waits, fn, incsem, incn) in p.stream[name]:
                    for (s, v) in waits:
                        if v > 0:
                            eng.wait_ge(sems[s], v)
                    inst = fn(eng) if fn is not None else None
                    if incsem is not None:
                        if inst is None:
                            inst = eng.nop()
                        inst.then_inc(sems[incsem], incn)
            return body

        block.sync(make_body("sync"))
        block.tensor(make_body("tensor"))
        block.vector(make_body("vector"))
        block.scalar(make_body("scalar"))
        block.gpsimd(make_body("gpsimd"))

    return nc


def _prep_inputs(inputs):
    f = lambda a: np.ascontiguousarray(a, dtype=np.float32)
    g = {}
    g["aWl1"] = f(inputs["L0_attn_a_Wl1"]); g["aWl2"] = f(inputs["L0_attn_a_Wl2"])
    g["aWf1"] = f(inputs["L0_attn_a_Wf1"]); g["aWf2"] = f(inputs["L0_attn_a_Wf2"])
    g["aW1"] = f(inputs["L0_attn_a_W1"]); g["aW2"] = f(inputs["L0_attn_a_W2"])
    g["aW3"] = f(inputs["L0_attn_a_W3"])
    aB = np.zeros((128, 26), np.float32)
    aB[:, 0] = inputs["L0_attn_a_bl1"]
    aB[:, 1:9] = np.asarray(inputs["L0_attn_a_bl2"]).reshape(8, 128).T
    aB[:, 9] = inputs["L0_attn_a_bf1"]
    aB[:, 10:18] = np.asarray(inputs["L0_attn_a_bf2"]).reshape(8, 128).T
    aB[:, 18:26] = np.asarray(inputs["L0_attn_a_b1"]).reshape(8, 128).T
    g["aBpp"] = aB
    g["aB23"] = f(np.stack([inputs["L0_attn_a_b2"], inputs["L0_attn_a_b3"]], axis=1))
    g["aA"] = f(np.asarray(inputs["L0_attn_a_A"]).reshape(8, 128, 16).transpose(1, 0, 2).reshape(128, 128))
    g["aST"] = f(np.asarray(inputs["L0_attn_a_state_init"]).reshape(8, 128, 16)
                 .transpose(1, 0, 2).reshape(128, 128))
    g["bIpjT"] = f(np.asarray(inputs["L0_attn_b_in_proj_weight"]).T)
    g["bOpjT"] = f(np.asarray(inputs["L0_attn_b_out_proj_weight"]).T)
    bG = np.zeros((128, 64), np.float32)
    bG[:, 0:24] = np.asarray(inputs["L0_attn_b_gamma"]).reshape(24, 128).T
    bG[:, 24:48] = np.asarray(inputs["L0_attn_b_in_proj_bias"]).reshape(24, 128).T
    bG[:, 48:56] = np.asarray(inputs["L0_attn_b_linear_alpha"]).reshape(8, 128).T
    bG[:, 56:64] = np.asarray(inputs["L0_attn_b_linear_gamma"]).reshape(8, 128).T
    g["bGpp"] = bG
    g["bOpb"] = f(np.asarray(inputs["L0_attn_b_out_proj_bias"]).reshape(8, 128).T)
    g["fGwT"] = f(np.asarray(inputs["L0_ffn_gate_w"]).T)
    g["fW1T"] = f(np.asarray(inputs["L0_ffn_w1"]).transpose(0, 2, 1))
    g["fW2T"] = f(np.asarray(inputs["L0_ffn_w2"]).transpose(0, 2, 1))
    g["fW3T"] = f(np.asarray(inputs["L0_ffn_w3"]).transpose(0, 2, 1))
    g["fSw1T"] = f(np.asarray(inputs["L0_ffn_sw1"]).T)
    g["fSw2T"] = f(np.asarray(inputs["L0_ffn_sw2"]).T)
    g["fSw3T"] = f(np.asarray(inputs["L0_ffn_sw3"]).T)
    fB = np.zeros((128, 192), np.float32)
    fB[:, 0:64] = np.asarray(inputs["L0_ffn_b1"]).reshape(8, 8, 128).transpose(2, 0, 1).reshape(128, 64)
    fB[:, 64:128] = np.asarray(inputs["L0_ffn_b2"]).reshape(8, 8, 128).transpose(2, 0, 1).reshape(128, 64)
    fB[:, 128:192] = np.asarray(inputs["L0_ffn_b3"]).reshape(8, 8, 128).transpose(2, 0, 1).reshape(128, 64)
    g["fBpp"] = fB
    g["c_ident"] = np.eye(128, dtype=np.float32)
    cc = np.zeros((128, 8), np.float32)
    cc[:, 0] = EPS; cc[:, 1] = 1.0; cc[:, 2] = -1.0; cc[:, 3] = -0.5; cc[:, 4] = 0.125
    g["c_const"] = cc
    sel16 = np.zeros((16, 2048), np.float32)
    for n in range(16):
        sel16[n, n * 128:(n + 1) * 128] = 1.0
    g["c_sel16"] = sel16.astype(ml_dtypes.bfloat16)
    sel8 = np.zeros((8, 1024), np.float32)
    for e in range(8):
        sel8[e, e * 128:(e + 1) * 128] = 1.0
    g["c_sel8"] = sel8
    e16 = np.zeros((128, 128), np.float32)
    for cc in range(8):
        for k in range(128):
            e16[k, cc * 16 + 2 * cc + (k // 64)] = 1.0
    g["c_E16"] = e16.astype(ml_dtypes.bfloat16)
    g["c_E16f"] = e16.astype(np.float32)
    f16 = np.zeros((16, 1024), np.float32)
    for cc in range(8):
        for m in range(128):
            f16[2 * cc + (m // 64), cc * 128 + m] = 1.0
    g["c_F16"] = f16.astype(ml_dtypes.bfloat16)
    return g


def kernel(**inputs):
    x = np.asarray(inputs["L0_attn_a_x"])
    B, L, _ = x.shape
    shared = _prep_inputs(inputs)
    nc = build_program(_UPTO, _SUB)
    in_maps = []
    for c in range(8):
        xc = np.ascontiguousarray(x[:, c * 512:(c + 1) * 512, :].reshape(B * 512, D).T)
        m = dict(shared)
        m["x_fm"] = xc
        in_maps.append(m)
    res = run_bass_kernel_spmd(nc, in_maps, core_ids=list(range(8)))
    out = np.empty((B, L, D), np.float32)
    for c in range(8):
        out[:, c * 512:(c + 1) * 512, :] = res.results[c]["out_fm"].T.reshape(B, 512, D)
    return out


# version 05
# speedup vs baseline: 1.8348x; 1.0021x over previous
import numpy as np
import ml_dtypes
import concourse.bass as bass
import concourse.mybir as mybir
from concourse.bass_utils import run_bass_kernel_spmd
from contextlib import ExitStack

F32 = mybir.dt.float32
BF16 = mybir.dt.bfloat16
AF = mybir.ActivationFunctionType
ALU = mybir.AluOpType
AX = mybir.AxisListType

D = 1024
NCK = 8
TPC = 2048
TCH = 512
NT = 4
N16 = 16
E8 = 8
EPS = 1e-5

_UPTO = 99
_SUB = 99
_DBG_PAIRS = 2
_DBG_MC = 8
_DBG_DOPS = 9
_DBG_W1OFF = True
_DBG_NODEL = False

O_X = 1536
O_XNR = O_X + 65536
O_WBUF = O_XNR + 32768
O_BIGA = O_WBUF + 49152
O_BIGB = O_BIGA + 32768
O_SCR = O_BIGB + 16384
O_WST = O_SCR + 16384
O_SMALL = O_WST + 8192

ENG2SEM = {"tensor": "sT", "vector": "sV", "scalar": "sS", "gpsimd": "sG", "sync": None}
ALLSEMS = ("qW", "qP", "sT", "sV", "sS", "sG")


class Prog:
    ENGINES = ("sync", "tensor", "vector", "scalar", "gpsimd")

    def __init__(self):
        self.stream = {e: [] for e in self.ENGINES}
        self.count = {s: 0 for s in ALLSEMS}
        self.buf = {}
        self.hiwater = {e: {} for e in self.ENGINES}

    def _b(self, name):
        return self.buf.setdefault(name, {"w": None, "r": []})

    def op(self, eng, fn, r=(), w=(), dma=None, extra=()):
        deps = []
        myeng0 = None if dma else eng
        for name in r:
            b = self._b(name)
            if b["w"]:
                deps.append(b["w"])
            if name.startswith("ps"):
                deps.extend(m for m in b["r"] if m[2] != myeng0)
        for name in w:
            b = self._b(name)
            if b["w"]:
                deps.append(b["w"])
            deps.extend(b["r"])
        waits = {}
        for (sem, val, weng) in deps:
            if weng == "tensor" and myeng0 == "tensor":
                continue
            if val > waits.get(sem, 0):
                waits[sem] = val
        if dma:
            prev = self.count[dma]
            if prev > waits.get(dma, 0):
                waits[dma] = prev
        for (sem, val) in extra:
            if val > waits.get(sem, 0):
                waits[sem] = val
        hw = self.hiwater[eng]
        waits = {s: v for s, v in waits.items() if v > hw.get(s, 0)}
        for s, v in waits.items():
            hw[s] = v
        if dma:
            sem = dma
            self.count[sem] += 16
            mi = (sem, self.count[sem], None)
        else:
            sem = ENG2SEM[eng]
            if sem is None:
                sem = "sG"
            self.count[sem] += 1
            mi = (sem, self.count[sem], eng)
        self.stream[eng].append((tuple(sorted(waits.items())), fn, mi[0], 16 if dma else 1))
        for name in r:
            self._b(name)["r"].append(mi)
        for name in w:
            b = self._b(name)
            b["w"] = mi
            b["r"] = []
        return mi

    def barrier(self):
        snap = [(s, self.count[s]) for s in ALLSEMS if self.count[s] > 0]
        for e in self.ENGINES:
            self.stream[e].append((tuple(snap), None, None, 0))
            for s, v in snap:
                if v > self.hiwater[e].get(s, 0):
                    self.hiwater[e][s] = v
        self.buf = {}


def build_program(emit_upto=99, sub=None):
    if sub is None:
        sub = _SUB
    _DBG_DOPS = globals()["_DBG_DOPS"]
    _DBG_W1OFF = globals()["_DBG_W1OFF"]
    _DBG_NODEL = globals()["_DBG_NODEL"]
    nc = bass.Bass("TRN2", target_bir_lowering=False, dynamic_dma_scratch_size=512)
    dr = {}

    def din(name, shape, dtype=F32):
        dr[name] = nc.dram_tensor(name, list(shape), dtype, kind="ExternalInput")
        return dr[name]

    din("x_fm", (D, TPC))
    din("aWl1", (D, 128)); din("aWl2", (128, D)); din("aWf1", (2 * D, 128))
    din("aWf2", (128, D)); din("aW1", (D, D)); din("aW2", (D, N16)); din("aW3", (D, N16))
    din("aBpp", (128, 26)); din("aB23", (16, 2)); din("aA", (128, 128)); din("aST", (128, 128))
    din("bIpjT", (D, 3 * D)); din("bOpjT", (D, D)); din("bGpp", (128, 64)); din("bOpb", (128, 8))
    din("fGwT", (D, E8))
    din("fW1T", (E8, D, D)); din("fW2T", (E8, D, D)); din("fW3T", (E8, D, D))
    din("fSw1T", (D, 2 * D)); din("fSw2T", (D, 2 * D)); din("fSw3T", (2 * D, D))
    din("fBpp", (128, 192))
    din("c_ident", (128, 128)); din("c_E16", (128, 128), BF16); din("c_F16", (16, 1024), BF16)
    din("c_const", (128, 8))
    din("c_E16f", (128, 128))
    din("c_sel16", (16, 2048), BF16)
    din("c_sel8", (8, 1024))
    out_fm = nc.dram_tensor("out_fm", [D, TPC], F32, kind="ExternalOutput")

    p = Prog()

    with ExitStack() as ctx:
        at = nc.alloc_sbuf_tensor_at
        X = at("X", [128, 16384], F32, offset=O_X)
        XN = at("XN", [128, 16384], BF16, offset=O_XNR)
        XT32 = at("XT32", [128, 8192], F32, offset=O_XNR)
        ATTO = at("ATTO", [128, 16384], BF16, offset=O_XNR)
        AW = at("AW", [128, 16384], BF16, offset=O_WBUF)
        FW = [at(f"FW{i}", [128, 8192], BF16, offset=O_WBUF + 16384 * i) for i in range(3)]
        KBUF = at("KBUF", [128, 16384], BF16, offset=O_WBUF)
        V1B = at("V1B", [128, 8192], BF16, offset=O_WBUF + 32768)
        A_XN = at("A_XN", [128, 8192], BF16, offset=O_BIGA)
        A_LIF = at("A_LIF", [128, 8192], BF16, offset=O_BIGA + 16384)
        A_DEL = at("A_DEL", [128, 8192], F32, offset=O_BIGA)
        QBUF = at("QBUF", [128, 16384], BF16, offset=O_BIGA)
        FFT = at("FFT", [128, 4096], F32, offset=O_BIGA)
        FFT2 = at("FFT2", [128, 4096], F32, offset=O_BIGA + 16384)
        GTM = at("GTM", [128, 768], F32, offset=O_BIGA + 16384)
        GSML = at("GSML", [128, 64], F32, offset=O_BIGA + 16384 + 3072)
        A_XTB = at("A_XTB", [128, 8192], BF16, offset=O_BIGB)
        HB = at("HB", [128, 4096], BF16, offset=O_BIGB)
        PRB = at("PRB", [128, 8192], BF16, offset=O_BIGB)
        V2B = at("V2B", [128, 8192], BF16, offset=O_BIGB)
        SEL16B = at("SEL16B", [16, 2048], BF16, offset=O_WBUF + 28672)
        FSEL8F = at("FSEL8F", [8, 1024], F32, offset=O_BIGB + 8192)
        FSEL16 = at("FSEL16", [8, 1024], BF16, offset=O_BIGB + 12288)
        BWJ4 = at("BWJ4", [16, 2048], BF16, offset=O_WST + 4096)
        BF16C = at("BF16C", [16, 1024], BF16, offset=O_WST)
        BZB1 = at("BZB1", [128, 512], F32, offset=O_SCR + 12288)
        A_SCR = at("A_SCR", [128, 1536], F32, offset=O_SCR)
        A_WMX = at("A_WMX", [128, 512], F32, offset=O_SCR + 6144)
        A_CM = at("A_CM", [16, 1024], F32, offset=O_SCR + 8192)
        A_CMBF = at("A_CMBF", [16, 1024], BF16, offset=O_SCR + 12288)
        A_CMROW = at("A_CMROW", [1, 512], F32, offset=O_SCR + 14336)
        SFM = at("SFM", [8, 2048], F32, offset=O_SCR)
        MFMB = at("MFMB", [8, 2048], BF16, offset=O_SCR + 8192)
        BSCB = at("BSCB", [16, 2048], F32, offset=O_SCR)
        BMX = at("BMX", [16, 512], F32, offset=O_SCR + 8192)
        BSUM = at("BSUM", [16, 512], F32, offset=O_SCR + 10240)
        BRC = at("BRC", [16, 512], F32, offset=O_SCR + 12288)
        BOAC = at("BOAC", [128, 512], F32, offset=O_SCR + 14336)
        WST = [at(f"WST{i}", [128, 1024], F32, offset=O_WST + 4096 * i) for i in range(2)]
        A_LNROW = at("A_LNROW", [1, 1024], F32, offset=O_WST)
        A_BM = at("A_BM", [16, 512], F32, offset=O_WST + 4096)
        HLB = at("HLB", [128, 1024], BF16, offset=O_WST + 6144)
        BZB = at("BZB", [128, 1024], BF16, offset=O_WST)
        BWJ = at("BWJ", [16, 512], BF16, offset=O_WST + 2048)
        SMB = at("SMB", [128, 424], F32, offset=O_SMALL)
        SM_A = at("SM_A", [128, 128], F32, offset=O_SMALL + 1696)
        SM_AM = at("SM_AM", [128, 128], F32, offset=O_SMALL + 2208)
        SM_NAP = at("SM_NAP", [128, 128], F32, offset=O_SMALL + 2720)
        SM_ST = at("SM_ST", [128, 128], F32, offset=O_SMALL + 3232)
        SM_B23 = at("SM_B23", [16, 2], F32, offset=O_SMALL + 3744)
        IDENT = at("IDENT", [128, 128], F32, offset=O_SMALL + 3776)
        ONES = at("ONES", [1, 128], F32, offset=O_SMALL + 4288)
        BE16F = at("BE16F", [128, 128], F32, offset=O_SMALL + 4800)
        ONESC = at("ONESC", [128, 1], F32, offset=O_SMALL + 6336)
        CC = at("CC", [128, 8], F32, offset=O_SMALL + 6368)
        PS = [ctx.enter_context(nc.psum_tensor(f"ps{i}", [128, 512], F32)) for i in range(8)]
        sems = {s: ctx.enter_context(nc.semaphore(s)) for s in ALLSEMS}
        block = ctx.enter_context(nc.Block())

        def Xs(dc, tc, lo=0, ln=TCH):
            return X[:, dc * 2048 + tc * TCH + lo: dc * 2048 + tc * TCH + lo + ln]

        def xb(dc, tc):
            return f"x{dc}_{tc}"

        def XNs(dc, tc, n=TCH):
            return XN[:, dc * 2048 + tc * TCH: dc * 2048 + tc * TCH + n]

        def dma(q, out, in_, w, r=(), queue_sem="qW"):
            return p.op(q, (lambda o, i: (lambda e: e.dma_start(out=o, in_=i)))(out, in_),
                        r=r, w=w, dma=queue_sem)

        dma("sync", CC[:, :], dr["c_const"][:, :], w=["cc"])
        dma("sync", SEL16B[:, :], dr["c_sel16"][:, :], w=["sel16"])
        dma("sync", BE16F[:, :], dr["c_E16f"][:, :], w=["be16f"])
        for dc in range(NCK):
            dma("sync", X[:, dc * 2048:(dc + 1) * 2048], dr["x_fm"][dc * 128:(dc + 1) * 128, :],
                w=[xb(dc, t) for t in range(NT)])
        dma("sync", SMB[:, 0:26], dr["aBpp"][:, :], w=["smb"])
        dma("sync", SMB[:, 26:90], dr["bGpp"][:, :], w=["smb"])
        dma("sync", SMB[:, 90:282], dr["fBpp"][:, :], w=["smb"])
        dma("sync", SMB[:, 394:402], dr["bOpb"][:, :], w=["smb"])
        for k in range(NCK):
            dma("sync", SMB[:, 282 + k * 8:282 + (k + 1) * 8], dr["fGwT"][k * 128:(k + 1) * 128, :],
                w=["smb"])
        dma("sync", SM_A[:, :], dr["aA"][:, :], w=["sma"])
        dma("sync", SM_ST[:, :], dr["aST"][:, :], w=["smst"])
        dma("sync", SM_B23[:, :], dr["aB23"][:, :], w=["smb23"])
        dma("sync", IDENT[:, :], dr["c_ident"][:, :], w=["ident"])
        p.op("vector", lambda e: e.tensor_scalar_min(SM_AM[:, :], SM_A[:, :], 0.0),
             r=["sma"], w=["smam"])
        p.op("vector", lambda e: e.tensor_scalar(SM_NAP[:, :], SM_A[:, :], 0.0, -1.0, ALU.max, ALU.mult),
             r=["sma"], w=["smnap"])
        p.op("vector", lambda e: e.tensor_tensor(SMB[:, 346:370], SMB[:, 26:50], SMB[:, 50:74], ALU.mult),
             r=["smb"], w=["smb"])
        p.op("vector", lambda e: e.tensor_copy(SMB[:, 370:394], SMB[:, 26:50]), r=["smb"], w=["smb"])
        p.op("vector", lambda e: e.tensor_scalar_mul(SMB[:, 370:378], SMB[:, 370:378], 0.125),
             r=["smb"], w=["smb"])
        p.op("vector", lambda e: e.tensor_scalar_mul(SMB[:, 346:354], SMB[:, 346:354], 0.125),
             r=["smb"], w=["smb"])
        p.op("vector", lambda e: e.tensor_tensor(SMB[:, 402:410], SMB[:, 82:90], SMB[:, 394:402], ALU.mult),
             r=["smb"], w=["smb"])
        p.op("vector", lambda e: e.memset(ONES[:, :], 1.0), w=["ones"])
        p.op("vector", lambda e: e.memset(ONESC[:, :], 1.0), w=["onesc"])

        AWO = {"wl1": 0, "wl2": 1024, "wf1": 2048, "wf2": 4096, "w1": 5120, "w2": 13312, "w3": 13440}
        aw_specs = []
        aw_specs += [(dr["aWl1"], k * 128, 128, AWO["wl1"] + k * 128) for k in range(8)]
        aw_specs += [(dr["aWl2"], 0, 1024, AWO["wl2"])]
        aw_specs += [(dr["aWf1"], k * 128, 128, AWO["wf1"] + k * 128) for k in range(16)]
        aw_specs += [(dr["aWf2"], 0, 1024, AWO["wf2"])]
        aw_specs += [(dr["aW1"], k * 128, 1024, AWO["w1"] + k * 1024) for k in range(8)]
        aw_specs += [(dr["aW2"], k * 128, 16, AWO["w2"] + k * 16) for k in range(8)]
        aw_specs += [(dr["aW3"], k * 128, 16, AWO["w3"] + k * 16) for k in range(8)]
        for i, (dt_, r0, ncols, ao) in enumerate(aw_specs):
            st = WST[i % 2]
            stn = f"wst{i % 2}"
            dma("sync", st[:, 0:ncols], dt_[r0:r0 + 128, :], w=[stn])
            p.op("scalar",
                 (lambda s_, n_, a_: (lambda e: e.copy(AW[:, a_:a_ + n_], s_[:, 0:n_])))(st, ncols, ao),
                 r=[stn, "cc"], w=["aw"])

        p.barrier()

        def emit_ln(tc, dst_fn, rowbuf, sqbuf, rown, sqn, xn32=None):
            dsts = [dst_fn(d) for d in range(NCK)]
            b_s1, b_s2, b_mb, b_rb = PS[4], PS[5], PS[6], PS[7]
            for dc in range(NCK):
                p.op("tensor",
                     (lambda d: (lambda e: e.matmul(b_s1[0:1, :], ONESC[:, :], Xs(d, tc),
                                                    start=(d == 0), stop=(d == NCK - 1))))(dc),
                     r=["onesc", xb(dc, tc)], w=(["ps4"] if dc == 0 else ["ps4"]))
            for dc in range(NCK):
                p.op("scalar",
                     (lambda d: (lambda e: e.activation(sqbuf[:, :], Xs(d, tc), AF.Square)))(dc),
                     r=[xb(dc, tc)], w=[sqn])
                p.op("tensor",
                     (lambda d: (lambda e: e.matmul(b_s2[0:1, :], ONESC[:, :], sqbuf[:, :],
                                                    start=(d == 0), stop=(d == NCK - 1))))(dc),
                     r=["onesc", sqn], w=["ps5"])
            mrow = rowbuf[0:1, 0:512]
            rrow = rowbuf[0:1, 512:1024]
            p.op("vector", lambda e, m=mrow: e.tensor_scalar_mul(m, b_s1[0:1, :], 1.0 / D),
                 r=["ps4"], w=[rown])
            p.op("vector", lambda e, rr=rrow: e.tensor_scalar_mul(rr, b_s2[0:1, :], 1.0 / D),
                 r=["ps5"], w=[rown])
            p.op("vector", lambda e, m=mrow: e.tensor_tensor(sqbuf[0:1, 0:512], m, m, ALU.mult),
                 r=[rown], w=[sqn])
            p.op("vector", lambda e, rr=rrow: e.tensor_tensor(rr, rr, sqbuf[0:1, 0:512], ALU.subtract),
                 r=[rown, sqn], w=[rown])
            p.op("scalar", lambda e, rr=rrow: e.activation(rr, rr, AF.Ln, bias=CC[0:1, 0:1]),
                 r=[rown, "cc"], w=[rown])
            p.op("scalar", lambda e, rr=rrow: e.activation(rr, rr, AF.Exp, scale=CC[0:1, 3:4]),
                 r=[rown, "cc"], w=[rown])
            p.op("tensor", lambda e, m=mrow: e.matmul(b_mb[:, :], ONES[0:1, 0:128], m, start=True, stop=True),
                 r=["ones", rown], w=["ps6"])
            p.op("tensor", lambda e, rr=rrow: e.matmul(b_rb[:, :], ONES[0:1, 0:128], rr, start=True, stop=True),
                 r=["ones", rown], w=["ps7"])
            for dc in range(NCK):
                p.op("vector",
                     (lambda d: (lambda e: e.tensor_tensor(sqbuf[:, :], Xs(d, tc), b_mb[:, :], ALU.subtract)))(dc),
                     r=[xb(dc, tc), "ps6"], w=[sqn])
                if xn32 is not None:
                    p.op("vector",
                         (lambda d: (lambda e: e.tensor_tensor(xn32[:, d * TCH:(d + 1) * TCH], sqbuf[:, :],
                                                               b_rb[:, :], ALU.mult)))(dc),
                         r=[sqn, "ps7"], w=["xn32"])
                    p.op("scalar",
                         (lambda d, dst=dsts[dc]: (lambda e: e.copy(dst, xn32[:, d * TCH:(d + 1) * TCH])))(dc),
                         r=["xn32"], w=["xn" + str(dc)])
                else:
                    p.op("vector",
                         (lambda d, dst=dsts[dc]: (lambda e: e.tensor_tensor(dst, sqbuf[:, :], b_rb[:, :], ALU.mult)))(dc),
                         r=[sqn, "ps7"], w=["xn" + str(dc)])

        if emit_upto >= 1:
            for pr in range(2):
                tcs = (2 * pr, 2 * pr + 1)
                for j, tc in enumerate(tcs):
                    emit_ln(tc, (lambda jj: (lambda d: A_XN[:, jj * 4096 + d * TCH:jj * 4096 + (d + 1) * TCH]))(j),
                            A_LNROW, A_WMX, "lnrow", "wmx")
                    for dc in range(NCK):
                        p.buf[f"axn{j}_{dc}"] = p.buf.pop("xn" + str(dc))
                for j, tc in enumerate(tcs if sub >= 2 else ()):
                    hl = HLB[:, j * 512:(j + 1) * 512]
                    hln = f"hlb{j}"
                    b_h = PS[0]
                    for k in range(NCK):
                        p.op("tensor",
                             (lambda kk, jj: (lambda e: e.matmul(
                                 b_h[:, :], AW[:, AWO["wl1"] + kk * 128:AWO["wl1"] + (kk + 1) * 128],
                                 A_XN[:, jj * 4096 + kk * TCH:jj * 4096 + (kk + 1) * TCH],
                                 start=(kk == 0), stop=(kk == NCK - 1))))(k, j),
                             r=["aw", f"axn{j}_{k}"], w=["ps0"])
                    p.op("scalar", lambda e, h=hl: e.activation(h, b_h[:, :], AF.Relu, bias=SMB[:, 0:1]),
                         r=["ps0", "smb"], w=[hln])
                    for mc in range(NCK):
                        bi = mc % 2
                        bnk = PS[bi]
                        p.op("tensor",
                             (lambda m, h=hl, b=bnk: (lambda e: e.matmul(
                                 b[:, :], AW[:, AWO["wl2"] + m * 128:AWO["wl2"] + (m + 1) * 128], h,
                                 start=True, stop=True)))(mc),
                             r=["aw", hln], w=[f"ps{bi}"])
                        p.op("scalar",
                             (lambda m, jj, b=bnk: (lambda e: e.activation(
                                 A_LIF[:, jj * 4096 + m * TCH:jj * 4096 + (m + 1) * TCH], b[:, :],
                                 AF.Tanh, bias=SMB[:, 1 + m:2 + m])))(mc, j),
                             r=[f"ps{bi}", "smb"], w=[f"alif{j}_{mc}"])
                    b_h1 = PS[2]
                    for k in range(16):
                        src = A_XN if k < 8 else A_LIF
                        srcn = (f"axn{j}_{k}" if k < 8 else f"alif{j}_{k - 8}")
                        kk = k % 8
                        p.op("tensor",
                             (lambda sr, k2, kk2, jj: (lambda e: e.matmul(
                                 b_h1[:, :], AW[:, AWO["wf1"] + k2 * 128:AWO["wf1"] + (k2 + 1) * 128],
                                 sr[:, jj * 4096 + kk2 * TCH:jj * 4096 + (kk2 + 1) * TCH],
                                 start=(k2 == 0), stop=(k2 == 15))))(src, k, kk, j),
                             r=["aw", srcn], w=["ps2"])
                    p.op("scalar", lambda e, h=hl: e.activation(h, b_h1[:, :], AF.Relu, bias=SMB[:, 9:10]),
                         r=["ps2", "smb"], w=[hln])
                    for mc in range(NCK):
                        bi = mc % 2
                        bnk = PS[bi]
                        p.op("tensor",
                             (lambda m, h=hl, b=bnk: (lambda e: e.matmul(
                                 b[:, :], AW[:, AWO["wf2"] + m * 128:AWO["wf2"] + (m + 1) * 128], h,
                                 start=True, stop=True)))(mc),
                             r=["aw", hln], w=[f"ps{bi}"])
                        p.op("vector",
                             (lambda m, jj, b=bnk: (lambda e: e.tensor_scalar_add(
                                 XT32[:, jj * 4096 + m * TCH:jj * 4096 + (m + 1) * TCH], b[:, :],
                                 SMB[:, 10 + m:11 + m])))(mc, j),
                             r=[f"ps{bi}", "smb"], w=[f"xt32_{j}_{mc}"])
                        p.op("vector",
                             (lambda m, jj: (lambda e: e.tensor_copy(
                                 A_XTB[:, jj * 4096 + m * TCH:jj * 4096 + (m + 1) * TCH],
                                 XT32[:, jj * 4096 + m * TCH:jj * 4096 + (m + 1) * TCH])))(mc, j),
                             r=[f"xt32_{j}_{mc}"], w=[f"xtb{j}_{mc}"])
                for j, tc in enumerate(tcs if sub >= 3 else ()):
                    if pr > _DBG_PAIRS - 1:
                        continue
                    for mc in range(min(NCK, _DBG_MC)):
                        bi = mc % 2
                        bnk = PS[bi]
                        w1base = AWO["w1"] if _DBG_W1OFF else AWO["wl1"]
                        for k in range(NCK):
                            off = (w1base + kk2 * 1024 + mc * 128 if _DBG_W1OFF else AWO["wl1"]) if False else None
                            p.op("tensor",
                                 (lambda kk, m, jj, b=bnk: (lambda e: e.matmul(
                                     b[:, :],
                                     (AW[:, AWO["w1"] + kk * 1024 + m * 128:AWO["w1"] + kk * 1024 + (m + 1) * 128]
                                      if _DBG_W1OFF else
                                      AW[:, AWO["wl1"] + kk * 128:AWO["wl1"] + (kk + 1) * 128]),
                                     A_XTB[:, jj * 4096 + kk * TCH:jj * 4096 + (kk + 1) * TCH],
                                     start=(kk == 0), stop=(kk == NCK - 1))))(k, mc, j),
                                 r=["aw", f"xtb{j}_{k}"], w=[f"ps{bi}"])
                        bcol = SMB[:, 18 + mc:19 + mc]
                        if _DBG_DOPS >= 1:
                            p.op("vector",
                                 (lambda b=bnk, bc=bcol: (lambda e: e.tensor_scalar(
                                     A_WMX[:, :], b[:, :], bc, 0.0, ALU.add, ALU.max)))(),
                                 r=[f"ps{bi}", "smb"], w=["wmx"])
                        else:
                            p.op("vector", (lambda b=bnk: (lambda e: e.tensor_copy(A_WMX[:, :], b[:, :])))(),
                                 r=[f"ps{bi}"], w=["wmx"])
                        if _DBG_DOPS >= 2:
                            p.op("scalar",
                                 (lambda b=bnk, bc=bcol: (lambda e: e.activation(
                                     A_SCR[:, 0:512], b[:, :], AF.Abs, bias=bc)))(),
                                 r=[f"ps{bi}", "smb"], w=["scrP"])
                        else:
                            p.op("scalar", (lambda b=bnk: (lambda e: e.copy(A_SCR[:, 0:512], b[:, :])))(),
                                 r=[f"ps{bi}"], w=["scrP"])
                        if _DBG_DOPS >= 3:
                            p.op("scalar", lambda e: e.activation(A_SCR[:, 512:1024], A_SCR[:, 0:512],
                                                                  AF.Exp, scale=CC[:, 2:3]),
                                 r=["scrP", "cc"], w=["scrT"])
                        else:
                            p.op("scalar", lambda e: e.copy(A_SCR[:, 512:1024], A_SCR[:, 0:512]),
                                 r=["scrP"], w=["scrT"])
                        if _DBG_DOPS >= 4:
                            p.op("scalar", lambda e: e.activation(A_SCR[:, 1024:1536], A_SCR[:, 512:1024],
                                                                  AF.Ln, bias=CC[:, 1:2]),
                                 r=["scrT", "cc"], w=["scrF"])
                        else:
                            p.op("scalar", lambda e: e.copy(A_SCR[:, 1024:1536], A_SCR[:, 512:1024]),
                                 r=["scrT"], w=["scrF"])
                        if _DBG_NODEL:
                            p.op("vector",
                                 lambda e: e.tensor_tensor(A_WMX[:, :], A_WMX[:, :], A_SCR[:, 1024:1536],
                                                           ALU.add),
                                 r=["wmx", "scrF"], w=["wmx"])
                            p.buf[f"del{j}_{mc}"] = {"w": None, "r": []}
                        else:
                            p.op("vector",
                                 (lambda m, jj: (lambda e: e.tensor_tensor(
                                     A_DEL[:, jj * 4096 + m * TCH:jj * 4096 + (m + 1) * TCH],
                                     A_WMX[:, :], A_SCR[:, 1024:1536], ALU.add)))(mc, j),
                                 r=["wmx", "scrF"] + [f"axn{j}_{mc}", f"alif{j}_{mc}"],
                                 w=[f"del{j}_{mc}"])
                for j, tc in enumerate(tcs if sub >= 4 else ()):
                    b_bc = PS[2]
                    for wname, dst, dstn, bc in (("w2", A_BM[:, :], "abm", SM_B23[:, 0:1]),
                                                 ("w3", A_CM[:, j * 512:(j + 1) * 512], f"acm{j}", SM_B23[:, 1:2])):
                        for k in range(NCK):
                            p.op("tensor",
                                 (lambda kk, wn, jj: (lambda e: e.matmul(
                                     b_bc[0:16, :], AW[:, AWO[wn] + kk * 16:AWO[wn] + (kk + 1) * 16],
                                     A_XTB[:, jj * 4096 + kk * TCH:jj * 4096 + (kk + 1) * TCH],
                                     start=(kk == 0), stop=(kk == NCK - 1))))(k, wname, j),
                                 r=["aw", f"xtb{j}_{k}"], w=["ps2"])
                        p.op("vector",
                             (lambda d2=dst, b2=bc: (lambda e: e.tensor_scalar_add(d2, b_bc[0:16, :], b2)))(),
                             r=["ps2", "smb23"], w=[dstn])
                    p.op("vector",
                         (lambda jj: (lambda e: e.tensor_copy(A_CMBF[:, jj * 512:(jj + 1) * 512],
                                                              A_CM[:, jj * 512:(jj + 1) * 512])))(j),
                         r=[f"acm{j}"], w=[f"acmbf{j}"])
                    p.op("vector",
                         (lambda jj: (lambda e: e.tensor_tensor(A_BM[:, :], A_BM[:, :],
                                                                A_CM[:, jj * 512:(jj + 1) * 512], ALU.mult)))(j),
                         r=["abm", f"acm{j}"], w=["abm"])
                    p.op("tensor", lambda e: e.matmul(PS[3][0:1, :], ONESC[0:16, :], A_BM[:, :],
                                                      start=True, stop=True),
                         r=["onesc", "abm"], w=["ps3"])
                    p.op("scalar", lambda e: e.copy(A_CMROW[0:1, 0:512], PS[3][0:1, :]),
                         r=["ps3"], w=["cmrow0"])
                    p.op("tensor", lambda e: e.matmul(PS[3][:, :], ONES[0:1, 0:128], A_CMROW[0:1, 0:512],
                                                      start=True, stop=True),
                         r=["ones", "cmrow0"], w=["ps3"])
                    for dc in range(NCK):
                        p.op("vector",
                             (lambda d, jj: (lambda e: e.tensor_tensor(
                                 A_WMX[:, :], XT32[:, jj * 4096 + d * TCH:jj * 4096 + (d + 1) * TCH],
                                 A_DEL[:, jj * 4096 + d * TCH:jj * 4096 + (d + 1) * TCH], ALU.mult)))(dc, j),
                             r=[f"xt32_{j}_{dc}", f"del{j}_{dc}"], w=["wmx"])
                        p.op("vector", lambda e: e.tensor_tensor(A_WMX[:, :], A_WMX[:, :], PS[3][:, :], ALU.mult),
                             r=["wmx", "ps3"], w=["wmx"])
                        p.op("vector",
                             (lambda d, t: (lambda e: e.tensor_tensor(Xs(d, t), Xs(d, t), A_WMX[:, :],
                                                                      ALU.add)))(dc, tc),
                             r=["wmx", xb(dc, tc)], w=[xb(dc, tc)])
                for ni in range(N16 if sub >= 5 else 0):
                    for j, tc in enumerate(tcs):
                        ridx = (ni * 2 + j) % 2
                        bi = 2 + ridx
                        bnk = PS[bi]
                        p.op("tensor",
                             (lambda nn, jj, b=bnk: (lambda e: e.matmul(
                                 b[:, :], SEL16B[:, nn * 128:(nn + 1) * 128],
                                 A_CMBF[:, jj * 512:(jj + 1) * 512], start=True, stop=True)))(ni, j),
                             r=["sel16", f"acmbf{j}"], w=[f"ps{bi}"])
                        for dc in range(NCK):
                            ca = dc * 16 + ni
                            p.op("scalar",
                                 (lambda d, jj, c: (lambda e: e.activation(
                                     A_SCR[:, 0:512], A_DEL[:, jj * 4096 + d * TCH:jj * 4096 + (d + 1) * TCH],
                                     AF.Exp, scale=SM_AM[:, c:c + 1])))(dc, j, ca),
                                 r=[f"del{j}_{dc}", "smam"], w=["scrP"])
                            p.op("vector",
                                 (lambda d, jj, c: (lambda e: e.scalar_tensor_tensor(
                                     A_SCR[:, 512:1024], A_DEL[:, jj * 4096 + d * TCH:jj * 4096 + (d + 1) * TCH],
                                     SM_NAP[:, c:c + 1], A_SCR[:, 0:512], ALU.mult, ALU.subtract)))(dc, j, ca),
                                 r=[f"del{j}_{dc}", "smnap", "scrP"], w=["scrT"])
                            p.op("scalar", lambda e: e.activation(A_SCR[:, 1024:1536], A_SCR[:, 512:1024],
                                                                  AF.Exp, bias=CC[:, 1:2]),
                                 r=["scrT", "cc"], w=["scrF"])
                            p.op("vector",
                                 (lambda c, b=bnk: (lambda e: e.scalar_tensor_tensor(
                                     A_WMX[:, :], A_SCR[:, 1024:1536], SM_ST[:, c:c + 1], b[:, :],
                                     ALU.mult, ALU.mult)))(ca),
                                 r=["scrF", "smst", f"ps{bi}"], w=["wmx"])
                            p.op("vector",
                                 (lambda d, t: (lambda e: e.tensor_tensor(Xs(d, t), Xs(d, t), A_WMX[:, :],
                                                                          ALU.add)))(dc, tc),
                                 r=["wmx", xb(dc, tc)], w=[xb(dc, tc)])
                p.barrier()


        def emit_ffn(call_idx):
            ci = str(call_idx)
            FLNROW = A_CM
            flnrow = nc.alloc_sbuf_tensor_at(f"FLNROW{ci}", [1, 1024], F32, offset=O_SCR + 12288)
            sq = FFT2[:, 3584:4096]
            p.op("vector", lambda e: e.memset(XN[:, :], 0.0), w=["xnall"])
            for tc in range(NT):
                emit_ln(tc, (lambda d: XNs(d, tc)), flnrow, sq, "flnrow", "fsq", xn32=FFT)
                for dc in range(NCK):
                    p.buf[f"fxn{tc}_{dc}"] = p.buf.pop("xn" + str(dc))
                for k in range(NCK):
                    p.op("tensor",
                         (lambda kk, t: (lambda e: e.matmul(
                             PS[2][0:8, :], SMB[:, 282 + kk * 8:282 + (kk + 1) * 8],
                             FFT[:, kk * TCH:(kk + 1) * TCH], start=(kk == 0), stop=(kk == NCK - 1))))(k, tc),
                         r=["smb", "xn32"], w=["ps2"])
                p.op("scalar",
                     (lambda t: (lambda e: e.copy(SFM[:, t * TCH:(t + 1) * TCH], PS[2][0:8, :])))(tc),
                     r=["ps2"], w=["glog"])
            for i in range(16):
                p.op("tensor",
                     (lambda ii: (lambda e: e.transpose(PS[3][:, ii * 8:(ii + 1) * 8],
                                                        SFM[0:8, ii * 128:(ii + 1) * 128],
                                                        IDENT[0:8, 0:8])))(i),
                     r=["glog", "ident"], w=["ps3"])
            p.op("scalar", lambda e: e.copy(GTM[:, 0:128], PS[3][:, 0:128]), r=["ps3"], w=["gt_l"])
            p.op("scalar", lambda e: e.activation(GTM[:, 128:256], GTM[:, 0:128], AF.Exp),
                 r=["gt_l"], w=["gt_e"])
            p.op("vector", lambda e: e.tensor_reduce(GSML[:, 0:16],
                                                     GTM[:, 128:256].rearrange("p (a b) -> p a b", b=8),
                                                     AX.X, ALU.add),
                 r=["gt_e"], w=["g_sum"])
            p.op("vector", lambda e: e.reciprocal(GSML[:, 16:32], GSML[:, 0:16]), r=["g_sum"], w=["g_rcp"])
            for i in range(16):
                p.op("vector",
                     (lambda ii: (lambda e: e.tensor_scalar_mul(GTM[:, 256 + ii * 8:256 + (ii + 1) * 8],
                                                                GTM[:, 128 + ii * 8:128 + (ii + 1) * 8],
                                                                GSML[:, 16 + ii:17 + ii])))(i),
                     r=["gt_e", "g_rcp"], w=["gt_s"])
            p.op("vector", lambda e: e.tensor_reduce(GSML[:, 32:48],
                                                     GTM[:, 256:384].rearrange("p (a b) -> p a b", b=8),
                                                     AX.X, ALU.max),
                 r=["gt_s"], w=["g_m1"])
            for i in range(16):
                p.op("vector",
                     (lambda ii: (lambda e: e.tensor_scalar(GTM[:, 384 + ii * 8:384 + (ii + 1) * 8],
                                                            GTM[:, 256 + ii * 8:256 + (ii + 1) * 8],
                                                            GSML[:, 32 + ii:33 + ii], None, ALU.is_ge)))(i),
                     r=["gt_s", "g_m1"], w=["gt_ge"])
            p.op("vector", lambda e: e.scalar_tensor_tensor(GTM[:, 384:512], GTM[:, 384:512], -1e30,
                                                            GTM[:, 256:384], ALU.mult, ALU.add),
                 r=["gt_ge", "gt_s"], w=["gt_mk"])
            p.op("vector", lambda e: e.tensor_reduce(GSML[:, 48:64],
                                                     GTM[:, 384:512].rearrange("p (a b) -> p a b", b=8),
                                                     AX.X, ALU.max),
                 r=["gt_mk"], w=["g_m2"])
            for i in range(16):
                p.op("vector",
                     (lambda ii: (lambda e: e.tensor_scalar(GTM[:, 640 + ii * 8:640 + (ii + 1) * 8],
                                                            GTM[:, 256 + ii * 8:256 + (ii + 1) * 8],
                                                            GSML[:, 48 + ii:49 + ii], None, ALU.is_ge)))(i),
                     r=["gt_s", "g_m2"], w=["gt_mask"])
            for tc in range(NT):
                for w4 in range(4):
                    i = tc * 4 + w4
                    p.op("tensor",
                         (lambda ii, ww: (lambda e: e.transpose(PS[2][0:8, ww * 128:(ww + 1) * 128],
                                                                GTM[:, 256 + ii * 8:256 + (ii + 1) * 8],
                                                                IDENT[:, 0:128])))(i, w4),
                         r=["gt_s", "ident"], w=["ps2"])
                p.op("scalar",
                     (lambda t: (lambda e: e.copy(SFM[:, t * TCH:(t + 1) * TCH], PS[2][0:8, :])))(tc),
                     r=["ps2"], w=["sfm"])
                for w4 in range(4):
                    i = tc * 4 + w4
                    p.op("tensor",
                         (lambda ii, ww: (lambda e: e.transpose(PS[3][0:8, ww * 128:(ww + 1) * 128],
                                                                GTM[:, 640 + ii * 8:640 + (ii + 1) * 8],
                                                                IDENT[:, 0:128])))(i, w4),
                         r=["gt_mask", "ident"], w=["ps3"])
                p.op("scalar",
                     (lambda t: (lambda e: e.copy(MFMB[:, t * TCH:(t + 1) * TCH], PS[3][0:8, :])))(tc),
                     r=["ps3"], w=["mfm"])
            p.barrier()

            stcnt = [0]

            def stage(drt, row0, colspec, slot, sloto, slotname):
                i = stcnt[0]
                stcnt[0] += 1
                st = WST[i % 2]
                stn = f"fwst{i % 2}"
                dma("sync", st[:, 0:1024], drt[row0:row0 + 128, colspec[0]:colspec[1]]
                    if colspec else drt[row0:row0 + 128, :], w=[stn])
                p.op("scalar",
                     (lambda s_, so: (lambda e: e.copy(FW[slot][:, so:so + 1024], s_[:, 0:1024])))(st, sloto),
                     r=[stn, "cc"], w=[slotname])

            for h in range(2):
                for kc in range(NCK):
                    stage(dr["fSw1T"], kc * 128, (h * 1024, (h + 1) * 1024), 0, kc * 1024, "fw0")
                for kc in range(NCK):
                    stage(dr["fSw2T"], kc * 128, (h * 1024, (h + 1) * 1024), 1, kc * 1024, "fw1")
                for icl in range(8):
                    stage(dr["fSw3T"], (h * 8 + icl) * 128, None, 2, icl * 1024, "fw2")
                for tc in range(NT):
                    for icl in range(8):
                        bi = icl % 2
                        for kc in range(NCK):
                            p.op("tensor",
                                 (lambda k2, il, t, b=PS[icl % 2]: (lambda e: e.matmul(
                                     b[:, :], FW[0][:, k2 * 1024 + il * 128:k2 * 1024 + (il + 1) * 128],
                                     XNs(k2, t), start=(k2 == 0), stop=(k2 == NCK - 1))))(kc, icl, tc),
                                 r=["fw0", f"fxn{tc}_{kc}"], w=[f"ps{icl % 2}"])
                        for kc in range(NCK):
                            p.op("tensor",
                                 (lambda k2, il, t, b=PS[2 + icl % 2]: (lambda e: e.matmul(
                                     b[:, :], FW[1][:, k2 * 1024 + il * 128:k2 * 1024 + (il + 1) * 128],
                                     XNs(k2, t), start=(k2 == 0), stop=(k2 == NCK - 1))))(kc, icl, tc),
                                 r=["fw1", f"fxn{tc}_{kc}"], w=[f"ps{2 + icl % 2}"])
                        p.op("scalar",
                             (lambda b=PS[icl % 2]: (lambda e: e.activation(FFT[:, 0:512], b[:, :],
                                                                            AF.Silu)))(),
                             r=[f"ps{icl % 2}"], w=["fab"])
                        p.op("vector",
                             (lambda il, b=PS[2 + icl % 2]: (lambda e: e.tensor_tensor(
                                 PRB[:, il * 512:(il + 1) * 512], FFT[:, 0:512], b[:, :], ALU.mult)))(icl),
                             r=["fab", f"ps{2 + icl % 2}"], w=[f"prb{icl}"])
                    for dc in range(NCK):
                        bi = 4 + dc % 2
                        for icl in range(8):
                            p.op("tensor",
                                 (lambda il, d2, b=PS[4 + dc % 2]: (lambda e: e.matmul(
                                     b[:, :], FW[2][:, il * 1024 + d2 * 128:il * 1024 + (d2 + 1) * 128],
                                     PRB[:, il * 512:(il + 1) * 512],
                                     start=(il == 0), stop=(il == 7))))(icl, dc),
                                 r=["fw2"] + [f"prb{il2}" for il2 in range(8)], w=[f"ps{4 + dc % 2}"])
                        p.op("vector",
                             (lambda d2, t=tc, b=PS[4 + dc % 2]: (lambda e: e.tensor_tensor(
                                 Xs(d2, t), Xs(d2, t), b[:, :], ALU.add)))(dc),
                             r=[f"ps{4 + dc % 2}", xb(dc, tc)], w=[xb(dc, tc)])

            def stage_expert_matrix(e_i, mat, slot, slotname):
                drt = {0: dr["fW1T"], 1: dr["fW3T"], 2: dr["fW2T"]}[mat]
                for kc in range(NCK):
                    i = stcnt[0]
                    stcnt[0] += 1
                    st = WST[i % 2]
                    stn = f"fwst{i % 2}"
                    dma("sync", st[:, 0:1024], drt[e_i, kc * 128:(kc + 1) * 128, :], w=[stn])
                    p.op("scalar",
                         (lambda s_, so: (lambda e: e.copy(FW[slot][:, so:so + 1024], s_[:, 0:1024])))(st, kc * 1024),
                         r=[stn, "cc"], w=[slotname])

            dma("sync", FSEL8F[:, :], dr["c_sel8"][:, :], w=["fsel8"])
            dma("sync", FSEL16[:, :], dr["c_sel16"][0:8, 0:1024], w=["fsel16m"])
            stage_expert_matrix(0, 0, 0, "fw0")
            stage_expert_matrix(0, 1, 1, "fw1")
            stage_expert_matrix(0, 2, 2, "fw2")
            for e_i in range(E8):
                for tc in range(NT):
                    p.op("tensor",
                         (lambda ee, t: (lambda e: e.matmul(PS[2][:, :], FSEL8F[:, ee * 128:(ee + 1) * 128],
                                                            SFM[0:8, t * TCH:(t + 1) * TCH],
                                                            start=True, stop=True)))(e_i, tc),
                         r=["fsel8", "sfm"], w=["ps2"])
                    p.op("scalar", (lambda: (lambda e: e.copy(FFT[:, 1536:2048], PS[2][:, :])))(),
                         r=["ps2"], w=["fsbb"])
                    for hc in range(NCK):
                        for kc in range(NCK):
                            p.op("tensor",
                                 (lambda k2, h2, t, b=PS[hc % 2]: (lambda e: e.matmul(
                                     b[:, :], FW[0][:, k2 * 1024 + h2 * 128:k2 * 1024 + (h2 + 1) * 128],
                                     XNs(k2, t), start=(k2 == 0), stop=(k2 == NCK - 1))))(kc, hc, tc),
                                 r=["fw0", f"fxn{tc}_{kc}"], w=[f"ps{hc % 2}"])
                        p.op("vector",
                             (lambda b=PS[hc % 2]: (lambda e: e.tensor_tensor(FFT[:, 0:512], b[:, :],
                                                                              FFT[:, 1536:2048], ALU.mult)))(),
                             r=[f"ps{hc % 2}", "fsbb"], w=["fhp"])
                        p.op("scalar",
                             (lambda ee, h2: (lambda e: e.activation(HB[:, h2 * 512:(h2 + 1) * 512],
                                                                     FFT[:, 0:512], AF.Silu,
                                                                     bias=SMB[:, 90 + ee * 8 + h2:91 + ee * 8 + h2])))(e_i, hc),
                             r=["fhp", "smb"], w=[f"fh{hc}"])
                    p.op("tensor",
                         (lambda ee, t: (lambda e: e.matmul(PS[3][:, :], FSEL16[:, ee * 128:(ee + 1) * 128],
                                                            MFMB[0:8, t * TCH:(t + 1) * TCH],
                                                            start=True, stop=True)))(e_i, tc),
                         r=["fsel16m", "mfm"], w=["ps3"])
                    p.op("scalar", (lambda: (lambda e: e.copy(FFT[:, 3072:3584], PS[3][:, :])))(),
                         r=["ps3"], w=["fmbb"])
                    for dc in range(NCK):
                        for kc in range(NCK):
                            p.op("tensor",
                                 (lambda k2, d2, t, b=PS[4 + dc % 2]: (lambda e: e.matmul(
                                     b[:, :], FW[1][:, k2 * 1024 + d2 * 128:k2 * 1024 + (d2 + 1) * 128],
                                     XNs(k2, t), start=(k2 == 0), stop=(k2 == NCK - 1))))(kc, dc, tc),
                                 r=["fw1", f"fxn{tc}_{kc}"], w=[f"ps{4 + dc % 2}"])
                        p.op("vector",
                             (lambda b=PS[4 + dc % 2]: (lambda e: e.tensor_tensor(
                                 FFT[:, 512:1024], b[:, :], FFT[:, 1536:2048], ALU.mult)))(),
                             r=[f"ps{4 + dc % 2}", "fsbb"], w=["ft3"])
                        p.op("vector",
                             (lambda ee, d2: (lambda e: e.tensor_scalar_add(
                                 FFT2[:, d2 * 512:(d2 + 1) * 512], FFT[:, 512:1024],
                                 SMB[:, 218 + ee * 8 + d2:219 + ee * 8 + d2])))(e_i, dc),
                             r=["ft3", "smb"], w=[f"fx3_{dc}"])
                    for dc in range(NCK):
                        for hc in range(NCK):
                            p.op("tensor",
                                 (lambda h2, d2, b=PS[6 + dc % 2]: (lambda e: e.matmul(
                                     b[:, :], FW[2][:, h2 * 1024 + d2 * 128:h2 * 1024 + (d2 + 1) * 128],
                                     HB[:, h2 * 512:(h2 + 1) * 512],
                                     start=(h2 == 0), stop=(h2 == NCK - 1))))(hc, dc),
                                 r=["fw2"] + [f"fh{h3}" for h3 in range(NCK)], w=[f"ps{6 + dc % 2}"])
                        p.op("vector",
                             (lambda ee, d2, b=PS[6 + dc % 2]: (lambda e: e.scalar_tensor_tensor(
                                 FFT[:, 2048:2560], b[:, :],
                                 SMB[:, 154 + ee * 8 + d2:155 + ee * 8 + d2],
                                 FFT2[:, d2 * 512:(d2 + 1) * 512], ALU.add, ALU.mult)))(e_i, dc),
                             r=[f"ps{6 + dc % 2}", "smb", f"fx3_{dc}"], w=["fc0"])
                        cmslot = 2560 if dc % 2 == 0 else 3584
                        p.op("vector",
                             (lambda cs=cmslot: (lambda e: e.tensor_tensor(FFT[:, cs:cs + 512],
                                                                           FFT[:, 2048:2560],
                                                                           FFT[:, 3072:3584], ALU.mult)))(),
                             r=["fc0", "fmbb"], w=[f"fcm{dc % 2}"])
                        p.op("gpsimd",
                             (lambda d2, t=tc, cs=cmslot: (lambda e: e.tensor_tensor(
                                 Xs(d2, t), Xs(d2, t), FFT[:, cs:cs + 512], ALU.add)))(dc),
                             r=[f"fcm{dc % 2}", xb(dc, tc)], w=[xb(dc, tc)])
                    if tc == 0 and e_i + 1 < E8:
                        pass
                if e_i + 1 < E8:
                    stage_expert_matrix(e_i + 1, 0, 0, "fw0")
                    stage_expert_matrix(e_i + 1, 1, 1, "fw1")
                    stage_expert_matrix(e_i + 1, 2, 2, "fw2")
            p.barrier()

        if emit_upto >= 2:
            emit_ffn(1)

        def emit_attnb():
            blnrow = nc.alloc_sbuf_tensor_at("BLNROW", [1, 1024], F32, offset=O_SCR)
            bsq = nc.alloc_sbuf_tensor_at("BSQ", [128, 512], F32, offset=O_SCR + 4096)
            for tc in range(NT):
                emit_ln(tc, (lambda d: XNs(d, tc)), blnrow, bsq, "blnrow", "bsq")
                for dc in range(NCK):
                    p.buf[f"bxn{tc}_{dc}"] = p.buf.pop("xn" + str(dc))
            IPJW = [nc.alloc_sbuf_tensor_at(f"IPJW{i}", [128, 1024], BF16, offset=O_WST + 4096 + 2048 * i)
                    for i in range(2)]
            def qkv_dst(oc, tc):
                if oc < 8:
                    return QBUF[:, oc * 2048 + tc * TCH: oc * 2048 + (tc + 1) * TCH], f"q{oc}"
                if oc < 16:
                    cc = oc - 8
                    return KBUF[:, cc * 2048 + tc * TCH: cc * 2048 + (tc + 1) * TCH], f"k{cc}"
                cc = oc - 16
                if cc < 4:
                    return V1B[:, cc * 2048 + tc * TCH: cc * 2048 + (tc + 1) * TCH], f"v{cc}"
                return V2B[:, (cc - 4) * 2048 + tc * TCH: (cc - 4) * 2048 + (tc + 1) * TCH], f"v{cc}"

            for oc in range(24):
                ipw = IPJW[oc % 2]
                ipwn = f"ipjw{oc % 2}"
                src_ap = bass.AP(dr["bIpjT"], oc * 128, [[3072, 128], [128 * 3072, 8], [1, 128]])
                dma("sync", WST[0][:, :].rearrange("p (a b) -> p a b", b=128), src_ap, w=["bwst0"])
                p.op("scalar", (lambda ip=ipw: (lambda e: e.copy(ip[:, :], WST[0][:, :])))(),
                     r=["bwst0", "cc"], w=[ipwn])
                for kc in range(NCK):
                    for tc in range(NT):
                        p.op("tensor",
                             (lambda k2, t, ip=ipw, b=PS[4 + tc]: (lambda e: e.matmul(
                                 b[:, :], ip[:, k2 * 128:(k2 + 1) * 128], XNs(k2, t),
                                 start=(k2 == 0), stop=(k2 == NCK - 1))))(kc, tc),
                             r=[ipwn, f"bxn{tc}_{kc}"], w=[f"ps{4 + tc}"])
                for tc in range(NT):
                    dst, dstn = qkv_dst(oc, tc)
                    p.op("vector",
                         (lambda o2=oc, d2=dst, t=tc, b=PS[4 + tc]: (lambda e: e.tensor_scalar(
                             d2, b[:, :], SMB[:, 370 + o2:371 + o2], SMB[:, 346 + o2:347 + o2],
                             ALU.mult, ALU.add)))(),
                         r=[f"ps{4 + tc}", "smb"], w=[dstn])
            p.barrier()
            dma("sync", BF16C[:, :], dr["c_F16"][:, :], w=["bf16c"])

            def vslice(cc, tc):
                if cc < 4:
                    return V1B[:, cc * 2048 + tc * TCH: cc * 2048 + (tc + 1) * TCH]
                return V2B[:, (cc - 4) * 2048 + tc * TCH: (cc - 4) * 2048 + (tc + 1) * TCH]

            for s in range(4):
                if s == 0:
                    for cc in range(NCK):
                        p.op("vector",
                             (lambda c2: (lambda e: e.tensor_scalar_mul(
                                 ATTO[:, c2 * 2048:c2 * 2048 + TCH], vslice(c2, 0),
                                 SMB[:, 74 + c2:75 + c2])))(cc),
                             r=[f"v{cc}", "smb"], w=[f"atto{cc}_0"])
                    continue
                for sp in range(s + 1):
                    for cc in range(NCK):
                        p.op("vector",
                             (lambda c2, s2=s, s3=sp: (lambda e: e.tensor_tensor(
                                 BZB1[:, :], QBUF[:, c2 * 2048 + s2 * TCH:c2 * 2048 + (s2 + 1) * TCH],
                                 KBUF[:, c2 * 2048 + s3 * TCH:c2 * 2048 + (s3 + 1) * TCH], ALU.mult)))(cc),
                             r=[f"q{cc}", f"k{cc}"], w=["bzb"])
                        p.op("tensor",
                             (lambda c2, b=PS[sp]: (lambda e: e.matmul(
                                 b[0:16, :], BE16F[:, c2 * 16:(c2 + 1) * 16], BZB1[:, :],
                                 start=(c2 == 0), stop=(c2 == NCK - 1))))(cc),
                             r=["be16f", "bzb"], w=[f"ps{sp}"])
                    p.op("scalar",
                         (lambda s3=sp, b=PS[sp]: (lambda e: e.copy(
                             BSCB[:, s3 * TCH:(s3 + 1) * TCH], b[0:16, :])))(),
                         r=[f"ps{sp}"], w=[f"scb{sp}"])
                p.op("vector", (lambda s2=s: (lambda e: e.tensor_tensor(
                    BMX[:, :], BSCB[:, 0:TCH], BSCB[:, TCH:2 * TCH], ALU.max)))(),
                    r=["scb0", "scb1"], w=["bmx"])
                for sp in range(2, s + 1):
                    p.op("vector", (lambda s3=sp: (lambda e: e.tensor_tensor(
                        BMX[:, :], BMX[:, :], BSCB[:, s3 * TCH:(s3 + 1) * TCH], ALU.max)))(),
                        r=["bmx", f"scb{sp}"], w=["bmx"])
                for sp in range(s + 1):
                    p.op("vector", (lambda s3=sp: (lambda e: e.tensor_tensor(
                        BSCB[:, s3 * TCH:(s3 + 1) * TCH], BSCB[:, s3 * TCH:(s3 + 1) * TCH],
                        BMX[:, :], ALU.subtract)))(),
                        r=[f"scb{sp}", "bmx"], w=[f"scb{sp}"])
                    p.op("scalar", (lambda s3=sp: (lambda e: e.activation(
                        BSCB[:, s3 * TCH:(s3 + 1) * TCH], BSCB[:, s3 * TCH:(s3 + 1) * TCH], AF.Exp)))(),
                        r=[f"scb{sp}"], w=[f"scb{sp}"])
                p.op("vector", lambda e: e.tensor_tensor(BSUM[:, :], BSCB[:, 0:TCH],
                                                         BSCB[:, TCH:2 * TCH], ALU.add),
                     r=["scb0", "scb1"], w=["bsum"])
                for sp in range(2, s + 1):
                    p.op("vector", (lambda s3=sp: (lambda e: e.tensor_tensor(
                        BSUM[:, :], BSUM[:, :], BSCB[:, s3 * TCH:(s3 + 1) * TCH], ALU.add)))(),
                        r=["bsum", f"scb{sp}"], w=["bsum"])
                p.op("vector", lambda e: e.reciprocal(BMX[:, :], BSUM[:, :]), r=["bsum"], w=["brc"])
                for sp in range(s + 1):
                    p.op("vector", (lambda s3=sp: (lambda e: e.tensor_tensor(
                        BSCB[:, s3 * TCH:(s3 + 1) * TCH], BSCB[:, s3 * TCH:(s3 + 1) * TCH],
                        BMX[:, :], ALU.mult)))(),
                        r=[f"scb{sp}", "brc"], w=[f"scb{sp}"])
                    p.op("scalar", (lambda s3=sp: (lambda e: e.copy(
                        BWJ4[0:16, s3 * TCH:(s3 + 1) * TCH], BSCB[:, s3 * TCH:(s3 + 1) * TCH])))(),
                        r=[f"scb{sp}"], w=[f"bwj{sp}"])
                for cc in range(NCK):
                    for sp in range(s + 1):
                        p.op("tensor",
                             (lambda c2, s3=sp, b=PS[4 + sp % 2]: (lambda e: e.matmul(
                                 b[:, :], BF16C[:, c2 * 128:(c2 + 1) * 128],
                                 BWJ4[0:16, s3 * TCH:(s3 + 1) * TCH], start=True, stop=True)))(cc),
                             r=["bf16c", f"bwj{sp}"], w=[f"ps{4 + sp % 2}"])
                        if sp == 0:
                            p.op("vector",
                                 (lambda c2, s3=sp, b=PS[4 + sp % 2]: (lambda e: e.tensor_tensor(
                                     BOAC[:, :], b[:, :], vslice(c2, s3), ALU.mult)))(cc),
                                 r=[f"ps{4 + sp % 2}", f"v{cc}"], w=["boac"])
                        else:
                            p.op("vector",
                                 (lambda c2, s3=sp, b=PS[4 + sp % 2]: (lambda e: e.tensor_tensor(
                                     WST[0][:, 512:1024], b[:, :], vslice(c2, s3), ALU.mult)))(cc),
                                 r=[f"ps{4 + sp % 2}", f"v{cc}"], w=["bwst0h"])
                            p.op("vector", lambda e: e.tensor_tensor(BOAC[:, :], BOAC[:, :],
                                                                     WST[0][:, 512:1024], ALU.add),
                                 r=["boac", "bwst0h"], w=["boac"])
                    p.op("vector",
                         (lambda c2, s2=s: (lambda e: e.tensor_scalar_mul(
                             ATTO[:, c2 * 2048 + s2 * TCH:c2 * 2048 + (s2 + 1) * TCH], BOAC[:, :],
                             SMB[:, 74 + c2:75 + c2])))(cc),
                         r=["boac", "smb"], w=[f"atto{cc}_{s}"])
            p.barrier()
            for oc in range(NCK):
                ipw = IPJW[oc % 2]
                ipwn = f"opjw{oc % 2}"
                src_ap = bass.AP(dr["bOpjT"], oc * 128, [[1024, 128], [128 * 1024, 8], [1, 128]])
                dma("sync", WST[0][:, :].rearrange("p (a b) -> p a b", b=128), src_ap, w=["bwst0"])
                p.op("scalar", (lambda ip=ipw: (lambda e: e.copy(ip[:, :], WST[0][:, :])))(),
                     r=["bwst0", "cc"], w=[ipwn])
                for tc in range(NT):
                    for kc in range(NCK):
                        p.op("tensor",
                             (lambda k2, t, ip=ipw, b=PS[6 + oc % 2]: (lambda e: e.matmul(
                                 b[:, :], ip[:, k2 * 128:(k2 + 1) * 128],
                                 ATTO[:, k2 * 2048 + t * TCH:k2 * 2048 + (t + 1) * TCH],
                                 start=(k2 == 0), stop=(k2 == NCK - 1))))(kc, tc),
                             r=[ipwn, f"atto{kc}_{tc}"], w=[f"ps{6 + oc % 2}"])
                    p.op("vector",
                         (lambda o2=oc, b=PS[6 + oc % 2]: (lambda e: e.tensor_scalar(
                             BOAC[:, :], b[:, :], SMB[:, 82 + o2:83 + o2], SMB[:, 402 + o2:403 + o2],
                             ALU.mult, ALU.add)))(),
                         r=[f"ps{6 + oc % 2}", "smb"], w=["boac"])
                    p.op("vector",
                         (lambda o2=oc, t=tc: (lambda e: e.tensor_tensor(Xs(o2, t), Xs(o2, t),
                                                                         BOAC[:, :], ALU.add)))(),
                         r=["boac", xb(oc, tc)], w=[xb(oc, tc)])
            p.barrier()

        if emit_upto >= 3:
            emit_attnb()

        if emit_upto >= 4:
            emit_ffn(2)

        for dc in range(NCK):
            dma("gpsimd", out_fm[dc * 128:(dc + 1) * 128, :], X[:, dc * 2048:(dc + 1) * 2048],
                w=[f"storedma{dc}"], r=[xb(dc, t) for t in range(NT)], queue_sem="qP")
        p.stream["gpsimd"].append(((("qP", p.count["qP"]),), None, None, 0))

        def make_body(name):
            def body(eng):
                for (waits, fn, incsem, incn) in p.stream[name]:
                    for (s, v) in waits:
                        if v > 0:
                            eng.wait_ge(sems[s], v)
                    inst = fn(eng) if fn is not None else None
                    if incsem is not None:
                        if inst is None:
                            inst = eng.nop()
                        inst.then_inc(sems[incsem], incn)
            return body

        block.sync(make_body("sync"))
        block.tensor(make_body("tensor"))
        block.vector(make_body("vector"))
        block.scalar(make_body("scalar"))
        block.gpsimd(make_body("gpsimd"))

    return nc


def _prep_inputs(inputs):
    f = lambda a: np.ascontiguousarray(a, dtype=np.float32)
    g = {}
    g["aWl1"] = f(inputs["L0_attn_a_Wl1"]); g["aWl2"] = f(inputs["L0_attn_a_Wl2"])
    g["aWf1"] = f(inputs["L0_attn_a_Wf1"]); g["aWf2"] = f(inputs["L0_attn_a_Wf2"])
    g["aW1"] = f(inputs["L0_attn_a_W1"]); g["aW2"] = f(inputs["L0_attn_a_W2"])
    g["aW3"] = f(inputs["L0_attn_a_W3"])
    aB = np.zeros((128, 26), np.float32)
    aB[:, 0] = inputs["L0_attn_a_bl1"]
    aB[:, 1:9] = np.asarray(inputs["L0_attn_a_bl2"]).reshape(8, 128).T
    aB[:, 9] = inputs["L0_attn_a_bf1"]
    aB[:, 10:18] = np.asarray(inputs["L0_attn_a_bf2"]).reshape(8, 128).T
    aB[:, 18:26] = np.asarray(inputs["L0_attn_a_b1"]).reshape(8, 128).T
    g["aBpp"] = aB
    g["aB23"] = f(np.stack([inputs["L0_attn_a_b2"], inputs["L0_attn_a_b3"]], axis=1))
    g["aA"] = f(np.asarray(inputs["L0_attn_a_A"]).reshape(8, 128, 16).transpose(1, 0, 2).reshape(128, 128))
    g["aST"] = f(np.asarray(inputs["L0_attn_a_state_init"]).reshape(8, 128, 16)
                 .transpose(1, 0, 2).reshape(128, 128))
    g["bIpjT"] = f(np.asarray(inputs["L0_attn_b_in_proj_weight"]).T)
    g["bOpjT"] = f(np.asarray(inputs["L0_attn_b_out_proj_weight"]).T)
    bG = np.zeros((128, 64), np.float32)
    bG[:, 0:24] = np.asarray(inputs["L0_attn_b_gamma"]).reshape(24, 128).T
    bG[:, 24:48] = np.asarray(inputs["L0_attn_b_in_proj_bias"]).reshape(24, 128).T
    bG[:, 48:56] = np.asarray(inputs["L0_attn_b_linear_alpha"]).reshape(8, 128).T
    bG[:, 56:64] = np.asarray(inputs["L0_attn_b_linear_gamma"]).reshape(8, 128).T
    g["bGpp"] = bG
    g["bOpb"] = f(np.asarray(inputs["L0_attn_b_out_proj_bias"]).reshape(8, 128).T)
    g["fGwT"] = f(np.asarray(inputs["L0_ffn_gate_w"]).T)
    g["fW1T"] = f(np.asarray(inputs["L0_ffn_w1"]).transpose(0, 2, 1))
    g["fW2T"] = f(np.asarray(inputs["L0_ffn_w2"]).transpose(0, 2, 1))
    g["fW3T"] = f(np.asarray(inputs["L0_ffn_w3"]).transpose(0, 2, 1))
    g["fSw1T"] = f(np.asarray(inputs["L0_ffn_sw1"]).T)
    g["fSw2T"] = f(np.asarray(inputs["L0_ffn_sw2"]).T)
    g["fSw3T"] = f(np.asarray(inputs["L0_ffn_sw3"]).T)
    fB = np.zeros((128, 192), np.float32)
    fB[:, 0:64] = np.asarray(inputs["L0_ffn_b1"]).reshape(8, 8, 128).transpose(2, 0, 1).reshape(128, 64)
    fB[:, 64:128] = np.asarray(inputs["L0_ffn_b2"]).reshape(8, 8, 128).transpose(2, 0, 1).reshape(128, 64)
    fB[:, 128:192] = np.asarray(inputs["L0_ffn_b3"]).reshape(8, 8, 128).transpose(2, 0, 1).reshape(128, 64)
    g["fBpp"] = fB
    g["c_ident"] = np.eye(128, dtype=np.float32)
    cc = np.zeros((128, 8), np.float32)
    cc[:, 0] = EPS; cc[:, 1] = 1.0; cc[:, 2] = -1.0; cc[:, 3] = -0.5; cc[:, 4] = 0.125
    g["c_const"] = cc
    sel16 = np.zeros((16, 2048), np.float32)
    for n in range(16):
        sel16[n, n * 128:(n + 1) * 128] = 1.0
    g["c_sel16"] = sel16.astype(ml_dtypes.bfloat16)
    sel8 = np.zeros((8, 1024), np.float32)
    for e in range(8):
        sel8[e, e * 128:(e + 1) * 128] = 1.0
    g["c_sel8"] = sel8
    e16 = np.zeros((128, 128), np.float32)
    for cc in range(8):
        for k in range(128):
            e16[k, cc * 16 + 2 * cc + (k // 64)] = 1.0
    g["c_E16"] = e16.astype(ml_dtypes.bfloat16)
    g["c_E16f"] = e16.astype(np.float32)
    f16 = np.zeros((16, 1024), np.float32)
    for cc in range(8):
        for m in range(128):
            f16[2 * cc + (m // 64), cc * 128 + m] = 1.0
    g["c_F16"] = f16.astype(ml_dtypes.bfloat16)
    return g


def kernel(**inputs):
    x = np.asarray(inputs["L0_attn_a_x"])
    B, L, _ = x.shape
    shared = _prep_inputs(inputs)
    nc = build_program(_UPTO, _SUB)
    in_maps = []
    for c in range(8):
        xc = np.ascontiguousarray(x[:, c * 512:(c + 1) * 512, :].reshape(B * 512, D).T)
        m = dict(shared)
        m["x_fm"] = xc
        in_maps.append(m)
    res = run_bass_kernel_spmd(nc, in_maps, core_ids=list(range(8)))
    out = np.empty((B, L, D), np.float32)
    for c in range(8):
        out[:, c * 512:(c + 1) * 512, :] = res.results[c]["out_fm"].T.reshape(B, 512, D)
    return out


# version 06
# speedup vs baseline: 1.9788x; 1.0785x over previous
import numpy as np
import ml_dtypes
import concourse.bass as bass
import concourse.mybir as mybir
from concourse.bass_utils import run_bass_kernel_spmd
from contextlib import ExitStack

F32 = mybir.dt.float32
BF16 = mybir.dt.bfloat16
AF = mybir.ActivationFunctionType
ALU = mybir.AluOpType
AX = mybir.AxisListType

D = 1024
NCK = 8
TPC = 2048
TCH = 512
NT = 4
N16 = 16
E8 = 8
EPS = 1e-5

_UPTO = 99
_SUB = 99
_DBG_PAIRS = 2
_DBG_MC = 8
_DBG_DOPS = 9
_DBG_W1OFF = True
_DBG_NODEL = False

O_X = 1536
O_XNR = O_X + 65536
O_WBUF = O_XNR + 32768
O_BIGA = O_WBUF + 49152
O_BIGB = O_BIGA + 32768
O_SCR = O_BIGB + 16384
O_WST = O_SCR + 16384
O_SMALL = O_WST + 8192

ENG2SEM = {"tensor": "sT", "vector": "sV", "scalar": "sS", "gpsimd": "sG", "sync": None}
ALLSEMS = ("qW", "qW1", "qW2", "qP", "sT", "sV", "sS", "sG")
QROT = {"qW": ("qW", "qW1", "qW2"), "qP": ("qP",)}


class Prog:
    ENGINES = ("sync", "tensor", "vector", "scalar", "gpsimd")

    def __init__(self):
        self.stream = {e: [] for e in self.ENGINES}
        self.count = {s: 0 for s in ALLSEMS}
        self.buf = {}
        self.hiwater = {e: {} for e in self.ENGINES}
        self.qidx = {q: 0 for q in QROT}

    def _b(self, name):
        return self.buf.setdefault(name, {"w": None, "r": []})

    def op(self, eng, fn, r=(), w=(), dma=None, extra=()):
        deps = []
        myeng0 = None if dma else eng
        for name in r:
            b = self._b(name)
            if b["w"]:
                deps.append(b["w"])
            if name.startswith("ps"):
                deps.extend(m for m in b["r"] if m[2] != myeng0)
        for name in w:
            b = self._b(name)
            if b["w"]:
                deps.append(b["w"])
            deps.extend(b["r"])
        waits = {}
        for (sem, val, weng) in deps:
            if weng == "tensor" and myeng0 == "tensor":
                continue
            if val > waits.get(sem, 0):
                waits[sem] = val
        if dma:
            rot = QROT[dma]
            dma = rot[self.qidx[rot[0]] % len(rot)]
            self.qidx[rot[0]] += 1
            prev = self.count[dma]
            if prev > waits.get(dma, 0):
                waits[dma] = prev
        for (sem, val) in extra:
            if val > waits.get(sem, 0):
                waits[sem] = val
        hw = self.hiwater[eng]
        waits = {s: v for s, v in waits.items() if v > hw.get(s, 0)}
        for s, v in waits.items():
            hw[s] = v
        if dma:
            sem = dma
            self.count[sem] += 16
            mi = (sem, self.count[sem], None)
        else:
            sem = ENG2SEM[eng]
            if sem is None:
                sem = "sG"
            self.count[sem] += 1
            mi = (sem, self.count[sem], eng)
        self.stream[eng].append((tuple(sorted(waits.items())), fn, mi[0], 16 if dma else 1))
        for name in r:
            self._b(name)["r"].append(mi)
        for name in w:
            b = self._b(name)
            b["w"] = mi
            b["r"] = []
        return mi

    def barrier(self):
        snap = [(s, self.count[s]) for s in ALLSEMS if self.count[s] > 0]
        for e in self.ENGINES:
            self.stream[e].append((tuple(snap), None, None, 0))
            for s, v in snap:
                if v > self.hiwater[e].get(s, 0):
                    self.hiwater[e][s] = v
        self.buf = {}


def build_program(emit_upto=99, sub=None):
    if sub is None:
        sub = _SUB
    _DBG_DOPS = globals()["_DBG_DOPS"]
    _DBG_W1OFF = globals()["_DBG_W1OFF"]
    _DBG_NODEL = globals()["_DBG_NODEL"]
    nc = bass.Bass("TRN2", target_bir_lowering=False, dynamic_dma_scratch_size=512)
    dr = {}

    def din(name, shape, dtype=F32):
        dr[name] = nc.dram_tensor(name, list(shape), dtype, kind="ExternalInput")
        return dr[name]

    din("x_fm", (D, TPC))
    din("aWl1", (D, 128)); din("aWl2", (128, D)); din("aWf1", (2 * D, 128))
    din("aWf2", (128, D)); din("aW1", (D, D)); din("aW2", (D, N16)); din("aW3", (D, N16))
    din("aBpp", (128, 26)); din("aB23", (16, 2)); din("aA", (128, 128)); din("aST", (128, 128))
    din("bIpjT", (D, 3 * D)); din("bOpjT", (D, D)); din("bGpp", (128, 64)); din("bOpb", (128, 8))
    din("fGwT", (D, E8))
    din("fW1T", (E8, D, D)); din("fW2T", (E8, D, D)); din("fW3T", (E8, D, D))
    din("fSw1T", (D, 2 * D)); din("fSw2T", (D, 2 * D)); din("fSw3T", (2 * D, D))
    din("fBpp", (128, 192))
    din("c_ident", (128, 128)); din("c_E16", (128, 128), BF16); din("c_F16", (16, 1024), BF16)
    din("c_const", (128, 8))
    din("c_E16f", (128, 128))
    din("c_sel16", (16, 2048), BF16)
    din("c_sel8", (8, 1024))
    out_fm = nc.dram_tensor("out_fm", [D, TPC], F32, kind="ExternalOutput")

    p = Prog()

    with ExitStack() as ctx:
        at = nc.alloc_sbuf_tensor_at
        X = at("X", [128, 16384], F32, offset=O_X)
        XN = at("XN", [128, 16384], BF16, offset=O_XNR)
        XT32 = at("XT32", [128, 8192], F32, offset=O_XNR)
        ATTO = at("ATTO", [128, 16384], BF16, offset=O_XNR)
        AW = at("AW", [128, 16384], BF16, offset=O_WBUF)
        FW = [at(f"FW{i}", [128, 8192], BF16, offset=O_WBUF + 16384 * i) for i in range(3)]
        KBUF = at("KBUF", [128, 16384], BF16, offset=O_WBUF)
        V1B = at("V1B", [128, 8192], BF16, offset=O_WBUF + 32768)
        A_XN = at("A_XN", [128, 8192], BF16, offset=O_BIGA)
        A_LIF = at("A_LIF", [128, 8192], BF16, offset=O_BIGA + 16384)
        A_DEL = at("A_DEL", [128, 8192], F32, offset=O_BIGA)
        QBUF = at("QBUF", [128, 16384], BF16, offset=O_BIGA)
        FFT = at("FFT", [128, 4096], F32, offset=O_BIGA)
        FFT2 = at("FFT2", [128, 4096], F32, offset=O_BIGA + 16384)
        GTM = at("GTM", [128, 768], F32, offset=O_BIGA + 16384)
        GSML = at("GSML", [128, 64], F32, offset=O_BIGA + 16384 + 3072)
        A_XTB = at("A_XTB", [128, 8192], BF16, offset=O_BIGB)
        HB = at("HB", [128, 4096], BF16, offset=O_BIGB)
        PRB = at("PRB", [128, 8192], BF16, offset=O_BIGB)
        V2B = at("V2B", [128, 8192], BF16, offset=O_BIGB)
        SEL16B = at("SEL16B", [16, 2048], BF16, offset=O_WBUF + 28672)
        FSEL8F = at("FSEL8F", [8, 1024], F32, offset=O_BIGB + 8192)
        FSEL16 = at("FSEL16", [8, 1024], BF16, offset=O_BIGB + 12288)
        BWJ4 = at("BWJ4", [16, 2048], BF16, offset=O_WST + 4096)
        BF16C = at("BF16C", [16, 1024], BF16, offset=O_WST)
        BZB1 = at("BZB1", [128, 512], F32, offset=O_SCR + 12288)
        A_SCR = at("A_SCR", [128, 1536], F32, offset=O_SCR)
        A_WMX = at("A_WMX", [128, 512], F32, offset=O_SCR + 6144)
        A_CM = at("A_CM", [16, 1024], F32, offset=O_SCR + 8192)
        A_CMBF = at("A_CMBF", [16, 1024], BF16, offset=O_SCR + 12288)
        A_CMROW = at("A_CMROW", [1, 512], F32, offset=O_SCR + 14336)
        SFM = at("SFM", [8, 2048], F32, offset=O_SCR)
        MFMB = at("MFMB", [8, 2048], BF16, offset=O_SCR + 8192)
        BSCB = at("BSCB", [16, 2048], F32, offset=O_SCR)
        BMX = at("BMX", [16, 512], F32, offset=O_SCR + 8192)
        BSUM = at("BSUM", [16, 512], F32, offset=O_SCR + 10240)
        BRC = at("BRC", [16, 512], F32, offset=O_SCR + 12288)
        BOAC = at("BOAC", [128, 512], F32, offset=O_SCR + 14336)
        WST = [at(f"WST{i}", [128, 1024], F32, offset=O_WST + 4096 * i) for i in range(2)]
        A_LNROW = at("A_LNROW", [1, 1024], F32, offset=O_WST)
        A_BM = at("A_BM", [16, 512], F32, offset=O_WST + 4096)
        HLB = at("HLB", [128, 1024], BF16, offset=O_WST + 6144)
        BZB = at("BZB", [128, 1024], BF16, offset=O_WST)
        BWJ = at("BWJ", [16, 512], BF16, offset=O_WST + 2048)
        SMB = at("SMB", [128, 424], F32, offset=O_SMALL)
        SM_A = at("SM_A", [128, 128], F32, offset=O_SMALL + 1696)
        SM_AM = at("SM_AM", [128, 128], F32, offset=O_SMALL + 2208)
        SM_NAP = at("SM_NAP", [128, 128], F32, offset=O_SMALL + 2720)
        SM_ST = at("SM_ST", [128, 128], F32, offset=O_SMALL + 3232)
        SM_B23 = at("SM_B23", [16, 2], F32, offset=O_SMALL + 3744)
        IDENT = at("IDENT", [128, 128], F32, offset=O_SMALL + 3776)
        ONES = at("ONES", [1, 128], F32, offset=O_SMALL + 4288)
        BE16F = at("BE16F", [128, 128], F32, offset=O_SMALL + 4800)
        ONESC = at("ONESC", [128, 1], F32, offset=O_SMALL + 6336)
        CC = at("CC", [128, 8], F32, offset=O_SMALL + 6368)
        PS = [ctx.enter_context(nc.psum_tensor(f"ps{i}", [128, 512], F32)) for i in range(8)]
        sems = {s: ctx.enter_context(nc.semaphore(s)) for s in ALLSEMS}
        block = ctx.enter_context(nc.Block())

        def Xs(dc, tc, lo=0, ln=TCH):
            return X[:, dc * 2048 + tc * TCH + lo: dc * 2048 + tc * TCH + lo + ln]

        def xb(dc, tc):
            return f"x{dc}_{tc}"

        def XNs(dc, tc, n=TCH):
            return XN[:, dc * 2048 + tc * TCH: dc * 2048 + tc * TCH + n]

        def dma(q, out, in_, w, r=(), queue_sem="qW"):
            return p.op(q, (lambda o, i: (lambda e: e.dma_start(out=o, in_=i)))(out, in_),
                        r=r, w=w, dma=queue_sem)

        dma("sync", CC[:, :], dr["c_const"][:, :], w=["cc"])
        dma("sync", SEL16B[:, :], dr["c_sel16"][:, :], w=["sel16"])
        dma("sync", BE16F[:, :], dr["c_E16f"][:, :], w=["be16f"])
        for dc in range(NCK):
            dma("sync", X[:, dc * 2048:(dc + 1) * 2048], dr["x_fm"][dc * 128:(dc + 1) * 128, :],
                w=[xb(dc, t) for t in range(NT)])
        dma("sync", SMB[:, 0:26], dr["aBpp"][:, :], w=["smb"])
        dma("sync", SMB[:, 26:90], dr["bGpp"][:, :], w=["smb"])
        dma("sync", SMB[:, 90:282], dr["fBpp"][:, :], w=["smb"])
        dma("sync", SMB[:, 394:402], dr["bOpb"][:, :], w=["smb"])
        for k in range(NCK):
            dma("sync", SMB[:, 282 + k * 8:282 + (k + 1) * 8], dr["fGwT"][k * 128:(k + 1) * 128, :],
                w=["smb"])
        dma("sync", SM_A[:, :], dr["aA"][:, :], w=["sma"])
        dma("sync", SM_ST[:, :], dr["aST"][:, :], w=["smst"])
        dma("sync", SM_B23[:, :], dr["aB23"][:, :], w=["smb23"])
        dma("sync", IDENT[:, :], dr["c_ident"][:, :], w=["ident"])
        p.op("vector", lambda e: e.tensor_scalar_min(SM_AM[:, :], SM_A[:, :], 0.0),
             r=["sma"], w=["smam"])
        p.op("vector", lambda e: e.tensor_scalar(SM_NAP[:, :], SM_A[:, :], 0.0, -1.0, ALU.max, ALU.mult),
             r=["sma"], w=["smnap"])
        p.op("vector", lambda e: e.tensor_tensor(SMB[:, 346:370], SMB[:, 26:50], SMB[:, 50:74], ALU.mult),
             r=["smb"], w=["smb"])
        p.op("vector", lambda e: e.tensor_copy(SMB[:, 370:394], SMB[:, 26:50]), r=["smb"], w=["smb"])
        p.op("vector", lambda e: e.tensor_scalar_mul(SMB[:, 370:378], SMB[:, 370:378], 0.125),
             r=["smb"], w=["smb"])
        p.op("vector", lambda e: e.tensor_scalar_mul(SMB[:, 346:354], SMB[:, 346:354], 0.125),
             r=["smb"], w=["smb"])
        p.op("vector", lambda e: e.tensor_tensor(SMB[:, 402:410], SMB[:, 82:90], SMB[:, 394:402], ALU.mult),
             r=["smb"], w=["smb"])
        p.op("vector", lambda e: e.memset(ONES[:, :], 1.0), w=["ones"])
        p.op("vector", lambda e: e.memset(ONESC[:, :], 1.0), w=["onesc"])

        AWO = {"wl1": 0, "wl2": 1024, "wf1": 2048, "wf2": 4096, "w1": 5120, "w2": 13312, "w3": 13440}
        aw_specs = []
        aw_specs += [(dr["aWl1"], k * 128, 128, AWO["wl1"] + k * 128) for k in range(8)]
        aw_specs += [(dr["aWl2"], 0, 1024, AWO["wl2"])]
        aw_specs += [(dr["aWf1"], k * 128, 128, AWO["wf1"] + k * 128) for k in range(16)]
        aw_specs += [(dr["aWf2"], 0, 1024, AWO["wf2"])]
        aw_specs += [(dr["aW1"], k * 128, 1024, AWO["w1"] + k * 1024) for k in range(8)]
        aw_specs += [(dr["aW2"], k * 128, 16, AWO["w2"] + k * 16) for k in range(8)]
        aw_specs += [(dr["aW3"], k * 128, 16, AWO["w3"] + k * 16) for k in range(8)]
        for i, (dt_, r0, ncols, ao) in enumerate(aw_specs):
            st = WST[i % 2]
            stn = f"wst{i % 2}"
            dma("sync", st[:, 0:ncols], dt_[r0:r0 + 128, :], w=[stn])
            p.op("scalar",
                 (lambda s_, n_, a_: (lambda e: e.copy(AW[:, a_:a_ + n_], s_[:, 0:n_])))(st, ncols, ao),
                 r=[stn, "cc"], w=["aw"])

        p.barrier()

        def emit_ln(tc, dst_fn, rowbuf, sqbuf, rown, sqn, xn32=None):
            dsts = [dst_fn(d) for d in range(NCK)]
            b_s1, b_s2, b_mb, b_rb = PS[4], PS[5], PS[6], PS[7]
            for dc in range(NCK):
                p.op("tensor",
                     (lambda d: (lambda e: e.matmul(b_s1[0:1, :], ONESC[:, :], Xs(d, tc),
                                                    start=(d == 0), stop=(d == NCK - 1))))(dc),
                     r=["onesc", xb(dc, tc)], w=(["ps4"] if dc == 0 else ["ps4"]))
            for dc in range(NCK):
                p.op("scalar",
                     (lambda d: (lambda e: e.activation(sqbuf[:, :], Xs(d, tc), AF.Square)))(dc),
                     r=[xb(dc, tc)], w=[sqn])
                p.op("tensor",
                     (lambda d: (lambda e: e.matmul(b_s2[0:1, :], ONESC[:, :], sqbuf[:, :],
                                                    start=(d == 0), stop=(d == NCK - 1))))(dc),
                     r=["onesc", sqn], w=["ps5"])
            mrow = rowbuf[0:1, 0:512]
            rrow = rowbuf[0:1, 512:1024]
            p.op("vector", lambda e, m=mrow: e.tensor_scalar_mul(m, b_s1[0:1, :], 1.0 / D),
                 r=["ps4"], w=[rown])
            p.op("vector", lambda e, rr=rrow: e.tensor_scalar_mul(rr, b_s2[0:1, :], 1.0 / D),
                 r=["ps5"], w=[rown])
            p.op("vector", lambda e, m=mrow: e.tensor_tensor(sqbuf[0:1, 0:512], m, m, ALU.mult),
                 r=[rown], w=[sqn])
            p.op("vector", lambda e, rr=rrow: e.tensor_tensor(rr, rr, sqbuf[0:1, 0:512], ALU.subtract),
                 r=[rown, sqn], w=[rown])
            p.op("scalar", lambda e, rr=rrow: e.activation(rr, rr, AF.Ln, bias=CC[0:1, 0:1]),
                 r=[rown, "cc"], w=[rown])
            p.op("scalar", lambda e, rr=rrow: e.activation(rr, rr, AF.Exp, scale=CC[0:1, 3:4]),
                 r=[rown, "cc"], w=[rown])
            p.op("tensor", lambda e, m=mrow: e.matmul(b_mb[:, :], ONES[0:1, 0:128], m, start=True, stop=True),
                 r=["ones", rown], w=["ps6"])
            p.op("tensor", lambda e, rr=rrow: e.matmul(b_rb[:, :], ONES[0:1, 0:128], rr, start=True, stop=True),
                 r=["ones", rown], w=["ps7"])
            for dc in range(NCK):
                p.op("vector",
                     (lambda d: (lambda e: e.tensor_tensor(sqbuf[:, :], Xs(d, tc), b_mb[:, :], ALU.subtract)))(dc),
                     r=[xb(dc, tc), "ps6"], w=[sqn])
                if xn32 is not None:
                    p.op("vector",
                         (lambda d: (lambda e: e.tensor_tensor(xn32[:, d * TCH:(d + 1) * TCH], sqbuf[:, :],
                                                               b_rb[:, :], ALU.mult)))(dc),
                         r=[sqn, "ps7"], w=["xn32"])
                    p.op("scalar",
                         (lambda d, dst=dsts[dc]: (lambda e: e.copy(dst, xn32[:, d * TCH:(d + 1) * TCH])))(dc),
                         r=["xn32"], w=["xn" + str(dc)])
                else:
                    p.op("vector",
                         (lambda d, dst=dsts[dc]: (lambda e: e.tensor_tensor(dst, sqbuf[:, :], b_rb[:, :], ALU.mult)))(dc),
                         r=[sqn, "ps7"], w=["xn" + str(dc)])

        if emit_upto >= 1:
            for pr in range(2):
                tcs = (2 * pr, 2 * pr + 1)
                for j, tc in enumerate(tcs):
                    emit_ln(tc, (lambda jj: (lambda d: A_XN[:, jj * 4096 + d * TCH:jj * 4096 + (d + 1) * TCH]))(j),
                            A_LNROW, A_WMX, "lnrow", "wmx")
                    for dc in range(NCK):
                        p.buf[f"axn{j}_{dc}"] = p.buf.pop("xn" + str(dc))
                for j, tc in enumerate(tcs if sub >= 2 else ()):
                    hl = HLB[:, j * 512:(j + 1) * 512]
                    hln = f"hlb{j}"
                    b_h = PS[0]
                    for k in range(NCK):
                        p.op("tensor",
                             (lambda kk, jj: (lambda e: e.matmul(
                                 b_h[:, :], AW[:, AWO["wl1"] + kk * 128:AWO["wl1"] + (kk + 1) * 128],
                                 A_XN[:, jj * 4096 + kk * TCH:jj * 4096 + (kk + 1) * TCH],
                                 start=(kk == 0), stop=(kk == NCK - 1))))(k, j),
                             r=["aw", f"axn{j}_{k}"], w=["ps0"])
                    p.op("scalar", lambda e, h=hl: e.activation(h, b_h[:, :], AF.Relu, bias=SMB[:, 0:1]),
                         r=["ps0", "smb"], w=[hln])
                    for mc in range(NCK):
                        bi = mc % 2
                        bnk = PS[bi]
                        p.op("tensor",
                             (lambda m, h=hl, b=bnk: (lambda e: e.matmul(
                                 b[:, :], AW[:, AWO["wl2"] + m * 128:AWO["wl2"] + (m + 1) * 128], h,
                                 start=True, stop=True)))(mc),
                             r=["aw", hln], w=[f"ps{bi}"])
                        p.op("scalar",
                             (lambda m, jj, b=bnk: (lambda e: e.activation(
                                 A_LIF[:, jj * 4096 + m * TCH:jj * 4096 + (m + 1) * TCH], b[:, :],
                                 AF.Tanh, bias=SMB[:, 1 + m:2 + m])))(mc, j),
                             r=[f"ps{bi}", "smb"], w=[f"alif{j}_{mc}"])
                    b_h1 = PS[2]
                    for k in range(16):
                        src = A_XN if k < 8 else A_LIF
                        srcn = (f"axn{j}_{k}" if k < 8 else f"alif{j}_{k - 8}")
                        kk = k % 8
                        p.op("tensor",
                             (lambda sr, k2, kk2, jj: (lambda e: e.matmul(
                                 b_h1[:, :], AW[:, AWO["wf1"] + k2 * 128:AWO["wf1"] + (k2 + 1) * 128],
                                 sr[:, jj * 4096 + kk2 * TCH:jj * 4096 + (kk2 + 1) * TCH],
                                 start=(k2 == 0), stop=(k2 == 15))))(src, k, kk, j),
                             r=["aw", srcn], w=["ps2"])
                    p.op("scalar", lambda e, h=hl: e.activation(h, b_h1[:, :], AF.Relu, bias=SMB[:, 9:10]),
                         r=["ps2", "smb"], w=[hln])
                    for mc in range(NCK):
                        bi = mc % 2
                        bnk = PS[bi]
                        p.op("tensor",
                             (lambda m, h=hl, b=bnk: (lambda e: e.matmul(
                                 b[:, :], AW[:, AWO["wf2"] + m * 128:AWO["wf2"] + (m + 1) * 128], h,
                                 start=True, stop=True)))(mc),
                             r=["aw", hln], w=[f"ps{bi}"])
                        p.op("vector",
                             (lambda m, jj, b=bnk: (lambda e: e.tensor_scalar_add(
                                 XT32[:, jj * 4096 + m * TCH:jj * 4096 + (m + 1) * TCH], b[:, :],
                                 SMB[:, 10 + m:11 + m])))(mc, j),
                             r=[f"ps{bi}", "smb"], w=[f"xt32_{j}_{mc}"])
                        p.op("vector",
                             (lambda m, jj: (lambda e: e.tensor_copy(
                                 A_XTB[:, jj * 4096 + m * TCH:jj * 4096 + (m + 1) * TCH],
                                 XT32[:, jj * 4096 + m * TCH:jj * 4096 + (m + 1) * TCH])))(mc, j),
                             r=[f"xt32_{j}_{mc}"], w=[f"xtb{j}_{mc}"])
                for j, tc in enumerate(tcs if sub >= 3 else ()):
                    if pr > _DBG_PAIRS - 1:
                        continue
                    for mc in range(min(NCK, _DBG_MC)):
                        bi = mc % 2
                        bnk = PS[bi]
                        w1base = AWO["w1"] if _DBG_W1OFF else AWO["wl1"]
                        for k in range(NCK):
                            off = (w1base + kk2 * 1024 + mc * 128 if _DBG_W1OFF else AWO["wl1"]) if False else None
                            p.op("tensor",
                                 (lambda kk, m, jj, b=bnk: (lambda e: e.matmul(
                                     b[:, :],
                                     (AW[:, AWO["w1"] + kk * 1024 + m * 128:AWO["w1"] + kk * 1024 + (m + 1) * 128]
                                      if _DBG_W1OFF else
                                      AW[:, AWO["wl1"] + kk * 128:AWO["wl1"] + (kk + 1) * 128]),
                                     A_XTB[:, jj * 4096 + kk * TCH:jj * 4096 + (kk + 1) * TCH],
                                     start=(kk == 0), stop=(kk == NCK - 1))))(k, mc, j),
                                 r=["aw", f"xtb{j}_{k}"], w=[f"ps{bi}"])
                        bcol = SMB[:, 18 + mc:19 + mc]
                        if _DBG_DOPS >= 1:
                            p.op("vector",
                                 (lambda b=bnk, bc=bcol: (lambda e: e.tensor_scalar(
                                     A_WMX[:, :], b[:, :], bc, 0.0, ALU.add, ALU.max)))(),
                                 r=[f"ps{bi}", "smb"], w=["wmx"])
                        else:
                            p.op("vector", (lambda b=bnk: (lambda e: e.tensor_copy(A_WMX[:, :], b[:, :])))(),
                                 r=[f"ps{bi}"], w=["wmx"])
                        if _DBG_DOPS >= 2:
                            p.op("scalar",
                                 (lambda b=bnk, bc=bcol: (lambda e: e.activation(
                                     A_SCR[:, 0:512], b[:, :], AF.Abs, bias=bc)))(),
                                 r=[f"ps{bi}", "smb"], w=["scrP"])
                        else:
                            p.op("scalar", (lambda b=bnk: (lambda e: e.copy(A_SCR[:, 0:512], b[:, :])))(),
                                 r=[f"ps{bi}"], w=["scrP"])
                        if _DBG_DOPS >= 3:
                            p.op("scalar", lambda e: e.activation(A_SCR[:, 512:1024], A_SCR[:, 0:512],
                                                                  AF.Exp, scale=CC[:, 2:3]),
                                 r=["scrP", "cc"], w=["scrT"])
                        else:
                            p.op("scalar", lambda e: e.copy(A_SCR[:, 512:1024], A_SCR[:, 0:512]),
                                 r=["scrP"], w=["scrT"])
                        if _DBG_DOPS >= 4:
                            p.op("scalar", lambda e: e.activation(A_SCR[:, 1024:1536], A_SCR[:, 512:1024],
                                                                  AF.Ln, bias=CC[:, 1:2]),
                                 r=["scrT", "cc"], w=["scrF"])
                        else:
                            p.op("scalar", lambda e: e.copy(A_SCR[:, 1024:1536], A_SCR[:, 512:1024]),
                                 r=["scrT"], w=["scrF"])
                        if _DBG_NODEL:
                            p.op("vector",
                                 lambda e: e.tensor_tensor(A_WMX[:, :], A_WMX[:, :], A_SCR[:, 1024:1536],
                                                           ALU.add),
                                 r=["wmx", "scrF"], w=["wmx"])
                            p.buf[f"del{j}_{mc}"] = {"w": None, "r": []}
                        else:
                            p.op("vector",
                                 (lambda m, jj: (lambda e: e.tensor_tensor(
                                     A_DEL[:, jj * 4096 + m * TCH:jj * 4096 + (m + 1) * TCH],
                                     A_WMX[:, :], A_SCR[:, 1024:1536], ALU.add)))(mc, j),
                                 r=["wmx", "scrF"] + [f"axn{j}_{mc}", f"alif{j}_{mc}"],
                                 w=[f"del{j}_{mc}"])
                for j, tc in enumerate(tcs if sub >= 4 else ()):
                    b_bc = PS[2]
                    for wname, dst, dstn, bc in (("w2", A_BM[:, :], "abm", SM_B23[:, 0:1]),
                                                 ("w3", A_CM[:, j * 512:(j + 1) * 512], f"acm{j}", SM_B23[:, 1:2])):
                        for k in range(NCK):
                            p.op("tensor",
                                 (lambda kk, wn, jj: (lambda e: e.matmul(
                                     b_bc[0:16, :], AW[:, AWO[wn] + kk * 16:AWO[wn] + (kk + 1) * 16],
                                     A_XTB[:, jj * 4096 + kk * TCH:jj * 4096 + (kk + 1) * TCH],
                                     start=(kk == 0), stop=(kk == NCK - 1))))(k, wname, j),
                                 r=["aw", f"xtb{j}_{k}"], w=["ps2"])
                        p.op("vector",
                             (lambda d2=dst, b2=bc: (lambda e: e.tensor_scalar_add(d2, b_bc[0:16, :], b2)))(),
                             r=["ps2", "smb23"], w=[dstn])
                    p.op("vector",
                         (lambda jj: (lambda e: e.tensor_copy(A_CMBF[:, jj * 512:(jj + 1) * 512],
                                                              A_CM[:, jj * 512:(jj + 1) * 512])))(j),
                         r=[f"acm{j}"], w=[f"acmbf{j}"])
                    p.op("vector",
                         (lambda jj: (lambda e: e.tensor_tensor(A_BM[:, :], A_BM[:, :],
                                                                A_CM[:, jj * 512:(jj + 1) * 512], ALU.mult)))(j),
                         r=["abm", f"acm{j}"], w=["abm"])
                    p.op("tensor", lambda e: e.matmul(PS[3][0:1, :], ONESC[0:16, :], A_BM[:, :],
                                                      start=True, stop=True),
                         r=["onesc", "abm"], w=["ps3"])
                    p.op("scalar", lambda e: e.copy(A_CMROW[0:1, 0:512], PS[3][0:1, :]),
                         r=["ps3"], w=["cmrow0"])
                    p.op("tensor", lambda e: e.matmul(PS[3][:, :], ONES[0:1, 0:128], A_CMROW[0:1, 0:512],
                                                      start=True, stop=True),
                         r=["ones", "cmrow0"], w=["ps3"])
                    for dc in range(NCK):
                        p.op("vector",
                             (lambda d, jj: (lambda e: e.tensor_tensor(
                                 A_WMX[:, :], XT32[:, jj * 4096 + d * TCH:jj * 4096 + (d + 1) * TCH],
                                 A_DEL[:, jj * 4096 + d * TCH:jj * 4096 + (d + 1) * TCH], ALU.mult)))(dc, j),
                             r=[f"xt32_{j}_{dc}", f"del{j}_{dc}"], w=["wmx"])
                        p.op("vector", lambda e: e.tensor_tensor(A_WMX[:, :], A_WMX[:, :], PS[3][:, :], ALU.mult),
                             r=["wmx", "ps3"], w=["wmx"])
                        p.op("vector",
                             (lambda d, t: (lambda e: e.tensor_tensor(Xs(d, t), Xs(d, t), A_WMX[:, :],
                                                                      ALU.add)))(dc, tc),
                             r=["wmx", xb(dc, tc)], w=[xb(dc, tc)])
                for ni in range(N16 if sub >= 5 else 0):
                    for j, tc in enumerate(tcs):
                        ridx = (ni * 2 + j) % 2
                        bi = 2 + ridx
                        bnk = PS[bi]
                        p.op("tensor",
                             (lambda nn, jj, b=bnk: (lambda e: e.matmul(
                                 b[:, :], SEL16B[:, nn * 128:(nn + 1) * 128],
                                 A_CMBF[:, jj * 512:(jj + 1) * 512], start=True, stop=True)))(ni, j),
                             r=["sel16", f"acmbf{j}"], w=[f"ps{bi}"])
                        for dc in range(NCK):
                            ca = dc * 16 + ni
                            p.op("scalar",
                                 (lambda d, jj, c: (lambda e: e.activation(
                                     A_SCR[:, 0:512], A_DEL[:, jj * 4096 + d * TCH:jj * 4096 + (d + 1) * TCH],
                                     AF.Exp, scale=SM_AM[:, c:c + 1])))(dc, j, ca),
                                 r=[f"del{j}_{dc}", "smam"], w=["scrP"])
                            p.op("vector",
                                 (lambda d, jj, c: (lambda e: e.scalar_tensor_tensor(
                                     A_SCR[:, 512:1024], A_DEL[:, jj * 4096 + d * TCH:jj * 4096 + (d + 1) * TCH],
                                     SM_NAP[:, c:c + 1], A_SCR[:, 0:512], ALU.mult, ALU.subtract)))(dc, j, ca),
                                 r=[f"del{j}_{dc}", "smnap", "scrP"], w=["scrT"])
                            p.op("scalar", lambda e: e.activation(A_SCR[:, 1024:1536], A_SCR[:, 512:1024],
                                                                  AF.Exp, bias=CC[:, 1:2]),
                                 r=["scrT", "cc"], w=["scrF"])
                            p.op("vector",
                                 (lambda c, b=bnk: (lambda e: e.scalar_tensor_tensor(
                                     A_WMX[:, :], A_SCR[:, 1024:1536], SM_ST[:, c:c + 1], b[:, :],
                                     ALU.mult, ALU.mult)))(ca),
                                 r=["scrF", "smst", f"ps{bi}"], w=["wmx"])
                            p.op("vector",
                                 (lambda d, t: (lambda e: e.tensor_tensor(Xs(d, t), Xs(d, t), A_WMX[:, :],
                                                                          ALU.add)))(dc, tc),
                                 r=["wmx", xb(dc, tc)], w=[xb(dc, tc)])
                p.barrier()


        def emit_ffn(call_idx):
            ci = str(call_idx)
            FLNROW = A_CM
            flnrow = nc.alloc_sbuf_tensor_at(f"FLNROW{ci}", [1, 1024], F32, offset=O_SCR + 12288)
            sq = FFT2[:, 3584:4096]
            p.op("vector", lambda e: e.memset(XN[:, :], 0.0), w=["xnall"])
            for tc in range(NT):
                emit_ln(tc, (lambda d: XNs(d, tc)), flnrow, sq, "flnrow", "fsq", xn32=FFT)
                for dc in range(NCK):
                    p.buf[f"fxn{tc}_{dc}"] = p.buf.pop("xn" + str(dc))
                for k in range(NCK):
                    p.op("tensor",
                         (lambda kk, t: (lambda e: e.matmul(
                             PS[2][0:8, :], SMB[:, 282 + kk * 8:282 + (kk + 1) * 8],
                             FFT[:, kk * TCH:(kk + 1) * TCH], start=(kk == 0), stop=(kk == NCK - 1))))(k, tc),
                         r=["smb", "xn32"], w=["ps2"])
                p.op("scalar",
                     (lambda t: (lambda e: e.copy(SFM[:, t * TCH:(t + 1) * TCH], PS[2][0:8, :])))(tc),
                     r=["ps2"], w=["glog"])
            for i in range(16):
                p.op("tensor",
                     (lambda ii: (lambda e: e.transpose(PS[3][:, ii * 8:(ii + 1) * 8],
                                                        SFM[0:8, ii * 128:(ii + 1) * 128],
                                                        IDENT[0:8, 0:8])))(i),
                     r=["glog", "ident"], w=["ps3"])
            p.op("scalar", lambda e: e.copy(GTM[:, 0:128], PS[3][:, 0:128]), r=["ps3"], w=["gt_l"])
            p.op("scalar", lambda e: e.activation(GTM[:, 128:256], GTM[:, 0:128], AF.Exp),
                 r=["gt_l"], w=["gt_e"])
            p.op("vector", lambda e: e.tensor_reduce(GSML[:, 0:16],
                                                     GTM[:, 128:256].rearrange("p (a b) -> p a b", b=8),
                                                     AX.X, ALU.add),
                 r=["gt_e"], w=["g_sum"])
            p.op("vector", lambda e: e.reciprocal(GSML[:, 16:32], GSML[:, 0:16]), r=["g_sum"], w=["g_rcp"])
            for i in range(16):
                p.op("vector",
                     (lambda ii: (lambda e: e.tensor_scalar_mul(GTM[:, 256 + ii * 8:256 + (ii + 1) * 8],
                                                                GTM[:, 128 + ii * 8:128 + (ii + 1) * 8],
                                                                GSML[:, 16 + ii:17 + ii])))(i),
                     r=["gt_e", "g_rcp"], w=["gt_s"])
            p.op("vector", lambda e: e.tensor_reduce(GSML[:, 32:48],
                                                     GTM[:, 256:384].rearrange("p (a b) -> p a b", b=8),
                                                     AX.X, ALU.max),
                 r=["gt_s"], w=["g_m1"])
            for i in range(16):
                p.op("vector",
                     (lambda ii: (lambda e: e.tensor_scalar(GTM[:, 384 + ii * 8:384 + (ii + 1) * 8],
                                                            GTM[:, 256 + ii * 8:256 + (ii + 1) * 8],
                                                            GSML[:, 32 + ii:33 + ii], None, ALU.is_ge)))(i),
                     r=["gt_s", "g_m1"], w=["gt_ge"])
            p.op("vector", lambda e: e.scalar_tensor_tensor(GTM[:, 384:512], GTM[:, 384:512], -1e30,
                                                            GTM[:, 256:384], ALU.mult, ALU.add),
                 r=["gt_ge", "gt_s"], w=["gt_mk"])
            p.op("vector", lambda e: e.tensor_reduce(GSML[:, 48:64],
                                                     GTM[:, 384:512].rearrange("p (a b) -> p a b", b=8),
                                                     AX.X, ALU.max),
                 r=["gt_mk"], w=["g_m2"])
            for i in range(16):
                p.op("vector",
                     (lambda ii: (lambda e: e.tensor_scalar(GTM[:, 640 + ii * 8:640 + (ii + 1) * 8],
                                                            GTM[:, 256 + ii * 8:256 + (ii + 1) * 8],
                                                            GSML[:, 48 + ii:49 + ii], None, ALU.is_ge)))(i),
                     r=["gt_s", "g_m2"], w=["gt_mask"])
            for tc in range(NT):
                for w4 in range(4):
                    i = tc * 4 + w4
                    p.op("tensor",
                         (lambda ii, ww: (lambda e: e.transpose(PS[2][0:8, ww * 128:(ww + 1) * 128],
                                                                GTM[:, 256 + ii * 8:256 + (ii + 1) * 8],
                                                                IDENT[:, 0:128])))(i, w4),
                         r=["gt_s", "ident"], w=["ps2"])
                p.op("scalar",
                     (lambda t: (lambda e: e.copy(SFM[:, t * TCH:(t + 1) * TCH], PS[2][0:8, :])))(tc),
                     r=["ps2"], w=["sfm"])
                for w4 in range(4):
                    i = tc * 4 + w4
                    p.op("tensor",
                         (lambda ii, ww: (lambda e: e.transpose(PS[3][0:8, ww * 128:(ww + 1) * 128],
                                                                GTM[:, 640 + ii * 8:640 + (ii + 1) * 8],
                                                                IDENT[:, 0:128])))(i, w4),
                         r=["gt_mask", "ident"], w=["ps3"])
                p.op("scalar",
                     (lambda t: (lambda e: e.copy(MFMB[:, t * TCH:(t + 1) * TCH], PS[3][0:8, :])))(tc),
                     r=["ps3"], w=["mfm"])
            p.barrier()

            stcnt = [0]

            def stage(drt, row0, colspec, slot, sloto, slotname):
                i = stcnt[0]
                stcnt[0] += 1
                st = WST[i % 2]
                stn = f"fwst{i % 2}"
                dma("sync", st[:, 0:1024], drt[row0:row0 + 128, colspec[0]:colspec[1]]
                    if colspec else drt[row0:row0 + 128, :], w=[stn])
                p.op("scalar",
                     (lambda s_, so: (lambda e: e.copy(FW[slot][:, so:so + 1024], s_[:, 0:1024])))(st, sloto),
                     r=[stn, "cc"], w=[slotname])

            for h in range(2):
                for kc in range(NCK):
                    stage(dr["fSw1T"], kc * 128, (h * 1024, (h + 1) * 1024), 0, kc * 1024, "fw0")
                for kc in range(NCK):
                    stage(dr["fSw2T"], kc * 128, (h * 1024, (h + 1) * 1024), 1, kc * 1024, "fw1")
                for icl in range(8):
                    stage(dr["fSw3T"], (h * 8 + icl) * 128, None, 2, icl * 1024, "fw2")
                for tc in range(NT):
                    for icl in range(8):
                        bi = icl % 2
                        for kc in range(NCK):
                            p.op("tensor",
                                 (lambda k2, il, t, b=PS[icl % 2]: (lambda e: e.matmul(
                                     b[:, :], FW[0][:, k2 * 1024 + il * 128:k2 * 1024 + (il + 1) * 128],
                                     XNs(k2, t), start=(k2 == 0), stop=(k2 == NCK - 1))))(kc, icl, tc),
                                 r=["fw0", f"fxn{tc}_{kc}"], w=[f"ps{icl % 2}"])
                        for kc in range(NCK):
                            p.op("tensor",
                                 (lambda k2, il, t, b=PS[2 + icl % 2]: (lambda e: e.matmul(
                                     b[:, :], FW[1][:, k2 * 1024 + il * 128:k2 * 1024 + (il + 1) * 128],
                                     XNs(k2, t), start=(k2 == 0), stop=(k2 == NCK - 1))))(kc, icl, tc),
                                 r=["fw1", f"fxn{tc}_{kc}"], w=[f"ps{2 + icl % 2}"])
                        p.op("scalar",
                             (lambda b=PS[icl % 2]: (lambda e: e.activation(FFT[:, 0:512], b[:, :],
                                                                            AF.Silu)))(),
                             r=[f"ps{icl % 2}"], w=["fab"])
                        p.op("vector",
                             (lambda il, b=PS[2 + icl % 2]: (lambda e: e.tensor_tensor(
                                 PRB[:, il * 512:(il + 1) * 512], FFT[:, 0:512], b[:, :], ALU.mult)))(icl),
                             r=["fab", f"ps{2 + icl % 2}"], w=[f"prb{icl}"])
                    for dc in range(NCK):
                        bi = 4 + dc % 2
                        for icl in range(8):
                            p.op("tensor",
                                 (lambda il, d2, b=PS[4 + dc % 2]: (lambda e: e.matmul(
                                     b[:, :], FW[2][:, il * 1024 + d2 * 128:il * 1024 + (d2 + 1) * 128],
                                     PRB[:, il * 512:(il + 1) * 512],
                                     start=(il == 0), stop=(il == 7))))(icl, dc),
                                 r=["fw2"] + [f"prb{il2}" for il2 in range(8)], w=[f"ps{4 + dc % 2}"])
                        p.op("vector",
                             (lambda d2, t=tc, b=PS[4 + dc % 2]: (lambda e: e.tensor_tensor(
                                 Xs(d2, t), Xs(d2, t), b[:, :], ALU.add)))(dc),
                             r=[f"ps{4 + dc % 2}", xb(dc, tc)], w=[xb(dc, tc)])

            def stage_expert_matrix(e_i, mat, slot, slotname):
                drt = {0: dr["fW1T"], 1: dr["fW3T"], 2: dr["fW2T"]}[mat]
                for kc in range(NCK):
                    i = stcnt[0]
                    stcnt[0] += 1
                    st = WST[i % 2]
                    stn = f"fwst{i % 2}"
                    dma("sync", st[:, 0:1024], drt[e_i, kc * 128:(kc + 1) * 128, :], w=[stn])
                    p.op("scalar",
                         (lambda s_, so: (lambda e: e.copy(FW[slot][:, so:so + 1024], s_[:, 0:1024])))(st, kc * 1024),
                         r=[stn, "cc"], w=[slotname])

            dma("sync", FSEL8F[:, :], dr["c_sel8"][:, :], w=["fsel8"])
            dma("sync", FSEL16[:, :], dr["c_sel16"][0:8, 0:1024], w=["fsel16m"])
            stage_expert_matrix(0, 0, 0, "fw0")
            stage_expert_matrix(0, 1, 1, "fw1")
            stage_expert_matrix(0, 2, 2, "fw2")
            for e_i in range(E8):
                for tc in range(NT):
                    p.op("tensor",
                         (lambda ee, t: (lambda e: e.matmul(PS[2][:, :], FSEL8F[:, ee * 128:(ee + 1) * 128],
                                                            SFM[0:8, t * TCH:(t + 1) * TCH],
                                                            start=True, stop=True)))(e_i, tc),
                         r=["fsel8", "sfm"], w=["ps2"])
                    p.op("scalar", (lambda: (lambda e: e.copy(FFT[:, 1536:2048], PS[2][:, :])))(),
                         r=["ps2"], w=["fsbb"])
                    for hc in range(NCK):
                        for kc in range(NCK):
                            p.op("tensor",
                                 (lambda k2, h2, t, b=PS[hc % 2]: (lambda e: e.matmul(
                                     b[:, :], FW[0][:, k2 * 1024 + h2 * 128:k2 * 1024 + (h2 + 1) * 128],
                                     XNs(k2, t), start=(k2 == 0), stop=(k2 == NCK - 1))))(kc, hc, tc),
                                 r=["fw0", f"fxn{tc}_{kc}"], w=[f"ps{hc % 2}"])
                        p.op("vector",
                             (lambda b=PS[hc % 2]: (lambda e: e.tensor_tensor(FFT[:, 0:512], b[:, :],
                                                                              FFT[:, 1536:2048], ALU.mult)))(),
                             r=[f"ps{hc % 2}", "fsbb"], w=["fhp"])
                        p.op("scalar",
                             (lambda ee, h2: (lambda e: e.activation(HB[:, h2 * 512:(h2 + 1) * 512],
                                                                     FFT[:, 0:512], AF.Silu,
                                                                     bias=SMB[:, 90 + ee * 8 + h2:91 + ee * 8 + h2])))(e_i, hc),
                             r=["fhp", "smb"], w=[f"fh{hc}"])
                    p.op("tensor",
                         (lambda ee, t: (lambda e: e.matmul(PS[3][:, :], FSEL16[:, ee * 128:(ee + 1) * 128],
                                                            MFMB[0:8, t * TCH:(t + 1) * TCH],
                                                            start=True, stop=True)))(e_i, tc),
                         r=["fsel16m", "mfm"], w=["ps3"])
                    p.op("scalar", (lambda: (lambda e: e.copy(FFT[:, 3072:3584], PS[3][:, :])))(),
                         r=["ps3"], w=["fmbb"])
                    for dc in range(NCK):
                        for kc in range(NCK):
                            p.op("tensor",
                                 (lambda k2, d2, t, b=PS[4 + dc % 2]: (lambda e: e.matmul(
                                     b[:, :], FW[1][:, k2 * 1024 + d2 * 128:k2 * 1024 + (d2 + 1) * 128],
                                     XNs(k2, t), start=(k2 == 0), stop=(k2 == NCK - 1))))(kc, dc, tc),
                                 r=["fw1", f"fxn{tc}_{kc}"], w=[f"ps{4 + dc % 2}"])
                        p.op("vector",
                             (lambda b=PS[4 + dc % 2]: (lambda e: e.tensor_tensor(
                                 FFT[:, 512:1024], b[:, :], FFT[:, 1536:2048], ALU.mult)))(),
                             r=[f"ps{4 + dc % 2}", "fsbb"], w=["ft3"])
                        p.op("vector",
                             (lambda ee, d2: (lambda e: e.tensor_scalar_add(
                                 FFT2[:, d2 * 512:(d2 + 1) * 512], FFT[:, 512:1024],
                                 SMB[:, 218 + ee * 8 + d2:219 + ee * 8 + d2])))(e_i, dc),
                             r=["ft3", "smb"], w=[f"fx3_{dc}"])
                    for dc in range(NCK):
                        for hc in range(NCK):
                            p.op("tensor",
                                 (lambda h2, d2, b=PS[6 + dc % 2]: (lambda e: e.matmul(
                                     b[:, :], FW[2][:, h2 * 1024 + d2 * 128:h2 * 1024 + (d2 + 1) * 128],
                                     HB[:, h2 * 512:(h2 + 1) * 512],
                                     start=(h2 == 0), stop=(h2 == NCK - 1))))(hc, dc),
                                 r=["fw2"] + [f"fh{h3}" for h3 in range(NCK)], w=[f"ps{6 + dc % 2}"])
                        p.op("vector",
                             (lambda ee, d2, b=PS[6 + dc % 2]: (lambda e: e.scalar_tensor_tensor(
                                 FFT[:, 2048:2560], b[:, :],
                                 SMB[:, 154 + ee * 8 + d2:155 + ee * 8 + d2],
                                 FFT2[:, d2 * 512:(d2 + 1) * 512], ALU.add, ALU.mult)))(e_i, dc),
                             r=[f"ps{6 + dc % 2}", "smb", f"fx3_{dc}"], w=["fc0"])
                        cmslot = 2560 if dc % 2 == 0 else 3584
                        p.op("vector",
                             (lambda cs=cmslot: (lambda e: e.tensor_tensor(FFT[:, cs:cs + 512],
                                                                           FFT[:, 2048:2560],
                                                                           FFT[:, 3072:3584], ALU.mult)))(),
                             r=["fc0", "fmbb"], w=[f"fcm{dc % 2}"])
                        p.op("gpsimd",
                             (lambda d2, t=tc, cs=cmslot: (lambda e: e.tensor_tensor(
                                 Xs(d2, t), Xs(d2, t), FFT[:, cs:cs + 512], ALU.add)))(dc),
                             r=[f"fcm{dc % 2}", xb(dc, tc)], w=[xb(dc, tc)])
                    if tc == 0 and e_i + 1 < E8:
                        pass
                if e_i + 1 < E8:
                    stage_expert_matrix(e_i + 1, 0, 0, "fw0")
                    stage_expert_matrix(e_i + 1, 1, 1, "fw1")
                    stage_expert_matrix(e_i + 1, 2, 2, "fw2")
            p.barrier()

        if emit_upto >= 2:
            emit_ffn(1)

        def emit_attnb():
            blnrow = nc.alloc_sbuf_tensor_at("BLNROW", [1, 1024], F32, offset=O_SCR)
            bsq = nc.alloc_sbuf_tensor_at("BSQ", [128, 512], F32, offset=O_SCR + 4096)
            for tc in range(NT):
                emit_ln(tc, (lambda d: XNs(d, tc)), blnrow, bsq, "blnrow", "bsq")
                for dc in range(NCK):
                    p.buf[f"bxn{tc}_{dc}"] = p.buf.pop("xn" + str(dc))
            IPJW = [nc.alloc_sbuf_tensor_at(f"IPJW{i}", [128, 1024], BF16, offset=O_WST + 4096 + 2048 * i)
                    for i in range(2)]
            def qkv_dst(oc, tc):
                if oc < 8:
                    return QBUF[:, oc * 2048 + tc * TCH: oc * 2048 + (tc + 1) * TCH], f"q{oc}"
                if oc < 16:
                    cc = oc - 8
                    return KBUF[:, cc * 2048 + tc * TCH: cc * 2048 + (tc + 1) * TCH], f"k{cc}"
                cc = oc - 16
                if cc < 4:
                    return V1B[:, cc * 2048 + tc * TCH: cc * 2048 + (tc + 1) * TCH], f"v{cc}"
                return V2B[:, (cc - 4) * 2048 + tc * TCH: (cc - 4) * 2048 + (tc + 1) * TCH], f"v{cc}"

            for oc in range(24):
                ipw = IPJW[oc % 2]
                ipwn = f"ipjw{oc % 2}"
                src_ap = bass.AP(dr["bIpjT"], oc * 128, [[3072, 128], [128 * 3072, 8], [1, 128]])
                dma("sync", WST[0][:, :].rearrange("p (a b) -> p a b", b=128), src_ap, w=["bwst0"])
                p.op("scalar", (lambda ip=ipw: (lambda e: e.copy(ip[:, :], WST[0][:, :])))(),
                     r=["bwst0", "cc"], w=[ipwn])
                for kc in range(NCK):
                    for tc in range(NT):
                        p.op("tensor",
                             (lambda k2, t, ip=ipw, b=PS[4 + tc]: (lambda e: e.matmul(
                                 b[:, :], ip[:, k2 * 128:(k2 + 1) * 128], XNs(k2, t),
                                 start=(k2 == 0), stop=(k2 == NCK - 1))))(kc, tc),
                             r=[ipwn, f"bxn{tc}_{kc}"], w=[f"ps{4 + tc}"])
                for tc in range(NT):
                    dst, dstn = qkv_dst(oc, tc)
                    p.op("vector",
                         (lambda o2=oc, d2=dst, t=tc, b=PS[4 + tc]: (lambda e: e.tensor_scalar(
                             d2, b[:, :], SMB[:, 370 + o2:371 + o2], SMB[:, 346 + o2:347 + o2],
                             ALU.mult, ALU.add)))(),
                         r=[f"ps{4 + tc}", "smb"], w=[dstn])
            p.barrier()
            dma("sync", BF16C[:, :], dr["c_F16"][:, :], w=["bf16c"])

            def vslice(cc, tc):
                if cc < 4:
                    return V1B[:, cc * 2048 + tc * TCH: cc * 2048 + (tc + 1) * TCH]
                return V2B[:, (cc - 4) * 2048 + tc * TCH: (cc - 4) * 2048 + (tc + 1) * TCH]

            for s in range(4):
                if s == 0:
                    for cc in range(NCK):
                        p.op("vector",
                             (lambda c2: (lambda e: e.tensor_scalar_mul(
                                 ATTO[:, c2 * 2048:c2 * 2048 + TCH], vslice(c2, 0),
                                 SMB[:, 74 + c2:75 + c2])))(cc),
                             r=[f"v{cc}", "smb"], w=[f"atto{cc}_0"])
                    continue
                for sp in range(s + 1):
                    for cc in range(NCK):
                        p.op("vector",
                             (lambda c2, s2=s, s3=sp: (lambda e: e.tensor_tensor(
                                 BZB1[:, :], QBUF[:, c2 * 2048 + s2 * TCH:c2 * 2048 + (s2 + 1) * TCH],
                                 KBUF[:, c2 * 2048 + s3 * TCH:c2 * 2048 + (s3 + 1) * TCH], ALU.mult)))(cc),
                             r=[f"q{cc}", f"k{cc}"], w=["bzb"])
                        p.op("tensor",
                             (lambda c2, b=PS[sp]: (lambda e: e.matmul(
                                 b[0:16, :], BE16F[:, c2 * 16:(c2 + 1) * 16], BZB1[:, :],
                                 start=(c2 == 0), stop=(c2 == NCK - 1))))(cc),
                             r=["be16f", "bzb"], w=[f"ps{sp}"])
                    p.op("scalar",
                         (lambda s3=sp, b=PS[sp]: (lambda e: e.copy(
                             BSCB[:, s3 * TCH:(s3 + 1) * TCH], b[0:16, :])))(),
                         r=[f"ps{sp}"], w=[f"scb{sp}"])
                p.op("vector", (lambda s2=s: (lambda e: e.tensor_tensor(
                    BMX[:, :], BSCB[:, 0:TCH], BSCB[:, TCH:2 * TCH], ALU.max)))(),
                    r=["scb0", "scb1"], w=["bmx"])
                for sp in range(2, s + 1):
                    p.op("vector", (lambda s3=sp: (lambda e: e.tensor_tensor(
                        BMX[:, :], BMX[:, :], BSCB[:, s3 * TCH:(s3 + 1) * TCH], ALU.max)))(),
                        r=["bmx", f"scb{sp}"], w=["bmx"])
                for sp in range(s + 1):
                    p.op("vector", (lambda s3=sp: (lambda e: e.tensor_tensor(
                        BSCB[:, s3 * TCH:(s3 + 1) * TCH], BSCB[:, s3 * TCH:(s3 + 1) * TCH],
                        BMX[:, :], ALU.subtract)))(),
                        r=[f"scb{sp}", "bmx"], w=[f"scb{sp}"])
                    p.op("scalar", (lambda s3=sp: (lambda e: e.activation(
                        BSCB[:, s3 * TCH:(s3 + 1) * TCH], BSCB[:, s3 * TCH:(s3 + 1) * TCH], AF.Exp)))(),
                        r=[f"scb{sp}"], w=[f"scb{sp}"])
                p.op("vector", lambda e: e.tensor_tensor(BSUM[:, :], BSCB[:, 0:TCH],
                                                         BSCB[:, TCH:2 * TCH], ALU.add),
                     r=["scb0", "scb1"], w=["bsum"])
                for sp in range(2, s + 1):
                    p.op("vector", (lambda s3=sp: (lambda e: e.tensor_tensor(
                        BSUM[:, :], BSUM[:, :], BSCB[:, s3 * TCH:(s3 + 1) * TCH], ALU.add)))(),
                        r=["bsum", f"scb{sp}"], w=["bsum"])
                p.op("vector", lambda e: e.reciprocal(BMX[:, :], BSUM[:, :]), r=["bsum"], w=["brc"])
                for sp in range(s + 1):
                    p.op("vector", (lambda s3=sp: (lambda e: e.tensor_tensor(
                        BSCB[:, s3 * TCH:(s3 + 1) * TCH], BSCB[:, s3 * TCH:(s3 + 1) * TCH],
                        BMX[:, :], ALU.mult)))(),
                        r=[f"scb{sp}", "brc"], w=[f"scb{sp}"])
                    p.op("scalar", (lambda s3=sp: (lambda e: e.copy(
                        BWJ4[0:16, s3 * TCH:(s3 + 1) * TCH], BSCB[:, s3 * TCH:(s3 + 1) * TCH])))(),
                        r=[f"scb{sp}"], w=[f"bwj{sp}"])
                for cc in range(NCK):
                    for sp in range(s + 1):
                        p.op("tensor",
                             (lambda c2, s3=sp, b=PS[4 + sp % 2]: (lambda e: e.matmul(
                                 b[:, :], BF16C[:, c2 * 128:(c2 + 1) * 128],
                                 BWJ4[0:16, s3 * TCH:(s3 + 1) * TCH], start=True, stop=True)))(cc),
                             r=["bf16c", f"bwj{sp}"], w=[f"ps{4 + sp % 2}"])
                        if sp == 0:
                            p.op("vector",
                                 (lambda c2, s3=sp, b=PS[4 + sp % 2]: (lambda e: e.tensor_tensor(
                                     BOAC[:, :], b[:, :], vslice(c2, s3), ALU.mult)))(cc),
                                 r=[f"ps{4 + sp % 2}", f"v{cc}"], w=["boac"])
                        else:
                            p.op("vector",
                                 (lambda c2, s3=sp, b=PS[4 + sp % 2]: (lambda e: e.tensor_tensor(
                                     WST[0][:, 512:1024], b[:, :], vslice(c2, s3), ALU.mult)))(cc),
                                 r=[f"ps{4 + sp % 2}", f"v{cc}"], w=["bwst0h"])
                            p.op("vector", lambda e: e.tensor_tensor(BOAC[:, :], BOAC[:, :],
                                                                     WST[0][:, 512:1024], ALU.add),
                                 r=["boac", "bwst0h"], w=["boac"])
                    p.op("vector",
                         (lambda c2, s2=s: (lambda e: e.tensor_scalar_mul(
                             ATTO[:, c2 * 2048 + s2 * TCH:c2 * 2048 + (s2 + 1) * TCH], BOAC[:, :],
                             SMB[:, 74 + c2:75 + c2])))(cc),
                         r=["boac", "smb"], w=[f"atto{cc}_{s}"])
            p.barrier()
            for oc in range(NCK):
                ipw = IPJW[oc % 2]
                ipwn = f"opjw{oc % 2}"
                src_ap = bass.AP(dr["bOpjT"], oc * 128, [[1024, 128], [128 * 1024, 8], [1, 128]])
                dma("sync", WST[0][:, :].rearrange("p (a b) -> p a b", b=128), src_ap, w=["bwst0"])
                p.op("scalar", (lambda ip=ipw: (lambda e: e.copy(ip[:, :], WST[0][:, :])))(),
                     r=["bwst0", "cc"], w=[ipwn])
                for tc in range(NT):
                    for kc in range(NCK):
                        p.op("tensor",
                             (lambda k2, t, ip=ipw, b=PS[6 + oc % 2]: (lambda e: e.matmul(
                                 b[:, :], ip[:, k2 * 128:(k2 + 1) * 128],
                                 ATTO[:, k2 * 2048 + t * TCH:k2 * 2048 + (t + 1) * TCH],
                                 start=(k2 == 0), stop=(k2 == NCK - 1))))(kc, tc),
                             r=[ipwn, f"atto{kc}_{tc}"], w=[f"ps{6 + oc % 2}"])
                    p.op("vector",
                         (lambda o2=oc, b=PS[6 + oc % 2]: (lambda e: e.tensor_scalar(
                             BOAC[:, :], b[:, :], SMB[:, 82 + o2:83 + o2], SMB[:, 402 + o2:403 + o2],
                             ALU.mult, ALU.add)))(),
                         r=[f"ps{6 + oc % 2}", "smb"], w=["boac"])
                    p.op("vector",
                         (lambda o2=oc, t=tc: (lambda e: e.tensor_tensor(Xs(o2, t), Xs(o2, t),
                                                                         BOAC[:, :], ALU.add)))(),
                         r=["boac", xb(oc, tc)], w=[xb(oc, tc)])
            p.barrier()

        if emit_upto >= 3:
            emit_attnb()

        if emit_upto >= 4:
            emit_ffn(2)

        for dc in range(NCK):
            dma("gpsimd", out_fm[dc * 128:(dc + 1) * 128, :], X[:, dc * 2048:(dc + 1) * 2048],
                w=[f"storedma{dc}"], r=[xb(dc, t) for t in range(NT)], queue_sem="qP")
        p.stream["gpsimd"].append(((("qP", p.count["qP"]),), None, None, 0))

        def make_body(name):
            def body(eng):
                for (waits, fn, incsem, incn) in p.stream[name]:
                    for (s, v) in waits:
                        if v > 0:
                            eng.wait_ge(sems[s], v)
                    inst = fn(eng) if fn is not None else None
                    if incsem is not None:
                        if inst is None:
                            inst = eng.nop()
                        inst.then_inc(sems[incsem], incn)
            return body

        block.sync(make_body("sync"))
        block.tensor(make_body("tensor"))
        block.vector(make_body("vector"))
        block.scalar(make_body("scalar"))
        block.gpsimd(make_body("gpsimd"))

    return nc


def _prep_inputs(inputs):
    f = lambda a: np.ascontiguousarray(a, dtype=np.float32)
    g = {}
    g["aWl1"] = f(inputs["L0_attn_a_Wl1"]); g["aWl2"] = f(inputs["L0_attn_a_Wl2"])
    g["aWf1"] = f(inputs["L0_attn_a_Wf1"]); g["aWf2"] = f(inputs["L0_attn_a_Wf2"])
    g["aW1"] = f(inputs["L0_attn_a_W1"]); g["aW2"] = f(inputs["L0_attn_a_W2"])
    g["aW3"] = f(inputs["L0_attn_a_W3"])
    aB = np.zeros((128, 26), np.float32)
    aB[:, 0] = inputs["L0_attn_a_bl1"]
    aB[:, 1:9] = np.asarray(inputs["L0_attn_a_bl2"]).reshape(8, 128).T
    aB[:, 9] = inputs["L0_attn_a_bf1"]
    aB[:, 10:18] = np.asarray(inputs["L0_attn_a_bf2"]).reshape(8, 128).T
    aB[:, 18:26] = np.asarray(inputs["L0_attn_a_b1"]).reshape(8, 128).T
    g["aBpp"] = aB
    g["aB23"] = f(np.stack([inputs["L0_attn_a_b2"], inputs["L0_attn_a_b3"]], axis=1))
    g["aA"] = f(np.asarray(inputs["L0_attn_a_A"]).reshape(8, 128, 16).transpose(1, 0, 2).reshape(128, 128))
    g["aST"] = f(np.asarray(inputs["L0_attn_a_state_init"]).reshape(8, 128, 16)
                 .transpose(1, 0, 2).reshape(128, 128))
    g["bIpjT"] = f(np.asarray(inputs["L0_attn_b_in_proj_weight"]).T)
    g["bOpjT"] = f(np.asarray(inputs["L0_attn_b_out_proj_weight"]).T)
    bG = np.zeros((128, 64), np.float32)
    bG[:, 0:24] = np.asarray(inputs["L0_attn_b_gamma"]).reshape(24, 128).T
    bG[:, 24:48] = np.asarray(inputs["L0_attn_b_in_proj_bias"]).reshape(24, 128).T
    bG[:, 48:56] = np.asarray(inputs["L0_attn_b_linear_alpha"]).reshape(8, 128).T
    bG[:, 56:64] = np.asarray(inputs["L0_attn_b_linear_gamma"]).reshape(8, 128).T
    g["bGpp"] = bG
    g["bOpb"] = f(np.asarray(inputs["L0_attn_b_out_proj_bias"]).reshape(8, 128).T)
    g["fGwT"] = f(np.asarray(inputs["L0_ffn_gate_w"]).T)
    g["fW1T"] = f(np.asarray(inputs["L0_ffn_w1"]).transpose(0, 2, 1))
    g["fW2T"] = f(np.asarray(inputs["L0_ffn_w2"]).transpose(0, 2, 1))
    g["fW3T"] = f(np.asarray(inputs["L0_ffn_w3"]).transpose(0, 2, 1))
    g["fSw1T"] = f(np.asarray(inputs["L0_ffn_sw1"]).T)
    g["fSw2T"] = f(np.asarray(inputs["L0_ffn_sw2"]).T)
    g["fSw3T"] = f(np.asarray(inputs["L0_ffn_sw3"]).T)
    fB = np.zeros((128, 192), np.float32)
    fB[:, 0:64] = np.asarray(inputs["L0_ffn_b1"]).reshape(8, 8, 128).transpose(2, 0, 1).reshape(128, 64)
    fB[:, 64:128] = np.asarray(inputs["L0_ffn_b2"]).reshape(8, 8, 128).transpose(2, 0, 1).reshape(128, 64)
    fB[:, 128:192] = np.asarray(inputs["L0_ffn_b3"]).reshape(8, 8, 128).transpose(2, 0, 1).reshape(128, 64)
    g["fBpp"] = fB
    g["c_ident"] = np.eye(128, dtype=np.float32)
    cc = np.zeros((128, 8), np.float32)
    cc[:, 0] = EPS; cc[:, 1] = 1.0; cc[:, 2] = -1.0; cc[:, 3] = -0.5; cc[:, 4] = 0.125
    g["c_const"] = cc
    sel16 = np.zeros((16, 2048), np.float32)
    for n in range(16):
        sel16[n, n * 128:(n + 1) * 128] = 1.0
    g["c_sel16"] = sel16.astype(ml_dtypes.bfloat16)
    sel8 = np.zeros((8, 1024), np.float32)
    for e in range(8):
        sel8[e, e * 128:(e + 1) * 128] = 1.0
    g["c_sel8"] = sel8
    e16 = np.zeros((128, 128), np.float32)
    for cc in range(8):
        for k in range(128):
            e16[k, cc * 16 + 2 * cc + (k // 64)] = 1.0
    g["c_E16"] = e16.astype(ml_dtypes.bfloat16)
    g["c_E16f"] = e16.astype(np.float32)
    f16 = np.zeros((16, 1024), np.float32)
    for cc in range(8):
        for m in range(128):
            f16[2 * cc + (m // 64), cc * 128 + m] = 1.0
    g["c_F16"] = f16.astype(ml_dtypes.bfloat16)
    return g


def kernel(**inputs):
    x = np.asarray(inputs["L0_attn_a_x"])
    B, L, _ = x.shape
    shared = _prep_inputs(inputs)
    nc = build_program(_UPTO, _SUB)
    in_maps = []
    for c in range(8):
        xc = np.ascontiguousarray(x[:, c * 512:(c + 1) * 512, :].reshape(B * 512, D).T)
        m = dict(shared)
        m["x_fm"] = xc
        in_maps.append(m)
    res = run_bass_kernel_spmd(nc, in_maps, core_ids=list(range(8)))
    out = np.empty((B, L, D), np.float32)
    for c in range(8):
        out[:, c * 512:(c + 1) * 512, :] = res.results[c]["out_fm"].T.reshape(B, 512, D)
    return out


# version 07
# speedup vs baseline: 2.0112x; 1.0164x over previous
import numpy as np
import ml_dtypes
import concourse.bass as bass
import concourse.mybir as mybir
from concourse.bass_utils import run_bass_kernel_spmd
from contextlib import ExitStack

F32 = mybir.dt.float32
BF16 = mybir.dt.bfloat16
AF = mybir.ActivationFunctionType
ALU = mybir.AluOpType
AX = mybir.AxisListType

D = 1024
NCK = 8
TPC = 2048
TCH = 512
NT = 4
N16 = 16
E8 = 8
EPS = 1e-5

_UPTO = 99
_SUB = 99
_DBG_PAIRS = 2
_DBG_MC = 8
_DBG_DOPS = 9
_DBG_W1OFF = True
_DBG_NODEL = False

O_X = 1536
O_XNR = O_X + 65536
O_WBUF = O_XNR + 32768
O_BIGA = O_WBUF + 49152
O_BIGB = O_BIGA + 32768
O_SCR = O_BIGB + 16384
O_WST = O_SCR + 16384
O_SMALL = O_WST + 8192

ENG2SEM = {"tensor": "sT", "vector": "sV", "scalar": "sS", "gpsimd": "sG", "sync": None}
ALLSEMS = ("qW", "qW1", "qW2", "qP", "sT", "sV", "sS", "sG")
QROT = {"qW": ("qW", "qW1", "qW2"), "qP": ("qP",)}


class Prog:
    ENGINES = ("sync", "tensor", "vector", "scalar", "gpsimd")

    def __init__(self):
        self.stream = {e: [] for e in self.ENGINES}
        self.count = {s: 0 for s in ALLSEMS}
        self.buf = {}
        self.hiwater = {e: {} for e in self.ENGINES}
        self.qidx = {q: 0 for q in QROT}

    def _b(self, name):
        return self.buf.setdefault(name, {"w": None, "r": []})

    def op(self, eng, fn, r=(), w=(), dma=None, extra=()):
        deps = []
        myeng0 = None if dma else eng
        for name in r:
            b = self._b(name)
            if b["w"]:
                deps.append(b["w"])
            if name.startswith("ps"):
                deps.extend(m for m in b["r"] if m[2] != myeng0)
        for name in w:
            b = self._b(name)
            if b["w"]:
                deps.append(b["w"])
            deps.extend(b["r"])
        waits = {}
        for (sem, val, weng) in deps:
            if weng == "tensor" and myeng0 == "tensor":
                continue
            if val > waits.get(sem, 0):
                waits[sem] = val
        if dma:
            rot = QROT[dma]
            dma = rot[self.qidx[rot[0]] % len(rot)]
            self.qidx[rot[0]] += 1
            prev = self.count[dma]
            if prev > waits.get(dma, 0):
                waits[dma] = prev
        for (sem, val) in extra:
            if val > waits.get(sem, 0):
                waits[sem] = val
        hw = self.hiwater[eng]
        waits = {s: v for s, v in waits.items() if v > hw.get(s, 0)}
        for s, v in waits.items():
            hw[s] = v
        if dma:
            sem = dma
            self.count[sem] += 16
            mi = (sem, self.count[sem], None)
        else:
            sem = ENG2SEM[eng]
            if sem is None:
                sem = "sG"
            self.count[sem] += 1
            mi = (sem, self.count[sem], eng)
        self.stream[eng].append((tuple(sorted(waits.items())), fn, mi[0], 16 if dma else 1))
        for name in r:
            self._b(name)["r"].append(mi)
        for name in w:
            b = self._b(name)
            b["w"] = mi
            b["r"] = []
        return mi

    def barrier(self):
        snap = [(s, self.count[s]) for s in ALLSEMS if self.count[s] > 0]
        for e in self.ENGINES:
            self.stream[e].append((tuple(snap), None, None, 0))
            for s, v in snap:
                if v > self.hiwater[e].get(s, 0):
                    self.hiwater[e][s] = v
        self.buf = {}


def build_program(emit_upto=99, sub=None):
    if sub is None:
        sub = _SUB
    _DBG_DOPS = globals()["_DBG_DOPS"]
    _DBG_W1OFF = globals()["_DBG_W1OFF"]
    _DBG_NODEL = globals()["_DBG_NODEL"]
    nc = bass.Bass("TRN2", target_bir_lowering=False, dynamic_dma_scratch_size=512)
    dr = {}

    def din(name, shape, dtype=F32):
        dr[name] = nc.dram_tensor(name, list(shape), dtype, kind="ExternalInput")
        return dr[name]

    din("x_fm", (D, TPC))
    din("aWl1", (D, 128)); din("aWl2", (128, D)); din("aWf1", (2 * D, 128))
    din("aWf2", (128, D)); din("aW1", (D, D)); din("aW2", (D, N16)); din("aW3", (D, N16))
    din("aBpp", (128, 26)); din("aB23", (16, 2)); din("aA", (128, 128)); din("aST", (128, 128))
    din("bIpjT", (D, 3 * D)); din("bOpjT", (D, D)); din("bGpp", (128, 64)); din("bOpb", (128, 8))
    din("fGwT", (D, E8))
    din("fW1T", (E8, D, D)); din("fW2T", (E8, D, D)); din("fW3T", (E8, D, D))
    din("fSw1T", (D, 2 * D)); din("fSw2T", (D, 2 * D)); din("fSw3T", (2 * D, D))
    din("fBpp", (128, 192))
    din("c_ident", (128, 128)); din("c_E16", (128, 128), BF16); din("c_F16", (16, 1024), BF16)
    din("c_const", (128, 8))
    din("c_E16f", (128, 128))
    din("c_sel16", (16, 2048), BF16)
    din("c_sel8", (8, 1024))
    out_fm = nc.dram_tensor("out_fm", [D, TPC], F32, kind="ExternalOutput")

    p = Prog()

    with ExitStack() as ctx:
        at = nc.alloc_sbuf_tensor_at
        X = at("X", [128, 16384], F32, offset=O_X)
        XN = at("XN", [128, 16384], BF16, offset=O_XNR)
        XT32 = at("XT32", [128, 8192], F32, offset=O_XNR)
        ATTO = at("ATTO", [128, 16384], BF16, offset=O_XNR)
        AW = at("AW", [128, 16384], BF16, offset=O_WBUF)
        FW = [at(f"FW{i}", [128, 8192], BF16, offset=O_WBUF + 16384 * i) for i in range(3)]
        KBUF = at("KBUF", [128, 16384], BF16, offset=O_WBUF)
        V1B = at("V1B", [128, 8192], BF16, offset=O_WBUF + 32768)
        A_XN = at("A_XN", [128, 8192], BF16, offset=O_BIGA)
        A_LIF = at("A_LIF", [128, 8192], BF16, offset=O_BIGA + 16384)
        A_DEL = at("A_DEL", [128, 8192], F32, offset=O_BIGA)
        QBUF = at("QBUF", [128, 16384], BF16, offset=O_BIGA)
        FFT = at("FFT", [128, 4096], F32, offset=O_BIGA)
        FFT2 = at("FFT2", [128, 4096], F32, offset=O_BIGA + 16384)
        GTM = at("GTM", [128, 768], F32, offset=O_BIGA + 16384)
        GSML = at("GSML", [128, 64], F32, offset=O_BIGA + 16384 + 3072)
        A_XTB = at("A_XTB", [128, 8192], BF16, offset=O_BIGB)
        HB = at("HB", [128, 4096], BF16, offset=O_BIGB)
        PRB = at("PRB", [128, 8192], BF16, offset=O_BIGB)
        V2B = at("V2B", [128, 8192], BF16, offset=O_BIGB)
        SEL16B = at("SEL16B", [16, 2048], BF16, offset=O_WBUF + 28672)
        FSEL8F = at("FSEL8F", [8, 1024], F32, offset=O_BIGB + 8192)
        FSEL16 = at("FSEL16", [8, 1024], BF16, offset=O_BIGB + 12288)
        BWJ4 = at("BWJ4", [16, 2048], BF16, offset=O_WST + 4096)
        BF16C = at("BF16C", [16, 1024], BF16, offset=O_WST)
        BZB1 = at("BZB1", [128, 512], F32, offset=O_SCR + 12288)
        A_SCR = at("A_SCR", [128, 1536], F32, offset=O_SCR)
        A_WMX = at("A_WMX", [128, 512], F32, offset=O_SCR + 6144)
        A_CM = at("A_CM", [16, 1024], F32, offset=O_SCR + 8192)
        A_CMBF = at("A_CMBF", [16, 1024], BF16, offset=O_SCR + 12288)
        A_CMROW = at("A_CMROW", [1, 512], F32, offset=O_SCR + 14336)
        SFM = at("SFM", [8, 2048], F32, offset=O_SCR)
        MFMB = at("MFMB", [8, 2048], BF16, offset=O_SCR + 8192)
        BSCB = at("BSCB", [16, 2048], F32, offset=O_SCR)
        BMX = at("BMX", [16, 512], F32, offset=O_SCR + 8192)
        BSUM = at("BSUM", [16, 512], F32, offset=O_SCR + 10240)
        BRC = at("BRC", [16, 512], F32, offset=O_SCR + 12288)
        BOAC = at("BOAC", [128, 512], F32, offset=O_SCR + 14336)
        WST = [at(f"WST{i}", [128, 1024], F32, offset=O_WST + 4096 * i) for i in range(2)]
        A_LNROW = at("A_LNROW", [1, 1024], F32, offset=O_WST)
        A_BM = at("A_BM", [16, 512], F32, offset=O_WST + 4096)
        HLB = at("HLB", [128, 1024], BF16, offset=O_WST + 6144)
        BZB = at("BZB", [128, 1024], BF16, offset=O_WST)
        BWJ = at("BWJ", [16, 512], BF16, offset=O_WST + 2048)
        SMB = at("SMB", [128, 424], F32, offset=O_SMALL)
        SM_A = at("SM_A", [128, 128], F32, offset=O_SMALL + 1696)
        SM_AM = at("SM_AM", [128, 128], F32, offset=O_SMALL + 2208)
        SM_NAP = at("SM_NAP", [128, 128], F32, offset=O_SMALL + 2720)
        SM_ST = at("SM_ST", [128, 128], F32, offset=O_SMALL + 3232)
        SM_B23 = at("SM_B23", [16, 2], F32, offset=O_SMALL + 3744)
        IDENT = at("IDENT", [128, 128], F32, offset=O_SMALL + 3776)
        ONES = at("ONES", [1, 128], F32, offset=O_SMALL + 4288)
        BE16F = at("BE16F", [128, 128], F32, offset=O_SMALL + 4800)
        ONESC = at("ONESC", [128, 1], F32, offset=O_SMALL + 6336)
        CC = at("CC", [128, 8], F32, offset=O_SMALL + 6368)
        PS = [ctx.enter_context(nc.psum_tensor(f"ps{i}", [128, 512], F32)) for i in range(8)]
        sems = {s: ctx.enter_context(nc.semaphore(s)) for s in ALLSEMS}
        block = ctx.enter_context(nc.Block())

        def Xs(dc, tc, lo=0, ln=TCH):
            return X[:, dc * 2048 + tc * TCH + lo: dc * 2048 + tc * TCH + lo + ln]

        def xb(dc, tc):
            return f"x{dc}_{tc}"

        def XNs(dc, tc, n=TCH):
            return XN[:, dc * 2048 + tc * TCH: dc * 2048 + tc * TCH + n]

        def dma(q, out, in_, w, r=(), queue_sem="qW"):
            return p.op(q, (lambda o, i: (lambda e: e.dma_start(out=o, in_=i)))(out, in_),
                        r=r, w=w, dma=queue_sem)

        dma("sync", CC[:, :], dr["c_const"][:, :], w=["cc"])
        dma("sync", SEL16B[:, :], dr["c_sel16"][:, :], w=["sel16"])
        dma("sync", BE16F[:, :], dr["c_E16f"][:, :], w=["be16f"])
        for dc in range(NCK):
            dma("sync", X[:, dc * 2048:(dc + 1) * 2048], dr["x_fm"][dc * 128:(dc + 1) * 128, :],
                w=[xb(dc, t) for t in range(NT)])
        dma("sync", SMB[:, 0:26], dr["aBpp"][:, :], w=["smb"])
        dma("sync", SMB[:, 26:90], dr["bGpp"][:, :], w=["smb"])
        dma("sync", SMB[:, 90:282], dr["fBpp"][:, :], w=["smb"])
        dma("sync", SMB[:, 394:402], dr["bOpb"][:, :], w=["smb"])
        for k in range(NCK):
            dma("sync", SMB[:, 282 + k * 8:282 + (k + 1) * 8], dr["fGwT"][k * 128:(k + 1) * 128, :],
                w=["smb"])
        dma("sync", SM_A[:, :], dr["aA"][:, :], w=["sma"])
        dma("sync", SM_ST[:, :], dr["aST"][:, :], w=["smst"])
        dma("sync", SM_B23[:, :], dr["aB23"][:, :], w=["smb23"])
        dma("sync", IDENT[:, :], dr["c_ident"][:, :], w=["ident"])
        p.op("vector", lambda e: e.tensor_scalar_min(SM_AM[:, :], SM_A[:, :], 0.0),
             r=["sma"], w=["smam"])
        p.op("vector", lambda e: e.tensor_scalar(SM_NAP[:, :], SM_A[:, :], 0.0, -1.0, ALU.max, ALU.mult),
             r=["sma"], w=["smnap"])
        p.op("vector", lambda e: e.tensor_tensor(SMB[:, 346:370], SMB[:, 26:50], SMB[:, 50:74], ALU.mult),
             r=["smb"], w=["smb"])
        p.op("vector", lambda e: e.tensor_copy(SMB[:, 370:394], SMB[:, 26:50]), r=["smb"], w=["smb"])
        p.op("vector", lambda e: e.tensor_scalar_mul(SMB[:, 370:378], SMB[:, 370:378], 0.125),
             r=["smb"], w=["smb"])
        p.op("vector", lambda e: e.tensor_scalar_mul(SMB[:, 346:354], SMB[:, 346:354], 0.125),
             r=["smb"], w=["smb"])
        p.op("vector", lambda e: e.tensor_tensor(SMB[:, 402:410], SMB[:, 82:90], SMB[:, 394:402], ALU.mult),
             r=["smb"], w=["smb"])
        p.op("vector", lambda e: e.memset(ONES[:, :], 1.0), w=["ones"])
        p.op("vector", lambda e: e.memset(ONESC[:, :], 1.0), w=["onesc"])

        AWO = {"wl1": 0, "wl2": 1024, "wf1": 2048, "wf2": 4096, "w1": 5120, "w2": 13312, "w3": 13440}
        aw_specs = []
        aw_specs += [(dr["aWl1"], k * 128, 128, AWO["wl1"] + k * 128) for k in range(8)]
        aw_specs += [(dr["aWl2"], 0, 1024, AWO["wl2"])]
        aw_specs += [(dr["aWf1"], k * 128, 128, AWO["wf1"] + k * 128) for k in range(16)]
        aw_specs += [(dr["aWf2"], 0, 1024, AWO["wf2"])]
        aw_specs += [(dr["aW1"], k * 128, 1024, AWO["w1"] + k * 1024) for k in range(8)]
        aw_specs += [(dr["aW2"], k * 128, 16, AWO["w2"] + k * 16) for k in range(8)]
        aw_specs += [(dr["aW3"], k * 128, 16, AWO["w3"] + k * 16) for k in range(8)]
        for i, (dt_, r0, ncols, ao) in enumerate(aw_specs):
            st = WST[i % 2]
            stn = f"wst{i % 2}"
            dma("sync", st[:, 0:ncols], dt_[r0:r0 + 128, :], w=[stn])
            p.op("scalar",
                 (lambda s_, n_, a_: (lambda e: e.copy(AW[:, a_:a_ + n_], s_[:, 0:n_])))(st, ncols, ao),
                 r=[stn, "cc"], w=["aw"])

        p.barrier()

        def emit_ln(tc, dst_fn, rowbuf, sqbuf, rown, sqn, xn32=None):
            dsts = [dst_fn(d) for d in range(NCK)]
            b_s1, b_s2, b_mb, b_rb = PS[4], PS[5], PS[6], PS[7]
            for dc in range(NCK):
                p.op("tensor",
                     (lambda d: (lambda e: e.matmul(b_s1[0:1, :], ONESC[:, :], Xs(d, tc),
                                                    start=(d == 0), stop=(d == NCK - 1))))(dc),
                     r=["onesc", xb(dc, tc)], w=(["ps4"] if dc == 0 else ["ps4"]))
            for dc in range(NCK):
                p.op("scalar",
                     (lambda d: (lambda e: e.activation(sqbuf[:, :], Xs(d, tc), AF.Square)))(dc),
                     r=[xb(dc, tc)], w=[sqn])
                p.op("tensor",
                     (lambda d: (lambda e: e.matmul(b_s2[0:1, :], ONESC[:, :], sqbuf[:, :],
                                                    start=(d == 0), stop=(d == NCK - 1))))(dc),
                     r=["onesc", sqn], w=["ps5"])
            mrow = rowbuf[0:1, 0:512]
            rrow = rowbuf[0:1, 512:1024]
            p.op("vector", lambda e, m=mrow: e.tensor_scalar_mul(m, b_s1[0:1, :], 1.0 / D),
                 r=["ps4"], w=[rown])
            p.op("vector", lambda e, rr=rrow: e.tensor_scalar_mul(rr, b_s2[0:1, :], 1.0 / D),
                 r=["ps5"], w=[rown])
            p.op("vector", lambda e, m=mrow: e.tensor_tensor(sqbuf[0:1, 0:512], m, m, ALU.mult),
                 r=[rown], w=[sqn])
            p.op("vector", lambda e, rr=rrow: e.tensor_tensor(rr, rr, sqbuf[0:1, 0:512], ALU.subtract),
                 r=[rown, sqn], w=[rown])
            p.op("scalar", lambda e, rr=rrow: e.activation(rr, rr, AF.Ln, bias=CC[0:1, 0:1]),
                 r=[rown, "cc"], w=[rown])
            p.op("scalar", lambda e, rr=rrow: e.activation(rr, rr, AF.Exp, scale=CC[0:1, 3:4]),
                 r=[rown, "cc"], w=[rown])
            p.op("tensor", lambda e, m=mrow: e.matmul(b_mb[:, :], ONES[0:1, 0:128], m, start=True, stop=True),
                 r=["ones", rown], w=["ps6"])
            p.op("tensor", lambda e, rr=rrow: e.matmul(b_rb[:, :], ONES[0:1, 0:128], rr, start=True, stop=True),
                 r=["ones", rown], w=["ps7"])
            for dc in range(NCK):
                p.op("vector",
                     (lambda d: (lambda e: e.tensor_tensor(sqbuf[:, :], Xs(d, tc), b_mb[:, :], ALU.subtract)))(dc),
                     r=[xb(dc, tc), "ps6"], w=[sqn])
                if xn32 is not None:
                    p.op("vector",
                         (lambda d: (lambda e: e.tensor_tensor(xn32[:, d * TCH:(d + 1) * TCH], sqbuf[:, :],
                                                               b_rb[:, :], ALU.mult)))(dc),
                         r=[sqn, "ps7"], w=["xn32"])
                    p.op("scalar",
                         (lambda d, dst=dsts[dc]: (lambda e: e.copy(dst, xn32[:, d * TCH:(d + 1) * TCH])))(dc),
                         r=["xn32"], w=["xn" + str(dc)])
                else:
                    p.op("vector",
                         (lambda d, dst=dsts[dc]: (lambda e: e.tensor_tensor(dst, sqbuf[:, :], b_rb[:, :], ALU.mult)))(dc),
                         r=[sqn, "ps7"], w=["xn" + str(dc)])

        if emit_upto >= 1:
            for pr in range(2):
                tcs = (2 * pr, 2 * pr + 1)
                for j, tc in enumerate(tcs):
                    emit_ln(tc, (lambda jj: (lambda d: A_XN[:, jj * 4096 + d * TCH:jj * 4096 + (d + 1) * TCH]))(j),
                            A_LNROW, A_WMX, "lnrow", "wmx")
                    for dc in range(NCK):
                        p.buf[f"axn{j}_{dc}"] = p.buf.pop("xn" + str(dc))
                for j, tc in enumerate(tcs if sub >= 2 else ()):
                    hl = HLB[:, j * 512:(j + 1) * 512]
                    hln = f"hlb{j}"
                    b_h = PS[0]
                    for k in range(NCK):
                        p.op("tensor",
                             (lambda kk, jj: (lambda e: e.matmul(
                                 b_h[:, :], AW[:, AWO["wl1"] + kk * 128:AWO["wl1"] + (kk + 1) * 128],
                                 A_XN[:, jj * 4096 + kk * TCH:jj * 4096 + (kk + 1) * TCH],
                                 start=(kk == 0), stop=(kk == NCK - 1))))(k, j),
                             r=["aw", f"axn{j}_{k}"], w=["ps0"])
                    p.op("scalar", lambda e, h=hl: e.activation(h, b_h[:, :], AF.Relu, bias=SMB[:, 0:1]),
                         r=["ps0", "smb"], w=[hln])
                    for mc in range(NCK):
                        bi = mc % 2
                        bnk = PS[bi]
                        p.op("tensor",
                             (lambda m, h=hl, b=bnk: (lambda e: e.matmul(
                                 b[:, :], AW[:, AWO["wl2"] + m * 128:AWO["wl2"] + (m + 1) * 128], h,
                                 start=True, stop=True)))(mc),
                             r=["aw", hln], w=[f"ps{bi}"])
                        p.op("scalar",
                             (lambda m, jj, b=bnk: (lambda e: e.activation(
                                 A_LIF[:, jj * 4096 + m * TCH:jj * 4096 + (m + 1) * TCH], b[:, :],
                                 AF.Tanh, bias=SMB[:, 1 + m:2 + m])))(mc, j),
                             r=[f"ps{bi}", "smb"], w=[f"alif{j}_{mc}"])
                    b_h1 = PS[2]
                    for k in range(16):
                        src = A_XN if k < 8 else A_LIF
                        srcn = (f"axn{j}_{k}" if k < 8 else f"alif{j}_{k - 8}")
                        kk = k % 8
                        p.op("tensor",
                             (lambda sr, k2, kk2, jj: (lambda e: e.matmul(
                                 b_h1[:, :], AW[:, AWO["wf1"] + k2 * 128:AWO["wf1"] + (k2 + 1) * 128],
                                 sr[:, jj * 4096 + kk2 * TCH:jj * 4096 + (kk2 + 1) * TCH],
                                 start=(k2 == 0), stop=(k2 == 15))))(src, k, kk, j),
                             r=["aw", srcn], w=["ps2"])
                    p.op("scalar", lambda e, h=hl: e.activation(h, b_h1[:, :], AF.Relu, bias=SMB[:, 9:10]),
                         r=["ps2", "smb"], w=[hln])
                    for mc in range(NCK):
                        bi = mc % 2
                        bnk = PS[bi]
                        p.op("tensor",
                             (lambda m, h=hl, b=bnk: (lambda e: e.matmul(
                                 b[:, :], AW[:, AWO["wf2"] + m * 128:AWO["wf2"] + (m + 1) * 128], h,
                                 start=True, stop=True)))(mc),
                             r=["aw", hln], w=[f"ps{bi}"])
                        p.op("vector",
                             (lambda m, jj, b=bnk: (lambda e: e.tensor_scalar_add(
                                 XT32[:, jj * 4096 + m * TCH:jj * 4096 + (m + 1) * TCH], b[:, :],
                                 SMB[:, 10 + m:11 + m])))(mc, j),
                             r=[f"ps{bi}", "smb"], w=[f"xt32_{j}_{mc}"])
                        p.op("vector",
                             (lambda m, jj: (lambda e: e.tensor_copy(
                                 A_XTB[:, jj * 4096 + m * TCH:jj * 4096 + (m + 1) * TCH],
                                 XT32[:, jj * 4096 + m * TCH:jj * 4096 + (m + 1) * TCH])))(mc, j),
                             r=[f"xt32_{j}_{mc}"], w=[f"xtb{j}_{mc}"])
                for j, tc in enumerate(tcs if sub >= 3 else ()):
                    if pr > _DBG_PAIRS - 1:
                        continue
                    for mc in range(min(NCK, _DBG_MC)):
                        bi = mc % 2
                        bnk = PS[bi]
                        w1base = AWO["w1"] if _DBG_W1OFF else AWO["wl1"]
                        for k in range(NCK):
                            off = (w1base + kk2 * 1024 + mc * 128 if _DBG_W1OFF else AWO["wl1"]) if False else None
                            p.op("tensor",
                                 (lambda kk, m, jj, b=bnk: (lambda e: e.matmul(
                                     b[:, :],
                                     (AW[:, AWO["w1"] + kk * 1024 + m * 128:AWO["w1"] + kk * 1024 + (m + 1) * 128]
                                      if _DBG_W1OFF else
                                      AW[:, AWO["wl1"] + kk * 128:AWO["wl1"] + (kk + 1) * 128]),
                                     A_XTB[:, jj * 4096 + kk * TCH:jj * 4096 + (kk + 1) * TCH],
                                     start=(kk == 0), stop=(kk == NCK - 1))))(k, mc, j),
                                 r=["aw", f"xtb{j}_{k}"], w=[f"ps{bi}"])
                        bcol = SMB[:, 18 + mc:19 + mc]
                        if _DBG_DOPS >= 1:
                            p.op("vector",
                                 (lambda b=bnk, bc=bcol: (lambda e: e.tensor_scalar(
                                     A_WMX[:, :], b[:, :], bc, 0.0, ALU.add, ALU.max)))(),
                                 r=[f"ps{bi}", "smb"], w=["wmx"])
                        else:
                            p.op("vector", (lambda b=bnk: (lambda e: e.tensor_copy(A_WMX[:, :], b[:, :])))(),
                                 r=[f"ps{bi}"], w=["wmx"])
                        if _DBG_DOPS >= 2:
                            p.op("scalar",
                                 (lambda b=bnk, bc=bcol: (lambda e: e.activation(
                                     A_SCR[:, 0:512], b[:, :], AF.Abs, bias=bc)))(),
                                 r=[f"ps{bi}", "smb"], w=["scrP"])
                        else:
                            p.op("scalar", (lambda b=bnk: (lambda e: e.copy(A_SCR[:, 0:512], b[:, :])))(),
                                 r=[f"ps{bi}"], w=["scrP"])
                        if _DBG_DOPS >= 3:
                            p.op("scalar", lambda e: e.activation(A_SCR[:, 512:1024], A_SCR[:, 0:512],
                                                                  AF.Exp, scale=CC[:, 2:3]),
                                 r=["scrP", "cc"], w=["scrT"])
                        else:
                            p.op("scalar", lambda e: e.copy(A_SCR[:, 512:1024], A_SCR[:, 0:512]),
                                 r=["scrP"], w=["scrT"])
                        if _DBG_DOPS >= 4:
                            p.op("scalar", lambda e: e.activation(A_SCR[:, 1024:1536], A_SCR[:, 512:1024],
                                                                  AF.Ln, bias=CC[:, 1:2]),
                                 r=["scrT", "cc"], w=["scrF"])
                        else:
                            p.op("scalar", lambda e: e.copy(A_SCR[:, 1024:1536], A_SCR[:, 512:1024]),
                                 r=["scrT"], w=["scrF"])
                        if _DBG_NODEL:
                            p.op("vector",
                                 lambda e: e.tensor_tensor(A_WMX[:, :], A_WMX[:, :], A_SCR[:, 1024:1536],
                                                           ALU.add),
                                 r=["wmx", "scrF"], w=["wmx"])
                            p.buf[f"del{j}_{mc}"] = {"w": None, "r": []}
                        else:
                            p.op("vector",
                                 (lambda m, jj: (lambda e: e.tensor_tensor(
                                     A_DEL[:, jj * 4096 + m * TCH:jj * 4096 + (m + 1) * TCH],
                                     A_WMX[:, :], A_SCR[:, 1024:1536], ALU.add)))(mc, j),
                                 r=["wmx", "scrF"] + [f"axn{j}_{mc}", f"alif{j}_{mc}"],
                                 w=[f"del{j}_{mc}"])
                for j, tc in enumerate(tcs if sub >= 4 else ()):
                    b_bc = PS[2]
                    for wname, dst, dstn, bc in (("w2", A_BM[:, :], "abm", SM_B23[:, 0:1]),
                                                 ("w3", A_CM[:, j * 512:(j + 1) * 512], f"acm{j}", SM_B23[:, 1:2])):
                        for k in range(NCK):
                            p.op("tensor",
                                 (lambda kk, wn, jj: (lambda e: e.matmul(
                                     b_bc[0:16, :], AW[:, AWO[wn] + kk * 16:AWO[wn] + (kk + 1) * 16],
                                     A_XTB[:, jj * 4096 + kk * TCH:jj * 4096 + (kk + 1) * TCH],
                                     start=(kk == 0), stop=(kk == NCK - 1))))(k, wname, j),
                                 r=["aw", f"xtb{j}_{k}"], w=["ps2"])
                        p.op("vector",
                             (lambda d2=dst, b2=bc: (lambda e: e.tensor_scalar_add(d2, b_bc[0:16, :], b2)))(),
                             r=["ps2", "smb23"], w=[dstn])
                    p.op("vector",
                         (lambda jj: (lambda e: e.tensor_copy(A_CMBF[:, jj * 512:(jj + 1) * 512],
                                                              A_CM[:, jj * 512:(jj + 1) * 512])))(j),
                         r=[f"acm{j}"], w=[f"acmbf{j}"])
                    p.op("vector",
                         (lambda jj: (lambda e: e.tensor_tensor(A_BM[:, :], A_BM[:, :],
                                                                A_CM[:, jj * 512:(jj + 1) * 512], ALU.mult)))(j),
                         r=["abm", f"acm{j}"], w=["abm"])
                    p.op("tensor", lambda e: e.matmul(PS[3][0:1, :], ONESC[0:16, :], A_BM[:, :],
                                                      start=True, stop=True),
                         r=["onesc", "abm"], w=["ps3"])
                    p.op("scalar", lambda e: e.copy(A_CMROW[0:1, 0:512], PS[3][0:1, :]),
                         r=["ps3"], w=["cmrow0"])
                    p.op("tensor", lambda e: e.matmul(PS[3][:, :], ONES[0:1, 0:128], A_CMROW[0:1, 0:512],
                                                      start=True, stop=True),
                         r=["ones", "cmrow0"], w=["ps3"])
                    for dc in range(NCK):
                        p.op("vector",
                             (lambda d, jj: (lambda e: e.tensor_tensor(
                                 A_WMX[:, :], XT32[:, jj * 4096 + d * TCH:jj * 4096 + (d + 1) * TCH],
                                 A_DEL[:, jj * 4096 + d * TCH:jj * 4096 + (d + 1) * TCH], ALU.mult)))(dc, j),
                             r=[f"xt32_{j}_{dc}", f"del{j}_{dc}"], w=["wmx"])
                        p.op("vector", lambda e: e.tensor_tensor(A_WMX[:, :], A_WMX[:, :], PS[3][:, :], ALU.mult),
                             r=["wmx", "ps3"], w=["wmx"])
                        p.op("vector",
                             (lambda d, t: (lambda e: e.tensor_tensor(Xs(d, t), Xs(d, t), A_WMX[:, :],
                                                                      ALU.add)))(dc, tc),
                             r=["wmx", xb(dc, tc)], w=[xb(dc, tc)])
                for ni in range(N16 if sub >= 5 else 0):
                    for j, tc in enumerate(tcs):
                        ridx = (ni * 2 + j) % 2
                        bi = 2 + ridx
                        bnk = PS[bi]
                        p.op("tensor",
                             (lambda nn, jj, b=bnk: (lambda e: e.matmul(
                                 b[:, :], SEL16B[:, nn * 128:(nn + 1) * 128],
                                 A_CMBF[:, jj * 512:(jj + 1) * 512], start=True, stop=True)))(ni, j),
                             r=["sel16", f"acmbf{j}"], w=[f"ps{bi}"])
                        for dc in range(NCK):
                            ca = dc * 16 + ni
                            p.op("scalar",
                                 (lambda d, jj, c: (lambda e: e.activation(
                                     A_SCR[:, 0:512], A_DEL[:, jj * 4096 + d * TCH:jj * 4096 + (d + 1) * TCH],
                                     AF.Exp, scale=SM_AM[:, c:c + 1])))(dc, j, ca),
                                 r=[f"del{j}_{dc}", "smam"], w=["scrP"])
                            p.op("vector",
                                 (lambda d, jj, c: (lambda e: e.scalar_tensor_tensor(
                                     A_SCR[:, 512:1024], A_DEL[:, jj * 4096 + d * TCH:jj * 4096 + (d + 1) * TCH],
                                     SM_NAP[:, c:c + 1], A_SCR[:, 0:512], ALU.mult, ALU.subtract)))(dc, j, ca),
                                 r=[f"del{j}_{dc}", "smnap", "scrP"], w=["scrT"])
                            p.op("scalar", lambda e: e.activation(A_SCR[:, 1024:1536], A_SCR[:, 512:1024],
                                                                  AF.Exp, bias=CC[:, 1:2]),
                                 r=["scrT", "cc"], w=["scrF"])
                            p.op("vector",
                                 (lambda c, b=bnk: (lambda e: e.scalar_tensor_tensor(
                                     A_WMX[:, :], A_SCR[:, 1024:1536], SM_ST[:, c:c + 1], b[:, :],
                                     ALU.mult, ALU.mult)))(ca),
                                 r=["scrF", "smst", f"ps{bi}"], w=["wmx"])
                            p.op("vector",
                                 (lambda d, t: (lambda e: e.tensor_tensor(Xs(d, t), Xs(d, t), A_WMX[:, :],
                                                                          ALU.add)))(dc, tc),
                                 r=["wmx", xb(dc, tc)], w=[xb(dc, tc)])
                p.barrier()


        def emit_ffn(call_idx):
            ci = str(call_idx)
            FLNROW = A_CM
            flnrow = nc.alloc_sbuf_tensor_at(f"FLNROW{ci}", [1, 1024], F32, offset=O_SCR + 12288)
            sq = FFT2[:, 3584:4096]
            p.op("vector", lambda e: e.memset(XN[:, :], 0.0), w=["xnall"])
            for tc in range(NT):
                emit_ln(tc, (lambda d: XNs(d, tc)), flnrow, sq, "flnrow", "fsq", xn32=FFT)
                for dc in range(NCK):
                    p.buf[f"fxn{tc}_{dc}"] = p.buf.pop("xn" + str(dc))
                for k in range(NCK):
                    p.op("tensor",
                         (lambda kk, t: (lambda e: e.matmul(
                             PS[2][0:8, :], SMB[:, 282 + kk * 8:282 + (kk + 1) * 8],
                             FFT[:, kk * TCH:(kk + 1) * TCH], start=(kk == 0), stop=(kk == NCK - 1))))(k, tc),
                         r=["smb", "xn32"], w=["ps2"])
                p.op("scalar",
                     (lambda t: (lambda e: e.copy(SFM[:, t * TCH:(t + 1) * TCH], PS[2][0:8, :])))(tc),
                     r=["ps2"], w=["glog"])
            for i in range(16):
                p.op("tensor",
                     (lambda ii: (lambda e: e.transpose(PS[3][:, ii * 8:(ii + 1) * 8],
                                                        SFM[0:8, ii * 128:(ii + 1) * 128],
                                                        IDENT[0:8, 0:8])))(i),
                     r=["glog", "ident"], w=["ps3"])
            p.op("scalar", lambda e: e.copy(GTM[:, 0:128], PS[3][:, 0:128]), r=["ps3"], w=["gt_l"])
            p.op("scalar", lambda e: e.activation(GTM[:, 128:256], GTM[:, 0:128], AF.Exp),
                 r=["gt_l"], w=["gt_e"])
            p.op("vector", lambda e: e.tensor_reduce(GSML[:, 0:16],
                                                     GTM[:, 128:256].rearrange("p (a b) -> p a b", b=8),
                                                     AX.X, ALU.add),
                 r=["gt_e"], w=["g_sum"])
            p.op("vector", lambda e: e.reciprocal(GSML[:, 16:32], GSML[:, 0:16]), r=["g_sum"], w=["g_rcp"])
            for i in range(16):
                p.op("vector",
                     (lambda ii: (lambda e: e.tensor_scalar_mul(GTM[:, 256 + ii * 8:256 + (ii + 1) * 8],
                                                                GTM[:, 128 + ii * 8:128 + (ii + 1) * 8],
                                                                GSML[:, 16 + ii:17 + ii])))(i),
                     r=["gt_e", "g_rcp"], w=["gt_s"])
            p.op("vector", lambda e: e.tensor_reduce(GSML[:, 32:48],
                                                     GTM[:, 256:384].rearrange("p (a b) -> p a b", b=8),
                                                     AX.X, ALU.max),
                 r=["gt_s"], w=["g_m1"])
            for i in range(16):
                p.op("vector",
                     (lambda ii: (lambda e: e.tensor_scalar(GTM[:, 384 + ii * 8:384 + (ii + 1) * 8],
                                                            GTM[:, 256 + ii * 8:256 + (ii + 1) * 8],
                                                            GSML[:, 32 + ii:33 + ii], None, ALU.is_ge)))(i),
                     r=["gt_s", "g_m1"], w=["gt_ge"])
            p.op("vector", lambda e: e.scalar_tensor_tensor(GTM[:, 384:512], GTM[:, 384:512], -1e30,
                                                            GTM[:, 256:384], ALU.mult, ALU.add),
                 r=["gt_ge", "gt_s"], w=["gt_mk"])
            p.op("vector", lambda e: e.tensor_reduce(GSML[:, 48:64],
                                                     GTM[:, 384:512].rearrange("p (a b) -> p a b", b=8),
                                                     AX.X, ALU.max),
                 r=["gt_mk"], w=["g_m2"])
            for i in range(16):
                p.op("vector",
                     (lambda ii: (lambda e: e.tensor_scalar(GTM[:, 640 + ii * 8:640 + (ii + 1) * 8],
                                                            GTM[:, 256 + ii * 8:256 + (ii + 1) * 8],
                                                            GSML[:, 48 + ii:49 + ii], None, ALU.is_ge)))(i),
                     r=["gt_s", "g_m2"], w=["gt_mask"])
            for tc in range(NT):
                for w4 in range(4):
                    i = tc * 4 + w4
                    p.op("tensor",
                         (lambda ii, ww: (lambda e: e.transpose(PS[2][0:8, ww * 128:(ww + 1) * 128],
                                                                GTM[:, 256 + ii * 8:256 + (ii + 1) * 8],
                                                                IDENT[:, 0:128])))(i, w4),
                         r=["gt_s", "ident"], w=["ps2"])
                p.op("scalar",
                     (lambda t: (lambda e: e.copy(SFM[:, t * TCH:(t + 1) * TCH], PS[2][0:8, :])))(tc),
                     r=["ps2"], w=["sfm"])
                for w4 in range(4):
                    i = tc * 4 + w4
                    p.op("tensor",
                         (lambda ii, ww: (lambda e: e.transpose(PS[3][0:8, ww * 128:(ww + 1) * 128],
                                                                GTM[:, 640 + ii * 8:640 + (ii + 1) * 8],
                                                                IDENT[:, 0:128])))(i, w4),
                         r=["gt_mask", "ident"], w=["ps3"])
                p.op("scalar",
                     (lambda t: (lambda e: e.copy(MFMB[:, t * TCH:(t + 1) * TCH], PS[3][0:8, :])))(tc),
                     r=["ps3"], w=["mfm"])
            p.barrier()

            stcnt = [0]

            def stage(drt, row0, colspec, slot, sloto, slotname):
                i = stcnt[0]
                stcnt[0] += 1
                st = WST[i % 2]
                stn = f"fwst{i % 2}"
                dma("sync", st[:, 0:1024], drt[row0:row0 + 128, colspec[0]:colspec[1]]
                    if colspec else drt[row0:row0 + 128, :], w=[stn])
                p.op("scalar",
                     (lambda s_, so: (lambda e: e.copy(FW[slot][:, so:so + 1024], s_[:, 0:1024])))(st, sloto),
                     r=[stn, "cc"], w=[slotname])

            for h in range(2):
                for kc in range(NCK):
                    stage(dr["fSw1T"], kc * 128, (h * 1024, (h + 1) * 1024), 0, kc * 1024, "fw0")
                for kc in range(NCK):
                    stage(dr["fSw2T"], kc * 128, (h * 1024, (h + 1) * 1024), 1, kc * 1024, "fw1")
                for icl in range(8):
                    stage(dr["fSw3T"], (h * 8 + icl) * 128, None, 2, icl * 1024, "fw2")
                for tc in range(NT):
                    for icl in range(8):
                        bi = icl % 2
                        for kc in range(NCK):
                            p.op("tensor",
                                 (lambda k2, il, t, b=PS[icl % 2]: (lambda e: e.matmul(
                                     b[:, :], FW[0][:, k2 * 1024 + il * 128:k2 * 1024 + (il + 1) * 128],
                                     XNs(k2, t), start=(k2 == 0), stop=(k2 == NCK - 1))))(kc, icl, tc),
                                 r=["fw0", f"fxn{tc}_{kc}"], w=[f"ps{icl % 2}"])
                        for kc in range(NCK):
                            p.op("tensor",
                                 (lambda k2, il, t, b=PS[2 + icl % 2]: (lambda e: e.matmul(
                                     b[:, :], FW[1][:, k2 * 1024 + il * 128:k2 * 1024 + (il + 1) * 128],
                                     XNs(k2, t), start=(k2 == 0), stop=(k2 == NCK - 1))))(kc, icl, tc),
                                 r=["fw1", f"fxn{tc}_{kc}"], w=[f"ps{2 + icl % 2}"])
                        abo = 0 if icl % 2 == 0 else 1024
                        p.op("scalar",
                             (lambda o=abo, b=PS[icl % 2]: (lambda e: e.activation(FFT[:, o:o + 512], b[:, :],
                                                                                   AF.Silu)))(),
                             r=[f"ps{icl % 2}"], w=[f"fab{icl % 2}"])
                        p.op("vector",
                             (lambda il, o=abo, b=PS[2 + icl % 2]: (lambda e: e.tensor_tensor(
                                 PRB[:, il * 512:(il + 1) * 512], FFT[:, o:o + 512], b[:, :], ALU.mult)))(icl),
                             r=[f"fab{icl % 2}", f"ps{2 + icl % 2}"], w=[f"prb{icl}"])
                    for dc in range(NCK):
                        bi = 4 + dc % 2
                        for icl in range(8):
                            p.op("tensor",
                                 (lambda il, d2, b=PS[4 + dc % 2]: (lambda e: e.matmul(
                                     b[:, :], FW[2][:, il * 1024 + d2 * 128:il * 1024 + (d2 + 1) * 128],
                                     PRB[:, il * 512:(il + 1) * 512],
                                     start=(il == 0), stop=(il == 7))))(icl, dc),
                                 r=["fw2"] + [f"prb{il2}" for il2 in range(8)], w=[f"ps{4 + dc % 2}"])
                        p.op("vector",
                             (lambda d2, t=tc, b=PS[4 + dc % 2]: (lambda e: e.tensor_tensor(
                                 Xs(d2, t), Xs(d2, t), b[:, :], ALU.add)))(dc),
                             r=[f"ps{4 + dc % 2}", xb(dc, tc)], w=[xb(dc, tc)])

            def stage_expert_matrix(e_i, mat, slot, slotname):
                drt = {0: dr["fW1T"], 1: dr["fW3T"], 2: dr["fW2T"]}[mat]
                for kc in range(NCK):
                    i = stcnt[0]
                    stcnt[0] += 1
                    st = WST[i % 2]
                    stn = f"fwst{i % 2}"
                    dma("sync", st[:, 0:1024], drt[e_i, kc * 128:(kc + 1) * 128, :], w=[stn])
                    p.op("scalar",
                         (lambda s_, so: (lambda e: e.copy(FW[slot][:, so:so + 1024], s_[:, 0:1024])))(st, kc * 1024),
                         r=[stn, "cc"], w=[slotname])

            dma("sync", FSEL8F[:, :], dr["c_sel8"][:, :], w=["fsel8"])
            dma("sync", FSEL16[:, :], dr["c_sel16"][0:8, 0:1024], w=["fsel16m"])
            stage_expert_matrix(0, 0, 0, "fw0")
            stage_expert_matrix(0, 1, 1, "fw1")
            stage_expert_matrix(0, 2, 2, "fw2")
            for e_i in range(E8):
                for tc in range(NT):
                    p.op("tensor",
                         (lambda ee, t: (lambda e: e.matmul(PS[2][:, :], FSEL8F[:, ee * 128:(ee + 1) * 128],
                                                            SFM[0:8, t * TCH:(t + 1) * TCH],
                                                            start=True, stop=True)))(e_i, tc),
                         r=["fsel8", "sfm"], w=["ps2"])
                    p.op("scalar", (lambda: (lambda e: e.copy(FFT[:, 1536:2048], PS[2][:, :])))(),
                         r=["ps2"], w=["fsbb"])
                    for hc in range(NCK):
                        hb = (0, 1, 3)[hc % 3]
                        hpo = 0 if hc % 2 == 0 else 1024
                        for kc in range(NCK):
                            p.op("tensor",
                                 (lambda k2, h2, t, b=PS[(0, 1, 3)[hc % 3]]: (lambda e: e.matmul(
                                     b[:, :], FW[0][:, k2 * 1024 + h2 * 128:k2 * 1024 + (h2 + 1) * 128],
                                     XNs(k2, t), start=(k2 == 0), stop=(k2 == NCK - 1))))(kc, hc, tc),
                                 r=["fw0", f"fxn{tc}_{kc}"], w=[f"ps{hb}"])
                        p.op("vector",
                             (lambda o=hpo, b=PS[hb]: (lambda e: e.tensor_tensor(FFT[:, o:o + 512], b[:, :],
                                                                                 FFT[:, 1536:2048], ALU.mult)))(),
                             r=[f"ps{hb}", "fsbb"], w=[f"fhp{hc % 2}"])
                        p.op("scalar",
                             (lambda ee, h2, o=hpo: (lambda e: e.activation(HB[:, h2 * 512:(h2 + 1) * 512],
                                                                            FFT[:, o:o + 512], AF.Silu,
                                                                            bias=SMB[:, 90 + ee * 8 + h2:91 + ee * 8 + h2])))(e_i, hc),
                             r=[f"fhp{hc % 2}", "smb"], w=[f"fh{hc}"])
                    p.op("tensor",
                         (lambda ee, t: (lambda e: e.matmul(PS[2][:, :], FSEL16[:, ee * 128:(ee + 1) * 128],
                                                            MFMB[0:8, t * TCH:(t + 1) * TCH],
                                                            start=True, stop=True)))(e_i, tc),
                         r=["fsel16m", "mfm"], w=["ps2"])
                    p.op("scalar", (lambda: (lambda e: e.copy(FFT[:, 3072:3584], PS[2][:, :])))(),
                         r=["ps2"], w=["fmbb"])
                    for dc in range(NCK):
                        for kc in range(NCK):
                            p.op("tensor",
                                 (lambda k2, d2, t, b=PS[4 + dc % 2]: (lambda e: e.matmul(
                                     b[:, :], FW[1][:, k2 * 1024 + d2 * 128:k2 * 1024 + (d2 + 1) * 128],
                                     XNs(k2, t), start=(k2 == 0), stop=(k2 == NCK - 1))))(kc, dc, tc),
                                 r=["fw1", f"fxn{tc}_{kc}"], w=[f"ps{4 + dc % 2}"])
                        p.op("vector",
                             (lambda b=PS[4 + dc % 2]: (lambda e: e.tensor_tensor(
                                 FFT[:, 512:1024], b[:, :], FFT[:, 1536:2048], ALU.mult)))(),
                             r=[f"ps{4 + dc % 2}", "fsbb"], w=["ft3"])
                        p.op("vector",
                             (lambda ee, d2: (lambda e: e.tensor_scalar_add(
                                 FFT2[:, d2 * 512:(d2 + 1) * 512], FFT[:, 512:1024],
                                 SMB[:, 218 + ee * 8 + d2:219 + ee * 8 + d2])))(e_i, dc),
                             r=["ft3", "smb"], w=[f"fx3_{dc}"])
                    for dc in range(NCK):
                        for hc in range(NCK):
                            p.op("tensor",
                                 (lambda h2, d2, b=PS[6 + dc % 2]: (lambda e: e.matmul(
                                     b[:, :], FW[2][:, h2 * 1024 + d2 * 128:h2 * 1024 + (d2 + 1) * 128],
                                     HB[:, h2 * 512:(h2 + 1) * 512],
                                     start=(h2 == 0), stop=(h2 == NCK - 1))))(hc, dc),
                                 r=["fw2"] + [f"fh{h3}" for h3 in range(NCK)], w=[f"ps{6 + dc % 2}"])
                        p.op("vector",
                             (lambda ee, d2, b=PS[6 + dc % 2]: (lambda e: e.scalar_tensor_tensor(
                                 FFT[:, 2048:2560], b[:, :],
                                 SMB[:, 154 + ee * 8 + d2:155 + ee * 8 + d2],
                                 FFT2[:, d2 * 512:(d2 + 1) * 512], ALU.add, ALU.mult)))(e_i, dc),
                             r=[f"ps{6 + dc % 2}", "smb", f"fx3_{dc}"], w=["fc0"])
                        cmslot = 2560 if dc % 2 == 0 else 3584
                        p.op("vector",
                             (lambda cs=cmslot: (lambda e: e.tensor_tensor(FFT[:, cs:cs + 512],
                                                                           FFT[:, 2048:2560],
                                                                           FFT[:, 3072:3584], ALU.mult)))(),
                             r=["fc0", "fmbb"], w=[f"fcm{dc % 2}"])
                        p.op("gpsimd",
                             (lambda d2, t=tc, cs=cmslot: (lambda e: e.tensor_tensor(
                                 Xs(d2, t), Xs(d2, t), FFT[:, cs:cs + 512], ALU.add)))(dc),
                             r=[f"fcm{dc % 2}", xb(dc, tc)], w=[xb(dc, tc)])
                    if tc == 0 and e_i + 1 < E8:
                        pass
                if e_i + 1 < E8:
                    stage_expert_matrix(e_i + 1, 0, 0, "fw0")
                    stage_expert_matrix(e_i + 1, 1, 1, "fw1")
                    stage_expert_matrix(e_i + 1, 2, 2, "fw2")
            p.barrier()

        if emit_upto >= 2:
            emit_ffn(1)

        def emit_attnb():
            blnrow = nc.alloc_sbuf_tensor_at("BLNROW", [1, 1024], F32, offset=O_SCR)
            bsq = nc.alloc_sbuf_tensor_at("BSQ", [128, 512], F32, offset=O_SCR + 4096)
            for tc in range(NT):
                emit_ln(tc, (lambda d: XNs(d, tc)), blnrow, bsq, "blnrow", "bsq")
                for dc in range(NCK):
                    p.buf[f"bxn{tc}_{dc}"] = p.buf.pop("xn" + str(dc))
            IPJW = [nc.alloc_sbuf_tensor_at(f"IPJW{i}", [128, 1024], BF16, offset=O_WST + 4096 + 2048 * i)
                    for i in range(2)]
            def qkv_dst(oc, tc):
                if oc < 8:
                    return QBUF[:, oc * 2048 + tc * TCH: oc * 2048 + (tc + 1) * TCH], f"q{oc}"
                if oc < 16:
                    cc = oc - 8
                    return KBUF[:, cc * 2048 + tc * TCH: cc * 2048 + (tc + 1) * TCH], f"k{cc}"
                cc = oc - 16
                if cc < 4:
                    return V1B[:, cc * 2048 + tc * TCH: cc * 2048 + (tc + 1) * TCH], f"v{cc}"
                return V2B[:, (cc - 4) * 2048 + tc * TCH: (cc - 4) * 2048 + (tc + 1) * TCH], f"v{cc}"

            for oc in range(24):
                ipw = IPJW[oc % 2]
                ipwn = f"ipjw{oc % 2}"
                src_ap = bass.AP(dr["bIpjT"], oc * 128, [[3072, 128], [128 * 3072, 8], [1, 128]])
                dma("sync", WST[0][:, :].rearrange("p (a b) -> p a b", b=128), src_ap, w=["bwst0"])
                p.op("scalar", (lambda ip=ipw: (lambda e: e.copy(ip[:, :], WST[0][:, :])))(),
                     r=["bwst0", "cc"], w=[ipwn])
                for kc in range(NCK):
                    for tc in range(NT):
                        p.op("tensor",
                             (lambda k2, t, ip=ipw, b=PS[4 + tc]: (lambda e: e.matmul(
                                 b[:, :], ip[:, k2 * 128:(k2 + 1) * 128], XNs(k2, t),
                                 start=(k2 == 0), stop=(k2 == NCK - 1))))(kc, tc),
                             r=[ipwn, f"bxn{tc}_{kc}"], w=[f"ps{4 + tc}"])
                for tc in range(NT):
                    dst, dstn = qkv_dst(oc, tc)
                    p.op("vector",
                         (lambda o2=oc, d2=dst, t=tc, b=PS[4 + tc]: (lambda e: e.tensor_scalar(
                             d2, b[:, :], SMB[:, 370 + o2:371 + o2], SMB[:, 346 + o2:347 + o2],
                             ALU.mult, ALU.add)))(),
                         r=[f"ps{4 + tc}", "smb"], w=[dstn])
            p.barrier()
            dma("sync", BF16C[:, :], dr["c_F16"][:, :], w=["bf16c"])

            def vslice(cc, tc):
                if cc < 4:
                    return V1B[:, cc * 2048 + tc * TCH: cc * 2048 + (tc + 1) * TCH]
                return V2B[:, (cc - 4) * 2048 + tc * TCH: (cc - 4) * 2048 + (tc + 1) * TCH]

            for s in range(4):
                if s == 0:
                    for cc in range(NCK):
                        p.op("vector",
                             (lambda c2: (lambda e: e.tensor_scalar_mul(
                                 ATTO[:, c2 * 2048:c2 * 2048 + TCH], vslice(c2, 0),
                                 SMB[:, 74 + c2:75 + c2])))(cc),
                             r=[f"v{cc}", "smb"], w=[f"atto{cc}_0"])
                    continue
                for sp in range(s + 1):
                    for cc in range(NCK):
                        p.op("vector",
                             (lambda c2, s2=s, s3=sp: (lambda e: e.tensor_tensor(
                                 BZB1[:, :], QBUF[:, c2 * 2048 + s2 * TCH:c2 * 2048 + (s2 + 1) * TCH],
                                 KBUF[:, c2 * 2048 + s3 * TCH:c2 * 2048 + (s3 + 1) * TCH], ALU.mult)))(cc),
                             r=[f"q{cc}", f"k{cc}"], w=["bzb"])
                        p.op("tensor",
                             (lambda c2, b=PS[sp]: (lambda e: e.matmul(
                                 b[0:16, :], BE16F[:, c2 * 16:(c2 + 1) * 16], BZB1[:, :],
                                 start=(c2 == 0), stop=(c2 == NCK - 1))))(cc),
                             r=["be16f", "bzb"], w=[f"ps{sp}"])
                    p.op("scalar",
                         (lambda s3=sp, b=PS[sp]: (lambda e: e.copy(
                             BSCB[:, s3 * TCH:(s3 + 1) * TCH], b[0:16, :])))(),
                         r=[f"ps{sp}"], w=[f"scb{sp}"])
                p.op("vector", (lambda s2=s: (lambda e: e.tensor_tensor(
                    BMX[:, :], BSCB[:, 0:TCH], BSCB[:, TCH:2 * TCH], ALU.max)))(),
                    r=["scb0", "scb1"], w=["bmx"])
                for sp in range(2, s + 1):
                    p.op("vector", (lambda s3=sp: (lambda e: e.tensor_tensor(
                        BMX[:, :], BMX[:, :], BSCB[:, s3 * TCH:(s3 + 1) * TCH], ALU.max)))(),
                        r=["bmx", f"scb{sp}"], w=["bmx"])
                for sp in range(s + 1):
                    p.op("vector", (lambda s3=sp: (lambda e: e.tensor_tensor(
                        BSCB[:, s3 * TCH:(s3 + 1) * TCH], BSCB[:, s3 * TCH:(s3 + 1) * TCH],
                        BMX[:, :], ALU.subtract)))(),
                        r=[f"scb{sp}", "bmx"], w=[f"scb{sp}"])
                    p.op("scalar", (lambda s3=sp: (lambda e: e.activation(
                        BSCB[:, s3 * TCH:(s3 + 1) * TCH], BSCB[:, s3 * TCH:(s3 + 1) * TCH], AF.Exp)))(),
                        r=[f"scb{sp}"], w=[f"scb{sp}"])
                p.op("vector", lambda e: e.tensor_tensor(BSUM[:, :], BSCB[:, 0:TCH],
                                                         BSCB[:, TCH:2 * TCH], ALU.add),
                     r=["scb0", "scb1"], w=["bsum"])
                for sp in range(2, s + 1):
                    p.op("vector", (lambda s3=sp: (lambda e: e.tensor_tensor(
                        BSUM[:, :], BSUM[:, :], BSCB[:, s3 * TCH:(s3 + 1) * TCH], ALU.add)))(),
                        r=["bsum", f"scb{sp}"], w=["bsum"])
                p.op("vector", lambda e: e.reciprocal(BMX[:, :], BSUM[:, :]), r=["bsum"], w=["brc"])
                for sp in range(s + 1):
                    p.op("vector", (lambda s3=sp: (lambda e: e.tensor_tensor(
                        BSCB[:, s3 * TCH:(s3 + 1) * TCH], BSCB[:, s3 * TCH:(s3 + 1) * TCH],
                        BMX[:, :], ALU.mult)))(),
                        r=[f"scb{sp}", "brc"], w=[f"scb{sp}"])
                    p.op("scalar", (lambda s3=sp: (lambda e: e.copy(
                        BWJ4[0:16, s3 * TCH:(s3 + 1) * TCH], BSCB[:, s3 * TCH:(s3 + 1) * TCH])))(),
                        r=[f"scb{sp}"], w=[f"bwj{sp}"])
                for cc in range(NCK):
                    for sp in range(s + 1):
                        p.op("tensor",
                             (lambda c2, s3=sp, b=PS[4 + sp % 2]: (lambda e: e.matmul(
                                 b[:, :], BF16C[:, c2 * 128:(c2 + 1) * 128],
                                 BWJ4[0:16, s3 * TCH:(s3 + 1) * TCH], start=True, stop=True)))(cc),
                             r=["bf16c", f"bwj{sp}"], w=[f"ps{4 + sp % 2}"])
                        if sp == 0:
                            p.op("vector",
                                 (lambda c2, s3=sp, b=PS[4 + sp % 2]: (lambda e: e.tensor_tensor(
                                     BOAC[:, :], b[:, :], vslice(c2, s3), ALU.mult)))(cc),
                                 r=[f"ps{4 + sp % 2}", f"v{cc}"], w=["boac"])
                        else:
                            p.op("vector",
                                 (lambda c2, s3=sp, b=PS[4 + sp % 2]: (lambda e: e.tensor_tensor(
                                     WST[0][:, 512:1024], b[:, :], vslice(c2, s3), ALU.mult)))(cc),
                                 r=[f"ps{4 + sp % 2}", f"v{cc}"], w=["bwst0h"])
                            p.op("vector", lambda e: e.tensor_tensor(BOAC[:, :], BOAC[:, :],
                                                                     WST[0][:, 512:1024], ALU.add),
                                 r=["boac", "bwst0h"], w=["boac"])
                    p.op("vector",
                         (lambda c2, s2=s: (lambda e: e.tensor_scalar_mul(
                             ATTO[:, c2 * 2048 + s2 * TCH:c2 * 2048 + (s2 + 1) * TCH], BOAC[:, :],
                             SMB[:, 74 + c2:75 + c2])))(cc),
                         r=["boac", "smb"], w=[f"atto{cc}_{s}"])
            p.barrier()
            for oc in range(NCK):
                ipw = IPJW[oc % 2]
                ipwn = f"opjw{oc % 2}"
                src_ap = bass.AP(dr["bOpjT"], oc * 128, [[1024, 128], [128 * 1024, 8], [1, 128]])
                dma("sync", WST[0][:, :].rearrange("p (a b) -> p a b", b=128), src_ap, w=["bwst0"])
                p.op("scalar", (lambda ip=ipw: (lambda e: e.copy(ip[:, :], WST[0][:, :])))(),
                     r=["bwst0", "cc"], w=[ipwn])
                for tc in range(NT):
                    for kc in range(NCK):
                        p.op("tensor",
                             (lambda k2, t, ip=ipw, b=PS[6 + oc % 2]: (lambda e: e.matmul(
                                 b[:, :], ip[:, k2 * 128:(k2 + 1) * 128],
                                 ATTO[:, k2 * 2048 + t * TCH:k2 * 2048 + (t + 1) * TCH],
                                 start=(k2 == 0), stop=(k2 == NCK - 1))))(kc, tc),
                             r=[ipwn, f"atto{kc}_{tc}"], w=[f"ps{6 + oc % 2}"])
                    p.op("vector",
                         (lambda o2=oc, b=PS[6 + oc % 2]: (lambda e: e.tensor_scalar(
                             BOAC[:, :], b[:, :], SMB[:, 82 + o2:83 + o2], SMB[:, 402 + o2:403 + o2],
                             ALU.mult, ALU.add)))(),
                         r=[f"ps{6 + oc % 2}", "smb"], w=["boac"])
                    p.op("vector",
                         (lambda o2=oc, t=tc: (lambda e: e.tensor_tensor(Xs(o2, t), Xs(o2, t),
                                                                         BOAC[:, :], ALU.add)))(),
                         r=["boac", xb(oc, tc)], w=[xb(oc, tc)])
            p.barrier()

        if emit_upto >= 3:
            emit_attnb()

        if emit_upto >= 4:
            emit_ffn(2)

        for dc in range(NCK):
            dma("gpsimd", out_fm[dc * 128:(dc + 1) * 128, :], X[:, dc * 2048:(dc + 1) * 2048],
                w=[f"storedma{dc}"], r=[xb(dc, t) for t in range(NT)], queue_sem="qP")
        p.stream["gpsimd"].append(((("qP", p.count["qP"]),), None, None, 0))

        def make_body(name):
            def body(eng):
                for (waits, fn, incsem, incn) in p.stream[name]:
                    for (s, v) in waits:
                        if v > 0:
                            eng.wait_ge(sems[s], v)
                    inst = fn(eng) if fn is not None else None
                    if incsem is not None:
                        if inst is None:
                            inst = eng.nop()
                        inst.then_inc(sems[incsem], incn)
            return body

        block.sync(make_body("sync"))
        block.tensor(make_body("tensor"))
        block.vector(make_body("vector"))
        block.scalar(make_body("scalar"))
        block.gpsimd(make_body("gpsimd"))

    return nc


def _prep_inputs(inputs):
    f = lambda a: np.ascontiguousarray(a, dtype=np.float32)
    g = {}
    g["aWl1"] = f(inputs["L0_attn_a_Wl1"]); g["aWl2"] = f(inputs["L0_attn_a_Wl2"])
    g["aWf1"] = f(inputs["L0_attn_a_Wf1"]); g["aWf2"] = f(inputs["L0_attn_a_Wf2"])
    g["aW1"] = f(inputs["L0_attn_a_W1"]); g["aW2"] = f(inputs["L0_attn_a_W2"])
    g["aW3"] = f(inputs["L0_attn_a_W3"])
    aB = np.zeros((128, 26), np.float32)
    aB[:, 0] = inputs["L0_attn_a_bl1"]
    aB[:, 1:9] = np.asarray(inputs["L0_attn_a_bl2"]).reshape(8, 128).T
    aB[:, 9] = inputs["L0_attn_a_bf1"]
    aB[:, 10:18] = np.asarray(inputs["L0_attn_a_bf2"]).reshape(8, 128).T
    aB[:, 18:26] = np.asarray(inputs["L0_attn_a_b1"]).reshape(8, 128).T
    g["aBpp"] = aB
    g["aB23"] = f(np.stack([inputs["L0_attn_a_b2"], inputs["L0_attn_a_b3"]], axis=1))
    g["aA"] = f(np.asarray(inputs["L0_attn_a_A"]).reshape(8, 128, 16).transpose(1, 0, 2).reshape(128, 128))
    g["aST"] = f(np.asarray(inputs["L0_attn_a_state_init"]).reshape(8, 128, 16)
                 .transpose(1, 0, 2).reshape(128, 128))
    g["bIpjT"] = f(np.asarray(inputs["L0_attn_b_in_proj_weight"]).T)
    g["bOpjT"] = f(np.asarray(inputs["L0_attn_b_out_proj_weight"]).T)
    bG = np.zeros((128, 64), np.float32)
    bG[:, 0:24] = np.asarray(inputs["L0_attn_b_gamma"]).reshape(24, 128).T
    bG[:, 24:48] = np.asarray(inputs["L0_attn_b_in_proj_bias"]).reshape(24, 128).T
    bG[:, 48:56] = np.asarray(inputs["L0_attn_b_linear_alpha"]).reshape(8, 128).T
    bG[:, 56:64] = np.asarray(inputs["L0_attn_b_linear_gamma"]).reshape(8, 128).T
    g["bGpp"] = bG
    g["bOpb"] = f(np.asarray(inputs["L0_attn_b_out_proj_bias"]).reshape(8, 128).T)
    g["fGwT"] = f(np.asarray(inputs["L0_ffn_gate_w"]).T)
    g["fW1T"] = f(np.asarray(inputs["L0_ffn_w1"]).transpose(0, 2, 1))
    g["fW2T"] = f(np.asarray(inputs["L0_ffn_w2"]).transpose(0, 2, 1))
    g["fW3T"] = f(np.asarray(inputs["L0_ffn_w3"]).transpose(0, 2, 1))
    g["fSw1T"] = f(np.asarray(inputs["L0_ffn_sw1"]).T)
    g["fSw2T"] = f(np.asarray(inputs["L0_ffn_sw2"]).T)
    g["fSw3T"] = f(np.asarray(inputs["L0_ffn_sw3"]).T)
    fB = np.zeros((128, 192), np.float32)
    fB[:, 0:64] = np.asarray(inputs["L0_ffn_b1"]).reshape(8, 8, 128).transpose(2, 0, 1).reshape(128, 64)
    fB[:, 64:128] = np.asarray(inputs["L0_ffn_b2"]).reshape(8, 8, 128).transpose(2, 0, 1).reshape(128, 64)
    fB[:, 128:192] = np.asarray(inputs["L0_ffn_b3"]).reshape(8, 8, 128).transpose(2, 0, 1).reshape(128, 64)
    g["fBpp"] = fB
    g["c_ident"] = np.eye(128, dtype=np.float32)
    cc = np.zeros((128, 8), np.float32)
    cc[:, 0] = EPS; cc[:, 1] = 1.0; cc[:, 2] = -1.0; cc[:, 3] = -0.5; cc[:, 4] = 0.125
    g["c_const"] = cc
    sel16 = np.zeros((16, 2048), np.float32)
    for n in range(16):
        sel16[n, n * 128:(n + 1) * 128] = 1.0
    g["c_sel16"] = sel16.astype(ml_dtypes.bfloat16)
    sel8 = np.zeros((8, 1024), np.float32)
    for e in range(8):
        sel8[e, e * 128:(e + 1) * 128] = 1.0
    g["c_sel8"] = sel8
    e16 = np.zeros((128, 128), np.float32)
    for cc in range(8):
        for k in range(128):
            e16[k, cc * 16 + 2 * cc + (k // 64)] = 1.0
    g["c_E16"] = e16.astype(ml_dtypes.bfloat16)
    g["c_E16f"] = e16.astype(np.float32)
    f16 = np.zeros((16, 1024), np.float32)
    for cc in range(8):
        for m in range(128):
            f16[2 * cc + (m // 64), cc * 128 + m] = 1.0
    g["c_F16"] = f16.astype(ml_dtypes.bfloat16)
    return g


def kernel(**inputs):
    x = np.asarray(inputs["L0_attn_a_x"])
    B, L, _ = x.shape
    shared = _prep_inputs(inputs)
    nc = build_program(_UPTO, _SUB)
    in_maps = []
    for c in range(8):
        xc = np.ascontiguousarray(x[:, c * 512:(c + 1) * 512, :].reshape(B * 512, D).T)
        m = dict(shared)
        m["x_fm"] = xc
        in_maps.append(m)
    res = run_bass_kernel_spmd(nc, in_maps, core_ids=list(range(8)))
    out = np.empty((B, L, D), np.float32)
    for c in range(8):
        out[:, c * 512:(c + 1) * 512, :] = res.results[c]["out_fm"].T.reshape(B, 512, D)
    return out
